# version 11
# baseline (speedup 1.0000x reference)
import numpy as np
import ml_dtypes
import concourse.bass as bass
import concourse.mybir as mybir
from concourse.bass import IndirectOffsetOnAxis
from concourse.bass_utils import run_bass_kernel_spmd
from concourse import library_config as _LC

F32 = mybir.dt.float32
BF16 = mybir.dt.bfloat16
I32 = mybir.dt.int32
AF = mybir.ActivationFunctionType
ALU = mybir.AluOpType
AX = mybir.AxisListType
BF = ml_dtypes.bfloat16

ENGINES = ("sync", "scalar", "vector", "tensor", "gpsimd")

NCORE = 8
SEQ = 2048
DM = 1024
NH = 16
DH = 64
HID = 3072
NHC = HID // 128
NE = 8
EH = 4096
CAP = 512
NT = SEQ // 128
NCH = DM // 128
THETA = 10000.0
LN_EPS = 1e-5
RMS_EPS = 1.1920929e-07
QSC = 1.0 / 127.0
N_BISECT = 34
OOB_BIG = 1.0e7


class Chan:
    __slots__ = ("name", "sem", "promised")
    def __init__(self, nc, name):
        self.name = name
        self.sem = nc.alloc_semaphore(name)
        self.promised = 0
    def last(self):
        return (self, self.promised)


class Prog:
    def __init__(self, nc):
        self.nc = nc
        self.streams = {e: [] for e in ENGINES}
        self.n_ops = 0

    def chan(self, name):
        return Chan(self.nc, name)

    def emit(self, eng, fn, wait=(), inc=(), tag="", raw_inc=False, multi=False):
        wait = tuple(wait)
        inc = tuple(inc)
        for ch, v in wait:
            assert v <= ch.promised, (
                f"wait {ch.name}>={v} but promised {ch.promised} ({tag})")
        for ch, k in inc:
            ch.promised += k
        if multi and eng not in ("sync", "tensor"):
            self.streams[eng].append((wait, fn, (), tag))
            self.n_ops += 1
            rest = inc
        else:
            self.streams[eng].append((wait, fn, inc[:1], tag))
            self.n_ops += 1
            rest = inc[1:]
        for ch, k in rest:
            self.streams[eng].append(((), (lambda e: e.drain()), ((ch, k),), tag + "+i"))
            self.n_ops += 1

    def dma(self, eng, out, in_, inc=(), wait=(), tag="", **kw):
        self.emit(eng, (lambda e, o=out, i=in_, k=kw: e.dma_start(out=o, in_=i, **k)),
                  wait=wait, inc=inc, tag=tag, raw_inc=True)

    def finalize(self):
        nc = self.nc
        with nc.Block() as block:
            def runner(stream):
                def run(engine):
                    for wait, fn, inc, tag in stream:
                        for ch, v in wait:
                            engine.wait_ge(ch.sem, v)
                        try:
                            ins = fn(engine)
                        except Exception:
                            import traceback as _tb
                            _tb.print_exc()
                            raise
                        if inc:
                            assert ins is not None, f"inc w/o instruction ({tag})"
                            for ch, k in inc:
                                ins = ins.then_inc(ch.sem, k)
                return run
            block.sync(runner(self.streams["sync"]))
            block.scalar(runner(self.streams["scalar"]))
            block.vector(runner(self.streams["vector"]))
            block.tensor(runner(self.streams["tensor"]))
            block.gpsimd(runner(self.streams["gpsimd"]))



def rope_perm():
    perm = np.zeros(DM, np.int64)
    for h in range(NH):
        for k in range(DH):
            i, s_ = k // 2, k % 2
            perm[h * DH + (i if s_ == 0 else 32 + i)] = h * DH + k
    return perm


def rope_tables():
    t = np.arange(SEQ, dtype=np.float64)
    i = np.arange(32, dtype=np.float64)
    ang = np.outer(i / THETA, t)
    cos = np.cos(ang); sin = np.sin(ang)
    T1 = np.tile(np.concatenate([cos, cos], 0), (1, 2))
    T2 = np.tile(np.concatenate([-sin, sin], 0), (1, 2))
    return T1.astype(BF), T2.astype(BF)


def host_prepare(inp):
    perm = rope_perm()
    g = lambda n: np.asarray(inp[n])
    d = {}
    d["WqT"] = np.ascontiguousarray(g("L0_attn_Wq")[perm, :].T.astype(np.float32))
    d["WkT"] = np.ascontiguousarray(g("L0_attn_Wk")[perm, :].T.astype(np.float32))
    d["WvT"] = np.ascontiguousarray(g("L0_attn_Wv").T.astype(np.float32))
    d["WoT"] = np.ascontiguousarray(g("L0_attn_Wo").T.astype(np.float32))
    d["bq8"] = np.ascontiguousarray(g("L0_attn_bq")[perm].reshape(NCH, 128).T.astype(np.float32))
    d["bk8"] = np.ascontiguousarray(g("L0_attn_bk")[perm].reshape(NCH, 128).T.astype(np.float32))
    d["bv_row"] = g("L0_attn_bv").reshape(1, DM).astype(np.float32)
    d["bo_row"] = g("L0_attn_bo").reshape(1, DM).astype(np.float32)
    d["wgT"] = np.ascontiguousarray(g("L0_ffn_w_gate").T.astype(np.int32))
    d["wuT"] = np.ascontiguousarray(g("L0_ffn_w_up").T.astype(np.int32))
    d["wdT"] = np.ascontiguousarray(g("L0_ffn_w_down").T.astype(np.int32))
    d["W2qT"] = np.ascontiguousarray(g("L1_attn_w_q").T.astype(np.float32))
    d["W2kT"] = np.ascontiguousarray(g("L1_attn_w_k").T.astype(np.float32))
    d["W2vT"] = np.ascontiguousarray(g("L1_attn_w_v").T.astype(np.float32))
    d["W2oT"] = np.ascontiguousarray(g("L1_attn_w_o").T.astype(np.float32))
    d["gateW"] = np.ascontiguousarray(g("L1_ffn_gate_w").astype(np.float32))
    d["w1"] = np.ascontiguousarray(g("L1_ffn_w1").astype(np.float32))
    d["w2"] = np.ascontiguousarray(g("L1_ffn_w2").astype(np.float32))
    T1, T2 = rope_tables()
    d["cT1"] = T1; d["cT2"] = T2
    d["ident_bf"] = np.eye(128).astype(BF)
    d["ident_f32"] = np.eye(128, dtype=np.float32)
    d["tri_bf"] = (np.arange(128)[None, :] >= np.arange(128)[:, None]).astype(BF)
    d["ones_bf"] = np.ones((128, 128), BF)
    d["iota512"] = np.tile(np.arange(512, dtype=np.float32), (128, 1))
    d["pcol"] = np.arange(128, dtype=np.float32).reshape(128, 1)
    return d



class Build:
    def __init__(self, stop_after=None):
        nc = bass.Bass("TRN2", target_bir_lowering=False)
        self.nc = nc
        self.p = Prog(nc)
        self.stop_after = stop_after
        self.stopped = False
        self._mk_dram()
        self._mk_sbuf()
        self._mk_psum()
        self._mk_chans()

    def _mk_dram(self):
        nc = self.nc
        D = {}
        def ein(name, shape, dt):
            D[name] = nc.dram_tensor(name, list(shape), dt, kind="ExternalInput").ap()
        ein("x", (2, SEQ, DM), F32)
        for n in ("WqT", "WkT", "WvT", "WoT", "W2qT", "W2kT", "W2vT", "W2oT"):
            ein(n, (DM, DM), F32)
        ein("bq8", (128, NCH), F32); ein("bk8", (128, NCH), F32)
        ein("bv_row", (1, DM), F32); ein("bo_row", (1, DM), F32)
        ein("wgT", (DM, HID), I32); ein("wuT", (DM, HID), I32); ein("wdT", (HID, DM), I32)
        ein("gateW", (DM, NE), F32)
        ein("w1", (NE, DM, EH), F32); ein("w2", (NE, EH, DM), F32)
        ein("cT1", (64, 2 * SEQ), BF16); ein("cT2", (64, 2 * SEQ), BF16)
        ein("ident_bf", (128, 128), BF16); ein("ident_f32", (128, 128), F32)
        ein("tri_bf", (128, 128), BF16); ein("ones_bf", (128, 128), BF16)
        ein("iota512", (128, 512), F32); ein("pcol", (128, 1), F32)
        D["y"] = nc.dram_tensor("y", [2, SEQ, DM], F32, kind="ExternalOutput").ap()
        D["dbg"] = nc.dram_tensor("dbg", [128, NT * DM], F32, kind="ExternalOutput").ap()
        D["r_spill"] = nc.dram_tensor("r_spill", [SEQ, DM], F32, kind="Internal").ap()
        D["r3A"] = nc.dram_tensor("r3A", [SEQ, DM], F32, kind="Internal").ap()
        D["xmoe"] = [nc.dram_tensor("xmoe0", [SEQ, DM], BF16, kind="Internal").ap(),
                     nc.dram_tensor("xmoe1", [SEQ, DM], BF16, kind="Internal").ap()]
        D["ymoe"] = [nc.dram_tensor("ymoe0", [SEQ, DM], F32, kind="Internal").ap(),
                     nc.dram_tensor("ymoe1", [SEQ, DM], F32, kind="Internal").ap()]
        D["rows"] = nc.dram_tensor("rows", [2, NE, 3, 512], F32, kind="Internal").ap()
        D["wg_c"] = nc.dram_tensor("wg_c", [NHC, 128, NCH * 128], BF16, kind="Internal").ap()
        D["wu_c"] = nc.dram_tensor("wu_c", [NHC, 128, NCH * 128], BF16, kind="Internal").ap()
        D["wd_c"] = nc.dram_tensor("wd_c", [NHC, 128, DM], BF16, kind="Internal").ap()
        D["w1_c"] = nc.dram_tensor("w1_c", [NE, NCH, 128, EH], BF16, kind="Internal").ap()
        D["w2_c"] = nc.dram_tensor("w2_c", [NE, EH // 128, 128, DM], BF16, kind="Internal").ap()
        self.D = D

    def _mk_sbuf(self):
        base = (self.nc.sbuf_base + 127) // 128 * 128
        at = self.nc.alloc_sbuf_tensor_at
        S = {}
        def A(name, shape, dt, offset):
            assert offset % 32 == 0, (name, offset)
            S[name] = at(name, list(shape), dt, offset=base + offset).ap()
        A("r", [128, NT * DM], F32, 0)
        A("q_fm", [128, NCH * SEQ], BF16, 0)
        A("k_fm", [128, NCH * SEQ], BF16, 32768)
        A("T1", [64, 2 * SEQ], BF16, 49152)
        A("T2", [64, 2 * SEQ], BF16, 57344)
        A("x_T", [128, NCH * SEQ], BF16, 65536)
        A("xin_tok", [128, 4 * DM], BF16, 65536)
        A("xin_T", [128, NCH * 512], BF16, 65536 + 8192)
        A("gidxf", [128, 4], F32, 65536 + 16384)
        A("gidxg", [128, 4], F32, 65536 + 16576)
        A("gidx", [128, 4], I32, 65536 + 16448)
        A("pcomp", [128, 4], F32, 65536 + 16512)
        A("V", [128, NT * NH * 65], BF16, 98304)
        A("h_grp", [128, NHC * 512], BF16, 98304)
        A("attn_fm", [128, NCH * SEQ], BF16, 131584)
        A("h_e", [128, (EH // 128) * 512], BF16, 131584)
        A("P_buf", [128, 8 * 512], BF16, 164352)
        A("mc_st", [128, 2, DM], F32, 164352)
        A("gate_t", [128, 512], BF16, 164352 + 4096)
        A("up_t", [128, 512], BF16, 164352 + 5120)
        A("y_ev4", [128, 4, 512], F32, 164352)
        A("prb8", [128, 8], F32, 164352 + 8192)
        A("ln6", [128, 12], F32, 164352 + 8224)
        A("ln2", [128, 2], F32, 164352 + 8288)
        A("lncol", [128, 3], F32, 164352 + 8320)
        A("rms16", [128, 16], F32, 164352 + 8352)
        A("rms16b", [128, 16], F32, 164352 + 8416)
        QKB = 164352 + 8512
        A("mc_cb", [128, 2, DM], BF16, QKB)
        A("Q_store", [65, 2 * SEQ], BF16, QKB)
        A("K_store", [65, 2 * SEQ], BF16, QKB + 8192)
        A("w_str", [128, NCH * DM], BF16, QKB)
        A("w1col", [128, 2, NCH * 512], BF16, QKB)
        A("probs_et", [NE, SEQ], F32, QKB)
        A("mask_et", [NE, SEQ], BF16, QKB + 8192)
        A("w2_str", [128, 2, 512], BF16, 65536 + 17408)
        A("sc_rows", [10, 512], F32, QKB + 12288)
        A("srch", [NE, 8], F32, QKB + 14336)
        A("gw_bf", [128, NCH * 8], BF16, QKB + 14368)
        A("bigc", [128, 8], F32, QKB + 14496)
        A("srch_i", [NE, 2], I32, QKB + 14528)
        A("wg_hc", [128, 2, NCH * 128], BF16, QKB)
        A("wu_hc", [128, 2, NCH * 128], BF16, QKB + 4096)
        A("wd_hc", [128, 2, 512], BF16, QKB + 8192)
        TMB = 164352 + 8512 + 16384
        A("stage", [128, 2, DM], F32, TMB)
        A("stage_i", [128, 2, DM], I32, TMB)
        A("SWQ", [64, SEQ], BF16, TMB + 8192)
        A("cast_t2", [128, DM], BF16, TMB + 8192)
        A("tanhb", [128, 512], F32, TMB + 8192)
        A("iota512", [128, 512], F32, TMB + 8192)
        A("scat_all", [128, 128], F32, TMB + 8192 + 2048)
        A("maskT_sb", [128, 128], BF16, TMB + 8192 + 2560)
        A("idx_t", [128, 8], F32, TMB + 8192 + 2816)
        A("idx_t2", [128, 8], F32, TMB + 8192 + 2848)
        A("idx_m", [128, 8], I32, TMB + 8192 + 2880)
        A("qk_w", [128, NCH * 128], BF16, TMB + 12288)
        A("cast_t", [128, DM], BF16, TMB + 14336)
        S["att_t"] = S["cast_t"][0:64, 0:512]
        A("inv_t", [1, 512], F32, TMB + 16384)
        A("row_bf", [1, DM], BF16, TMB + 18432)
        off = TMB + 20480
        def C(name, shape, dt, nbytes):
            nonlocal off
            assert off % 32 == 0, (name, off)
            S[name] = at(name, list(shape), dt, offset=base + off).ap()
            off += (nbytes + 31) // 32 * 32
        C("ident_bf", (128, 128), BF16, 256)
        C("tri_bf", (128, 128), BF16, 256)
        C("ones_bf", (128, 128), BF16, 256)
        C("pcol", (128, 1), F32, 4)
        C("eps2", (128, 2), F32, 8)
        S["eps_ln"] = S["eps2"][:, 0:1]
        S["eps_rms"] = S["eps2"][:, 1:2]
        C("ones_col", (128, 1), BF16, 4)
        C("ones1f", (1, 64), F32, 256)
        C("bq8", (128, NCH), F32, 32)
        C("bk8", (128, NCH), F32, 32)
        C("probsT", (128, 2, NT * 10), BF16, 2 * NT * 10 * 2)
        C("ident_f32", (128, 128), F32, 512)
        C("kmax", (1, 2), F32, 8)
        assert base + off <= self.nc.sbuf_top, (base, off, self.nc.sbuf_top)
        self.S = S

    def _mk_psum(self):
        nc = self.nc
        P = {}
        for n in ("psA", "psB", "psC", "psD", "psE", "psF", "psU"):
            P[n] = nc.alloc_psum_tensor(n, [128, 512], F32).ap()
        P["psT"] = nc.alloc_psum_tensor("psT", [128, 1024], BF16).ap()
        self.PS = P

    def _mk_chans(self):
        p = self.p
        self.c = {}
        names = ["ld", "st", "wA", "cA", "wB", "cB", "mm", "ev", "vx", "gx",
                 "gdma", "fin", "bar_s", "bar_c", "bar_v", "bar_t", "bar_g",
                 "wA0", "wA1", "wB0", "wB1", "ld0", "ld1", "gd0", "gd1",
                 "mcA", "mcB", "mcC", "mcD"]
        names += [f"f{b}" for b in ("A", "B", "C", "D", "E", "F", "U", "T")]
        for n in names:
            self.c[n] = p.chan(n)

    def v(self, fn, wait=(), inc=(), tag="", multi=False):
        self.p.emit("vector", fn, wait, inc, tag, multi=multi)
    def s(self, fn, wait=(), inc=(), tag="", multi=False):
        self.p.emit("scalar", fn, wait, inc, tag, multi=multi)
    def t(self, fn, wait=(), inc=(), tag=""): self.p.emit("tensor", fn, wait, inc, tag)
    def g(self, fn, wait=(), inc=(), tag="", raw_inc=False, multi=False):
        self.p.emit("gpsimd", fn, wait, inc, tag, raw_inc=raw_inc, multi=multi)
    def vdrain(self, wait=(), inc=()): self.v(lambda v: v.drain(), wait, inc)
    def sdrain(self, wait=(), inc=()): self.s(lambda s: s.drain(), wait, inc)
    def gdrain(self, wait=(), inc=()): self.g(lambda g: g.drain(), wait, inc)

    def bank(self, n):
        return self.PS["ps" + n], self.c["f" + n]

    def barrier(self):
        c = self.c
        pairs = [("sync", "bar_s"), ("scalar", "bar_c"), ("vector", "bar_v"),
                 ("tensor", "bar_t"), ("gpsimd", "bar_g")]
        for eng, ch in pairs:
            self.p.emit(eng, lambda e: e.nop(), inc=[(c[ch], 1)], tag="bar", multi=True)
        toks = [c[ch].last() for _, ch in pairs]
        for eng, _ in pairs:
            self.p.emit(eng, lambda e: e.nop(), wait=toks, tag="barw")

    def phase_done(self, name):
        if self.stop_after is not None and name == self.stop_after:
            self.stopped = True
            return True
        return False

    def dump_r_and_finish(self):
        self.barrier()
        c = self.c
        self.p.dma("sync", self.D["dbg"][:, :], self.S["r"], inc=[(c["fin"], 16)], tag="dbg")
        self.p.emit("sync", lambda e, v=c["fin"].promised: e.wait_ge(c["fin"].sem, v), tag="finw")

    def load_consts(self):
        p, S, D, c = self.p, self.S, self.D, self.c
        ld = c["ld"]
        for n in ("ident_bf", "ident_f32", "tri_bf", "ones_bf",
                  "pcol", "bq8", "bk8"):
            p.dma("sync", S[n], D[n][:, :], inc=[(ld, 16)], tag=f"cst_{n}")
        self.v(lambda v: v.memset(S["eps_ln"], LN_EPS))
        self.v(lambda v: v.memset(S["eps_rms"], RMS_EPS))
        self.v(lambda v: v.memset(S["ones_col"], 1.0))
        self.v(lambda v: v.memset(S["ones1f"], 1.0))
        self.vdrain(inc=[(c["vx"], 1)])
        self.g(lambda g: g.nop(), inc=[(c["gx"], 1)])
        self.p.emit("sync", lambda e, v=ld.promised: e.wait_ge(ld.sem, v), tag="cstw")
        self.barrier()

    def load_T12(self):
        p, S, D, c = self.p, self.S, self.D, self.c
        ld = c["ld"]
        p.dma("sync", S["T1"], D["cT1"][:, :], inc=[(ld, 16)], tag="T1")
        p.dma("sync", S["T2"], D["cT2"][:, :], inc=[(ld, 16)], tag="T2")
        p.emit("sync", lambda e, v=ld.promised: e.wait_ge(ld.sem, v), tag="T12w")
        self.barrier()

    def build_xT(self, src_tile_fn, tag="xT"):
        S, c = self.S, self.c
        psT, fT = self.bank("T")
        mm, ev, vx = c["mm"], c["ev"], c["vx"]
        cmm = []
        for t in range(NT):
            src, w = src_tile_fn(t)
            ct = S["cast_t"] if t % 2 == 0 else S["cast_t2"]
            self.v(lambda v, s_=src, ct_=ct: v.tensor_copy(out=ct_, in_=s_),
                   wait=list(w) + ([(mm, cmm[t - 2])] if t >= 2 else []),
                   inc=[(vx, 1)], tag=f"{tag}c{t}")
            for cc in range(NCH):
                self.t(lambda T, cc_=cc, ct_=ct: T.transpose(
                    psT[:, cc_ * 128:(cc_ + 1) * 128],
                    ct_[:, cc_ * 128:(cc_ + 1) * 128], S["ident_bf"]),
                    wait=[vx.last(), fT.last()] if cc == 0 else (),
                    inc=[(mm, 1)] if cc == NCH - 1 else (), tag=f"{tag}t{t}")
            cmm.append(mm.promised)
            self.s(lambda s, t_=t: s.copy(
                out=S["x_T"].rearrange("p (c q) -> p c q", c=NCH)[:, :, t_ * 128:(t_ + 1) * 128],
                in_=psT.rearrange("p (c q) -> p c q", c=NCH)),
                wait=[mm.last()], inc=[(ev, 1), (fT, 1)], tag=f"{tag}e{t}")
        return ev.last()

    def hbm_tile(self, dram2d, t, slot):
        S, c = self.S, self.c
        ld = c["ld0"] if slot == 0 else c["ld1"]
        sl = S["stage"][:, slot]
        self.p.dma("sync", sl, dram2d[t * 128:(t + 1) * 128, :], inc=[(ld, 16)],
                   wait=[c["vx"].last(), c["mm"].last(), c["ev"].last(),
                         c["cA"].last(), c["cB"].last()], tag=f"ht{t}")
        return sl, [ld.last()]

    def stream_w(self, dst, srcs, tag="w", eng="vector", chunk_elems=DM, src_int=False,
                 extra_wait=()):
        S, c = self.S, self.c
        ca = c["cA"]
        stage = S["stage_i"] if src_int else S["stage"]
        for i, src in enumerate(srcs):
            w = c["wA0"] if i % 2 == 0 else c["wA1"]
            sl = stage[:, i % 2, :chunk_elems]
            self.p.dma("sync", sl, src, inc=[(w, 16)],
                       wait=[w.last(), (ca, max(0, ca.promised - 1))] +
                            (list(extra_wait) if i == 0 else []),
                       tag=f"{tag}l{i}")
            self.p.emit(eng, lambda e, i_=i, sl_=sl: e.tensor_copy(
                out=dst[:, i_ * chunk_elems:(i_ + 1) * chunk_elems], in_=sl_),
                wait=[w.last()], inc=[(ca, 1)], tag=f"{tag}c{i}")
        return ca.last()

    def ln_tile(self, ytile, wait=()):
        S, c = self.S, self.c
        ev, vx = c["ev"], c["vx"]
        self.v(lambda v: v.bn_stats(S["ln6"][:, 0:6], ytile[:, 0:512]),
               wait=list(wait) + [c["ev"].last()], tag="lns0")
        self.v(lambda v: v.bn_stats(S["ln6"][:, 6:12], ytile[:, 512:1024]), tag="lns1")
        self.vdrain()
        self.v(lambda v: v.bn_aggr(S["ln2"], S["ln6"]), tag="lnagg")
        self.vdrain(inc=[(vx, 1)])
        self.s(lambda s: s.activation(S["lncol"][:, 0:1], S["ln2"][:, 1:2], AF.Sqrt,
                                      bias=S["eps_ln"], scale=1.0),
               wait=[vx.last()], inc=[(ev, 1)], tag="lnsqrt")
        self.v(lambda v: v.reciprocal(S["lncol"][:, 1:2], S["lncol"][:, 0:1]),
               wait=[ev.last()], tag="lnrec")
        self.vdrain()
        self.v(lambda v: v.tensor_scalar(S["lncol"][:, 2:3], S["ln2"][:, 0:1],
                                         S["lncol"][:, 1:2], -1.0, ALU.mult, ALU.mult),
               tag="lnnb")
        self.vdrain(inc=[(vx, 1)])
        self.s(lambda s: s.activation(ytile, ytile, AF.Identity,
                                      bias=S["lncol"][:, 2:3], scale=S["lncol"][:, 1:2]),
               wait=[vx.last()], inc=[(ev, 1)], tag="lnap")
        return ev.last()

    def l0_attn(self, s):
        S, D, c = self.S, self.D, self.c
        mm, ev, vx = c["mm"], c["ev"], c["vx"]

        def src_tile(t):
            return self.hbm_tile(D["x"][s], t, t % 2)
        self.build_xT(src_tile, tag=f"s{s}x0")
        self.barrier()

        self.p.dma("sync", S["stage"][0:1, 0, :], D["bv_row"][:, :],
                   wait=[c["cA"].last(), c["vx"].last()], inc=[(c["ld"], 16)], tag="bv")
        self.v(lambda v: v.tensor_copy(out=S["row_bf"], in_=S["stage"][0:1, 0, :]),
               wait=[c["ld"].last()], tag="bvc")
        self.vdrain(inc=[(c["vx"], 1)])
        self.stream_w(S["w_str"], [D["WvT"][i * 128:(i + 1) * 128, :] for i in range(NCH)],
                      extra_wait=[c["vx"].last()], tag=f"s{s}wv")
        self.v(lambda v: v.memset(
            S["V"].rearrange("p (t h u) -> p t h u", t=NT, h=NH)[:, :, :, 64:65], 1.0),
            tag="vones")
        self.vdrain(inc=[(vx, 1)])
        self.proj_tokmajor(dst_V=True, wq=None, bias_bf=S["row_bf"], tag=f"s{s}V")
        self.barrier()

        self.v(lambda v: v.memset(S["K_store"][64:65, :], 1.0), inc=[(vx, 1)], tag="kone")
        for hp in range(NH // 2):
            self.l0_hp(s, hp)
        self.barrier()

        self.wo_ln(res_dram=D["x"][s], wT=D["WoT"], bias_dram=D["bo_row"],
                   scale=1.0, tag=f"s{s}wo0")
        self.barrier()

    def proj_tokmajor(self, dst_V, wq, bias_bf, tag):
        S, c = self.S, self.c
        mm, ev = c["mm"], c["ev"]
        pa, fa = self.bank("A"); pb, fb = self.bank("B")
        pc, fc = self.bank("C"); pd, fd = self.bank("D")
        for t in range(NT):
            p0, f0, p1, f1 = (pa, fa, pb, fb) if t % 2 == 0 else (pc, fc, pd, fd)
            for half, (ph, fh) in enumerate(((p0, f0), (p1, f1))):
                last_h = (half == 1)
                for cc in range(NCH):
                    self.t(lambda T, t_=t, cc_=cc, h_=half, ps_=ph: T.matmul(
                        ps_[:, :],
                        S["x_T"][:, cc_ * SEQ + t_ * 128: cc_ * SEQ + (t_ + 1) * 128],
                        S["w_str"][:, cc_ * DM + h_ * 512: cc_ * DM + h_ * 512 + 512],
                        start=(cc_ == 0), stop=(bias_bf is None and cc_ == NCH - 1)),
                        wait=[c["cA"].last(), c["vx"].last(), fh.last()] if cc == 0 else (),
                        inc=[(mm, 1)] if (bias_bf is None and cc == NCH - 1 and last_h) else (),
                        tag=f"{tag}m{t}")
                if bias_bf is not None:
                    self.t(lambda T, h_=half, ps_=ph: T.matmul(
                        ps_[:, :], S["ones_bf"][0:1, :],
                        bias_bf[0:1, h_ * 512:(h_ + 1) * 512], start=False, stop=True),
                        inc=[(mm, 1)] if last_h else (), tag=f"{tag}b{t}")
            if dst_V:
                def ev_v(sc, t_=t, p0_=p0, p1_=p1):
                    dst = S["V"].rearrange("p (t h u) -> p t h u", t=NT, h=NH)[:, t_, :, 0:64]
                    sc.copy(out=dst[:, 0:8, :], in_=p0_.rearrange("p (h u) -> p h u", u=64))
                    return sc.copy(out=dst[:, 8:16, :], in_=p1_.rearrange("p (h u) -> p h u", u=64))
                self.s(ev_v, wait=[mm.last()], inc=[(f0, 1), (f1, 1)], tag=f"{tag}e{t}", multi=True)
                self.s(lambda sc: sc.nop(), inc=[(ev, 1)], tag=f"{tag}en{t}")

    def l0_hp(self, s, hp):
        S, D, c = self.S, self.D, self.c
        mm, ev, vx = c["mm"], c["ev"], c["vx"]
        cB = c["cB"]
        pa, fa = self.bank("A"); pb, fb = self.bank("B")
        for which, wname, store, b8 in ((1, "WkT", S["K_store"], S["bk8"]),
                                        (0, "WqT", S["Q_store"], S["bq8"])):
            for cc in range(NCH):
                i = which * NCH + cc
                wB = c["wB0"] if i % 2 == 0 else c["wB1"]
                sl = S["stage"][:, i % 2, :128]
                self.p.dma("sync", sl, D[wname][cc * 128:(cc + 1) * 128, hp * 128:(hp + 1) * 128],
                           inc=[(wB, 16)], wait=[wB.last(), (cB, max(0, cB.promised - 1))] +
                           ([mm.last()] if cc == 0 else []),
                           tag=f"hp{hp}wl")
                self.v(lambda v, sl_=sl, cc_=cc: v.tensor_copy(
                    out=S["qk_w"][:, cc_ * 128:(cc_ + 1) * 128], in_=sl_),
                    wait=[wB.last()], inc=[(cB, 1)], tag=f"hp{hp}wc")
            w_rdy = cB.last()
            for qg in range(4):
                ph, fh = (pa, fa) if qg % 2 == 0 else (pb, fb)
                for cc in range(NCH):
                    self.t(lambda T, cc_=cc, qg_=qg, ps_=ph: T.matmul(
                        ps_[:, :], S["qk_w"][:, cc_ * 128:(cc_ + 1) * 128],
                        S["x_T"][:, cc_ * SEQ + qg_ * 512: cc_ * SEQ + (qg_ + 1) * 512],
                        start=(cc_ == 0), stop=(cc_ == NCH - 1)),
                        wait=[w_rdy, fh.last()] if cc == 0 else (),
                        inc=[(mm, 1)] if cc == NCH - 1 else (), tag=f"hp{hp}p{which}{qg}")
                def ev_qk(sc, ps_=ph, st_=store, qg_=qg, b8_=b8, hp_=hp):
                    sc.activation(st_[0:64, qg_ * 512:(qg_ + 1) * 512], ps_[0:64, :],
                                  AF.Identity, bias=b8_[0:64, hp_:hp_ + 1], scale=1.0)
                    return sc.activation(st_[0:64, SEQ + qg_ * 512: SEQ + (qg_ + 1) * 512],
                                         ps_[64:128, :], AF.Identity,
                                         bias=b8_[64:128, hp_:hp_ + 1], scale=1.0)
                self.s(ev_qk, wait=[mm.last()], inc=[(ev, 1), (fh, 1)], tag=f"hp{hp}e{which}{qg}", multi=True)
            self.rope(store)
        self.l0_m_and_attn(s, hp)

    def rope(self, store):
        S, c = self.S, self.c
        vx, ev = c["vx"], c["ev"]
        SW = S["SWQ"]
        for sd in range(2):
            st = store[0:64, sd * SEQ:(sd + 1) * SEQ]
            self.v(lambda v, st_=st: v.tensor_copy(out=SW[0:32, :], in_=st_[32:64, :]),
                   wait=[ev.last(), vx.last()], tag="sw1")
            self.v(lambda v, st_=st: v.tensor_copy(out=SW[32:64, :], in_=st_[0:32, :]), tag="sw2")
            self.vdrain()
            self.v(lambda v, sd_=sd: v.tensor_tensor(SW, SW, S["T2"][:, sd_ * SEQ:(sd_ + 1) * SEQ],
                                                     ALU.mult), tag="swT2")
            self.v(lambda v, st_=st, sd_=sd: v.tensor_tensor(
                st_, st_, S["T1"][:, sd_ * SEQ:(sd_ + 1) * SEQ], ALU.mult), tag="qT1")
            self.vdrain()
            self.v(lambda v, st_=st: v.tensor_tensor(st_, st_, SW, ALU.add), tag="ropea")
            self.vdrain(inc=[(vx, 1)])

    def l0_m_and_attn(self, s, hp):
        S, c = self.S, self.c
        mm, ev, vx = c["mm"], c["ev"], c["vx"]
        pg, fg = self.PS["psU"], self.c["fU"]
        SW = S["SWQ"]
        self.v(lambda v: v.memset(S["kmax"], 0.0), wait=[vx.last()], tag="km0")
        self.vdrain(inc=[(vx, 1)])
        for side in range(2):
            self.v(lambda v, sd=side: v.tensor_tensor(
                SW, S["K_store"][0:64, sd * SEQ:(sd + 1) * SEQ],
                S["K_store"][0:64, sd * SEQ:(sd + 1) * SEQ], ALU.mult),
                wait=[vx.last()], tag="KK")
            self.vdrain(inc=[(vx, 1)])
            for seg in range(4):
                self.t(lambda T, sg=seg: T.matmul(
                    pg[0:1, :], S["ones_col"][0:64, :],
                    SW[:, sg * 512:(sg + 1) * 512], start=True, stop=True),
                    wait=[vx.last(), fg.last()], inc=[(mm, 1)], tag="kn2")
                self.v(lambda v: v.reduce_max(S["inv_t"][0:1, 0:1], pg[0:1, :], AX.X),
                       wait=[mm.last()], tag="knm")
                self.vdrain()
                self.v(lambda v, sd=side: v.tensor_max(
                    S["kmax"][0:1, sd:sd + 1], S["kmax"][0:1, sd:sd + 1],
                    S["inv_t"][0:1, 0:1]), inc=[(vx, 1), (fg, 1)], tag="knc")
                self.vdrain()
        self.s(lambda sc: sc.activation(S["kmax"], S["kmax"], AF.Sqrt, bias=0.0, scale=1.0),
               wait=[vx.last()], inc=[(ev, 1)], tag="kms")
        for side in range(2):
            self.v(lambda v, sd=side: v.tensor_tensor(
                SW, S["Q_store"][0:64, sd * SEQ:(sd + 1) * SEQ],
                S["Q_store"][0:64, sd * SEQ:(sd + 1) * SEQ], ALU.mult),
                wait=[ev.last(), vx.last()], tag="QQ")
            self.vdrain(inc=[(vx, 1)])
            for seg in range(4):
                self.t(lambda T, sg=seg: T.matmul(
                    pg[0:1, :], S["ones_col"][0:64, :],
                    SW[:, sg * 512:(sg + 1) * 512], start=True, stop=True),
                    wait=[vx.last(), fg.last()], inc=[(mm, 1)], tag="qn2")
                self.s(lambda sc: sc.activation(S["inv_t"], pg[0:1, :], AF.Sqrt,
                                                bias=0.0, scale=1.0),
                       wait=[mm.last()], inc=[(ev, 1)], tag="qns")
                self.v(lambda v, sd=side, sg=seg: v.tensor_scalar(
                    S["Q_store"][64:65, sd * SEQ + sg * 512: sd * SEQ + (sg + 1) * 512],
                    S["inv_t"], S["kmax"][0:1, sd:sd + 1], -1.0, ALU.mult, ALU.mult),
                    wait=[ev.last()], inc=[(vx, 1), (fg, 1)], tag="qrow")
                self.vdrain()
        prev = None
        ix = 0
        for side in range(2):
            h = 2 * hp + side
            for qg in range(4):
                cur = self.attn_core(h, qg, side, causal=False, tag=f"h{h}g{qg}", pv_ix=ix)
                if prev is not None:
                    self.attn_tail(prev)
                prev = cur
                ix += 1
        self.attn_tail(prev)

    def attn_core(self, h, qg, side, causal, tag, pv_ix):
        S, c = self.S, self.c
        mm, ev, vx = c["mm"], c["ev"], c["vx"]
        rot = [self.bank("A"), self.bank("B"), self.bank("E"), self.bank("F")]
        pc, fc = self.bank("C") if pv_ix % 2 == 0 else self.bank("D")
        kt_hi = NT if not causal else min(NT, (qg + 1) * 4)
        ch_ = h // 2
        ev_cnt = {}
        vx_cnt = {}
        for kt0 in range(0, kt_hi, 8):
            kts = range(kt0, min(kt0 + 8, kt_hi))
            for kt in kts:
                ph, fh = rot[kt % 4]
                qlo = 0 if not causal else max(0, kt * 128 - qg * 512)
                if not causal:
                    lhsT = S["K_store"][0:65, side * SEQ + kt * 128: side * SEQ + (kt + 1) * 128]
                    rhs = S["Q_store"][0:65, side * SEQ + qg * 512: side * SEQ + (qg + 1) * 512]
                else:
                    lhsT = S["k_fm"][side * 64:side * 64 + 64,
                                     ch_ * SEQ + kt * 128: ch_ * SEQ + (kt + 1) * 128]
                    rhs = S["q_fm"][side * 64:side * 64 + 64,
                                    ch_ * SEQ + qg * 512 + qlo: ch_ * SEQ + (qg + 1) * 512]
                self.t(lambda T, l_=lhsT, r_=rhs, ps_=ph, q_=qlo: T.matmul(
                    ps_[:, q_:512], l_, r_, start=True, stop=True),
                    wait=[fh.last()], inc=[(mm, 1)], tag=f"{tag}s{kt}")
                sl = (kt % 8) * 512
                if not causal:
                    self.s(lambda sc, ps_=ph, sl_=sl: sc.activation(
                        S["P_buf"][:, sl_:sl_ + 512], ps_[:, :], AF.Exp,
                        bias=0.0, scale=0.125),
                        wait=[mm.last()], inc=[(ev, 1), (fh, 1)], tag=f"{tag}x{kt}")
                else:
                    self.s(lambda sc, ps_=ph, q_=qlo: sc.activation(
                        S["tanhb"][:, q_:512], ps_[:, q_:512], AF.Tanh,
                        bias=0.0, scale=1.0 / (8.0 * 50.0)), wait=[mm.last()], tag=f"{tag}t{kt}")
                    self.sdrain()
                    self.s(lambda sc, sl_=sl, q_=qlo: sc.activation(
                        S["P_buf"][:, sl_ + q_:sl_ + 512], S["tanhb"][:, q_:512],
                        AF.Exp, bias=0.0, scale=50.0),
                        inc=[(ev, 1), (fh, 1)], tag=f"{tag}x{kt}")
                ev_cnt[kt] = ev.promised
                if causal:
                    if qlo > 0:
                        self.v(lambda v, sl_=sl, q_=qlo: v.memset(
                            S["P_buf"][:, sl_:sl_ + q_], 0.0),
                            wait=[(ev, ev_cnt[kt])], tag=f"{tag}z{kt}")
                        self.vdrain(inc=[(vx, 1)])
                    if qg * 512 <= kt * 128 < (qg + 1) * 512:
                        off = sl + (kt * 128 - qg * 512)
                        self.v(lambda v, o_=off: v.tensor_tensor(
                            S["P_buf"][:, o_:o_ + 128], S["P_buf"][:, o_:o_ + 128],
                            S["tri_bf"], ALU.mult), wait=[(ev, ev_cnt[kt])], tag=f"{tag}d{kt}")
                        self.vdrain(inc=[(vx, 1)])
                vx_cnt[kt] = vx.promised
            for kt in kts:
                self.t(lambda T, kt_=kt, pc_=pc: T.matmul(
                    pc_[0:65, :],
                    S["V"][:, kt_ * NH * 65 + h * 65: kt_ * NH * 65 + h * 65 + 65],
                    S["P_buf"][:, (kt_ % 8) * 512:(kt_ % 8) * 512 + 512],
                    start=(kt_ == 0), stop=(kt_ == kt_hi - 1)),
                    wait=[(ev, ev_cnt[kt]), (vx, vx_cnt[kt])] + ([fc.last()] if kt == 0 else []),
                    inc=[(mm, 1)] if kt == kt_hi - 1 else (), tag=f"{tag}pv{kt}")
        return (h, qg, side, pc, fc, tag, mm.promised)

    def attn_tail(self, ctx):
        S, c = self.S, self.c
        mm, ev, vx = c["mm"], c["ev"], c["vx"]
        h, qg, side, pc, fc, tag, mm_pv = ctx
        pu, fu = self.bank("U")
        ch_ = h // 2
        self.v(lambda v: v.reciprocal(S["inv_t"], pc[64:65, :]),
               wait=[(mm, mm_pv)], tag=f"{tag}rc")
        self.vdrain(inc=[(vx, 1)])
        vx_rc = vx.promised
        self.t(lambda T: T.matmul(pu[0:64, :], S["ones1f"], S["inv_t"],
                                  start=True, stop=True),
               wait=[(vx, vx_rc), fu.last()], inc=[(mm, 1)], tag=f"{tag}r1")
        mm_r1 = mm.promised
        self.s(lambda sc: sc.copy(out=S["att_t"], in_=pc[0:64, :]),
               wait=[(mm, mm_pv), (vx, vx_rc)], inc=[(ev, 1), (fc, 1)], tag=f"{tag}ae")
        self.v(lambda v: v.tensor_tensor(
            S["attn_fm"][side * 64:side * 64 + 64,
                         ch_ * SEQ + qg * 512:(ch_) * SEQ + (qg + 1) * 512],
            S["att_t"], pu[0:64, :], ALU.mult),
            wait=[ev.last(), (mm, mm_r1)], inc=[(vx, 1), (fu, 1)], tag=f"{tag}nm")
        self.vdrain()

    def wo_ln(self, res_dram, wT, bias_dram, scale, tag, spill_dram=None):
        S, D, c = self.S, self.D, self.c
        mm, ev, vx = c["mm"], c["ev"], c["vx"]
        if bias_dram is not None:
            self.p.dma("sync", S["stage"][0:1, 0, :], bias_dram[:, :],
                       wait=[c["cA"].last(), c["vx"].last()], inc=[(c["ld"], 16)], tag=f"{tag}b")
            self.v(lambda v: v.tensor_copy(out=S["row_bf"], in_=S["stage"][0:1, 0, :]),
                   wait=[c["ld"].last()], tag=f"{tag}bc")
            self.vdrain(inc=[(vx, 1)])
        self.stream_w(S["w_str"], [wT[i * 128:(i + 1) * 128, :] for i in range(NCH)],
                      extra_wait=[vx.last()], tag=f"{tag}w")
        pa, fa = self.bank("A"); pb, fb = self.bank("B")
        pc, fc = self.bank("C"); pd, fd = self.bank("D")
        for t in range(NT):
            p0, f0, p1, f1 = (pa, fa, pb, fb) if t % 2 == 0 else (pc, fc, pd, fd)
            res_ap, res_w = self.hbm_tile(res_dram, t, t % 2)
            for half, (ph, fh) in enumerate(((p0, f0), (p1, f1))):
                last_h = (half == 1)
                for cc in range(NCH):
                    self.t(lambda T, t_=t, cc_=cc, h_=half, ps_=ph: T.matmul(
                        ps_[:, :],
                        S["attn_fm"][:, cc_ * SEQ + t_ * 128: cc_ * SEQ + (t_ + 1) * 128],
                        S["w_str"][:, cc_ * DM + h_ * 512: cc_ * DM + h_ * 512 + 512],
                        start=(cc_ == 0), stop=(bias_dram is None and cc_ == NCH - 1)),
                        wait=[c["cA"].last(), c["vx"].last(), fh.last()] if cc == 0 else (),
                        inc=[(mm, 1)] if (bias_dram is None and cc == NCH - 1 and last_h) else (),
                        tag=f"{tag}m{t}")
                if bias_dram is not None:
                    self.t(lambda T, h_=half, ps_=ph: T.matmul(
                        ps_[:, :], S["ones_bf"][0:1, :],
                        S["row_bf"][0:1, h_ * 512:(h_ + 1) * 512], start=False, stop=True),
                        inc=[(mm, 1)] if last_h else (), tag=f"{tag}bb{t}")
            rt = S["r"][:, t * DM:(t + 1) * DM]
            for half, (ph, fh) in enumerate(((p0, f0), (p1, f1))):
                self.v(lambda v, ps_=ph, h_=half, rt_=rt, ra=res_ap: v.scalar_tensor_tensor(
                    rt_[:, h_ * 512:(h_ + 1) * 512], ps_[:, :], scale,
                    ra[:, h_ * 512:(h_ + 1) * 512], ALU.mult, ALU.add),
                    wait=[mm.last()] + res_w if half == 0 else (),
                    inc=[(fh, 1)], tag=f"{tag}ev{t}")
            self.vdrain(inc=[(vx, 1)])
            self.ln_tile(rt, wait=[vx.last()])
            if spill_dram is not None:
                self.p.dma("sync", spill_dram[t * 128:(t + 1) * 128, :], rt,
                           wait=[c["ev"].last()], inc=[(c["st"], 16)], tag=f"{tag}sp{t}")

    def build_ffn_cache(self):
        S, D, c = self.S, self.D, self.c
        gx = c["gx"]
        jobs = []
        for hc in range(NHC):
            jobs.append((D["wgT"][:, hc * 128:(hc + 1) * 128].rearrange(
                "(cc p) j -> p cc j", p=128), D["wg_c"][hc], True))
            jobs.append((D["wuT"][:, hc * 128:(hc + 1) * 128].rearrange(
                "(cc p) j -> p cc j", p=128), D["wu_c"][hc], True))
            jobs.append((D["wdT"][hc * 128:(hc + 1) * 128, :], D["wd_c"][hc], False))
        cast2 = [S["cast_t"], S["cast_t2"]]
        for i, (src_ap, dst_ap, slow) in enumerate(jobs):
            w = c["wB0"] if i % 2 == 0 else c["wB1"]
            gdp = c["gd0"] if i % 2 == 0 else c["gd1"]
            sl = S["stage_i"][:, i % 2]
            if len(src_ap.shape) == 3:
                sl = sl.rearrange("p (cc j) -> p cc j", j=128)
            self.p.dma("sync", sl, src_ap, inc=[(w, 16)],
                       wait=[w.last(), (gx, max(0, gx.promised - 1))],
                       allow_slow_non_contiguous=slow, tag=f"fcl{i}")
            self.g(lambda g_, sl_=sl, i_=i: g_.tensor_copy(out=cast2[i_ % 2], in_=sl_),
                   wait=[w.last(), gdp.last()],
                   inc=[(gx, 1)], tag=f"fcc{i}")
            self.p.dma("gpsimd", dst_ap, cast2[i % 2],
                       wait=[gx.last()], inc=[(gdp, 16)], tag=f"fcs{i}")
        self.g(lambda g_, v=self.c["gd0"].promised: g_.wait_ge(self.c["gd0"].sem, v), tag="fcw0")
        self.g(lambda g_, v=self.c["gd1"].promised: g_.wait_ge(self.c["gd1"].sem, v),
               inc=[(gx, 1)], tag="fcw1")
        self.barrier()

    def build_moe_cache(self):
        S, D, c = self.S, self.D, self.c
        gx = c["gx"]
        jobs = []
        for e in range(NE):
            for cc in range(NCH):
                for sub in range(4):
                    jobs.append((D["w1"][e][cc * 128:(cc + 1) * 128, sub * 1024:(sub + 1) * 1024],
                                 D["w1_c"][e, cc][:, sub * 1024:(sub + 1) * 1024]))
            for hc in range(EH // 128):
                jobs.append((D["w2"][e][hc * 128:(hc + 1) * 128, :], D["w2_c"][e, hc]))
        cast2 = [S["cast_t"], S["cast_t2"]]
        cast_engs = ["gpsimd", "vector", "scalar"]
        cast_ch = [c["gx"], c["vx"], c["ev"]]
        base_cnt = [ch.promised for ch in cast_ch]
        for i, (src_ap, dst_ap) in enumerate(jobs):
            w = c["wB0"] if i % 2 == 0 else c["wB1"]
            gdp = c["gd0"] if i % 2 == 0 else c["gd1"]
            sl = S["stage"][:, i % 2]
            ce = i % 3
            self.p.dma("sync", sl, src_ap, inc=[(w, 16)],
                       wait=[w.last(), cast_ch[(i + 1) % 3].last()], tag=f"mc_l{i}")
            if cast_engs[ce] == "scalar":
                fn = (lambda en, sl_=sl, i_=i: en.copy(out=cast2[i_ % 2], in_=sl_))
            else:
                fn = (lambda en, sl_=sl, i_=i: en.tensor_copy(out=cast2[i_ % 2], in_=sl_))
            self.p.emit(cast_engs[ce], fn,
                wait=[w.last(), gdp.last()], inc=[(cast_ch[ce], 1)], tag=f"mc_c{i}")
            self.p.dma("gpsimd", dst_ap, cast2[i % 2],
                       wait=[cast_ch[ce].last()], inc=[(gdp, 16)], tag=f"mc_s{i}")
        self.g(lambda g_, v=self.c["gd0"].promised: g_.wait_ge(self.c["gd0"].sem, v),
               tag="mcw0")
        self.g(lambda g_, v=self.c["gd1"].promised: g_.wait_ge(self.c["gd1"].sem, v),
               inc=[(gx, 1)], tag="mcw1")
        self.barrier()

    def l0_ffn(self, s):
        S, D, c = self.S, self.D, self.c
        mm, ev, vx = c["mm"], c["ev"], c["vx"]
        cB = c["cB"]

        def src_tile(t):
            return S["r"][:, t * DM:(t + 1) * DM], []
        self.build_xT(src_tile, tag=f"s{s}x1")
        self.barrier()

        pa, fa = self.bank("A"); pb, fb = self.bank("B")
        pc, fc = self.bank("C"); pd, fd = self.bank("D")
        for tg in range(4):
            gu_mm = []
            for hc in range(NHC):
                par = hc % 2
                wB = c["wB0"] if par == 0 else c["wB1"]
                for name, buf in (("wg_c", "wg_hc"), ("wu_c", "wu_hc")):
                    self.p.dma("sync", S[buf][:, par], D[name][hc], inc=[(wB, 16)],
                               wait=[wB.last()] + ([(mm, gu_mm[hc - 2])] if hc >= 2 else []),
                               tag=f"ff{name}{hc}")
                w_rdy = wB.last()
                ph, fh = (pa, fa) if par == 0 else (pb, fb)
                pu, fu = (pc, fc) if par == 0 else (pd, fd)
                for cc in range(NCH):
                    self.t(lambda T, cc_=cc, ps_=ph, par_=par, tg_=tg: T.matmul(
                        ps_[:, :], S["wg_hc"][:, par_, cc_ * 128:(cc_ + 1) * 128],
                        S["x_T"][:, cc_ * SEQ + tg_ * 512: cc_ * SEQ + (tg_ + 1) * 512],
                        start=(cc_ == 0), stop=(cc_ == NCH - 1)),
                        wait=[w_rdy, fh.last()] if cc == 0 else (), tag=f"ffg{hc}")
                for cc in range(NCH):
                    self.t(lambda T, cc_=cc, ps_=pu, par_=par, tg_=tg: T.matmul(
                        ps_[:, :], S["wu_hc"][:, par_, cc_ * 128:(cc_ + 1) * 128],
                        S["x_T"][:, cc_ * SEQ + tg_ * 512: cc_ * SEQ + (tg_ + 1) * 512],
                        start=(cc_ == 0), stop=(cc_ == NCH - 1)),
                        wait=[fu.last()] if cc == 0 else (),
                        inc=[(mm, 1)] if cc == NCH - 1 else (), tag=f"ffu{hc}")
                gu_mm.append(mm.promised)
                self.s(lambda sc, ps_=ph: sc.activation(S["gate_t"], ps_[:, :], AF.Silu,
                                                        bias=0.0, scale=QSC),
                       wait=[mm.last(), vx.last()], inc=[(fh, 1)], tag=f"ffsl{hc}")
                self.s(lambda sc, ps_=pu: sc.activation(S["up_t"], ps_[:, :], AF.Copy,
                                                        bias=0.0, scale=QSC),
                       inc=[(ev, 1), (fu, 1)], tag=f"ffup{hc}")
                self.v(lambda v, hc_=hc: v.tensor_tensor(
                    S["h_grp"][:, hc_ * 512:(hc_ + 1) * 512], S["gate_t"], S["up_t"],
                    ALU.mult), wait=[ev.last()], inc=[(vx, 1)], tag=f"ffh{hc}")
            self.vdrain(inc=[(vx, 1)])
            banks = [self.bank("A"), self.bank("B"), self.bank("C"), self.bank("D")]
            wd_mm = []
            for half in range(2):
                for hc in range(NHC):
                    par = hc % 2
                    wB = c["wB0"] if par == 0 else c["wB1"]
                    k = half * NHC + hc
                    self.p.dma("sync", S["wd_hc"][:, par],
                               D["wd_c"][hc][:, half * 512:(half + 1) * 512],
                               inc=[(wB, 16)],
                               wait=[wB.last()] + ([(mm, wd_mm[k - 2])] if k >= 2 else []),
                               tag=f"ffwd{hc}")
                    w_rdy = wB.last()
                    for tt in range(4):
                        ph, fh = banks[tt]
                        self.t(lambda T, hc_=hc, tt_=tt, ps_=ph, par_=par: T.matmul(
                            ps_[:, :], S["h_grp"][:, hc_ * 512 + tt_ * 128: hc_ * 512 + (tt_ + 1) * 128],
                            S["wd_hc"][:, par_, :],
                            start=(hc_ == 0), stop=(hc_ == NHC - 1)),
                            wait=[w_rdy, vx.last()] + ([fh.last()] if hc == 0 else []),
                            inc=[(mm, 1)] if tt == 3 else (), tag=f"ffd{half}{hc}{tt}")
                    wd_mm.append(mm.promised)
                for tt in range(4):
                    t_ = tg * 4 + tt
                    ph, fh = banks[tt]
                    rt = S["r"][:, t_ * DM:(t_ + 1) * DM]
                    self.v(lambda v, ps_=ph, rt_=rt, h_=half: v.scalar_tensor_tensor(
                        rt_[:, h_ * 512:(h_ + 1) * 512], ps_[:, :], QSC,
                        rt_[:, h_ * 512:(h_ + 1) * 512], ALU.mult, ALU.add),
                        wait=[mm.last()], inc=[(fh, 1)], tag=f"ffde{half}{tt}")
                self.vdrain(inc=[(vx, 1)])
            for tt in range(4):
                t_ = tg * 4 + tt
                rt = S["r"][:, t_ * DM:(t_ + 1) * DM]
                tok = self.ln_tile(rt, wait=[vx.last()])
                self.p.dma("sync", D["r_spill"][t_ * 128:(t_ + 1) * 128, :], rt,
                           wait=[tok], inc=[(c["st"], 16)], tag=f"ffsp{t_}")
        self.barrier()

    def l1_attn(self, s):
        S, D, c = self.S, self.D, self.c
        mm, ev, vx = c["mm"], c["ev"], c["vx"]

        def src_tile(t):
            return S["r"][:, t * DM:(t + 1) * DM], []
        self.build_xT(src_tile, tag=f"s{s}x2")
        self.barrier()
        self.p.emit("sync", lambda e, v=c["st"].promised: e.wait_ge(c["st"].sem, v), tag="spw")
        self.barrier()

        self.stream_w(S["w_str"], [D["W2vT"][i * 128:(i + 1) * 128, :] for i in range(NCH)],
                      tag=f"s{s}w2v")
        self.v(lambda v: v.memset(
            S["V"].rearrange("p (t h u) -> p t h u", t=NT, h=NH)[:, :, :, 64:65], 1.0),
            tag="vone1")
        self.vdrain(inc=[(vx, 1)])
        self.proj_tokmajor(dst_V=True, wq=None, bias_bf=None, tag=f"s{s}v2")
        self.barrier()
        for wname, dst in (("W2qT", "q_fm"), ("W2kT", "k_fm")):
            self.stream_w(S["w_str"], [D[wname][i * 128:(i + 1) * 128, :] for i in range(NCH)],
                          tag=f"s{s}{wname}")
            self.qk_rms_proj(dst, tag=f"s{s}{wname}p")
            self.barrier()
        prev = None
        ix = 0
        for h in range(NH):
            for qg in range(4):
                cur = self.attn_core(h, qg, h % 2, causal=True, tag=f"L1h{h}g{qg}", pv_ix=ix)
                if prev is not None:
                    self.attn_tail(prev)
                prev = cur
                ix += 1
        self.attn_tail(prev)
        self.barrier()
        self.wo_ln(res_dram=D["r_spill"], wT=D["W2oT"], bias_dram=None,
                   scale=1.0, tag=f"s{s}wo1")
        self.barrier()

    def qk_rms_proj(self, dst_name, tag):
        S, c = self.S, self.c
        mm, ev, vx = c["mm"], c["ev"], c["vx"]
        pa, fa = self.bank("A"); pb, fb = self.bank("B")
        psT, fT = self.bank("T")
        dst = S[dst_name]
        for t in range(NT):
            for half, (ph, fh) in enumerate(((pa, fa), (pb, fb))):
                for cc in range(NCH):
                    self.t(lambda T, t_=t, cc_=cc, h_=half, ps_=ph: T.matmul(
                        ps_[:, :],
                        S["x_T"][:, cc_ * SEQ + t_ * 128: cc_ * SEQ + (t_ + 1) * 128],
                        S["w_str"][:, cc_ * DM + h_ * 512: cc_ * DM + h_ * 512 + 512],
                        start=(cc_ == 0), stop=(cc_ == NCH - 1)),
                        wait=[c["cA"].last(), vx.last(), fh.last()] if cc == 0 else (),
                        inc=[(mm, 1)] if (cc == NCH - 1 and half == 1) else (),
                        tag=f"{tag}m{t}")
            self.s(lambda sc: sc.copy(out=S["cast_t"][:, 0:512], in_=pa[:, :]),
                   wait=[mm.last()], inc=[(fa, 1)], tag=f"{tag}e0{t}")
            self.s(lambda sc: sc.copy(out=S["cast_t"][:, 512:1024], in_=pb[:, :]),
                   inc=[(ev, 1), (fb, 1)], tag=f"{tag}e1{t}")
            sq = S["stage"][:, 0]
            self.v(lambda v: v.tensor_tensor(sq, S["cast_t"], S["cast_t"], ALU.mult),
                   wait=[ev.last()], tag=f"{tag}sq{t}")
            self.vdrain()
            self.v(lambda v: v.tensor_reduce(
                S["rms16"], sq.rearrange("p (h u) -> p h u", u=64), AX.X, ALU.add),
                tag=f"{tag}sr{t}")
            self.vdrain(inc=[(vx, 1)])
            self.s(lambda sc: sc.activation(S["rms16b"], S["rms16"], AF.Sqrt,
                                            bias=S["eps_rms"], scale=1.0 / 64.0),
                   wait=[vx.last()], inc=[(ev, 1)], tag=f"{tag}sd{t}")
            self.v(lambda v: v.reciprocal(S["rms16"], S["rms16b"]),
                   wait=[ev.last()], tag=f"{tag}rc{t}")
            self.vdrain()
            self.v(lambda v: v.tensor_tensor(
                S["cast_t"].rearrange("p (h u) -> p h u", u=64),
                S["cast_t"].rearrange("p (h u) -> p h u", u=64),
                S["rms16"].rearrange("p (h o) -> p h o", o=1).broadcast_to([128, 16, 64]),
                ALU.mult), tag=f"{tag}ap{t}")
            self.vdrain(inc=[(vx, 1)])
            for cc in range(NCH):
                self.t(lambda T, cc_=cc: T.transpose(
                    psT[:, cc_ * 128:(cc_ + 1) * 128],
                    S["cast_t"][:, cc_ * 128:(cc_ + 1) * 128], S["ident_bf"]),
                    wait=[vx.last(), fT.last()] if cc == 0 else (),
                    inc=[(mm, 1)] if cc == NCH - 1 else (), tag=f"{tag}tr{t}")
            self.s(lambda sc, t_=t: sc.copy(
                out=dst.rearrange("p (c q) -> p c q", c=NCH)[:, :, t_ * 128:(t_ + 1) * 128],
                in_=psT.rearrange("p (c q) -> p c q", c=NCH)),
                wait=[mm.last()], inc=[(ev, 1), (fT, 1)], tag=f"{tag}te{t}")

    def routing(self, s):
        S, D, c = self.S, self.D, self.c
        mm, ev, vx = c["mm"], c["ev"], c["vx"]
        for t in range(NT):
            self.v(lambda v, t_=t: v.tensor_copy(
                out=S["cast_t"], in_=S["r"][:, t_ * DM:(t_ + 1) * DM]),
                wait=[(c["st"], max(0, c["st"].promised))], inc=[(vx, 1)], tag=f"xm{t}")
            self.p.dma("sync", D["xmoe"][s][t * 128:(t + 1) * 128, :], S["cast_t"],
                       wait=[vx.last()], inc=[(c["st"], 16)], tag=f"xms{t}")
        self.p.emit("sync", lambda e, v=c["st"].promised: e.wait_ge(c["st"].sem, v),
                    tag="xmw")
        self.barrier()

        def src_tile(t):
            return S["r"][:, t * DM:(t + 1) * DM], []
        self.build_xT(src_tile, tag=f"s{s}x3")
        self.barrier()

        self.v(lambda v: v.memset(S["bigc"], OOB_BIG), wait=[c["ev"].last()], tag="bigc")
        self.vdrain(inc=[(c["vx"], 1)])
        self.p.dma("sync", S["iota512"], D["iota512"][:, :], inc=[(c["ld"], 16)], tag="iota")
        self.p.emit("sync", lambda e, v=c["ld"].promised: e.wait_ge(c["ld"].sem, v), tag="iotaw")
        self.barrier()
        src = D["gateW"].rearrange("(cc p) e -> p cc e", p=128)
        self.p.dma("sync", S["stage"][:, 0, 0:NCH * 8].rearrange("p (cc e) -> p cc e", e=NE),
                   src, inc=[(c["ld"], 16)],
                   allow_slow_non_contiguous=True, tag="gw")
        self.v(lambda v: v.tensor_copy(out=S["gw_bf"], in_=S["stage"][:, 0, 0:NCH * 8]),
               wait=[c["ld"].last()], inc=[(vx, 1)], tag="gwc")
        pe, fe = self.bank("E")
        for qg in range(4):
            for cc in range(NCH):
                self.t(lambda T, cc_=cc, qg_=qg: T.matmul(
                    pe[0:8, :], S["gw_bf"][:, cc_ * 8:(cc_ + 1) * 8],
                    S["x_T"][:, cc_ * SEQ + qg_ * 512: cc_ * SEQ + (qg_ + 1) * 512],
                    start=(cc_ == 0), stop=(cc_ == NCH - 1)),
                    wait=[vx.last(), fe.last()] if cc == 0 else (),
                    inc=[(mm, 1)] if cc == NCH - 1 else (), tag=f"gs{qg}")
            self.s(lambda sc, qg_=qg: sc.copy(
                out=S["probs_et"][:, qg_ * 512:(qg_ + 1) * 512], in_=pe[0:8, :]),
                wait=[mm.last()], inc=[(ev, 1), (fe, 1)], tag=f"gse{qg}")
        pf, ff = self.bank("F")
        for t in range(NT):
            self.t(lambda T, t_=t: T.transpose(
                pf[:, 0:8], S["probs_et"][:, t_ * 128:(t_ + 1) * 128],
                S["ident_f32"][0:8, 0:8]),
                wait=[ev.last(), ff.last()], inc=[(mm, 1)], tag=f"smT{t}")
            self.v(lambda v: v.reduce_max(S["rms16"][:, 0:1], pf[:, 0:8], AX.X, negate=True),
                   wait=[mm.last()], tag=f"smm{t}")
            self.vdrain(inc=[(vx, 1)])
            self.s(lambda sc: sc.activation(S["prb8"], pf[:, 0:8], AF.Exp,
                                            bias=S["rms16"][:, 0:1], scale=1.0,
                                            accum_out=S["rms16"][:, 1:2]),
                   wait=[vx.last()], inc=[(ev, 1), (ff, 1)], tag=f"smx{t}")
            self.v(lambda v: v.reciprocal(S["rms16"][:, 2:3], S["rms16"][:, 1:2]),
                   wait=[ev.last()], tag=f"smr{t}")
            self.vdrain()
            self.v(lambda v: v.tensor_scalar(S["prb8"], S["prb8"], S["rms16"][:, 2:3],
                                             None, ALU.mult), tag=f"smn{t}")
            self.vdrain()
            self.v(lambda v, t_=t: v.tensor_copy(
                out=S["probsT"][:, s, t_ * 10 + 2: t_ * 10 + 10], in_=S["prb8"]), tag=f"pT{t}")
            self.v(lambda v, t_=t: v.tensor_copy(
                out=S["probsT"][:, s, t_ * 10: t_ * 10 + 1], in_=S["pcol"]), tag=f"pTc{t}")
            self.v(lambda v, t_=t: v.memset(
                S["probsT"][:, s, t_ * 10 + 1: t_ * 10 + 2], float(t_)), tag=f"pTi{t}")
            self.vdrain(inc=[(vx, 1)])
            self.t(lambda T, t_=t: T.transpose(
                pf[0:8, 128:256], S["prb8"], S["ident_f32"]),
                wait=[vx.last(), ff.last()], inc=[(mm, 1)], tag=f"pb{t}")
            self.s(lambda sc, t_=t: sc.copy(
                out=S["probs_et"][:, t_ * 128:(t_ + 1) * 128], in_=pf[0:8, 128:256]),
                wait=[mm.last()], inc=[(ev, 1), (ff, 1)], tag=f"pbe{t}")
        self.barrier()
        sr = S["srch"]
        self.v(lambda v: v.memset(sr[:, 0:1], 0.0), tag="bl0")
        self.v(lambda v: v.memset(sr[:, 1:2], 1.0), tag="bh0")
        self.vdrain()
        for it in range(N_BISECT):
            self.v(lambda v: v.tensor_add(sr[:, 2:3], sr[:, 0:1], sr[:, 1:2]), tag=f"bm{it}")
            self.vdrain()
            self.v(lambda v: v.tensor_scalar(sr[:, 2:3], sr[:, 2:3], 0.5, None, ALU.mult),
                   tag=f"bm2{it}")
            self.vdrain()
            self.v(lambda v: v.tensor_scalar(S["mask_et"], S["probs_et"], sr[:, 2:3],
                                             None, ALU.is_gt, op1=ALU.add,
                                             accum_out=sr[:, 3:4]), tag=f"bc{it}")
            self.vdrain()
            self.v(lambda v: v.tensor_scalar(S["srch_i"][:, 0:1], sr[:, 3:4], 512.0,
                                             None, ALU.is_ge), tag=f"bg{it}")
            self.v(lambda v: v.tensor_scalar(S["srch_i"][:, 1:2], sr[:, 3:4], 512.0,
                                             None, ALU.is_lt), tag=f"bg2{it}")
            self.vdrain()
            self.v(lambda v: v.select(sr[:, 0:1], S["srch_i"][:, 0:1], sr[:, 2:3],
                                      sr[:, 0:1], add_drain=True), tag=f"bsl{it}")
            self.v(lambda v: v.select(sr[:, 1:2], S["srch_i"][:, 1:2], sr[:, 2:3],
                                      sr[:, 1:2], add_drain=True), tag=f"bsh{it}")
            self.vdrain()
        self.v(lambda v: v.tensor_scalar(S["mask_et"], S["probs_et"], sr[:, 0:1],
                                         None, ALU.is_gt), tag="bmf")
        self.vdrain(inc=[(vx, 1)])
        psT, fT = self.bank("T")
        for t in range(NT):
            self.t(lambda T, t_=t: T.transpose(
                psT[:, t_ * 8:(t_ + 1) * 8], S["mask_et"][:, t_ * 128:(t_ + 1) * 128],
                S["ident_bf"][0:8, 0:8]),
                wait=[vx.last(), fT.last()] if t == 0 else (),
                inc=[(mm, 1)] if t == NT - 1 else (), tag=f"mT{t}")
        self.s(lambda sc: sc.copy(out=S["maskT_sb"], in_=psT[:, 0:128]),
               wait=[mm.last()], inc=[(ev, 1), (fT, 1)], tag="mTe")
        pu, fu = self.bank("U")
        for j in range(NT):
            for i in range(j + 1):
                lhs = S["tri_bf"] if i == j else S["ones_bf"]
                self.t(lambda T, i_=i, j_=j, l_=lhs: T.matmul(
                    pu[:, 0:8], l_, S["maskT_sb"][:, i_ * 8:(i_ + 1) * 8],
                    start=(i_ == 0), stop=(i_ == j_)),
                    wait=[ev.last(), fu.last()] if i == 0 else (),
                    inc=[(mm, 1)] if i == j else (), tag=f"cs{j}")
            self.v(lambda v: v.tensor_scalar(S["idx_t"], pu[:, 0:8], 512.0, None, ALU.is_le),
                   wait=[mm.last()], tag=f"ix1{j}")
            self.vdrain()
            self.v(lambda v, j_=j: v.tensor_tensor(
                S["idx_m"], S["idx_t"], S["maskT_sb"][:, j_ * 8:(j_ + 1) * 8], ALU.mult),
                tag=f"ix2{j}")
            self.v(lambda v: v.tensor_scalar(S["idx_t2"], pu[:, 0:8], 1.0, None, ALU.subtract),
                   inc=[(fu, 1)], tag=f"ix3{j}")
            self.vdrain()
            self.v(lambda v, j_=j: v.select(
                S["scat_all"][:, j_ * 8:(j_ + 1) * 8], S["idx_m"], S["idx_t2"], S["bigc"],
                add_drain=True), tag=f"ix4{j}")
            self.vdrain(inc=[(vx, 1)])
        for e in range(NE):
            for j in range(NT):
                Sb = S["cast_t"][:, (j % 2) * 512:(j % 2) * 512 + 512]
                self.v(lambda v, Sb_=Sb, j_=j, e_=e: v.tensor_scalar(
                    Sb_, S["iota512"], S["scat_all"][:, j_ * 8 + e_: j_ * 8 + e_ + 1],
                    None, ALU.is_equal),
                    wait=[vx.last()] if j == 0 else [mm.last()], tag=f"oh{e}{j}")
                self.vdrain(inc=[(vx, 1)])
                self.t(lambda T, Sb_=Sb, j_=j, e_=e, s_=s: T.matmul(
                    pu[0:10, :], S["probsT"][:, s_, j_ * 10:(j_ + 1) * 10],
                    Sb_, start=(j_ == 0), stop=(j_ == NT - 1)),
                    wait=[vx.last()] + ([fu.last()] if j == 0 else []),
                    inc=[(mm, 1)], tag=f"ohm{e}{j}")
            self.s(lambda sc: sc.copy(out=S["sc_rows"], in_=pu[0:10, :]),
                   wait=[mm.last(), (c["st"], c["st"].promised)],
                   inc=[(ev, 1), (fu, 1)], tag=f"ohe{e}")
            self.p.dma("sync", D["rows"][s, e, 0], S["sc_rows"][0:1, :],
                       wait=[ev.last()], inc=[(c["st"], 16)], tag=f"rw0{e}")
            self.p.dma("sync", D["rows"][s, e, 1], S["sc_rows"][1:2, :],
                       inc=[(c["st"], 16)], tag=f"rw1{e}")
            self.p.dma("sync", D["rows"][s, e, 2], S["sc_rows"][2 + e:3 + e, :],
                       inc=[(c["st"], 16)], tag=f"rwp{e}")
        self.barrier()
        self.p.emit("sync", lambda e, v=c["st"].promised: e.wait_ge(c["st"].sem, v), tag="rw")
        self.barrier()

    def moe(self):
        S, D, c = self.S, self.D, self.c
        mm, ev, vx, gx, gd = c["mm"], c["ev"], c["vx"], c["gx"], c["gdma"]
        self.v(lambda v: v.memset(S["stage"][:, 0], 0.0), inc=[(vx, 1)], tag="yz")
        for s in range(2):
            for t in range(NT):
                self.p.dma("sync", D["ymoe"][s][t * 128:(t + 1) * 128, :], S["stage"][:, 0],
                           wait=[vx.last()], inc=[(c["st"], 16)], tag=f"yz{s}{t}")
        banks4 = [self.bank("A"), self.bank("B"), self.bank("C"), self.bank("D")]
        psT, fT = self.bank("T")
        cA = c["cA"]
        self.p.emit("sync", lambda x_, v=c["st"].promised: x_.wait_ge(c["st"].sem, v), tag="yzw")
        self.barrier()
        w1_mm, w2_mm = [], []
        for e in range(NE):
            for s in range(2):
                self.p.dma("sync", S["gidxf"],
                           D["rows"][s, e, 0].rearrange("(j p) -> p j", p=128),
                           inc=[(c["ld"], 16)], wait=[gd.last(), vx.last()],
                           allow_slow_non_contiguous=True, tag=f"gi{e}{s}")
                self.p.dma("sync", S["gidxg"],
                           D["rows"][s, e, 1].rearrange("(j p) -> p j", p=128),
                           inc=[(c["ld"], 16)],
                           allow_slow_non_contiguous=True, tag=f"gj{e}{s}")
                self.p.dma("sync", S["pcomp"],
                           D["rows"][s, e, 2].rearrange("(j p) -> p j", p=128),
                           inc=[(c["ld"], 16)],
                           allow_slow_non_contiguous=True, tag=f"gp{e}{s}")
                self.v(lambda v: v.scalar_tensor_tensor(
                    S["gidxf"], S["gidxg"], 128.0, S["gidxf"], ALU.mult, ALU.add),
                    wait=[c["ld"].last()], tag=f"giv{e}{s}")
                self.vdrain()
                self.v(lambda v: v.tensor_copy(out=S["gidx"], in_=S["gidxf"]),
                       inc=[(vx, 1)], tag=f"gic{e}{s}")
                self.vdrain()
                for j in range(4):
                    self.g(lambda g_, j_=j, s_=s: g_.indirect_dma_start(
                        S["xin_tok"][:, j_ * DM:(j_ + 1) * DM], None,
                        D["xmoe"][s_][:, :],
                        IndirectOffsetOnAxis(ap=S["gidx"][:, j_:j_ + 1], axis=0)),
                        wait=[vx.last(), mm.last()], inc=[(gd, 16)], tag=f"gx{e}{s}{j}",
                        raw_inc=True)
                for j in range(4):
                    for cc in range(NCH):
                        self.t(lambda T, j_=j, cc_=cc: T.transpose(
                            psT[:, ((j_ * NCH + cc_) % 8) * 128: ((j_ * NCH + cc_) % 8) * 128 + 128],
                            S["xin_tok"][:, j_ * DM + cc_ * 128: j_ * DM + (cc_ + 1) * 128],
                            S["ident_bf"]),
                            wait=[gd.last(), fT.last()] if (j == 0 and cc == 0) else
                                 ([fT.last()] if (j * NCH + cc) % 8 == 0 else ()),
                            inc=[(mm, 1)] if (j * NCH + cc) % 8 == 7 else (), tag=f"xt{e}{s}")
                        if (j * NCH + cc) % 8 == 7:
                            u0 = (j * NCH + cc) - 7
                            def ev_t(sc, u0_=u0):
                                outs = []
                                for u in range(8):
                                    jj, ccx = divmod(u0_ + u, NCH)
                                    outs.append(sc.copy(
                                        out=S["xin_T"][:, ccx * 512 + jj * 128: ccx * 512 + (jj + 1) * 128],
                                        in_=psT[:, u * 128:(u + 1) * 128]))
                                return outs[-1]
                            self.s(ev_t, wait=[mm.last()], inc=[(ev, 1), (fT, 1)], tag=f"xte{e}{s}", multi=True)
                for hs in range(8):
                    k1 = len(w1_mm)
                    par = hs % 2
                    wAp = c["wA0"] if par == 0 else c["wA1"]
                    self.p.dma("sync",
                               S["w1col"][:, par].rearrange("p (cc x) -> p cc x", x=512),
                               D["w1_c"][e][:, :, hs * 512:(hs + 1) * 512].rearrange(
                                   "cc p x -> p cc x"),
                               inc=[(wAp, 16)],
                               wait=[wAp.last()] + ([(mm, w1_mm[k1 - 2])] if k1 >= 2 else []),
                               tag=f"w1l{e}{hs}")
                    w_rdy = wAp.last()
                    for st in range(4):
                        ph, fh = banks4[st]
                        for cc in range(NCH):
                            self.t(lambda T, st_=st, cc_=cc, par_=par, ps_=ph: T.matmul(
                                ps_[:, :], S["xin_T"][:, cc_ * 512 + st_ * 128: cc_ * 512 + (st_ + 1) * 128],
                                S["w1col"][:, par_, cc_ * 512:(cc_ + 1) * 512],
                                start=(cc_ == 0), stop=(cc_ == NCH - 1)),
                                wait=[w_rdy, ev.last(), fh.last()] if cc == 0 else (),
                                inc=[(mm, 1)] if cc == NCH - 1 else (),
                                tag=f"w1m{e}{hs}{st}")
                        self.s(lambda sc, ps_=ph, st_=st: sc.activation(
                            S["cast_t"][:, (st_ % 2) * 512:(st_ % 2) * 512 + 512],
                            ps_[:, :], AF.Gelu, bias=0.0, scale=1.0),
                            wait=[mm.last()], inc=[(ev, 1), (fh, 1)], tag=f"gl{e}{hs}{st}")
                        for u in range(4):
                            self.t(lambda T, st_=st, u_=u: T.transpose(
                                psT[:, (st_ % 2) * 512 + u_ * 128: (st_ % 2) * 512 + (u_ + 1) * 128],
                                S["cast_t"][:, (st_ % 2) * 512 + u_ * 128:(st_ % 2) * 512 + (u_ + 1) * 128],
                                S["ident_bf"]),
                                wait=[ev.last(), fT.last()] if u == 0 else (),
                                inc=[(mm, 1)] if u == 3 else (), tag=f"ht{e}{hs}{st}")
                        def ev_h(sc, hs_=hs, st_=st):
                            o = None
                            for u in range(4):
                                hc = hs_ * 4 + u
                                o = sc.copy(
                                    out=S["h_e"][:, hc * 512 + st_ * 128: hc * 512 + (st_ + 1) * 128],
                                    in_=psT[:, (st_ % 2) * 512 + u * 128:(st_ % 2) * 512 + (u + 1) * 128])
                            return o
                        self.s(ev_h, wait=[mm.last()], inc=[(ev, 1), (fT, 1)], tag=f"he{e}{hs}{st}", multi=True)
                    w1_mm.append(mm.promised)
                for half in range(2):
                    for hc in range(EH // 128):
                        k2 = len(w2_mm)
                        par = hc % 2
                        wAp = c["wA0"] if par == 0 else c["wA1"]
                        self.p.dma("sync", S["w2_str"][:, par],
                                   D["w2_c"][e, hc][:, half * 512:(half + 1) * 512],
                                   inc=[(wAp, 16)],
                                   wait=[wAp.last()] +
                                        ([(mm, w2_mm[k2 - 2])] if k2 >= 2 else []),
                                   tag=f"w2l{e}{half}{hc}")
                        w_rdy = wAp.last()
                        for st in range(4):
                            ph, fh = banks4[st]
                            self.t(lambda T, hc_=hc, st_=st, par_=par, ps_=ph: T.matmul(
                                ps_[:, :], S["h_e"][:, hc_ * 512 + st_ * 128: hc_ * 512 + (st_ + 1) * 128],
                                S["w2_str"][:, par_, :],
                                start=(hc_ == 0), stop=(hc_ == EH // 128 - 1)),
                                wait=[w_rdy, ev.last()] + ([fh.last()] if hc == 0 else []),
                                inc=[(mm, 1)] if st == 3 else (),
                                tag=f"w2m{e}{half}{st}")
                        w2_mm.append(mm.promised)
                    for st in range(4):
                        ph, fh = banks4[st]
                        self.v(lambda v, ps_=ph, st_=st: v.tensor_scalar(
                            S["y_ev4"][:, st_, :], ps_[:, :],
                            S["pcomp"][:, st_:st_ + 1], None, ALU.mult),
                            wait=[mm.last(), gd.last()] if st == 0 else [mm.last()],
                            inc=[(vx, 1), (fh, 1)], tag=f"w2e{e}{half}{st}")
                    self.vdrain()
                    for st in range(4):
                        self.g(lambda g_, s_=s, st_=st, h_=half: g_.indirect_dma_start(
                            D["ymoe"][s_][:, :],
                            IndirectOffsetOnAxis(ap=S["gidx"][:, st_:st_ + 1], axis=0),
                            S["y_ev4"][:, st_], None, compute_op=ALU.add,
                            element_offset=h_ * 512),
                            wait=[vx.last(), gd.last()], inc=[(gd, 16)], tag=f"sc{e}{s}{st}",
                            raw_inc=True)
        self.barrier()
        self.g(lambda g_, v=gd.promised: g_.wait_ge(gd.sem, v), inc=[(gx, 1)], tag="gdw")
        self.barrier()

    def final(self):
        S, D, c = self.S, self.D, self.c
        vx, ev = c["vx"], c["ev"]
        for s in range(2):
            for t in range(NT):
                self.p.dma("sync", S["stage"][:, 0], D["ymoe"][s][t * 128:(t + 1) * 128, :],
                           inc=[(c["ld"], 16)], wait=[ev.last(), c["fin"].last()], tag=f"fy{s}{t}")
                if s == 0:
                    self.p.dma("sync", S["stage"][:, 1], D["r3A"][t * 128:(t + 1) * 128, :],
                               inc=[(c["ld"], 16)], tag=f"fr{s}{t}")
                    res = S["stage"][:, 1]
                else:
                    res = S["r"][:, t * DM:(t + 1) * DM]
                yt = S["stage"][:, 0]
                self.v(lambda v, r_=res: v.tensor_add(yt, yt, r_),
                       wait=[c["ld"].last(), c["fin"].last()], inc=[(vx, 1)], tag=f"fa{s}{t}")
                self.vdrain()
                tok = self.ln_tile(yt, wait=[vx.last()])
                self.p.dma("sync", D["y"][s][t * 128:(t + 1) * 128, :], yt,
                           wait=[tok], inc=[(c["fin"], 16)], tag=f"fo{s}{t}")
        self.p.emit("sync", lambda e, v=c["fin"].promised: e.wait_ge(c["fin"].sem, v), tag="finw")

    def build(self):
        self.load_consts()
        if self.phase_done("consts"):
            return self.dump_r_and_finish()
        self.build_ffn_cache()
        self.build_moe_cache()
        for s in range(2):
            self._seq = s
            self.load_T12()
            self.l0_attn(s)
            if self.phase_done(f"l0a:{s}"):
                return self.dump_r_and_finish()
            self.l0_ffn(s)
            if self.phase_done(f"l0f:{s}"):
                return self.dump_r_and_finish()
            self.l1_attn(s)
            if self.phase_done(f"l1a:{s}"):
                return self.dump_r_and_finish()
            self.routing(s)
            if self.phase_done(f"rout:{s}"):
                return self.dump_r_and_finish()
            if s == 0:
                self.p.dma("sync",
                           self.D["r3A"].rearrange("(t p) d -> p t d", p=128),
                           self.S["r"].rearrange("p (t d) -> p t d", d=DM),
                           wait=[self.c["ev"].last(), self.c["vx"].last()],
                           inc=[(self.c["st"], 16)], tag="r3sp")
                self.p.emit("sync", lambda e, v=self.c["st"].promised: e.wait_ge(
                    self.c["st"].sem, v))
                self.barrier()
        self.moe()
        if self.phase_done("moe"):
            return self.dump_r_and_finish()
        self.final()


def build_program(stop_after=None):
    b = Build(stop_after=stop_after)
    b.build()
    b.p.finalize()
    return b


def kernel(**inputs):
    pre = host_prepare(inputs)
    x = np.asarray(inputs["L0_attn_x"], np.float32)
    b = build_program()
    in_maps = []
    for core in range(NCORE):
        m = dict(pre)
        m["x"] = np.ascontiguousarray(x[2 * core: 2 * core + 2])
        in_maps.append(m)
    res = run_bass_kernel_spmd(b.nc, in_maps, core_ids=list(range(NCORE)))
    out = np.concatenate([r["y"] for r in res.results], axis=0)
    return out.reshape(16, SEQ, DM)


# version 12
# speedup vs baseline: 1.0139x; 1.0139x over previous
import numpy as np
import ml_dtypes
import concourse.bass as bass
import concourse.mybir as mybir
from concourse.bass import IndirectOffsetOnAxis
from concourse.bass_utils import run_bass_kernel_spmd
from concourse import library_config as _LC

F32 = mybir.dt.float32
BF16 = mybir.dt.bfloat16
I32 = mybir.dt.int32
AF = mybir.ActivationFunctionType
ALU = mybir.AluOpType
AX = mybir.AxisListType
BF = ml_dtypes.bfloat16

ENGINES = ("sync", "scalar", "vector", "tensor", "gpsimd")

NCORE = 8
SEQ = 2048
DM = 1024
NH = 16
DH = 64
HID = 3072
NHC = HID // 128
NE = 8
EH = 4096
CAP = 512
NT = SEQ // 128
NCH = DM // 128
THETA = 10000.0
LN_EPS = 1e-5
RMS_EPS = 1.1920929e-07
QSC = 1.0 / 127.0
N_BISECT = 34
OOB_BIG = 1.0e7


class Chan:
    __slots__ = ("name", "sem", "promised")
    def __init__(self, nc, name):
        self.name = name
        self.sem = nc.alloc_semaphore(name)
        self.promised = 0
    def last(self):
        return (self, self.promised)


class Prog:
    def __init__(self, nc):
        self.nc = nc
        self.streams = {e: [] for e in ENGINES}
        self.n_ops = 0

    def chan(self, name):
        return Chan(self.nc, name)

    def emit(self, eng, fn, wait=(), inc=(), tag="", raw_inc=False, multi=False):
        wait = tuple(wait)
        inc = tuple(inc)
        for ch, v in wait:
            assert v <= ch.promised, (
                f"wait {ch.name}>={v} but promised {ch.promised} ({tag})")
        for ch, k in inc:
            ch.promised += k
        if multi and eng not in ("sync", "tensor"):
            self.streams[eng].append((wait, fn, (), tag))
            self.n_ops += 1
            rest = inc
        else:
            self.streams[eng].append((wait, fn, inc[:1], tag))
            self.n_ops += 1
            rest = inc[1:]
        for ch, k in rest:
            self.streams[eng].append(((), (lambda e: e.drain()), ((ch, k),), tag + "+i"))
            self.n_ops += 1

    def dma(self, eng, out, in_, inc=(), wait=(), tag="", **kw):
        self.emit(eng, (lambda e, o=out, i=in_, k=kw: e.dma_start(out=o, in_=i, **k)),
                  wait=wait, inc=inc, tag=tag, raw_inc=True)

    def finalize(self):
        nc = self.nc
        with nc.Block() as block:
            def runner(stream):
                def run(engine):
                    for wait, fn, inc, tag in stream:
                        for ch, v in wait:
                            engine.wait_ge(ch.sem, v)
                        try:
                            ins = fn(engine)
                        except Exception:
                            import traceback as _tb
                            _tb.print_exc()
                            raise
                        if inc:
                            assert ins is not None, f"inc w/o instruction ({tag})"
                            for ch, k in inc:
                                ins = ins.then_inc(ch.sem, k)
                return run
            block.sync(runner(self.streams["sync"]))
            block.scalar(runner(self.streams["scalar"]))
            block.vector(runner(self.streams["vector"]))
            block.tensor(runner(self.streams["tensor"]))
            block.gpsimd(runner(self.streams["gpsimd"]))



def rope_perm():
    perm = np.zeros(DM, np.int64)
    for h in range(NH):
        for k in range(DH):
            i, s_ = k // 2, k % 2
            perm[h * DH + (i if s_ == 0 else 32 + i)] = h * DH + k
    return perm


def rope_tables():
    t = np.arange(SEQ, dtype=np.float64)
    i = np.arange(32, dtype=np.float64)
    ang = np.outer(i / THETA, t)
    cos = np.cos(ang); sin = np.sin(ang)
    T1 = np.tile(np.concatenate([cos, cos], 0), (1, 2))
    T2 = np.tile(np.concatenate([-sin, sin], 0), (1, 2))
    return T1.astype(BF), T2.astype(BF)


def host_prepare(inp):
    perm = rope_perm()
    g = lambda n: np.asarray(inp[n])
    d = {}
    d["WqT"] = np.ascontiguousarray(g("L0_attn_Wq")[perm, :].T.astype(np.float32))
    d["WkT"] = np.ascontiguousarray(g("L0_attn_Wk")[perm, :].T.astype(np.float32))
    d["WvT"] = np.ascontiguousarray(g("L0_attn_Wv").T.astype(np.float32))
    d["WoT"] = np.ascontiguousarray(g("L0_attn_Wo").T.astype(np.float32))
    d["bq8"] = np.ascontiguousarray(g("L0_attn_bq")[perm].reshape(NCH, 128).T.astype(np.float32))
    d["bk8"] = np.ascontiguousarray(g("L0_attn_bk")[perm].reshape(NCH, 128).T.astype(np.float32))
    d["bv_row"] = g("L0_attn_bv").reshape(1, DM).astype(np.float32)
    d["bo_row"] = g("L0_attn_bo").reshape(1, DM).astype(np.float32)
    d["wgT"] = np.ascontiguousarray(g("L0_ffn_w_gate").T.astype(np.int32))
    d["wuT"] = np.ascontiguousarray(g("L0_ffn_w_up").T.astype(np.int32))
    d["wdT"] = np.ascontiguousarray(g("L0_ffn_w_down").T.astype(np.int32))
    d["W2qT"] = np.ascontiguousarray(g("L1_attn_w_q").T.astype(np.float32))
    d["W2kT"] = np.ascontiguousarray(g("L1_attn_w_k").T.astype(np.float32))
    d["W2vT"] = np.ascontiguousarray(g("L1_attn_w_v").T.astype(np.float32))
    d["W2oT"] = np.ascontiguousarray(g("L1_attn_w_o").T.astype(np.float32))
    d["gateW"] = np.ascontiguousarray(g("L1_ffn_gate_w").astype(np.float32))
    d["w1"] = np.ascontiguousarray(g("L1_ffn_w1").astype(np.float32))
    d["w2"] = np.ascontiguousarray(g("L1_ffn_w2").astype(np.float32))
    T1, T2 = rope_tables()
    d["cT1"] = T1; d["cT2"] = T2
    d["ident_bf"] = np.eye(128).astype(BF)
    d["ident_f32"] = np.eye(128, dtype=np.float32)
    d["tri_bf"] = (np.arange(128)[None, :] >= np.arange(128)[:, None]).astype(BF)
    d["ones_bf"] = np.ones((128, 128), BF)
    d["iota512"] = np.tile(np.arange(512, dtype=np.float32), (128, 1))
    d["pcol"] = np.arange(128, dtype=np.float32).reshape(128, 1)
    return d



class Build:
    def __init__(self, stop_after=None):
        nc = bass.Bass("TRN2", target_bir_lowering=False)
        self.nc = nc
        self.p = Prog(nc)
        self.stop_after = stop_after
        self.stopped = False
        self._mk_dram()
        self._mk_sbuf()
        self._mk_psum()
        self._mk_chans()

    def _mk_dram(self):
        nc = self.nc
        D = {}
        def ein(name, shape, dt):
            D[name] = nc.dram_tensor(name, list(shape), dt, kind="ExternalInput").ap()
        ein("x", (2, SEQ, DM), F32)
        for n in ("WqT", "WkT", "WvT", "WoT", "W2qT", "W2kT", "W2vT", "W2oT"):
            ein(n, (DM, DM), F32)
        ein("bq8", (128, NCH), F32); ein("bk8", (128, NCH), F32)
        ein("bv_row", (1, DM), F32); ein("bo_row", (1, DM), F32)
        ein("wgT", (DM, HID), I32); ein("wuT", (DM, HID), I32); ein("wdT", (HID, DM), I32)
        ein("gateW", (DM, NE), F32)
        ein("w1", (NE, DM, EH), F32); ein("w2", (NE, EH, DM), F32)
        ein("cT1", (64, 2 * SEQ), BF16); ein("cT2", (64, 2 * SEQ), BF16)
        ein("ident_bf", (128, 128), BF16); ein("ident_f32", (128, 128), F32)
        ein("tri_bf", (128, 128), BF16); ein("ones_bf", (128, 128), BF16)
        ein("iota512", (128, 512), F32); ein("pcol", (128, 1), F32)
        D["y"] = nc.dram_tensor("y", [2, SEQ, DM], F32, kind="ExternalOutput").ap()
        D["dbg"] = nc.dram_tensor("dbg", [128, NT * DM], F32, kind="ExternalOutput").ap()
        D["r_spill"] = nc.dram_tensor("r_spill", [SEQ, DM], F32, kind="Internal").ap()
        D["r3A"] = nc.dram_tensor("r3A", [SEQ, DM], F32, kind="Internal").ap()
        D["xmoe"] = [nc.dram_tensor("xmoe0", [SEQ, DM], BF16, kind="Internal").ap(),
                     nc.dram_tensor("xmoe1", [SEQ, DM], BF16, kind="Internal").ap()]
        D["ymoe"] = [nc.dram_tensor("ymoe0", [SEQ, DM], F32, kind="Internal").ap(),
                     nc.dram_tensor("ymoe1", [SEQ, DM], F32, kind="Internal").ap()]
        D["rows"] = nc.dram_tensor("rows", [2, NE, 3, 512], F32, kind="Internal").ap()
        D["wg_c"] = nc.dram_tensor("wg_c", [NHC, 128, NCH * 128], BF16, kind="Internal").ap()
        D["wu_c"] = nc.dram_tensor("wu_c", [NHC, 128, NCH * 128], BF16, kind="Internal").ap()
        D["wd_c"] = nc.dram_tensor("wd_c", [NHC, 128, DM], BF16, kind="Internal").ap()
        D["w1_c"] = nc.dram_tensor("w1_c", [NE, NCH, 128, EH], BF16, kind="Internal").ap()
        D["w2_c"] = nc.dram_tensor("w2_c", [NE, EH // 128, 128, DM], BF16, kind="Internal").ap()
        self.D = D

    def _mk_sbuf(self):
        base = (self.nc.sbuf_base + 127) // 128 * 128
        at = self.nc.alloc_sbuf_tensor_at
        S = {}
        def A(name, shape, dt, offset):
            assert offset % 32 == 0, (name, offset)
            S[name] = at(name, list(shape), dt, offset=base + offset).ap()
        A("r", [128, NT * DM], F32, 0)
        A("q_fm", [128, NCH * SEQ], BF16, 0)
        A("k_fm", [128, NCH * SEQ], BF16, 32768)
        A("T1", [64, 2 * SEQ], BF16, 49152)
        A("T2", [64, 2 * SEQ], BF16, 57344)
        A("x_T", [128, NCH * SEQ], BF16, 65536)
        A("xin_tok", [128, 4 * DM], BF16, 65536)
        A("xin_T", [128, NCH * 512], BF16, 65536 + 8192)
        A("gidxf", [128, 4], F32, 65536 + 16384)
        A("gidxg", [128, 4], F32, 65536 + 16576)
        A("gidx", [128, 4], I32, 65536 + 16448)
        A("pcomp", [128, 4], F32, 65536 + 16512)
        A("gidx2", [128, 2, 4], I32, 65536 + 16640)
        A("pcomp2", [128, 2, 4], F32, 65536 + 16704)
        A("V", [128, NT * NH * 65], BF16, 98304)
        A("h_grp", [128, NHC * 512], BF16, 98304)
        A("attn_fm", [128, NCH * SEQ], BF16, 131584)
        A("h_e", [128, (EH // 128) * 512], BF16, 131584)
        A("P_buf", [128, 8 * 512], BF16, 164352)
        A("mc_st", [128, 2, DM], F32, 164352)
        A("gate_t", [128, 512], BF16, 164352 + 4096)
        A("up_t", [128, 512], BF16, 164352 + 5120)
        A("y_ev4", [128, 4, 512], F32, 164352)
        A("prb8", [128, 8], F32, 164352 + 8192)
        A("ln6", [128, 12], F32, 164352 + 8224)
        A("ln2", [128, 2], F32, 164352 + 8288)
        A("lncol", [128, 3], F32, 164352 + 8320)
        A("rms16", [128, 16], F32, 164352 + 8352)
        A("rms16b", [128, 16], F32, 164352 + 8416)
        QKB = 164352 + 8512
        A("mc_cb", [128, 2, DM], BF16, QKB)
        A("Q_store", [65, 2 * SEQ], BF16, QKB)
        A("K_store", [65, 2 * SEQ], BF16, QKB + 8192)
        A("w_str", [128, NCH * DM], BF16, QKB)
        A("w1col", [128, 2, NCH * 512], BF16, QKB)
        A("probs_et", [NE, SEQ], F32, QKB)
        A("mask_et", [NE, SEQ], BF16, QKB + 8192)
        A("w2_str", [128, 2, 512], BF16, 65536 + 17408)
        A("sc_rows", [10, 512], F32, QKB + 12288)
        A("srch", [NE, 8], F32, QKB + 14336)
        A("gw_bf", [128, NCH * 8], BF16, QKB + 14368)
        A("bigc", [128, 8], F32, QKB + 14496)
        A("srch_i", [NE, 2], I32, QKB + 14528)
        A("wg_hc", [128, 2, NCH * 128], BF16, QKB)
        A("wu_hc", [128, 2, NCH * 128], BF16, QKB + 4096)
        A("wd_hc", [128, 2, 512], BF16, QKB + 8192)
        TMB = 164352 + 8512 + 16384
        A("stage", [128, 2, DM], F32, TMB)
        A("stage_i", [128, 2, DM], I32, TMB)
        A("SWQ", [64, SEQ], BF16, TMB + 8192)
        A("cast_t2", [128, DM], BF16, TMB + 8192)
        A("tanhb", [128, 512], F32, TMB + 8192)
        A("iota512", [128, 512], F32, TMB + 8192)
        A("scat_all", [128, 128], F32, TMB + 8192 + 2048)
        A("maskT_sb", [128, 128], BF16, TMB + 8192 + 2560)
        A("idx_t", [128, 8], F32, TMB + 8192 + 2816)
        A("idx_t2", [128, 8], F32, TMB + 8192 + 2848)
        A("idx_m", [128, 8], I32, TMB + 8192 + 2880)
        A("qk_w", [128, NCH * 128], BF16, TMB + 12288)
        A("cast_t", [128, DM], BF16, TMB + 14336)
        S["att_t"] = S["cast_t"][0:64, 0:512]
        A("inv_t", [1, 512], F32, TMB + 16384)
        A("row_bf", [1, DM], BF16, TMB + 18432)
        off = TMB + 20480
        def C(name, shape, dt, nbytes):
            nonlocal off
            assert off % 32 == 0, (name, off)
            S[name] = at(name, list(shape), dt, offset=base + off).ap()
            off += (nbytes + 31) // 32 * 32
        C("ident_bf", (128, 128), BF16, 256)
        C("tri_bf", (128, 128), BF16, 256)
        C("ones_bf", (128, 128), BF16, 256)
        C("pcol", (128, 1), F32, 4)
        C("eps2", (128, 2), F32, 8)
        S["eps_ln"] = S["eps2"][:, 0:1]
        S["eps_rms"] = S["eps2"][:, 1:2]
        C("ones_col", (128, 1), BF16, 4)
        C("ones1f", (1, 64), F32, 256)
        C("bq8", (128, NCH), F32, 32)
        C("bk8", (128, NCH), F32, 32)
        C("probsT", (128, 2, NT * 10), BF16, 2 * NT * 10 * 2)
        C("ident_f32", (128, 128), F32, 512)
        C("kmax", (1, 2), F32, 8)
        assert base + off <= self.nc.sbuf_top, (base, off, self.nc.sbuf_top)
        self.S = S

    def _mk_psum(self):
        nc = self.nc
        P = {}
        for n in ("psA", "psB", "psC", "psD", "psE", "psF", "psU"):
            P[n] = nc.alloc_psum_tensor(n, [128, 512], F32).ap()
        P["psT"] = nc.alloc_psum_tensor("psT", [128, 1024], BF16).ap()
        self.PS = P

    def _mk_chans(self):
        p = self.p
        self.c = {}
        names = ["ld", "st", "wA", "cA", "wB", "cB", "mm", "ev", "vx", "gx",
                 "gdma", "fin", "bar_s", "bar_c", "bar_v", "bar_t", "bar_g",
                 "wA0", "wA1", "wB0", "wB1", "ld0", "ld1", "gd0", "gd1",
                 "mcA", "mcB", "mcC", "mcD"]
        names += [f"f{b}" for b in ("A", "B", "C", "D", "E", "F", "U", "T")]
        for n in names:
            self.c[n] = p.chan(n)

    def v(self, fn, wait=(), inc=(), tag="", multi=False):
        self.p.emit("vector", fn, wait, inc, tag, multi=multi)
    def s(self, fn, wait=(), inc=(), tag="", multi=False):
        self.p.emit("scalar", fn, wait, inc, tag, multi=multi)
    def t(self, fn, wait=(), inc=(), tag=""): self.p.emit("tensor", fn, wait, inc, tag)
    def g(self, fn, wait=(), inc=(), tag="", raw_inc=False, multi=False):
        self.p.emit("gpsimd", fn, wait, inc, tag, raw_inc=raw_inc, multi=multi)
    def vdrain(self, wait=(), inc=()): self.v(lambda v: v.drain(), wait, inc)
    def sdrain(self, wait=(), inc=()): self.s(lambda s: s.drain(), wait, inc)
    def gdrain(self, wait=(), inc=()): self.g(lambda g: g.drain(), wait, inc)

    def bank(self, n):
        return self.PS["ps" + n], self.c["f" + n]

    def barrier(self):
        c = self.c
        pairs = [("sync", "bar_s"), ("scalar", "bar_c"), ("vector", "bar_v"),
                 ("tensor", "bar_t"), ("gpsimd", "bar_g")]
        for eng, ch in pairs:
            self.p.emit(eng, lambda e: e.nop(), inc=[(c[ch], 1)], tag="bar", multi=True)
        toks = [c[ch].last() for _, ch in pairs]
        for eng, _ in pairs:
            self.p.emit(eng, lambda e: e.nop(), wait=toks, tag="barw")

    def phase_done(self, name):
        if self.stop_after is not None and name == self.stop_after:
            self.stopped = True
            return True
        return False

    def dump_r_and_finish(self):
        self.barrier()
        c = self.c
        self.p.dma("sync", self.D["dbg"][:, :], self.S["r"], inc=[(c["fin"], 16)], tag="dbg")
        self.p.emit("sync", lambda e, v=c["fin"].promised: e.wait_ge(c["fin"].sem, v), tag="finw")

    def load_consts(self):
        p, S, D, c = self.p, self.S, self.D, self.c
        ld = c["ld"]
        for n in ("ident_bf", "ident_f32", "tri_bf", "ones_bf",
                  "pcol", "bq8", "bk8"):
            p.dma("sync", S[n], D[n][:, :], inc=[(ld, 16)], tag=f"cst_{n}")
        self.v(lambda v: v.memset(S["eps_ln"], LN_EPS))
        self.v(lambda v: v.memset(S["eps_rms"], RMS_EPS))
        self.v(lambda v: v.memset(S["ones_col"], 1.0))
        self.v(lambda v: v.memset(S["ones1f"], 1.0))
        self.vdrain(inc=[(c["vx"], 1)])
        self.g(lambda g: g.nop(), inc=[(c["gx"], 1)])
        self.p.emit("sync", lambda e, v=ld.promised: e.wait_ge(ld.sem, v), tag="cstw")
        self.barrier()

    def load_T12(self):
        p, S, D, c = self.p, self.S, self.D, self.c
        ld = c["ld"]
        p.dma("sync", S["T1"], D["cT1"][:, :], inc=[(ld, 16)], tag="T1")
        p.dma("sync", S["T2"], D["cT2"][:, :], inc=[(ld, 16)], tag="T2")
        p.emit("sync", lambda e, v=ld.promised: e.wait_ge(ld.sem, v), tag="T12w")
        self.barrier()

    def build_xT(self, src_tile_fn, tag="xT"):
        S, c = self.S, self.c
        psT, fT = self.bank("T")
        mm, ev, vx = c["mm"], c["ev"], c["vx"]
        cmm = []
        for t in range(NT):
            src, w = src_tile_fn(t)
            ct = S["cast_t"] if t % 2 == 0 else S["cast_t2"]
            self.v(lambda v, s_=src, ct_=ct: v.tensor_copy(out=ct_, in_=s_),
                   wait=list(w) + ([(mm, cmm[t - 2])] if t >= 2 else []),
                   inc=[(vx, 1)], tag=f"{tag}c{t}")
            for cc in range(NCH):
                self.t(lambda T, cc_=cc, ct_=ct: T.transpose(
                    psT[:, cc_ * 128:(cc_ + 1) * 128],
                    ct_[:, cc_ * 128:(cc_ + 1) * 128], S["ident_bf"]),
                    wait=[vx.last(), fT.last()] if cc == 0 else (),
                    inc=[(mm, 1)] if cc == NCH - 1 else (), tag=f"{tag}t{t}")
            cmm.append(mm.promised)
            self.s(lambda s, t_=t: s.copy(
                out=S["x_T"].rearrange("p (c q) -> p c q", c=NCH)[:, :, t_ * 128:(t_ + 1) * 128],
                in_=psT.rearrange("p (c q) -> p c q", c=NCH)),
                wait=[mm.last()], inc=[(ev, 1), (fT, 1)], tag=f"{tag}e{t}")
        return ev.last()

    def hbm_tile(self, dram2d, t, slot):
        S, c = self.S, self.c
        ld = c["ld0"] if slot == 0 else c["ld1"]
        sl = S["stage"][:, slot]
        self.p.dma("sync", sl, dram2d[t * 128:(t + 1) * 128, :], inc=[(ld, 16)],
                   wait=[c["vx"].last(), c["mm"].last(), c["ev"].last(),
                         c["cA"].last(), c["cB"].last()], tag=f"ht{t}")
        return sl, [ld.last()]

    def stream_w(self, dst, srcs, tag="w", eng="vector", chunk_elems=DM, src_int=False,
                 extra_wait=()):
        S, c = self.S, self.c
        ca = c["cA"]
        stage = S["stage_i"] if src_int else S["stage"]
        for i, src in enumerate(srcs):
            w = c["wA0"] if i % 2 == 0 else c["wA1"]
            sl = stage[:, i % 2, :chunk_elems]
            self.p.dma("sync", sl, src, inc=[(w, 16)],
                       wait=[w.last(), (ca, max(0, ca.promised - 1))] +
                            (list(extra_wait) if i == 0 else []),
                       tag=f"{tag}l{i}")
            self.p.emit(eng, lambda e, i_=i, sl_=sl: e.tensor_copy(
                out=dst[:, i_ * chunk_elems:(i_ + 1) * chunk_elems], in_=sl_),
                wait=[w.last()], inc=[(ca, 1)], tag=f"{tag}c{i}")
        return ca.last()

    def ln_tile(self, ytile, wait=()):
        S, c = self.S, self.c
        ev, vx = c["ev"], c["vx"]
        self.v(lambda v: v.bn_stats(S["ln6"][:, 0:6], ytile[:, 0:512]),
               wait=list(wait) + [c["ev"].last()], tag="lns0")
        self.v(lambda v: v.bn_stats(S["ln6"][:, 6:12], ytile[:, 512:1024]), tag="lns1")
        self.vdrain()
        self.v(lambda v: v.bn_aggr(S["ln2"], S["ln6"]), tag="lnagg")
        self.vdrain(inc=[(vx, 1)])
        self.s(lambda s: s.activation(S["lncol"][:, 0:1], S["ln2"][:, 1:2], AF.Sqrt,
                                      bias=S["eps_ln"], scale=1.0),
               wait=[vx.last()], inc=[(ev, 1)], tag="lnsqrt")
        self.v(lambda v: v.reciprocal(S["lncol"][:, 1:2], S["lncol"][:, 0:1]),
               wait=[ev.last()], tag="lnrec")
        self.vdrain()
        self.v(lambda v: v.tensor_scalar(S["lncol"][:, 2:3], S["ln2"][:, 0:1],
                                         S["lncol"][:, 1:2], -1.0, ALU.mult, ALU.mult),
               tag="lnnb")
        self.vdrain(inc=[(vx, 1)])
        self.s(lambda s: s.activation(ytile, ytile, AF.Identity,
                                      bias=S["lncol"][:, 2:3], scale=S["lncol"][:, 1:2]),
               wait=[vx.last()], inc=[(ev, 1)], tag="lnap")
        return ev.last()

    def l0_attn(self, s):
        S, D, c = self.S, self.D, self.c
        mm, ev, vx = c["mm"], c["ev"], c["vx"]

        def src_tile(t):
            return self.hbm_tile(D["x"][s], t, t % 2)
        self.build_xT(src_tile, tag=f"s{s}x0")
        self.barrier()

        self.p.dma("sync", S["stage"][0:1, 0, :], D["bv_row"][:, :],
                   wait=[c["cA"].last(), c["vx"].last()], inc=[(c["ld"], 16)], tag="bv")
        self.v(lambda v: v.tensor_copy(out=S["row_bf"], in_=S["stage"][0:1, 0, :]),
               wait=[c["ld"].last()], tag="bvc")
        self.vdrain(inc=[(c["vx"], 1)])
        self.stream_w(S["w_str"], [D["WvT"][i * 128:(i + 1) * 128, :] for i in range(NCH)],
                      extra_wait=[c["vx"].last()], tag=f"s{s}wv")
        self.v(lambda v: v.memset(
            S["V"].rearrange("p (t h u) -> p t h u", t=NT, h=NH)[:, :, :, 64:65], 1.0),
            tag="vones")
        self.vdrain(inc=[(vx, 1)])
        self.proj_tokmajor(dst_V=True, wq=None, bias_bf=S["row_bf"], tag=f"s{s}V")
        self.barrier()

        self.v(lambda v: v.memset(S["K_store"][64:65, :], 1.0), inc=[(vx, 1)], tag="kone")
        for hp in range(NH // 2):
            self.l0_hp(s, hp)
        self.barrier()

        self.wo_ln(res_dram=D["x"][s], wT=D["WoT"], bias_dram=D["bo_row"],
                   scale=1.0, tag=f"s{s}wo0")
        self.barrier()

    def proj_tokmajor(self, dst_V, wq, bias_bf, tag):
        S, c = self.S, self.c
        mm, ev = c["mm"], c["ev"]
        pa, fa = self.bank("A"); pb, fb = self.bank("B")
        pc, fc = self.bank("C"); pd, fd = self.bank("D")
        for t in range(NT):
            p0, f0, p1, f1 = (pa, fa, pb, fb) if t % 2 == 0 else (pc, fc, pd, fd)
            for half, (ph, fh) in enumerate(((p0, f0), (p1, f1))):
                last_h = (half == 1)
                for cc in range(NCH):
                    self.t(lambda T, t_=t, cc_=cc, h_=half, ps_=ph: T.matmul(
                        ps_[:, :],
                        S["x_T"][:, cc_ * SEQ + t_ * 128: cc_ * SEQ + (t_ + 1) * 128],
                        S["w_str"][:, cc_ * DM + h_ * 512: cc_ * DM + h_ * 512 + 512],
                        start=(cc_ == 0), stop=(bias_bf is None and cc_ == NCH - 1)),
                        wait=[c["cA"].last(), c["vx"].last(), fh.last()] if cc == 0 else (),
                        inc=[(mm, 1)] if (bias_bf is None and cc == NCH - 1 and last_h) else (),
                        tag=f"{tag}m{t}")
                if bias_bf is not None:
                    self.t(lambda T, h_=half, ps_=ph: T.matmul(
                        ps_[:, :], S["ones_bf"][0:1, :],
                        bias_bf[0:1, h_ * 512:(h_ + 1) * 512], start=False, stop=True),
                        inc=[(mm, 1)] if last_h else (), tag=f"{tag}b{t}")
            if dst_V:
                def ev_v(sc, t_=t, p0_=p0, p1_=p1):
                    dst = S["V"].rearrange("p (t h u) -> p t h u", t=NT, h=NH)[:, t_, :, 0:64]
                    sc.copy(out=dst[:, 0:8, :], in_=p0_.rearrange("p (h u) -> p h u", u=64))
                    return sc.copy(out=dst[:, 8:16, :], in_=p1_.rearrange("p (h u) -> p h u", u=64))
                self.s(ev_v, wait=[mm.last()], inc=[(f0, 1), (f1, 1)], tag=f"{tag}e{t}", multi=True)
                self.s(lambda sc: sc.nop(), inc=[(ev, 1)], tag=f"{tag}en{t}")

    def l0_hp(self, s, hp):
        S, D, c = self.S, self.D, self.c
        mm, ev, vx = c["mm"], c["ev"], c["vx"]
        cB = c["cB"]
        pa, fa = self.bank("A"); pb, fb = self.bank("B")
        for which, wname, store, b8 in ((1, "WkT", S["K_store"], S["bk8"]),
                                        (0, "WqT", S["Q_store"], S["bq8"])):
            for cc in range(NCH):
                i = which * NCH + cc
                wB = c["wB0"] if i % 2 == 0 else c["wB1"]
                sl = S["stage"][:, i % 2, :128]
                self.p.dma("sync", sl, D[wname][cc * 128:(cc + 1) * 128, hp * 128:(hp + 1) * 128],
                           inc=[(wB, 16)], wait=[wB.last(), (cB, max(0, cB.promised - 1))] +
                           ([mm.last()] if cc == 0 else []),
                           tag=f"hp{hp}wl")
                self.v(lambda v, sl_=sl, cc_=cc: v.tensor_copy(
                    out=S["qk_w"][:, cc_ * 128:(cc_ + 1) * 128], in_=sl_),
                    wait=[wB.last()], inc=[(cB, 1)], tag=f"hp{hp}wc")
            w_rdy = cB.last()
            for qg in range(4):
                ph, fh = (pa, fa) if qg % 2 == 0 else (pb, fb)
                for cc in range(NCH):
                    self.t(lambda T, cc_=cc, qg_=qg, ps_=ph: T.matmul(
                        ps_[:, :], S["qk_w"][:, cc_ * 128:(cc_ + 1) * 128],
                        S["x_T"][:, cc_ * SEQ + qg_ * 512: cc_ * SEQ + (qg_ + 1) * 512],
                        start=(cc_ == 0), stop=(cc_ == NCH - 1)),
                        wait=[w_rdy, fh.last()] if cc == 0 else (),
                        inc=[(mm, 1)] if cc == NCH - 1 else (), tag=f"hp{hp}p{which}{qg}")
                def ev_qk(sc, ps_=ph, st_=store, qg_=qg, b8_=b8, hp_=hp):
                    sc.activation(st_[0:64, qg_ * 512:(qg_ + 1) * 512], ps_[0:64, :],
                                  AF.Identity, bias=b8_[0:64, hp_:hp_ + 1], scale=1.0)
                    return sc.activation(st_[0:64, SEQ + qg_ * 512: SEQ + (qg_ + 1) * 512],
                                         ps_[64:128, :], AF.Identity,
                                         bias=b8_[64:128, hp_:hp_ + 1], scale=1.0)
                self.s(ev_qk, wait=[mm.last()], inc=[(ev, 1), (fh, 1)], tag=f"hp{hp}e{which}{qg}", multi=True)
            self.rope(store)
        self.l0_m_and_attn(s, hp)

    def rope(self, store):
        S, c = self.S, self.c
        vx, ev = c["vx"], c["ev"]
        SW = S["SWQ"]
        for sd in range(2):
            st = store[0:64, sd * SEQ:(sd + 1) * SEQ]
            self.v(lambda v, st_=st: v.tensor_copy(out=SW[0:32, :], in_=st_[32:64, :]),
                   wait=[ev.last(), vx.last()], tag="sw1")
            self.v(lambda v, st_=st: v.tensor_copy(out=SW[32:64, :], in_=st_[0:32, :]), tag="sw2")
            self.vdrain()
            self.v(lambda v, sd_=sd: v.tensor_tensor(SW, SW, S["T2"][:, sd_ * SEQ:(sd_ + 1) * SEQ],
                                                     ALU.mult), tag="swT2")
            self.v(lambda v, st_=st, sd_=sd: v.tensor_tensor(
                st_, st_, S["T1"][:, sd_ * SEQ:(sd_ + 1) * SEQ], ALU.mult), tag="qT1")
            self.vdrain()
            self.v(lambda v, st_=st: v.tensor_tensor(st_, st_, SW, ALU.add), tag="ropea")
            self.vdrain(inc=[(vx, 1)])

    def l0_m_and_attn(self, s, hp):
        S, c = self.S, self.c
        mm, ev, vx = c["mm"], c["ev"], c["vx"]
        pg, fg = self.PS["psU"], self.c["fU"]
        SW = S["SWQ"]
        self.v(lambda v: v.memset(S["kmax"], 0.0), wait=[vx.last()], tag="km0")
        self.vdrain(inc=[(vx, 1)])
        for side in range(2):
            self.v(lambda v, sd=side: v.tensor_tensor(
                SW, S["K_store"][0:64, sd * SEQ:(sd + 1) * SEQ],
                S["K_store"][0:64, sd * SEQ:(sd + 1) * SEQ], ALU.mult),
                wait=[vx.last()], tag="KK")
            self.vdrain(inc=[(vx, 1)])
            for seg in range(4):
                self.t(lambda T, sg=seg: T.matmul(
                    pg[0:1, :], S["ones_col"][0:64, :],
                    SW[:, sg * 512:(sg + 1) * 512], start=True, stop=True),
                    wait=[vx.last(), fg.last()], inc=[(mm, 1)], tag="kn2")
                self.v(lambda v: v.reduce_max(S["inv_t"][0:1, 0:1], pg[0:1, :], AX.X),
                       wait=[mm.last()], tag="knm")
                self.vdrain()
                self.v(lambda v, sd=side: v.tensor_max(
                    S["kmax"][0:1, sd:sd + 1], S["kmax"][0:1, sd:sd + 1],
                    S["inv_t"][0:1, 0:1]), inc=[(vx, 1), (fg, 1)], tag="knc")
                self.vdrain()
        self.s(lambda sc: sc.activation(S["kmax"], S["kmax"], AF.Sqrt, bias=0.0, scale=1.0),
               wait=[vx.last()], inc=[(ev, 1)], tag="kms")
        for side in range(2):
            self.v(lambda v, sd=side: v.tensor_tensor(
                SW, S["Q_store"][0:64, sd * SEQ:(sd + 1) * SEQ],
                S["Q_store"][0:64, sd * SEQ:(sd + 1) * SEQ], ALU.mult),
                wait=[ev.last(), vx.last()], tag="QQ")
            self.vdrain(inc=[(vx, 1)])
            for seg in range(4):
                self.t(lambda T, sg=seg: T.matmul(
                    pg[0:1, :], S["ones_col"][0:64, :],
                    SW[:, sg * 512:(sg + 1) * 512], start=True, stop=True),
                    wait=[vx.last(), fg.last()], inc=[(mm, 1)], tag="qn2")
                self.s(lambda sc: sc.activation(S["inv_t"], pg[0:1, :], AF.Sqrt,
                                                bias=0.0, scale=1.0),
                       wait=[mm.last()], inc=[(ev, 1)], tag="qns")
                self.v(lambda v, sd=side, sg=seg: v.tensor_scalar(
                    S["Q_store"][64:65, sd * SEQ + sg * 512: sd * SEQ + (sg + 1) * 512],
                    S["inv_t"], S["kmax"][0:1, sd:sd + 1], -1.0, ALU.mult, ALU.mult),
                    wait=[ev.last()], inc=[(vx, 1), (fg, 1)], tag="qrow")
                self.vdrain()
        prev = None
        ix = 0
        for side in range(2):
            h = 2 * hp + side
            for qg in range(4):
                cur = self.attn_core(h, qg, side, causal=False, tag=f"h{h}g{qg}", pv_ix=ix)
                if prev is not None:
                    self.attn_tail(prev)
                prev = cur
                ix += 1
        self.attn_tail(prev)

    def attn_core(self, h, qg, side, causal, tag, pv_ix):
        S, c = self.S, self.c
        mm, ev, vx = c["mm"], c["ev"], c["vx"]
        rot = [self.bank("A"), self.bank("B"), self.bank("E"), self.bank("F")]
        pc, fc = self.bank("C") if pv_ix % 2 == 0 else self.bank("D")
        kt_hi = NT if not causal else min(NT, (qg + 1) * 4)
        ch_ = h // 2
        ev_cnt = {}
        vx_cnt = {}
        for kt0 in range(0, kt_hi, 8):
            kts = range(kt0, min(kt0 + 8, kt_hi))
            for kt in kts:
                ph, fh = rot[kt % 4]
                qlo = 0 if not causal else max(0, kt * 128 - qg * 512)
                if not causal:
                    lhsT = S["K_store"][0:65, side * SEQ + kt * 128: side * SEQ + (kt + 1) * 128]
                    rhs = S["Q_store"][0:65, side * SEQ + qg * 512: side * SEQ + (qg + 1) * 512]
                else:
                    lhsT = S["k_fm"][side * 64:side * 64 + 64,
                                     ch_ * SEQ + kt * 128: ch_ * SEQ + (kt + 1) * 128]
                    rhs = S["q_fm"][side * 64:side * 64 + 64,
                                    ch_ * SEQ + qg * 512 + qlo: ch_ * SEQ + (qg + 1) * 512]
                self.t(lambda T, l_=lhsT, r_=rhs, ps_=ph, q_=qlo: T.matmul(
                    ps_[:, q_:512], l_, r_, start=True, stop=True),
                    wait=[fh.last()], inc=[(mm, 1)], tag=f"{tag}s{kt}")
                sl = (kt % 8) * 512
                if not causal:
                    self.s(lambda sc, ps_=ph, sl_=sl: sc.activation(
                        S["P_buf"][:, sl_:sl_ + 512], ps_[:, :], AF.Exp,
                        bias=0.0, scale=0.125),
                        wait=[mm.last()], inc=[(ev, 1), (fh, 1)], tag=f"{tag}x{kt}")
                else:
                    self.s(lambda sc, ps_=ph, q_=qlo: sc.activation(
                        S["tanhb"][:, q_:512], ps_[:, q_:512], AF.Tanh,
                        bias=0.0, scale=1.0 / (8.0 * 50.0)), wait=[mm.last()], tag=f"{tag}t{kt}")
                    self.sdrain()
                    self.s(lambda sc, sl_=sl, q_=qlo: sc.activation(
                        S["P_buf"][:, sl_ + q_:sl_ + 512], S["tanhb"][:, q_:512],
                        AF.Exp, bias=0.0, scale=50.0),
                        inc=[(ev, 1), (fh, 1)], tag=f"{tag}x{kt}")
                ev_cnt[kt] = ev.promised
                if causal:
                    if qlo > 0:
                        self.v(lambda v, sl_=sl, q_=qlo: v.memset(
                            S["P_buf"][:, sl_:sl_ + q_], 0.0),
                            wait=[(ev, ev_cnt[kt])], tag=f"{tag}z{kt}")
                        self.vdrain(inc=[(vx, 1)])
                    if qg * 512 <= kt * 128 < (qg + 1) * 512:
                        off = sl + (kt * 128 - qg * 512)
                        self.v(lambda v, o_=off: v.tensor_tensor(
                            S["P_buf"][:, o_:o_ + 128], S["P_buf"][:, o_:o_ + 128],
                            S["tri_bf"], ALU.mult), wait=[(ev, ev_cnt[kt])], tag=f"{tag}d{kt}")
                        self.vdrain(inc=[(vx, 1)])
                vx_cnt[kt] = vx.promised
            for kt in kts:
                self.t(lambda T, kt_=kt, pc_=pc: T.matmul(
                    pc_[0:65, :],
                    S["V"][:, kt_ * NH * 65 + h * 65: kt_ * NH * 65 + h * 65 + 65],
                    S["P_buf"][:, (kt_ % 8) * 512:(kt_ % 8) * 512 + 512],
                    start=(kt_ == 0), stop=(kt_ == kt_hi - 1)),
                    wait=[(ev, ev_cnt[kt]), (vx, vx_cnt[kt])] + ([fc.last()] if kt == 0 else []),
                    inc=[(mm, 1)] if kt == kt_hi - 1 else (), tag=f"{tag}pv{kt}")
        return (h, qg, side, pc, fc, tag, mm.promised)

    def attn_tail(self, ctx):
        S, c = self.S, self.c
        mm, ev, vx = c["mm"], c["ev"], c["vx"]
        h, qg, side, pc, fc, tag, mm_pv = ctx
        pu, fu = self.bank("U")
        ch_ = h // 2
        self.v(lambda v: v.reciprocal(S["inv_t"], pc[64:65, :]),
               wait=[(mm, mm_pv)], tag=f"{tag}rc")
        self.vdrain(inc=[(vx, 1)])
        vx_rc = vx.promised
        self.t(lambda T: T.matmul(pu[0:64, :], S["ones1f"], S["inv_t"],
                                  start=True, stop=True),
               wait=[(vx, vx_rc), fu.last()], inc=[(mm, 1)], tag=f"{tag}r1")
        mm_r1 = mm.promised
        self.s(lambda sc: sc.copy(out=S["att_t"], in_=pc[0:64, :]),
               wait=[(mm, mm_pv), (vx, vx_rc)], inc=[(ev, 1), (fc, 1)], tag=f"{tag}ae")
        self.v(lambda v: v.tensor_tensor(
            S["attn_fm"][side * 64:side * 64 + 64,
                         ch_ * SEQ + qg * 512:(ch_) * SEQ + (qg + 1) * 512],
            S["att_t"], pu[0:64, :], ALU.mult),
            wait=[ev.last(), (mm, mm_r1)], inc=[(vx, 1), (fu, 1)], tag=f"{tag}nm")
        self.vdrain()

    def wo_ln(self, res_dram, wT, bias_dram, scale, tag, spill_dram=None):
        S, D, c = self.S, self.D, self.c
        mm, ev, vx = c["mm"], c["ev"], c["vx"]
        if bias_dram is not None:
            self.p.dma("sync", S["stage"][0:1, 0, :], bias_dram[:, :],
                       wait=[c["cA"].last(), c["vx"].last()], inc=[(c["ld"], 16)], tag=f"{tag}b")
            self.v(lambda v: v.tensor_copy(out=S["row_bf"], in_=S["stage"][0:1, 0, :]),
                   wait=[c["ld"].last()], tag=f"{tag}bc")
            self.vdrain(inc=[(vx, 1)])
        self.stream_w(S["w_str"], [wT[i * 128:(i + 1) * 128, :] for i in range(NCH)],
                      extra_wait=[vx.last()], tag=f"{tag}w")
        pa, fa = self.bank("A"); pb, fb = self.bank("B")
        pc, fc = self.bank("C"); pd, fd = self.bank("D")
        for t in range(NT):
            p0, f0, p1, f1 = (pa, fa, pb, fb) if t % 2 == 0 else (pc, fc, pd, fd)
            res_ap, res_w = self.hbm_tile(res_dram, t, t % 2)
            for half, (ph, fh) in enumerate(((p0, f0), (p1, f1))):
                last_h = (half == 1)
                for cc in range(NCH):
                    self.t(lambda T, t_=t, cc_=cc, h_=half, ps_=ph: T.matmul(
                        ps_[:, :],
                        S["attn_fm"][:, cc_ * SEQ + t_ * 128: cc_ * SEQ + (t_ + 1) * 128],
                        S["w_str"][:, cc_ * DM + h_ * 512: cc_ * DM + h_ * 512 + 512],
                        start=(cc_ == 0), stop=(bias_dram is None and cc_ == NCH - 1)),
                        wait=[c["cA"].last(), c["vx"].last(), fh.last()] if cc == 0 else (),
                        inc=[(mm, 1)] if (bias_dram is None and cc == NCH - 1 and last_h) else (),
                        tag=f"{tag}m{t}")
                if bias_dram is not None:
                    self.t(lambda T, h_=half, ps_=ph: T.matmul(
                        ps_[:, :], S["ones_bf"][0:1, :],
                        S["row_bf"][0:1, h_ * 512:(h_ + 1) * 512], start=False, stop=True),
                        inc=[(mm, 1)] if last_h else (), tag=f"{tag}bb{t}")
            rt = S["r"][:, t * DM:(t + 1) * DM]
            for half, (ph, fh) in enumerate(((p0, f0), (p1, f1))):
                self.v(lambda v, ps_=ph, h_=half, rt_=rt, ra=res_ap: v.scalar_tensor_tensor(
                    rt_[:, h_ * 512:(h_ + 1) * 512], ps_[:, :], scale,
                    ra[:, h_ * 512:(h_ + 1) * 512], ALU.mult, ALU.add),
                    wait=[mm.last()] + res_w if half == 0 else (),
                    inc=[(fh, 1)], tag=f"{tag}ev{t}")
            self.vdrain(inc=[(vx, 1)])
            self.ln_tile(rt, wait=[vx.last()])
            if spill_dram is not None:
                self.p.dma("sync", spill_dram[t * 128:(t + 1) * 128, :], rt,
                           wait=[c["ev"].last()], inc=[(c["st"], 16)], tag=f"{tag}sp{t}")

    def build_ffn_cache(self):
        S, D, c = self.S, self.D, self.c
        gx = c["gx"]
        jobs = []
        for hc in range(NHC):
            jobs.append((D["wgT"][:, hc * 128:(hc + 1) * 128].rearrange(
                "(cc p) j -> p cc j", p=128), D["wg_c"][hc], True))
            jobs.append((D["wuT"][:, hc * 128:(hc + 1) * 128].rearrange(
                "(cc p) j -> p cc j", p=128), D["wu_c"][hc], True))
            jobs.append((D["wdT"][hc * 128:(hc + 1) * 128, :], D["wd_c"][hc], False))
        cast2 = [S["cast_t"], S["cast_t2"]]
        for i, (src_ap, dst_ap, slow) in enumerate(jobs):
            w = c["wB0"] if i % 2 == 0 else c["wB1"]
            gdp = c["gd0"] if i % 2 == 0 else c["gd1"]
            sl = S["stage_i"][:, i % 2]
            if len(src_ap.shape) == 3:
                sl = sl.rearrange("p (cc j) -> p cc j", j=128)
            self.p.dma("sync", sl, src_ap, inc=[(w, 16)],
                       wait=[w.last(), (gx, max(0, gx.promised - 1))],
                       allow_slow_non_contiguous=slow, tag=f"fcl{i}")
            self.g(lambda g_, sl_=sl, i_=i: g_.tensor_copy(out=cast2[i_ % 2], in_=sl_),
                   wait=[w.last(), gdp.last()],
                   inc=[(gx, 1)], tag=f"fcc{i}")
            self.p.dma("gpsimd", dst_ap, cast2[i % 2],
                       wait=[gx.last()], inc=[(gdp, 16)], tag=f"fcs{i}")
        self.g(lambda g_, v=self.c["gd0"].promised: g_.wait_ge(self.c["gd0"].sem, v), tag="fcw0")
        self.g(lambda g_, v=self.c["gd1"].promised: g_.wait_ge(self.c["gd1"].sem, v),
               inc=[(gx, 1)], tag="fcw1")
        self.barrier()

    def build_moe_cache(self):
        S, D, c = self.S, self.D, self.c
        gx = c["gx"]
        jobs = []
        for e in range(NE):
            for cc in range(NCH):
                for sub in range(4):
                    jobs.append((D["w1"][e][cc * 128:(cc + 1) * 128, sub * 1024:(sub + 1) * 1024],
                                 D["w1_c"][e, cc][:, sub * 1024:(sub + 1) * 1024]))
            for hc in range(EH // 128):
                jobs.append((D["w2"][e][hc * 128:(hc + 1) * 128, :], D["w2_c"][e, hc]))
        cast2 = [S["cast_t"], S["cast_t2"]]
        cast_engs = ["gpsimd", "vector", "scalar"]
        cast_ch = [c["gx"], c["vx"], c["ev"]]
        base_cnt = [ch.promised for ch in cast_ch]
        for i, (src_ap, dst_ap) in enumerate(jobs):
            w = c["wB0"] if i % 2 == 0 else c["wB1"]
            gdp = c["gd0"] if i % 2 == 0 else c["gd1"]
            sl = S["stage"][:, i % 2]
            ce = i % 3
            self.p.dma("sync", sl, src_ap, inc=[(w, 16)],
                       wait=[w.last(), cast_ch[(i + 1) % 3].last()], tag=f"mc_l{i}")
            if cast_engs[ce] == "scalar":
                fn = (lambda en, sl_=sl, i_=i: en.copy(out=cast2[i_ % 2], in_=sl_))
            else:
                fn = (lambda en, sl_=sl, i_=i: en.tensor_copy(out=cast2[i_ % 2], in_=sl_))
            self.p.emit(cast_engs[ce], fn,
                wait=[w.last(), gdp.last()], inc=[(cast_ch[ce], 1)], tag=f"mc_c{i}")
            self.p.dma("gpsimd", dst_ap, cast2[i % 2],
                       wait=[cast_ch[ce].last()], inc=[(gdp, 16)], tag=f"mc_s{i}")
        self.g(lambda g_, v=self.c["gd0"].promised: g_.wait_ge(self.c["gd0"].sem, v),
               tag="mcw0")
        self.g(lambda g_, v=self.c["gd1"].promised: g_.wait_ge(self.c["gd1"].sem, v),
               inc=[(gx, 1)], tag="mcw1")
        self.barrier()

    def l0_ffn(self, s):
        S, D, c = self.S, self.D, self.c
        mm, ev, vx = c["mm"], c["ev"], c["vx"]
        cB = c["cB"]

        def src_tile(t):
            return S["r"][:, t * DM:(t + 1) * DM], []
        self.build_xT(src_tile, tag=f"s{s}x1")
        self.barrier()

        pa, fa = self.bank("A"); pb, fb = self.bank("B")
        pc, fc = self.bank("C"); pd, fd = self.bank("D")
        for tg in range(4):
            gu_mm = []
            for hc in range(NHC):
                par = hc % 2
                wB = c["wB0"] if par == 0 else c["wB1"]
                for name, buf in (("wg_c", "wg_hc"), ("wu_c", "wu_hc")):
                    self.p.dma("sync", S[buf][:, par], D[name][hc], inc=[(wB, 16)],
                               wait=[wB.last()] + ([(mm, gu_mm[hc - 2])] if hc >= 2 else []),
                               tag=f"ff{name}{hc}")
                w_rdy = wB.last()
                ph, fh = (pa, fa) if par == 0 else (pb, fb)
                pu, fu = (pc, fc) if par == 0 else (pd, fd)
                for cc in range(NCH):
                    self.t(lambda T, cc_=cc, ps_=ph, par_=par, tg_=tg: T.matmul(
                        ps_[:, :], S["wg_hc"][:, par_, cc_ * 128:(cc_ + 1) * 128],
                        S["x_T"][:, cc_ * SEQ + tg_ * 512: cc_ * SEQ + (tg_ + 1) * 512],
                        start=(cc_ == 0), stop=(cc_ == NCH - 1)),
                        wait=[w_rdy, fh.last()] if cc == 0 else (), tag=f"ffg{hc}")
                for cc in range(NCH):
                    self.t(lambda T, cc_=cc, ps_=pu, par_=par, tg_=tg: T.matmul(
                        ps_[:, :], S["wu_hc"][:, par_, cc_ * 128:(cc_ + 1) * 128],
                        S["x_T"][:, cc_ * SEQ + tg_ * 512: cc_ * SEQ + (tg_ + 1) * 512],
                        start=(cc_ == 0), stop=(cc_ == NCH - 1)),
                        wait=[fu.last()] if cc == 0 else (),
                        inc=[(mm, 1)] if cc == NCH - 1 else (), tag=f"ffu{hc}")
                gu_mm.append(mm.promised)
                self.s(lambda sc, ps_=ph: sc.activation(S["gate_t"], ps_[:, :], AF.Silu,
                                                        bias=0.0, scale=QSC),
                       wait=[mm.last(), vx.last()], inc=[(fh, 1)], tag=f"ffsl{hc}")
                self.s(lambda sc, ps_=pu: sc.activation(S["up_t"], ps_[:, :], AF.Copy,
                                                        bias=0.0, scale=QSC),
                       inc=[(ev, 1), (fu, 1)], tag=f"ffup{hc}")
                self.v(lambda v, hc_=hc: v.tensor_tensor(
                    S["h_grp"][:, hc_ * 512:(hc_ + 1) * 512], S["gate_t"], S["up_t"],
                    ALU.mult), wait=[ev.last()], inc=[(vx, 1)], tag=f"ffh{hc}")
            self.vdrain(inc=[(vx, 1)])
            banks = [self.bank("A"), self.bank("B"), self.bank("C"), self.bank("D")]
            wd_mm = []
            for half in range(2):
                for hc in range(NHC):
                    par = hc % 2
                    wB = c["wB0"] if par == 0 else c["wB1"]
                    k = half * NHC + hc
                    self.p.dma("sync", S["wd_hc"][:, par],
                               D["wd_c"][hc][:, half * 512:(half + 1) * 512],
                               inc=[(wB, 16)],
                               wait=[wB.last()] + ([(mm, wd_mm[k - 2])] if k >= 2 else []),
                               tag=f"ffwd{hc}")
                    w_rdy = wB.last()
                    for tt in range(4):
                        ph, fh = banks[tt]
                        self.t(lambda T, hc_=hc, tt_=tt, ps_=ph, par_=par: T.matmul(
                            ps_[:, :], S["h_grp"][:, hc_ * 512 + tt_ * 128: hc_ * 512 + (tt_ + 1) * 128],
                            S["wd_hc"][:, par_, :],
                            start=(hc_ == 0), stop=(hc_ == NHC - 1)),
                            wait=[w_rdy, vx.last()] + ([fh.last()] if hc == 0 else []),
                            inc=[(mm, 1)] if tt == 3 else (), tag=f"ffd{half}{hc}{tt}")
                    wd_mm.append(mm.promised)
                for tt in range(4):
                    t_ = tg * 4 + tt
                    ph, fh = banks[tt]
                    rt = S["r"][:, t_ * DM:(t_ + 1) * DM]
                    self.v(lambda v, ps_=ph, rt_=rt, h_=half: v.scalar_tensor_tensor(
                        rt_[:, h_ * 512:(h_ + 1) * 512], ps_[:, :], QSC,
                        rt_[:, h_ * 512:(h_ + 1) * 512], ALU.mult, ALU.add),
                        wait=[mm.last()], inc=[(fh, 1)], tag=f"ffde{half}{tt}")
                self.vdrain(inc=[(vx, 1)])
            for tt in range(4):
                t_ = tg * 4 + tt
                rt = S["r"][:, t_ * DM:(t_ + 1) * DM]
                tok = self.ln_tile(rt, wait=[vx.last()])
                self.p.dma("sync", D["r_spill"][t_ * 128:(t_ + 1) * 128, :], rt,
                           wait=[tok], inc=[(c["st"], 16)], tag=f"ffsp{t_}")
        self.barrier()

    def l1_attn(self, s):
        S, D, c = self.S, self.D, self.c
        mm, ev, vx = c["mm"], c["ev"], c["vx"]

        def src_tile(t):
            return S["r"][:, t * DM:(t + 1) * DM], []
        self.build_xT(src_tile, tag=f"s{s}x2")
        self.barrier()
        self.p.emit("sync", lambda e, v=c["st"].promised: e.wait_ge(c["st"].sem, v), tag="spw")
        self.barrier()

        self.stream_w(S["w_str"], [D["W2vT"][i * 128:(i + 1) * 128, :] for i in range(NCH)],
                      tag=f"s{s}w2v")
        self.v(lambda v: v.memset(
            S["V"].rearrange("p (t h u) -> p t h u", t=NT, h=NH)[:, :, :, 64:65], 1.0),
            tag="vone1")
        self.vdrain(inc=[(vx, 1)])
        self.proj_tokmajor(dst_V=True, wq=None, bias_bf=None, tag=f"s{s}v2")
        self.barrier()
        for wname, dst in (("W2qT", "q_fm"), ("W2kT", "k_fm")):
            self.stream_w(S["w_str"], [D[wname][i * 128:(i + 1) * 128, :] for i in range(NCH)],
                          tag=f"s{s}{wname}")
            self.qk_rms_proj(dst, tag=f"s{s}{wname}p")
            self.barrier()
        prev = None
        ix = 0
        for h in range(NH):
            for qg in range(4):
                cur = self.attn_core(h, qg, h % 2, causal=True, tag=f"L1h{h}g{qg}", pv_ix=ix)
                if prev is not None:
                    self.attn_tail(prev)
                prev = cur
                ix += 1
        self.attn_tail(prev)
        self.barrier()
        self.wo_ln(res_dram=D["r_spill"], wT=D["W2oT"], bias_dram=None,
                   scale=1.0, tag=f"s{s}wo1")
        self.barrier()

    def qk_rms_proj(self, dst_name, tag):
        S, c = self.S, self.c
        mm, ev, vx = c["mm"], c["ev"], c["vx"]
        pa, fa = self.bank("A"); pb, fb = self.bank("B")
        psT, fT = self.bank("T")
        dst = S[dst_name]
        for t in range(NT):
            for half, (ph, fh) in enumerate(((pa, fa), (pb, fb))):
                for cc in range(NCH):
                    self.t(lambda T, t_=t, cc_=cc, h_=half, ps_=ph: T.matmul(
                        ps_[:, :],
                        S["x_T"][:, cc_ * SEQ + t_ * 128: cc_ * SEQ + (t_ + 1) * 128],
                        S["w_str"][:, cc_ * DM + h_ * 512: cc_ * DM + h_ * 512 + 512],
                        start=(cc_ == 0), stop=(cc_ == NCH - 1)),
                        wait=[c["cA"].last(), vx.last(), fh.last()] if cc == 0 else (),
                        inc=[(mm, 1)] if (cc == NCH - 1 and half == 1) else (),
                        tag=f"{tag}m{t}")
            self.s(lambda sc: sc.copy(out=S["cast_t"][:, 0:512], in_=pa[:, :]),
                   wait=[mm.last()], inc=[(fa, 1)], tag=f"{tag}e0{t}")
            self.s(lambda sc: sc.copy(out=S["cast_t"][:, 512:1024], in_=pb[:, :]),
                   inc=[(ev, 1), (fb, 1)], tag=f"{tag}e1{t}")
            sq = S["stage"][:, 0]
            self.v(lambda v: v.tensor_tensor(sq, S["cast_t"], S["cast_t"], ALU.mult),
                   wait=[ev.last()], tag=f"{tag}sq{t}")
            self.vdrain()
            self.v(lambda v: v.tensor_reduce(
                S["rms16"], sq.rearrange("p (h u) -> p h u", u=64), AX.X, ALU.add),
                tag=f"{tag}sr{t}")
            self.vdrain(inc=[(vx, 1)])
            self.s(lambda sc: sc.activation(S["rms16b"], S["rms16"], AF.Sqrt,
                                            bias=S["eps_rms"], scale=1.0 / 64.0),
                   wait=[vx.last()], inc=[(ev, 1)], tag=f"{tag}sd{t}")
            self.v(lambda v: v.reciprocal(S["rms16"], S["rms16b"]),
                   wait=[ev.last()], tag=f"{tag}rc{t}")
            self.vdrain()
            self.v(lambda v: v.tensor_tensor(
                S["cast_t"].rearrange("p (h u) -> p h u", u=64),
                S["cast_t"].rearrange("p (h u) -> p h u", u=64),
                S["rms16"].rearrange("p (h o) -> p h o", o=1).broadcast_to([128, 16, 64]),
                ALU.mult), tag=f"{tag}ap{t}")
            self.vdrain(inc=[(vx, 1)])
            for cc in range(NCH):
                self.t(lambda T, cc_=cc: T.transpose(
                    psT[:, cc_ * 128:(cc_ + 1) * 128],
                    S["cast_t"][:, cc_ * 128:(cc_ + 1) * 128], S["ident_bf"]),
                    wait=[vx.last(), fT.last()] if cc == 0 else (),
                    inc=[(mm, 1)] if cc == NCH - 1 else (), tag=f"{tag}tr{t}")
            self.s(lambda sc, t_=t: sc.copy(
                out=dst.rearrange("p (c q) -> p c q", c=NCH)[:, :, t_ * 128:(t_ + 1) * 128],
                in_=psT.rearrange("p (c q) -> p c q", c=NCH)),
                wait=[mm.last()], inc=[(ev, 1), (fT, 1)], tag=f"{tag}te{t}")

    def routing(self, s):
        S, D, c = self.S, self.D, self.c
        mm, ev, vx = c["mm"], c["ev"], c["vx"]
        for t in range(NT):
            self.v(lambda v, t_=t: v.tensor_copy(
                out=S["cast_t"], in_=S["r"][:, t_ * DM:(t_ + 1) * DM]),
                wait=[(c["st"], max(0, c["st"].promised))], inc=[(vx, 1)], tag=f"xm{t}")
            self.p.dma("sync", D["xmoe"][s][t * 128:(t + 1) * 128, :], S["cast_t"],
                       wait=[vx.last()], inc=[(c["st"], 16)], tag=f"xms{t}")
        self.p.emit("sync", lambda e, v=c["st"].promised: e.wait_ge(c["st"].sem, v),
                    tag="xmw")
        self.barrier()

        def src_tile(t):
            return S["r"][:, t * DM:(t + 1) * DM], []
        self.build_xT(src_tile, tag=f"s{s}x3")
        self.barrier()

        self.v(lambda v: v.memset(S["bigc"], OOB_BIG), wait=[c["ev"].last()], tag="bigc")
        self.vdrain(inc=[(c["vx"], 1)])
        self.p.dma("sync", S["iota512"], D["iota512"][:, :], inc=[(c["ld"], 16)], tag="iota")
        self.p.emit("sync", lambda e, v=c["ld"].promised: e.wait_ge(c["ld"].sem, v), tag="iotaw")
        self.barrier()
        src = D["gateW"].rearrange("(cc p) e -> p cc e", p=128)
        self.p.dma("sync", S["stage"][:, 0, 0:NCH * 8].rearrange("p (cc e) -> p cc e", e=NE),
                   src, inc=[(c["ld"], 16)],
                   allow_slow_non_contiguous=True, tag="gw")
        self.v(lambda v: v.tensor_copy(out=S["gw_bf"], in_=S["stage"][:, 0, 0:NCH * 8]),
               wait=[c["ld"].last()], inc=[(vx, 1)], tag="gwc")
        pe, fe = self.bank("E")
        for qg in range(4):
            for cc in range(NCH):
                self.t(lambda T, cc_=cc, qg_=qg: T.matmul(
                    pe[0:8, :], S["gw_bf"][:, cc_ * 8:(cc_ + 1) * 8],
                    S["x_T"][:, cc_ * SEQ + qg_ * 512: cc_ * SEQ + (qg_ + 1) * 512],
                    start=(cc_ == 0), stop=(cc_ == NCH - 1)),
                    wait=[vx.last(), fe.last()] if cc == 0 else (),
                    inc=[(mm, 1)] if cc == NCH - 1 else (), tag=f"gs{qg}")
            self.s(lambda sc, qg_=qg: sc.copy(
                out=S["probs_et"][:, qg_ * 512:(qg_ + 1) * 512], in_=pe[0:8, :]),
                wait=[mm.last()], inc=[(ev, 1), (fe, 1)], tag=f"gse{qg}")
        pf, ff = self.bank("F")
        for t in range(NT):
            self.t(lambda T, t_=t: T.transpose(
                pf[:, 0:8], S["probs_et"][:, t_ * 128:(t_ + 1) * 128],
                S["ident_f32"][0:8, 0:8]),
                wait=[ev.last(), ff.last()], inc=[(mm, 1)], tag=f"smT{t}")
            self.v(lambda v: v.reduce_max(S["rms16"][:, 0:1], pf[:, 0:8], AX.X, negate=True),
                   wait=[mm.last()], tag=f"smm{t}")
            self.vdrain(inc=[(vx, 1)])
            self.s(lambda sc: sc.activation(S["prb8"], pf[:, 0:8], AF.Exp,
                                            bias=S["rms16"][:, 0:1], scale=1.0,
                                            accum_out=S["rms16"][:, 1:2]),
                   wait=[vx.last()], inc=[(ev, 1), (ff, 1)], tag=f"smx{t}")
            self.v(lambda v: v.reciprocal(S["rms16"][:, 2:3], S["rms16"][:, 1:2]),
                   wait=[ev.last()], tag=f"smr{t}")
            self.vdrain()
            self.v(lambda v: v.tensor_scalar(S["prb8"], S["prb8"], S["rms16"][:, 2:3],
                                             None, ALU.mult), tag=f"smn{t}")
            self.vdrain()
            self.v(lambda v, t_=t: v.tensor_copy(
                out=S["probsT"][:, s, t_ * 10 + 2: t_ * 10 + 10], in_=S["prb8"]), tag=f"pT{t}")
            self.v(lambda v, t_=t: v.tensor_copy(
                out=S["probsT"][:, s, t_ * 10: t_ * 10 + 1], in_=S["pcol"]), tag=f"pTc{t}")
            self.v(lambda v, t_=t: v.memset(
                S["probsT"][:, s, t_ * 10 + 1: t_ * 10 + 2], float(t_)), tag=f"pTi{t}")
            self.vdrain(inc=[(vx, 1)])
            self.t(lambda T, t_=t: T.transpose(
                pf[0:8, 128:256], S["prb8"], S["ident_f32"]),
                wait=[vx.last(), ff.last()], inc=[(mm, 1)], tag=f"pb{t}")
            self.s(lambda sc, t_=t: sc.copy(
                out=S["probs_et"][:, t_ * 128:(t_ + 1) * 128], in_=pf[0:8, 128:256]),
                wait=[mm.last()], inc=[(ev, 1), (ff, 1)], tag=f"pbe{t}")
        self.barrier()
        sr = S["srch"]
        self.v(lambda v: v.memset(sr[:, 0:1], 0.0), tag="bl0")
        self.v(lambda v: v.memset(sr[:, 1:2], 1.0), tag="bh0")
        self.vdrain()
        for it in range(N_BISECT):
            self.v(lambda v: v.tensor_add(sr[:, 2:3], sr[:, 0:1], sr[:, 1:2]), tag=f"bm{it}")
            self.vdrain()
            self.v(lambda v: v.tensor_scalar(sr[:, 2:3], sr[:, 2:3], 0.5, None, ALU.mult),
                   tag=f"bm2{it}")
            self.vdrain()
            self.v(lambda v: v.tensor_scalar(S["mask_et"], S["probs_et"], sr[:, 2:3],
                                             None, ALU.is_gt, op1=ALU.add,
                                             accum_out=sr[:, 3:4]), tag=f"bc{it}")
            self.vdrain()
            self.v(lambda v: v.tensor_scalar(S["srch_i"][:, 0:1], sr[:, 3:4], 512.0,
                                             None, ALU.is_ge), tag=f"bg{it}")
            self.v(lambda v: v.tensor_scalar(S["srch_i"][:, 1:2], sr[:, 3:4], 512.0,
                                             None, ALU.is_lt), tag=f"bg2{it}")
            self.vdrain()
            self.v(lambda v: v.select(sr[:, 0:1], S["srch_i"][:, 0:1], sr[:, 2:3],
                                      sr[:, 0:1], add_drain=True), tag=f"bsl{it}")
            self.v(lambda v: v.select(sr[:, 1:2], S["srch_i"][:, 1:2], sr[:, 2:3],
                                      sr[:, 1:2], add_drain=True), tag=f"bsh{it}")
            self.vdrain()
        self.v(lambda v: v.tensor_scalar(S["mask_et"], S["probs_et"], sr[:, 0:1],
                                         None, ALU.is_gt), tag="bmf")
        self.vdrain(inc=[(vx, 1)])
        psT, fT = self.bank("T")
        for t in range(NT):
            self.t(lambda T, t_=t: T.transpose(
                psT[:, t_ * 8:(t_ + 1) * 8], S["mask_et"][:, t_ * 128:(t_ + 1) * 128],
                S["ident_bf"][0:8, 0:8]),
                wait=[vx.last(), fT.last()] if t == 0 else (),
                inc=[(mm, 1)] if t == NT - 1 else (), tag=f"mT{t}")
        self.s(lambda sc: sc.copy(out=S["maskT_sb"], in_=psT[:, 0:128]),
               wait=[mm.last()], inc=[(ev, 1), (fT, 1)], tag="mTe")
        pu, fu = self.bank("U")
        for j in range(NT):
            for i in range(j + 1):
                lhs = S["tri_bf"] if i == j else S["ones_bf"]
                self.t(lambda T, i_=i, j_=j, l_=lhs: T.matmul(
                    pu[:, 0:8], l_, S["maskT_sb"][:, i_ * 8:(i_ + 1) * 8],
                    start=(i_ == 0), stop=(i_ == j_)),
                    wait=[ev.last(), fu.last()] if i == 0 else (),
                    inc=[(mm, 1)] if i == j else (), tag=f"cs{j}")
            self.v(lambda v: v.tensor_scalar(S["idx_t"], pu[:, 0:8], 512.0, None, ALU.is_le),
                   wait=[mm.last()], tag=f"ix1{j}")
            self.vdrain()
            self.v(lambda v, j_=j: v.tensor_tensor(
                S["idx_m"], S["idx_t"], S["maskT_sb"][:, j_ * 8:(j_ + 1) * 8], ALU.mult),
                tag=f"ix2{j}")
            self.v(lambda v: v.tensor_scalar(S["idx_t2"], pu[:, 0:8], 1.0, None, ALU.subtract),
                   inc=[(fu, 1)], tag=f"ix3{j}")
            self.vdrain()
            self.v(lambda v, j_=j: v.select(
                S["scat_all"][:, j_ * 8:(j_ + 1) * 8], S["idx_m"], S["idx_t2"], S["bigc"],
                add_drain=True), tag=f"ix4{j}")
            self.vdrain(inc=[(vx, 1)])
        for e in range(NE):
            for j in range(NT):
                Sb = S["cast_t"][:, (j % 2) * 512:(j % 2) * 512 + 512]
                self.v(lambda v, Sb_=Sb, j_=j, e_=e: v.tensor_scalar(
                    Sb_, S["iota512"], S["scat_all"][:, j_ * 8 + e_: j_ * 8 + e_ + 1],
                    None, ALU.is_equal),
                    wait=[vx.last()] if j == 0 else [mm.last()], tag=f"oh{e}{j}")
                self.vdrain(inc=[(vx, 1)])
                self.t(lambda T, Sb_=Sb, j_=j, e_=e, s_=s: T.matmul(
                    pu[0:10, :], S["probsT"][:, s_, j_ * 10:(j_ + 1) * 10],
                    Sb_, start=(j_ == 0), stop=(j_ == NT - 1)),
                    wait=[vx.last()] + ([fu.last()] if j == 0 else []),
                    inc=[(mm, 1)], tag=f"ohm{e}{j}")
            self.s(lambda sc: sc.copy(out=S["sc_rows"], in_=pu[0:10, :]),
                   wait=[mm.last(), (c["st"], c["st"].promised)],
                   inc=[(ev, 1), (fu, 1)], tag=f"ohe{e}")
            self.p.dma("sync", D["rows"][s, e, 0], S["sc_rows"][0:1, :],
                       wait=[ev.last()], inc=[(c["st"], 16)], tag=f"rw0{e}")
            self.p.dma("sync", D["rows"][s, e, 1], S["sc_rows"][1:2, :],
                       inc=[(c["st"], 16)], tag=f"rw1{e}")
            self.p.dma("sync", D["rows"][s, e, 2], S["sc_rows"][2 + e:3 + e, :],
                       inc=[(c["st"], 16)], tag=f"rwp{e}")
        self.barrier()
        self.p.emit("sync", lambda e, v=c["st"].promised: e.wait_ge(c["st"].sem, v), tag="rw")
        self.barrier()

    def moe(self):
        S, D, c = self.S, self.D, self.c
        mm, ev, vx, gx, gd = c["mm"], c["ev"], c["vx"], c["gx"], c["gdma"]
        self.v(lambda v: v.memset(S["stage"][:, 0], 0.0), inc=[(vx, 1)], tag="yz")
        for s in range(2):
            for t in range(NT):
                self.p.dma("sync", D["ymoe"][s][t * 128:(t + 1) * 128, :], S["stage"][:, 0],
                           wait=[vx.last()], inc=[(c["st"], 16)], tag=f"yz{s}{t}")
        banks4 = [self.bank("A"), self.bank("B"), self.bank("C"), self.bank("D")]
        psT, fT = self.bank("T")
        cA = c["cA"]
        self.p.emit("sync", lambda x_, v=c["st"].promised: x_.wait_ge(c["st"].sem, v), tag="yzw")
        self.barrier()
        w1_mm, w2_mm = [], []
        ks = [(e, s) for e in range(NE) for s in range(2)]
        sc_gd = {}

        def prep(kk):
            e, s = ks[kk]
            par = kk % 2
            self.p.dma("sync", S["gidxf"],
                       D["rows"][s, e, 0].rearrange("(j p) -> p j", p=128),
                       inc=[(c["ld"], 16)], wait=[vx.last()],
                       allow_slow_non_contiguous=True, tag=f"gi{e}{s}")
            self.p.dma("sync", S["gidxg"],
                       D["rows"][s, e, 1].rearrange("(j p) -> p j", p=128),
                       inc=[(c["ld"], 16)],
                       allow_slow_non_contiguous=True, tag=f"gj{e}{s}")
            self.p.dma("sync", S["pcomp2"][:, par],
                       D["rows"][s, e, 2].rearrange("(j p) -> p j", p=128),
                       inc=[(c["ld"], 16)],
                       allow_slow_non_contiguous=True, tag=f"gp{e}{s}")
            self.v(lambda v: v.scalar_tensor_tensor(
                S["gidxf"], S["gidxg"], 128.0, S["gidxf"], ALU.mult, ALU.add),
                wait=[c["ld"].last()], tag=f"giv{e}{s}")
            self.vdrain()
            self.v(lambda v, par_=par: v.tensor_copy(out=S["gidx2"][:, par_],
                                                     in_=S["gidxf"]),
                   wait=[(gd, sc_gd[kk - 2])] if kk >= 2 else (),
                   inc=[(vx, 1)], tag=f"gic{e}{s}")
            self.vdrain()
            for j in range(4):
                self.g(lambda g_, j_=j, s_=s, par_=par: g_.indirect_dma_start(
                    S["xin_tok"][:, j_ * DM:(j_ + 1) * DM], None,
                    D["xmoe"][s_][:, :],
                    IndirectOffsetOnAxis(ap=S["gidx2"][:, par_, j_:j_ + 1], axis=0)),
                    wait=[vx.last(), mm.last()], inc=[(gd, 16)], tag=f"gx{e}{s}{j}",
                    raw_inc=True)

        prep(0)
        for kk, (e, s) in enumerate(ks):
            par = kk % 2
            for j in range(4):
                for cc in range(NCH):
                    self.t(lambda T, j_=j, cc_=cc: T.transpose(
                        psT[:, ((j_ * NCH + cc_) % 8) * 128: ((j_ * NCH + cc_) % 8) * 128 + 128],
                        S["xin_tok"][:, j_ * DM + cc_ * 128: j_ * DM + (cc_ + 1) * 128],
                        S["ident_bf"]),
                        wait=[gd.last(), fT.last()] if (j == 0 and cc == 0) else
                             ([fT.last()] if (j * NCH + cc) % 8 == 0 else ()),
                        inc=[(mm, 1)] if (j * NCH + cc) % 8 == 7 else (), tag=f"xt{e}{s}")
                    if (j * NCH + cc) % 8 == 7:
                        u0 = (j * NCH + cc) - 7
                        def ev_t(sc, u0_=u0):
                            outs = []
                            for u in range(8):
                                jj, ccx = divmod(u0_ + u, NCH)
                                outs.append(sc.copy(
                                    out=S["xin_T"][:, ccx * 512 + jj * 128: ccx * 512 + (jj + 1) * 128],
                                    in_=psT[:, u * 128:(u + 1) * 128]))
                            return outs[-1]
                        self.s(ev_t, wait=[mm.last()], inc=[(ev, 1), (fT, 1)], tag=f"xte{e}{s}", multi=True)
            if kk + 1 < len(ks):
                prep(kk + 1)
            for hs in range(8):
                k1 = len(w1_mm)
                wpar = hs % 2
                wAp = c["wA0"] if wpar == 0 else c["wA1"]
                self.p.dma("sync",
                           S["w1col"][:, wpar].rearrange("p (cc x) -> p cc x", x=512),
                           D["w1_c"][e][:, :, hs * 512:(hs + 1) * 512].rearrange(
                               "cc p x -> p cc x"),
                           inc=[(wAp, 16)],
                           wait=[wAp.last()] + ([(mm, w1_mm[k1 - 2])] if k1 >= 2 else []),
                           tag=f"w1l{e}{hs}")
                w_rdy = wAp.last()
                for st in range(4):
                    ph, fh = banks4[st]
                    for cc in range(NCH):
                        self.t(lambda T, st_=st, cc_=cc, par_=wpar, ps_=ph: T.matmul(
                            ps_[:, :], S["xin_T"][:, cc_ * 512 + st_ * 128: cc_ * 512 + (st_ + 1) * 128],
                            S["w1col"][:, par_, cc_ * 512:(cc_ + 1) * 512],
                            start=(cc_ == 0), stop=(cc_ == NCH - 1)),
                            wait=[w_rdy, ev.last(), fh.last()] if cc == 0 else (),
                            inc=[(mm, 1)] if cc == NCH - 1 else (),
                            tag=f"w1m{e}{hs}{st}")
                    self.s(lambda sc, ps_=ph, st_=st: sc.activation(
                        S["cast_t"][:, (st_ % 2) * 512:(st_ % 2) * 512 + 512],
                        ps_[:, :], AF.Gelu, bias=0.0, scale=1.0),
                        wait=[mm.last()], inc=[(ev, 1), (fh, 1)], tag=f"gl{e}{hs}{st}")
                    for u in range(4):
                        self.t(lambda T, st_=st, u_=u: T.transpose(
                            psT[:, (st_ % 2) * 512 + u_ * 128: (st_ % 2) * 512 + (u_ + 1) * 128],
                            S["cast_t"][:, (st_ % 2) * 512 + u_ * 128:(st_ % 2) * 512 + (u_ + 1) * 128],
                            S["ident_bf"]),
                            wait=[ev.last(), fT.last()] if u == 0 else (),
                            inc=[(mm, 1)] if u == 3 else (), tag=f"ht{e}{hs}{st}")
                    def ev_h(sc, hs_=hs, st_=st):
                        o = None
                        for u in range(4):
                            hc = hs_ * 4 + u
                            o = sc.copy(
                                out=S["h_e"][:, hc * 512 + st_ * 128: hc * 512 + (st_ + 1) * 128],
                                in_=psT[:, (st_ % 2) * 512 + u * 128:(st_ % 2) * 512 + (u + 1) * 128])
                        return o
                    self.s(ev_h, wait=[mm.last()], inc=[(ev, 1), (fT, 1)], tag=f"he{e}{hs}{st}", multi=True)
                w1_mm.append(mm.promised)
            for half in range(2):
                for hc in range(EH // 128):
                    k2 = len(w2_mm)
                    wpar = hc % 2
                    wAp = c["wA0"] if wpar == 0 else c["wA1"]
                    self.p.dma("sync", S["w2_str"][:, wpar],
                               D["w2_c"][e, hc][:, half * 512:(half + 1) * 512],
                               inc=[(wAp, 16)],
                               wait=[wAp.last()] +
                                    ([(mm, w2_mm[k2 - 2])] if k2 >= 2 else []),
                               tag=f"w2l{e}{half}{hc}")
                    w_rdy = wAp.last()
                    for st in range(4):
                        ph, fh = banks4[st]
                        self.t(lambda T, hc_=hc, st_=st, par_=wpar, ps_=ph: T.matmul(
                            ps_[:, :], S["h_e"][:, hc_ * 512 + st_ * 128: hc_ * 512 + (st_ + 1) * 128],
                            S["w2_str"][:, par_, :],
                            start=(hc_ == 0), stop=(hc_ == EH // 128 - 1)),
                            wait=[w_rdy, ev.last()] + ([fh.last()] if hc == 0 else []),
                            inc=[(mm, 1)] if st == 3 else (),
                            tag=f"w2m{e}{half}{st}")
                    w2_mm.append(mm.promised)
                for st in range(4):
                    ph, fh = banks4[st]
                    self.v(lambda v, ps_=ph, st_=st, par_=par: v.tensor_scalar(
                        S["y_ev4"][:, st_, :], ps_[:, :],
                        S["pcomp2"][:, par_, st_:st_ + 1], None, ALU.mult),
                        wait=[mm.last(), gd.last()] if st == 0 else [mm.last()],
                        inc=[(vx, 1), (fh, 1)], tag=f"w2e{e}{half}{st}")
                self.vdrain()
                for st in range(4):
                    self.g(lambda g_, s_=s, st_=st, h_=half, par_=par: g_.indirect_dma_start(
                        D["ymoe"][s_][:, :],
                        IndirectOffsetOnAxis(ap=S["gidx2"][:, par_, st_:st_ + 1], axis=0),
                        S["y_ev4"][:, st_], None, compute_op=ALU.add,
                        element_offset=h_ * 512),
                        wait=[vx.last(), gd.last()], inc=[(gd, 16)], tag=f"sc{e}{s}{st}",
                        raw_inc=True)
            sc_gd[kk] = gd.promised
        self.barrier()
        self.g(lambda g_, v=gd.promised: g_.wait_ge(gd.sem, v), inc=[(gx, 1)], tag="gdw")
        self.barrier()

    def final(self):
        S, D, c = self.S, self.D, self.c
        vx, ev = c["vx"], c["ev"]
        for s in range(2):
            for t in range(NT):
                self.p.dma("sync", S["stage"][:, 0], D["ymoe"][s][t * 128:(t + 1) * 128, :],
                           inc=[(c["ld"], 16)], wait=[ev.last(), c["fin"].last()], tag=f"fy{s}{t}")
                if s == 0:
                    self.p.dma("sync", S["stage"][:, 1], D["r3A"][t * 128:(t + 1) * 128, :],
                               inc=[(c["ld"], 16)], tag=f"fr{s}{t}")
                    res = S["stage"][:, 1]
                else:
                    res = S["r"][:, t * DM:(t + 1) * DM]
                yt = S["stage"][:, 0]
                self.v(lambda v, r_=res: v.tensor_add(yt, yt, r_),
                       wait=[c["ld"].last(), c["fin"].last()], inc=[(vx, 1)], tag=f"fa{s}{t}")
                self.vdrain()
                tok = self.ln_tile(yt, wait=[vx.last()])
                self.p.dma("sync", D["y"][s][t * 128:(t + 1) * 128, :], yt,
                           wait=[tok], inc=[(c["fin"], 16)], tag=f"fo{s}{t}")
        self.p.emit("sync", lambda e, v=c["fin"].promised: e.wait_ge(c["fin"].sem, v), tag="finw")

    def build(self):
        self.load_consts()
        if self.phase_done("consts"):
            return self.dump_r_and_finish()
        self.build_ffn_cache()
        self.build_moe_cache()
        for s in range(2):
            self._seq = s
            self.load_T12()
            self.l0_attn(s)
            if self.phase_done(f"l0a:{s}"):
                return self.dump_r_and_finish()
            self.l0_ffn(s)
            if self.phase_done(f"l0f:{s}"):
                return self.dump_r_and_finish()
            self.l1_attn(s)
            if self.phase_done(f"l1a:{s}"):
                return self.dump_r_and_finish()
            self.routing(s)
            if self.phase_done(f"rout:{s}"):
                return self.dump_r_and_finish()
            if s == 0:
                self.p.dma("sync",
                           self.D["r3A"].rearrange("(t p) d -> p t d", p=128),
                           self.S["r"].rearrange("p (t d) -> p t d", d=DM),
                           wait=[self.c["ev"].last(), self.c["vx"].last()],
                           inc=[(self.c["st"], 16)], tag="r3sp")
                self.p.emit("sync", lambda e, v=self.c["st"].promised: e.wait_ge(
                    self.c["st"].sem, v))
                self.barrier()
        self.moe()
        if self.phase_done("moe"):
            return self.dump_r_and_finish()
        self.final()


def build_program(stop_after=None):
    b = Build(stop_after=stop_after)
    b.build()
    b.p.finalize()
    return b


def kernel(**inputs):
    pre = host_prepare(inputs)
    x = np.asarray(inputs["L0_attn_x"], np.float32)
    b = build_program()
    in_maps = []
    for core in range(NCORE):
        m = dict(pre)
        m["x"] = np.ascontiguousarray(x[2 * core: 2 * core + 2])
        in_maps.append(m)
    res = run_bass_kernel_spmd(b.nc, in_maps, core_ids=list(range(NCORE)))
    out = np.concatenate([r["y"] for r in res.results], axis=0)
    return out.reshape(16, SEQ, DM)


# version 13
# speedup vs baseline: 1.0566x; 1.0422x over previous
import numpy as np
import ml_dtypes
import concourse.bass as bass
import concourse.mybir as mybir
from concourse.bass import IndirectOffsetOnAxis
from concourse.bass_utils import run_bass_kernel_spmd
from concourse import library_config as _LC

F32 = mybir.dt.float32
BF16 = mybir.dt.bfloat16
I32 = mybir.dt.int32
AF = mybir.ActivationFunctionType
ALU = mybir.AluOpType
AX = mybir.AxisListType
BF = ml_dtypes.bfloat16

ENGINES = ("sync", "scalar", "vector", "tensor", "gpsimd")

NCORE = 8
SEQ = 2048
DM = 1024
NH = 16
DH = 64
HID = 3072
NHC = HID // 128
NE = 8
EH = 4096
CAP = 512
NT = SEQ // 128
NCH = DM // 128
THETA = 10000.0
LN_EPS = 1e-5
RMS_EPS = 1.1920929e-07
QSC = 1.0 / 127.0
N_BISECT = 34
OOB_BIG = 1.0e7


class Chan:
    __slots__ = ("name", "sem", "promised")
    def __init__(self, nc, name):
        self.name = name
        self.sem = nc.alloc_semaphore(name)
        self.promised = 0
    def last(self):
        return (self, self.promised)


class Prog:
    def __init__(self, nc):
        self.nc = nc
        self.streams = {e: [] for e in ENGINES}
        self.n_ops = 0

    def chan(self, name):
        return Chan(self.nc, name)

    def emit(self, eng, fn, wait=(), inc=(), tag="", raw_inc=False, multi=False):
        wait = tuple(wait)
        inc = tuple(inc)
        for ch, v in wait:
            assert v <= ch.promised, (
                f"wait {ch.name}>={v} but promised {ch.promised} ({tag})")
        for ch, k in inc:
            ch.promised += k
        if multi and eng not in ("sync", "tensor"):
            self.streams[eng].append((wait, fn, (), tag))
            self.n_ops += 1
            rest = inc
        else:
            self.streams[eng].append((wait, fn, inc[:1], tag))
            self.n_ops += 1
            rest = inc[1:]
        for ch, k in rest:
            self.streams[eng].append(((), (lambda e: e.drain()), ((ch, k),), tag + "+i"))
            self.n_ops += 1

    def dma(self, eng, out, in_, inc=(), wait=(), tag="", **kw):
        self.emit(eng, (lambda e, o=out, i=in_, k=kw: e.dma_start(out=o, in_=i, **k)),
                  wait=wait, inc=inc, tag=tag, raw_inc=True)

    def finalize(self):
        nc = self.nc
        with nc.Block() as block:
            def runner(stream):
                def run(engine):
                    for wait, fn, inc, tag in stream:
                        for ch, v in wait:
                            engine.wait_ge(ch.sem, v)
                        try:
                            ins = fn(engine)
                        except Exception:
                            import traceback as _tb
                            _tb.print_exc()
                            raise
                        if inc:
                            assert ins is not None, f"inc w/o instruction ({tag})"
                            for ch, k in inc:
                                ins = ins.then_inc(ch.sem, k)
                return run
            block.sync(runner(self.streams["sync"]))
            block.scalar(runner(self.streams["scalar"]))
            block.vector(runner(self.streams["vector"]))
            block.tensor(runner(self.streams["tensor"]))
            block.gpsimd(runner(self.streams["gpsimd"]))



def rope_perm():
    perm = np.zeros(DM, np.int64)
    for h in range(NH):
        for k in range(DH):
            i, s_ = k // 2, k % 2
            perm[h * DH + (i if s_ == 0 else 32 + i)] = h * DH + k
    return perm


def rope_tables():
    t = np.arange(SEQ, dtype=np.float64)
    i = np.arange(32, dtype=np.float64)
    ang = np.outer(i / THETA, t)
    cos = np.cos(ang); sin = np.sin(ang)
    T1 = np.tile(np.concatenate([cos, cos], 0), (1, 2))
    T2 = np.tile(np.concatenate([-sin, sin], 0), (1, 2))
    return T1.astype(BF), T2.astype(BF)


def host_prepare(inp):
    perm = rope_perm()
    g = lambda n: np.asarray(inp[n])
    d = {}
    d["WqT"] = np.ascontiguousarray(g("L0_attn_Wq")[perm, :].T.astype(np.float32))
    d["WkT"] = np.ascontiguousarray(g("L0_attn_Wk")[perm, :].T.astype(np.float32))
    d["WvT"] = np.ascontiguousarray(g("L0_attn_Wv").T.astype(np.float32))
    d["WoT"] = np.ascontiguousarray(g("L0_attn_Wo").T.astype(np.float32))
    d["bq8"] = np.ascontiguousarray(g("L0_attn_bq")[perm].reshape(NCH, 128).T.astype(np.float32))
    d["bk8"] = np.ascontiguousarray(g("L0_attn_bk")[perm].reshape(NCH, 128).T.astype(np.float32))
    d["bv_row"] = g("L0_attn_bv").reshape(1, DM).astype(np.float32)
    d["bo_row"] = g("L0_attn_bo").reshape(1, DM).astype(np.float32)
    d["wgT"] = np.ascontiguousarray(g("L0_ffn_w_gate").T.astype(np.int32))
    d["wuT"] = np.ascontiguousarray(g("L0_ffn_w_up").T.astype(np.int32))
    d["wdT"] = np.ascontiguousarray(g("L0_ffn_w_down").T.astype(np.int32))
    d["W2qT"] = np.ascontiguousarray(g("L1_attn_w_q").T.astype(np.float32))
    d["W2kT"] = np.ascontiguousarray(g("L1_attn_w_k").T.astype(np.float32))
    d["W2vT"] = np.ascontiguousarray(g("L1_attn_w_v").T.astype(np.float32))
    d["W2oT"] = np.ascontiguousarray(g("L1_attn_w_o").T.astype(np.float32))
    d["gateW"] = np.ascontiguousarray(g("L1_ffn_gate_w").astype(np.float32))
    d["w1"] = np.ascontiguousarray(g("L1_ffn_w1").astype(np.float32))
    d["w2"] = np.ascontiguousarray(g("L1_ffn_w2").astype(np.float32))
    T1, T2 = rope_tables()
    d["cT1"] = T1; d["cT2"] = T2
    d["ident_bf"] = np.eye(128).astype(BF)
    d["ident_f32"] = np.eye(128, dtype=np.float32)
    d["tri_bf"] = (np.arange(128)[None, :] >= np.arange(128)[:, None]).astype(BF)
    d["ones_bf"] = np.ones((128, 128), BF)
    d["iota512"] = np.tile(np.arange(512, dtype=np.float32), (128, 1))
    d["pcol"] = np.arange(128, dtype=np.float32).reshape(128, 1)
    return d



class Build:
    def __init__(self, stop_after=None):
        nc = bass.Bass("TRN2", target_bir_lowering=False)
        self.nc = nc
        self.p = Prog(nc)
        self.stop_after = stop_after
        self.stopped = False
        self._mk_dram()
        self._mk_sbuf()
        self._mk_psum()
        self._mk_chans()

    def _mk_dram(self):
        nc = self.nc
        D = {}
        def ein(name, shape, dt):
            D[name] = nc.dram_tensor(name, list(shape), dt, kind="ExternalInput").ap()
        ein("x", (2, SEQ, DM), F32)
        for n in ("WqT", "WkT", "WvT", "WoT", "W2qT", "W2kT", "W2vT", "W2oT"):
            ein(n, (DM, DM), F32)
        ein("bq8", (128, NCH), F32); ein("bk8", (128, NCH), F32)
        ein("bv_row", (1, DM), F32); ein("bo_row", (1, DM), F32)
        ein("wgT", (DM, HID), I32); ein("wuT", (DM, HID), I32); ein("wdT", (HID, DM), I32)
        ein("gateW", (DM, NE), F32)
        ein("w1", (NE, DM, EH), F32); ein("w2", (NE, EH, DM), F32)
        ein("cT1", (64, 2 * SEQ), BF16); ein("cT2", (64, 2 * SEQ), BF16)
        ein("ident_bf", (128, 128), BF16); ein("ident_f32", (128, 128), F32)
        ein("tri_bf", (128, 128), BF16); ein("ones_bf", (128, 128), BF16)
        ein("iota512", (128, 512), F32); ein("pcol", (128, 1), F32)
        D["y"] = nc.dram_tensor("y", [2, SEQ, DM], F32, kind="ExternalOutput").ap()
        D["dbg"] = nc.dram_tensor("dbg", [128, NT * DM], F32, kind="ExternalOutput").ap()
        D["r_spill"] = nc.dram_tensor("r_spill", [SEQ, DM], F32, kind="Internal").ap()
        D["r3A"] = nc.dram_tensor("r3A", [SEQ, DM], F32, kind="Internal").ap()
        D["xmoe"] = [nc.dram_tensor("xmoe0", [SEQ, DM], BF16, kind="Internal").ap(),
                     nc.dram_tensor("xmoe1", [SEQ, DM], BF16, kind="Internal").ap()]
        D["ymoe"] = [nc.dram_tensor("ymoe0", [SEQ, DM], F32, kind="Internal").ap(),
                     nc.dram_tensor("ymoe1", [SEQ, DM], F32, kind="Internal").ap()]
        D["rows"] = nc.dram_tensor("rows", [2, NE, 3, 512], F32, kind="Internal").ap()
        D["wg_c"] = nc.dram_tensor("wg_c", [NHC, 128, NCH * 128], BF16, kind="Internal").ap()
        D["wu_c"] = nc.dram_tensor("wu_c", [NHC, 128, NCH * 128], BF16, kind="Internal").ap()
        D["wd_c"] = nc.dram_tensor("wd_c", [NHC, 128, DM], BF16, kind="Internal").ap()
        D["w1_c"] = nc.dram_tensor("w1_c", [NE, NCH, 128, EH], BF16, kind="Internal").ap()
        D["w2_c"] = nc.dram_tensor("w2_c", [NE, EH // 128, 128, DM], BF16, kind="Internal").ap()
        self.D = D

    def _mk_sbuf(self):
        base = (self.nc.sbuf_base + 127) // 128 * 128
        at = self.nc.alloc_sbuf_tensor_at
        S = {}
        def A(name, shape, dt, offset):
            assert offset % 32 == 0, (name, offset)
            S[name] = at(name, list(shape), dt, offset=base + offset).ap()
        A("r", [128, NT * DM], F32, 0)
        A("q_fm", [128, NCH * SEQ], BF16, 0)
        A("k_fm", [128, NCH * SEQ], BF16, 32768)
        A("T1", [64, 2 * SEQ], BF16, 49152)
        A("T2", [64, 2 * SEQ], BF16, 57344)
        A("x_T", [128, NCH * SEQ], BF16, 65536)
        A("xin_tok", [128, 4 * DM], BF16, 65536)
        A("xin_T", [128, NCH * 512], BF16, 65536 + 8192)
        A("gidxf", [128, 4], F32, 65536 + 16384)
        A("gidxg", [128, 4], F32, 65536 + 16576)
        A("gidx", [128, 4], I32, 65536 + 16448)
        A("pcomp", [128, 4], F32, 65536 + 16512)
        A("gidx2", [128, 2, 4], I32, 65536 + 16640)
        A("pcomp2", [128, 2, 4], F32, 65536 + 16704)
        A("V", [128, NT * NH * 65], BF16, 98304)
        A("h_grp", [128, NHC * 512], BF16, 98304)
        A("attn_fm", [128, NCH * SEQ], BF16, 131584)
        A("h_e", [128, (EH // 128) * 512], BF16, 131584)
        A("P_buf", [128, 8 * 512], BF16, 164352)
        A("mc_st", [128, 2, DM], F32, 164352)
        A("gate_t", [128, 512], BF16, 164352 + 4096)
        A("up_t", [128, 512], BF16, 164352 + 5120)
        A("y_ev4", [128, 4, 512], F32, 164352)
        A("prb8", [128, 8], F32, 164352 + 8192)
        A("ln6", [128, 12], F32, 164352 + 8224)
        A("ln2", [128, 2], F32, 164352 + 8288)
        A("lncol", [128, 3], F32, 164352 + 8320)
        A("rms16", [128, 16], F32, 164352 + 8352)
        A("rms16b", [128, 16], F32, 164352 + 8416)
        QKB = 164352 + 8512
        A("mc_cb", [128, 2, DM], BF16, QKB)
        A("Q_store", [65, 2 * SEQ], BF16, QKB)
        A("K_store", [65, 2 * SEQ], BF16, QKB + 8192)
        A("w_str", [128, NCH * DM], BF16, QKB)
        A("w1col", [128, 2, NCH * 512], BF16, QKB)
        A("probs_et", [NE, SEQ], F32, QKB)
        A("mask_et", [NE, SEQ], BF16, QKB + 8192)
        A("w2_str", [128, 4, 512], BF16, 65536 + 17408)
        A("sc_rows", [10, 512], F32, QKB + 12288)
        A("srch", [NE, 8], F32, QKB + 14336)
        A("gw_bf", [128, NCH * 8], BF16, QKB + 14368)
        A("bigc", [128, 8], F32, QKB + 14496)
        A("srch_i", [NE, 2], I32, QKB + 14528)
        A("wg_hc", [128, 2, NCH * 128], BF16, QKB)
        A("wu_hc", [128, 2, NCH * 128], BF16, QKB + 4096)
        A("wd_hc", [128, 2, 512], BF16, QKB + 8192)
        TMB = 164352 + 8512 + 16384
        A("stage", [128, 2, DM], F32, TMB)
        A("stage_i", [128, 2, DM], I32, TMB)
        A("SWQ", [64, SEQ], BF16, TMB + 8192)
        A("cast_t2", [128, DM], BF16, TMB + 8192)
        A("tanhb", [128, 512], F32, TMB + 8192)
        A("iota512", [128, 512], F32, TMB + 8192)
        A("scat_all", [128, 128], F32, TMB + 8192 + 2048)
        A("maskT_sb", [128, 128], BF16, TMB + 8192 + 2560)
        A("idx_t", [128, 8], F32, TMB + 8192 + 2816)
        A("idx_t2", [128, 8], F32, TMB + 8192 + 2848)
        A("idx_m", [128, 8], I32, TMB + 8192 + 2880)
        A("qk_w", [128, NCH * 128], BF16, TMB + 12288)
        A("cast_t", [128, DM], BF16, TMB + 14336)
        S["att_t"] = S["cast_t"][0:64, 0:512]
        A("inv_t", [1, 512], F32, TMB + 16384)
        A("row_bf", [1, DM], BF16, TMB + 18432)
        off = TMB + 20480
        def C(name, shape, dt, nbytes):
            nonlocal off
            assert off % 32 == 0, (name, off)
            S[name] = at(name, list(shape), dt, offset=base + off).ap()
            off += (nbytes + 31) // 32 * 32
        C("ident_bf", (128, 128), BF16, 256)
        C("tri_bf", (128, 128), BF16, 256)
        C("ones_bf", (128, 128), BF16, 256)
        C("pcol", (128, 1), F32, 4)
        C("eps2", (128, 2), F32, 8)
        S["eps_ln"] = S["eps2"][:, 0:1]
        S["eps_rms"] = S["eps2"][:, 1:2]
        C("ones_col", (128, 1), BF16, 4)
        C("ones1f", (1, 64), F32, 256)
        C("bq8", (128, NCH), F32, 32)
        C("bk8", (128, NCH), F32, 32)
        C("probsT", (128, 2, NT * 10), BF16, 2 * NT * 10 * 2)
        C("ident_f32", (128, 128), F32, 512)
        C("kmax", (1, 2), F32, 8)
        assert base + off <= self.nc.sbuf_top, (base, off, self.nc.sbuf_top)
        self.S = S

    def _mk_psum(self):
        nc = self.nc
        P = {}
        for n in ("psA", "psB", "psC", "psD", "psE", "psF", "psU"):
            P[n] = nc.alloc_psum_tensor(n, [128, 512], F32).ap()
        P["psT"] = nc.alloc_psum_tensor("psT", [128, 1024], BF16).ap()
        self.PS = P

    def _mk_chans(self):
        p = self.p
        self.c = {}
        names = ["ld", "st", "wA", "cA", "wB", "cB", "mm", "ev", "vx", "gx",
                 "gdma", "fin", "bar_s", "bar_c", "bar_v", "bar_t", "bar_g",
                 "wA0", "wA1", "wB0", "wB1", "ld0", "ld1", "gd0", "gd1",
                 "mcA", "mcB", "mcC", "mcD"]
        names += [f"f{b}" for b in ("A", "B", "C", "D", "E", "F", "U", "T")]
        for n in names:
            self.c[n] = p.chan(n)

    def v(self, fn, wait=(), inc=(), tag="", multi=False):
        self.p.emit("vector", fn, wait, inc, tag, multi=multi)
    def s(self, fn, wait=(), inc=(), tag="", multi=False):
        self.p.emit("scalar", fn, wait, inc, tag, multi=multi)
    def t(self, fn, wait=(), inc=(), tag=""): self.p.emit("tensor", fn, wait, inc, tag)
    def g(self, fn, wait=(), inc=(), tag="", raw_inc=False, multi=False):
        self.p.emit("gpsimd", fn, wait, inc, tag, raw_inc=raw_inc, multi=multi)
    def vdrain(self, wait=(), inc=()): self.v(lambda v: v.drain(), wait, inc)
    def sdrain(self, wait=(), inc=()): self.s(lambda s: s.drain(), wait, inc)
    def gdrain(self, wait=(), inc=()): self.g(lambda g: g.drain(), wait, inc)

    def bank(self, n):
        return self.PS["ps" + n], self.c["f" + n]

    def barrier(self):
        c = self.c
        pairs = [("sync", "bar_s"), ("scalar", "bar_c"), ("vector", "bar_v"),
                 ("tensor", "bar_t"), ("gpsimd", "bar_g")]
        for eng, ch in pairs:
            self.p.emit(eng, lambda e: e.nop(), inc=[(c[ch], 1)], tag="bar", multi=True)
        toks = [c[ch].last() for _, ch in pairs]
        for eng, _ in pairs:
            self.p.emit(eng, lambda e: e.nop(), wait=toks, tag="barw")

    def phase_done(self, name):
        if self.stop_after is not None and name == self.stop_after:
            self.stopped = True
            return True
        return False

    def dump_r_and_finish(self):
        self.barrier()
        c = self.c
        self.p.dma("sync", self.D["dbg"][:, :], self.S["r"], inc=[(c["fin"], 16)], tag="dbg")
        self.p.emit("sync", lambda e, v=c["fin"].promised: e.wait_ge(c["fin"].sem, v), tag="finw")

    def load_consts(self):
        p, S, D, c = self.p, self.S, self.D, self.c
        ld = c["ld"]
        for n in ("ident_bf", "ident_f32", "tri_bf", "ones_bf",
                  "pcol", "bq8", "bk8"):
            p.dma("sync", S[n], D[n][:, :], inc=[(ld, 16)], tag=f"cst_{n}")
        self.v(lambda v: v.memset(S["eps_ln"], LN_EPS))
        self.v(lambda v: v.memset(S["eps_rms"], RMS_EPS))
        self.v(lambda v: v.memset(S["ones_col"], 1.0))
        self.v(lambda v: v.memset(S["ones1f"], 1.0))
        self.vdrain(inc=[(c["vx"], 1)])
        self.g(lambda g: g.nop(), inc=[(c["gx"], 1)])
        self.p.emit("sync", lambda e, v=ld.promised: e.wait_ge(ld.sem, v), tag="cstw")
        self.barrier()

    def load_T12(self):
        p, S, D, c = self.p, self.S, self.D, self.c
        ld = c["ld"]
        p.dma("sync", S["T1"], D["cT1"][:, :], inc=[(ld, 16)], tag="T1")
        p.dma("sync", S["T2"], D["cT2"][:, :], inc=[(ld, 16)], tag="T2")
        p.emit("sync", lambda e, v=ld.promised: e.wait_ge(ld.sem, v), tag="T12w")
        self.barrier()

    def build_xT(self, src_tile_fn, tag="xT"):
        S, c = self.S, self.c
        psT, fT = self.bank("T")
        mm, ev, vx = c["mm"], c["ev"], c["vx"]
        cmm = []
        for t in range(NT):
            src, w = src_tile_fn(t)
            ct = S["cast_t"] if t % 2 == 0 else S["cast_t2"]
            self.v(lambda v, s_=src, ct_=ct: v.tensor_copy(out=ct_, in_=s_),
                   wait=list(w) + ([(mm, cmm[t - 2])] if t >= 2 else []),
                   inc=[(vx, 1)], tag=f"{tag}c{t}")
            for cc in range(NCH):
                self.t(lambda T, cc_=cc, ct_=ct: T.transpose(
                    psT[:, cc_ * 128:(cc_ + 1) * 128],
                    ct_[:, cc_ * 128:(cc_ + 1) * 128], S["ident_bf"]),
                    wait=[vx.last(), fT.last()] if cc == 0 else (),
                    inc=[(mm, 1)] if cc == NCH - 1 else (), tag=f"{tag}t{t}")
            cmm.append(mm.promised)
            self.s(lambda s, t_=t: s.copy(
                out=S["x_T"].rearrange("p (c q) -> p c q", c=NCH)[:, :, t_ * 128:(t_ + 1) * 128],
                in_=psT.rearrange("p (c q) -> p c q", c=NCH)),
                wait=[mm.last()], inc=[(ev, 1), (fT, 1)], tag=f"{tag}e{t}")
        return ev.last()

    def hbm_tile(self, dram2d, t, slot):
        S, c = self.S, self.c
        ld = c["ld0"] if slot == 0 else c["ld1"]
        sl = S["stage"][:, slot]
        self.p.dma("sync", sl, dram2d[t * 128:(t + 1) * 128, :], inc=[(ld, 16)],
                   wait=[c["vx"].last(), c["mm"].last(), c["ev"].last(),
                         c["cA"].last(), c["cB"].last()], tag=f"ht{t}")
        return sl, [ld.last()]

    def stream_w(self, dst, srcs, tag="w", eng="vector", chunk_elems=DM, src_int=False,
                 extra_wait=()):
        S, c = self.S, self.c
        ca = c["cA"]
        stage = S["stage_i"] if src_int else S["stage"]
        for i, src in enumerate(srcs):
            w = c["wA0"] if i % 2 == 0 else c["wA1"]
            sl = stage[:, i % 2, :chunk_elems]
            self.p.dma("sync", sl, src, inc=[(w, 16)],
                       wait=[w.last(), (ca, max(0, ca.promised - 1))] +
                            (list(extra_wait) if i == 0 else []),
                       tag=f"{tag}l{i}")
            self.p.emit(eng, lambda e, i_=i, sl_=sl: e.tensor_copy(
                out=dst[:, i_ * chunk_elems:(i_ + 1) * chunk_elems], in_=sl_),
                wait=[w.last()], inc=[(ca, 1)], tag=f"{tag}c{i}")
        return ca.last()

    def ln_tile(self, ytile, wait=()):
        S, c = self.S, self.c
        ev, vx = c["ev"], c["vx"]
        self.v(lambda v: v.bn_stats(S["ln6"][:, 0:6], ytile[:, 0:512]),
               wait=list(wait) + [c["ev"].last()], tag="lns0")
        self.v(lambda v: v.bn_stats(S["ln6"][:, 6:12], ytile[:, 512:1024]), tag="lns1")
        self.vdrain()
        self.v(lambda v: v.bn_aggr(S["ln2"], S["ln6"]), tag="lnagg")
        self.vdrain(inc=[(vx, 1)])
        self.s(lambda s: s.activation(S["lncol"][:, 0:1], S["ln2"][:, 1:2], AF.Sqrt,
                                      bias=S["eps_ln"], scale=1.0),
               wait=[vx.last()], inc=[(ev, 1)], tag="lnsqrt")
        self.v(lambda v: v.reciprocal(S["lncol"][:, 1:2], S["lncol"][:, 0:1]),
               wait=[ev.last()], tag="lnrec")
        self.vdrain()
        self.v(lambda v: v.tensor_scalar(S["lncol"][:, 2:3], S["ln2"][:, 0:1],
                                         S["lncol"][:, 1:2], -1.0, ALU.mult, ALU.mult),
               tag="lnnb")
        self.vdrain(inc=[(vx, 1)])
        self.s(lambda s: s.activation(ytile, ytile, AF.Identity,
                                      bias=S["lncol"][:, 2:3], scale=S["lncol"][:, 1:2]),
               wait=[vx.last()], inc=[(ev, 1)], tag="lnap")
        return ev.last()

    def l0_attn(self, s):
        S, D, c = self.S, self.D, self.c
        mm, ev, vx = c["mm"], c["ev"], c["vx"]

        def src_tile(t):
            return self.hbm_tile(D["x"][s], t, t % 2)
        self.build_xT(src_tile, tag=f"s{s}x0")
        self.barrier()

        self.p.dma("sync", S["stage"][0:1, 0, :], D["bv_row"][:, :],
                   wait=[c["cA"].last(), c["vx"].last()], inc=[(c["ld"], 16)], tag="bv")
        self.v(lambda v: v.tensor_copy(out=S["row_bf"], in_=S["stage"][0:1, 0, :]),
               wait=[c["ld"].last()], tag="bvc")
        self.vdrain(inc=[(c["vx"], 1)])
        self.stream_w(S["w_str"], [D["WvT"][i * 128:(i + 1) * 128, :] for i in range(NCH)],
                      extra_wait=[c["vx"].last()], tag=f"s{s}wv")
        self.v(lambda v: v.memset(
            S["V"].rearrange("p (t h u) -> p t h u", t=NT, h=NH)[:, :, :, 64:65], 1.0),
            tag="vones")
        self.vdrain(inc=[(vx, 1)])
        self.proj_tokmajor(dst_V=True, wq=None, bias_bf=S["row_bf"], tag=f"s{s}V")
        self.barrier()

        self.v(lambda v: v.memset(S["K_store"][64:65, :], 1.0), inc=[(vx, 1)], tag="kone")
        for hp in range(NH // 2):
            self.l0_hp(s, hp)
        self.barrier()

        self.wo_ln(res_dram=D["x"][s], wT=D["WoT"], bias_dram=D["bo_row"],
                   scale=1.0, tag=f"s{s}wo0")
        self.barrier()

    def proj_tokmajor(self, dst_V, wq, bias_bf, tag):
        S, c = self.S, self.c
        mm, ev = c["mm"], c["ev"]
        pa, fa = self.bank("A"); pb, fb = self.bank("B")
        pc, fc = self.bank("C"); pd, fd = self.bank("D")
        for t in range(NT):
            p0, f0, p1, f1 = (pa, fa, pb, fb) if t % 2 == 0 else (pc, fc, pd, fd)
            for half, (ph, fh) in enumerate(((p0, f0), (p1, f1))):
                last_h = (half == 1)
                for cc in range(NCH):
                    self.t(lambda T, t_=t, cc_=cc, h_=half, ps_=ph: T.matmul(
                        ps_[:, :],
                        S["x_T"][:, cc_ * SEQ + t_ * 128: cc_ * SEQ + (t_ + 1) * 128],
                        S["w_str"][:, cc_ * DM + h_ * 512: cc_ * DM + h_ * 512 + 512],
                        start=(cc_ == 0), stop=(bias_bf is None and cc_ == NCH - 1)),
                        wait=[c["cA"].last(), c["vx"].last(), fh.last()] if cc == 0 else (),
                        inc=[(mm, 1)] if (bias_bf is None and cc == NCH - 1 and last_h) else (),
                        tag=f"{tag}m{t}")
                if bias_bf is not None:
                    self.t(lambda T, h_=half, ps_=ph: T.matmul(
                        ps_[:, :], S["ones_bf"][0:1, :],
                        bias_bf[0:1, h_ * 512:(h_ + 1) * 512], start=False, stop=True),
                        inc=[(mm, 1)] if last_h else (), tag=f"{tag}b{t}")
            if dst_V:
                def ev_v(sc, t_=t, p0_=p0, p1_=p1):
                    dst = S["V"].rearrange("p (t h u) -> p t h u", t=NT, h=NH)[:, t_, :, 0:64]
                    sc.copy(out=dst[:, 0:8, :], in_=p0_.rearrange("p (h u) -> p h u", u=64))
                    return sc.copy(out=dst[:, 8:16, :], in_=p1_.rearrange("p (h u) -> p h u", u=64))
                self.s(ev_v, wait=[mm.last()], inc=[(f0, 1), (f1, 1)], tag=f"{tag}e{t}", multi=True)
                self.s(lambda sc: sc.nop(), inc=[(ev, 1)], tag=f"{tag}en{t}")

    def l0_hp(self, s, hp):
        S, D, c = self.S, self.D, self.c
        mm, ev, vx = c["mm"], c["ev"], c["vx"]
        cB = c["cB"]
        pa, fa = self.bank("A"); pb, fb = self.bank("B")
        for which, wname, store, b8 in ((1, "WkT", S["K_store"], S["bk8"]),
                                        (0, "WqT", S["Q_store"], S["bq8"])):
            for cc in range(NCH):
                i = which * NCH + cc
                wB = c["wB0"] if i % 2 == 0 else c["wB1"]
                sl = S["stage"][:, i % 2, :128]
                self.p.dma("sync", sl, D[wname][cc * 128:(cc + 1) * 128, hp * 128:(hp + 1) * 128],
                           inc=[(wB, 16)], wait=[wB.last(), (cB, max(0, cB.promised - 1))] +
                           ([mm.last()] if cc == 0 else []),
                           tag=f"hp{hp}wl")
                self.v(lambda v, sl_=sl, cc_=cc: v.tensor_copy(
                    out=S["qk_w"][:, cc_ * 128:(cc_ + 1) * 128], in_=sl_),
                    wait=[wB.last()], inc=[(cB, 1)], tag=f"hp{hp}wc")
            w_rdy = cB.last()
            for qg in range(4):
                ph, fh = (pa, fa) if qg % 2 == 0 else (pb, fb)
                for cc in range(NCH):
                    self.t(lambda T, cc_=cc, qg_=qg, ps_=ph: T.matmul(
                        ps_[:, :], S["qk_w"][:, cc_ * 128:(cc_ + 1) * 128],
                        S["x_T"][:, cc_ * SEQ + qg_ * 512: cc_ * SEQ + (qg_ + 1) * 512],
                        start=(cc_ == 0), stop=(cc_ == NCH - 1)),
                        wait=[w_rdy, fh.last()] if cc == 0 else (),
                        inc=[(mm, 1)] if cc == NCH - 1 else (), tag=f"hp{hp}p{which}{qg}")
                def ev_qk(sc, ps_=ph, st_=store, qg_=qg, b8_=b8, hp_=hp):
                    sc.activation(st_[0:64, qg_ * 512:(qg_ + 1) * 512], ps_[0:64, :],
                                  AF.Identity, bias=b8_[0:64, hp_:hp_ + 1], scale=1.0)
                    return sc.activation(st_[0:64, SEQ + qg_ * 512: SEQ + (qg_ + 1) * 512],
                                         ps_[64:128, :], AF.Identity,
                                         bias=b8_[64:128, hp_:hp_ + 1], scale=1.0)
                self.s(ev_qk, wait=[mm.last()], inc=[(ev, 1), (fh, 1)], tag=f"hp{hp}e{which}{qg}", multi=True)
            self.rope(store)
        self.l0_m_and_attn(s, hp)

    def rope(self, store):
        S, c = self.S, self.c
        vx, ev = c["vx"], c["ev"]
        SW = S["SWQ"]
        for sd in range(2):
            st = store[0:64, sd * SEQ:(sd + 1) * SEQ]
            self.v(lambda v, st_=st: v.tensor_copy(out=SW[0:32, :], in_=st_[32:64, :]),
                   wait=[ev.last(), vx.last()], tag="sw1")
            self.v(lambda v, st_=st: v.tensor_copy(out=SW[32:64, :], in_=st_[0:32, :]), tag="sw2")
            self.vdrain()
            self.v(lambda v, sd_=sd: v.tensor_tensor(SW, SW, S["T2"][:, sd_ * SEQ:(sd_ + 1) * SEQ],
                                                     ALU.mult), tag="swT2")
            self.v(lambda v, st_=st, sd_=sd: v.tensor_tensor(
                st_, st_, S["T1"][:, sd_ * SEQ:(sd_ + 1) * SEQ], ALU.mult), tag="qT1")
            self.vdrain()
            self.v(lambda v, st_=st: v.tensor_tensor(st_, st_, SW, ALU.add), tag="ropea")
            self.vdrain(inc=[(vx, 1)])

    def l0_m_and_attn(self, s, hp):
        S, c = self.S, self.c
        mm, ev, vx = c["mm"], c["ev"], c["vx"]
        pg, fg = self.PS["psU"], self.c["fU"]
        SW = S["SWQ"]
        self.v(lambda v: v.memset(S["kmax"], 0.0), wait=[vx.last()], tag="km0")
        self.vdrain(inc=[(vx, 1)])
        for side in range(2):
            self.v(lambda v, sd=side: v.tensor_tensor(
                SW, S["K_store"][0:64, sd * SEQ:(sd + 1) * SEQ],
                S["K_store"][0:64, sd * SEQ:(sd + 1) * SEQ], ALU.mult),
                wait=[vx.last()], tag="KK")
            self.vdrain(inc=[(vx, 1)])
            for seg in range(4):
                self.t(lambda T, sg=seg: T.matmul(
                    pg[0:1, :], S["ones_col"][0:64, :],
                    SW[:, sg * 512:(sg + 1) * 512], start=True, stop=True),
                    wait=[vx.last(), fg.last()], inc=[(mm, 1)], tag="kn2")
                self.v(lambda v: v.reduce_max(S["inv_t"][0:1, 0:1], pg[0:1, :], AX.X),
                       wait=[mm.last()], tag="knm")
                self.vdrain()
                self.v(lambda v, sd=side: v.tensor_max(
                    S["kmax"][0:1, sd:sd + 1], S["kmax"][0:1, sd:sd + 1],
                    S["inv_t"][0:1, 0:1]), inc=[(vx, 1), (fg, 1)], tag="knc")
                self.vdrain()
        self.s(lambda sc: sc.activation(S["kmax"], S["kmax"], AF.Sqrt, bias=0.0, scale=1.0),
               wait=[vx.last()], inc=[(ev, 1)], tag="kms")
        for side in range(2):
            self.v(lambda v, sd=side: v.tensor_tensor(
                SW, S["Q_store"][0:64, sd * SEQ:(sd + 1) * SEQ],
                S["Q_store"][0:64, sd * SEQ:(sd + 1) * SEQ], ALU.mult),
                wait=[ev.last(), vx.last()], tag="QQ")
            self.vdrain(inc=[(vx, 1)])
            for seg in range(4):
                self.t(lambda T, sg=seg: T.matmul(
                    pg[0:1, :], S["ones_col"][0:64, :],
                    SW[:, sg * 512:(sg + 1) * 512], start=True, stop=True),
                    wait=[vx.last(), fg.last()], inc=[(mm, 1)], tag="qn2")
                self.s(lambda sc: sc.activation(S["inv_t"], pg[0:1, :], AF.Sqrt,
                                                bias=0.0, scale=1.0),
                       wait=[mm.last()], inc=[(ev, 1)], tag="qns")
                self.v(lambda v, sd=side, sg=seg: v.tensor_scalar(
                    S["Q_store"][64:65, sd * SEQ + sg * 512: sd * SEQ + (sg + 1) * 512],
                    S["inv_t"], S["kmax"][0:1, sd:sd + 1], -1.0, ALU.mult, ALU.mult),
                    wait=[ev.last()], inc=[(vx, 1), (fg, 1)], tag="qrow")
                self.vdrain()
        prev = None
        ix = 0
        for side in range(2):
            h = 2 * hp + side
            for qg in range(4):
                cur = self.attn_core(h, qg, side, causal=False, tag=f"h{h}g{qg}", pv_ix=ix)
                if prev is not None:
                    self.attn_tail(prev)
                prev = cur
                ix += 1
        self.attn_tail(prev)

    def attn_core(self, h, qg, side, causal, tag, pv_ix):
        S, c = self.S, self.c
        mm, ev, vx = c["mm"], c["ev"], c["vx"]
        rot = [self.bank("A"), self.bank("B"), self.bank("E"), self.bank("F")]
        pc, fc = self.bank("C") if pv_ix % 2 == 0 else self.bank("D")
        kt_hi = NT if not causal else min(NT, (qg + 1) * 4)
        ch_ = h // 2
        ev_cnt = {}
        vx_cnt = {}
        for kt0 in range(0, kt_hi, 8):
            kts = range(kt0, min(kt0 + 8, kt_hi))
            for kt in kts:
                ph, fh = rot[kt % 4]
                qlo = 0 if not causal else max(0, kt * 128 - qg * 512)
                if not causal:
                    lhsT = S["K_store"][0:65, side * SEQ + kt * 128: side * SEQ + (kt + 1) * 128]
                    rhs = S["Q_store"][0:65, side * SEQ + qg * 512: side * SEQ + (qg + 1) * 512]
                else:
                    lhsT = S["k_fm"][side * 64:side * 64 + 64,
                                     ch_ * SEQ + kt * 128: ch_ * SEQ + (kt + 1) * 128]
                    rhs = S["q_fm"][side * 64:side * 64 + 64,
                                    ch_ * SEQ + qg * 512 + qlo: ch_ * SEQ + (qg + 1) * 512]
                self.t(lambda T, l_=lhsT, r_=rhs, ps_=ph, q_=qlo: T.matmul(
                    ps_[:, q_:512], l_, r_, start=True, stop=True),
                    wait=[fh.last()], inc=[(mm, 1)], tag=f"{tag}s{kt}")
                sl = (kt % 8) * 512
                if not causal:
                    self.s(lambda sc, ps_=ph, sl_=sl: sc.activation(
                        S["P_buf"][:, sl_:sl_ + 512], ps_[:, :], AF.Exp,
                        bias=0.0, scale=0.125),
                        wait=[mm.last()], inc=[(ev, 1), (fh, 1)], tag=f"{tag}x{kt}")
                else:
                    self.s(lambda sc, ps_=ph, q_=qlo: sc.activation(
                        S["tanhb"][:, q_:512], ps_[:, q_:512], AF.Tanh,
                        bias=0.0, scale=1.0 / (8.0 * 50.0)), wait=[mm.last()], tag=f"{tag}t{kt}")
                    self.sdrain()
                    self.s(lambda sc, sl_=sl, q_=qlo: sc.activation(
                        S["P_buf"][:, sl_ + q_:sl_ + 512], S["tanhb"][:, q_:512],
                        AF.Exp, bias=0.0, scale=50.0),
                        inc=[(ev, 1), (fh, 1)], tag=f"{tag}x{kt}")
                ev_cnt[kt] = ev.promised
                if causal:
                    if qlo > 0:
                        self.v(lambda v, sl_=sl, q_=qlo: v.memset(
                            S["P_buf"][:, sl_:sl_ + q_], 0.0),
                            wait=[(ev, ev_cnt[kt])], tag=f"{tag}z{kt}")
                        self.vdrain(inc=[(vx, 1)])
                    if qg * 512 <= kt * 128 < (qg + 1) * 512:
                        off = sl + (kt * 128 - qg * 512)
                        self.v(lambda v, o_=off: v.tensor_tensor(
                            S["P_buf"][:, o_:o_ + 128], S["P_buf"][:, o_:o_ + 128],
                            S["tri_bf"], ALU.mult), wait=[(ev, ev_cnt[kt])], tag=f"{tag}d{kt}")
                        self.vdrain(inc=[(vx, 1)])
                vx_cnt[kt] = vx.promised
            for kt in kts:
                self.t(lambda T, kt_=kt, pc_=pc: T.matmul(
                    pc_[0:65, :],
                    S["V"][:, kt_ * NH * 65 + h * 65: kt_ * NH * 65 + h * 65 + 65],
                    S["P_buf"][:, (kt_ % 8) * 512:(kt_ % 8) * 512 + 512],
                    start=(kt_ == 0), stop=(kt_ == kt_hi - 1)),
                    wait=[(ev, ev_cnt[kt]), (vx, vx_cnt[kt])] + ([fc.last()] if kt == 0 else []),
                    inc=[(mm, 1)] if kt == kt_hi - 1 else (), tag=f"{tag}pv{kt}")
        return (h, qg, side, pc, fc, tag, mm.promised)

    def attn_tail(self, ctx):
        S, c = self.S, self.c
        mm, ev, vx = c["mm"], c["ev"], c["vx"]
        h, qg, side, pc, fc, tag, mm_pv = ctx
        pu, fu = self.bank("U")
        ch_ = h // 2
        self.v(lambda v: v.reciprocal(S["inv_t"], pc[64:65, :]),
               wait=[(mm, mm_pv)], tag=f"{tag}rc")
        self.vdrain(inc=[(vx, 1)])
        vx_rc = vx.promised
        self.t(lambda T: T.matmul(pu[0:64, :], S["ones1f"], S["inv_t"],
                                  start=True, stop=True),
               wait=[(vx, vx_rc), fu.last()], inc=[(mm, 1)], tag=f"{tag}r1")
        mm_r1 = mm.promised
        self.s(lambda sc: sc.copy(out=S["att_t"], in_=pc[0:64, :]),
               wait=[(mm, mm_pv), (vx, vx_rc)], inc=[(ev, 1), (fc, 1)], tag=f"{tag}ae")
        self.v(lambda v: v.tensor_tensor(
            S["attn_fm"][side * 64:side * 64 + 64,
                         ch_ * SEQ + qg * 512:(ch_) * SEQ + (qg + 1) * 512],
            S["att_t"], pu[0:64, :], ALU.mult),
            wait=[ev.last(), (mm, mm_r1)], inc=[(vx, 1), (fu, 1)], tag=f"{tag}nm")
        self.vdrain()

    def wo_ln(self, res_dram, wT, bias_dram, scale, tag, spill_dram=None):
        S, D, c = self.S, self.D, self.c
        mm, ev, vx = c["mm"], c["ev"], c["vx"]
        if bias_dram is not None:
            self.p.dma("sync", S["stage"][0:1, 0, :], bias_dram[:, :],
                       wait=[c["cA"].last(), c["vx"].last()], inc=[(c["ld"], 16)], tag=f"{tag}b")
            self.v(lambda v: v.tensor_copy(out=S["row_bf"], in_=S["stage"][0:1, 0, :]),
                   wait=[c["ld"].last()], tag=f"{tag}bc")
            self.vdrain(inc=[(vx, 1)])
        self.stream_w(S["w_str"], [wT[i * 128:(i + 1) * 128, :] for i in range(NCH)],
                      extra_wait=[vx.last()], tag=f"{tag}w")
        pa, fa = self.bank("A"); pb, fb = self.bank("B")
        pc, fc = self.bank("C"); pd, fd = self.bank("D")
        for t in range(NT):
            p0, f0, p1, f1 = (pa, fa, pb, fb) if t % 2 == 0 else (pc, fc, pd, fd)
            res_ap, res_w = self.hbm_tile(res_dram, t, t % 2)
            for half, (ph, fh) in enumerate(((p0, f0), (p1, f1))):
                last_h = (half == 1)
                for cc in range(NCH):
                    self.t(lambda T, t_=t, cc_=cc, h_=half, ps_=ph: T.matmul(
                        ps_[:, :],
                        S["attn_fm"][:, cc_ * SEQ + t_ * 128: cc_ * SEQ + (t_ + 1) * 128],
                        S["w_str"][:, cc_ * DM + h_ * 512: cc_ * DM + h_ * 512 + 512],
                        start=(cc_ == 0), stop=(bias_dram is None and cc_ == NCH - 1)),
                        wait=[c["cA"].last(), c["vx"].last(), fh.last()] if cc == 0 else (),
                        inc=[(mm, 1)] if (bias_dram is None and cc == NCH - 1 and last_h) else (),
                        tag=f"{tag}m{t}")
                if bias_dram is not None:
                    self.t(lambda T, h_=half, ps_=ph: T.matmul(
                        ps_[:, :], S["ones_bf"][0:1, :],
                        S["row_bf"][0:1, h_ * 512:(h_ + 1) * 512], start=False, stop=True),
                        inc=[(mm, 1)] if last_h else (), tag=f"{tag}bb{t}")
            rt = S["r"][:, t * DM:(t + 1) * DM]
            for half, (ph, fh) in enumerate(((p0, f0), (p1, f1))):
                self.v(lambda v, ps_=ph, h_=half, rt_=rt, ra=res_ap: v.scalar_tensor_tensor(
                    rt_[:, h_ * 512:(h_ + 1) * 512], ps_[:, :], scale,
                    ra[:, h_ * 512:(h_ + 1) * 512], ALU.mult, ALU.add),
                    wait=[mm.last()] + res_w if half == 0 else (),
                    inc=[(fh, 1)], tag=f"{tag}ev{t}")
            self.vdrain(inc=[(vx, 1)])
            self.ln_tile(rt, wait=[vx.last()])
            if spill_dram is not None:
                self.p.dma("sync", spill_dram[t * 128:(t + 1) * 128, :], rt,
                           wait=[c["ev"].last()], inc=[(c["st"], 16)], tag=f"{tag}sp{t}")

    def build_ffn_cache(self):
        S, D, c = self.S, self.D, self.c
        gx = c["gx"]
        jobs = []
        for hc in range(NHC):
            jobs.append((D["wgT"][:, hc * 128:(hc + 1) * 128].rearrange(
                "(cc p) j -> p cc j", p=128), D["wg_c"][hc], True))
            jobs.append((D["wuT"][:, hc * 128:(hc + 1) * 128].rearrange(
                "(cc p) j -> p cc j", p=128), D["wu_c"][hc], True))
            jobs.append((D["wdT"][hc * 128:(hc + 1) * 128, :], D["wd_c"][hc], False))
        cast2 = [S["cast_t"], S["cast_t2"]]
        for i, (src_ap, dst_ap, slow) in enumerate(jobs):
            w = c["wB0"] if i % 2 == 0 else c["wB1"]
            gdp = c["gd0"] if i % 2 == 0 else c["gd1"]
            sl = S["stage_i"][:, i % 2]
            if len(src_ap.shape) == 3:
                sl = sl.rearrange("p (cc j) -> p cc j", j=128)
            self.p.dma("sync", sl, src_ap, inc=[(w, 16)],
                       wait=[w.last(), (gx, max(0, gx.promised - 1))],
                       allow_slow_non_contiguous=slow, tag=f"fcl{i}")
            self.g(lambda g_, sl_=sl, i_=i: g_.tensor_copy(out=cast2[i_ % 2], in_=sl_),
                   wait=[w.last(), gdp.last()],
                   inc=[(gx, 1)], tag=f"fcc{i}")
            self.p.dma("gpsimd", dst_ap, cast2[i % 2],
                       wait=[gx.last()], inc=[(gdp, 16)], tag=f"fcs{i}")
        self.g(lambda g_, v=self.c["gd0"].promised: g_.wait_ge(self.c["gd0"].sem, v), tag="fcw0")
        self.g(lambda g_, v=self.c["gd1"].promised: g_.wait_ge(self.c["gd1"].sem, v),
               inc=[(gx, 1)], tag="fcw1")
        self.barrier()

    def build_moe_cache(self):
        S, D, c = self.S, self.D, self.c
        gx = c["gx"]
        jobs = []
        for e in range(NE):
            for cc in range(NCH):
                for sub in range(4):
                    jobs.append((D["w1"][e][cc * 128:(cc + 1) * 128, sub * 1024:(sub + 1) * 1024],
                                 D["w1_c"][e, cc][:, sub * 1024:(sub + 1) * 1024]))
            for hc in range(EH // 128):
                jobs.append((D["w2"][e][hc * 128:(hc + 1) * 128, :], D["w2_c"][e, hc]))
        cast2 = [S["cast_t"], S["cast_t2"]]
        cast_engs = ["gpsimd", "vector", "scalar"]
        cast_ch = [c["gx"], c["vx"], c["ev"]]
        base_cnt = [ch.promised for ch in cast_ch]
        for i, (src_ap, dst_ap) in enumerate(jobs):
            w = c["wB0"] if i % 2 == 0 else c["wB1"]
            gdp = c["gd0"] if i % 2 == 0 else c["gd1"]
            sl = S["stage"][:, i % 2]
            ce = i % 3
            self.p.dma("sync", sl, src_ap, inc=[(w, 16)],
                       wait=[w.last(), cast_ch[(i + 1) % 3].last()], tag=f"mc_l{i}")
            if cast_engs[ce] == "scalar":
                fn = (lambda en, sl_=sl, i_=i: en.copy(out=cast2[i_ % 2], in_=sl_))
            else:
                fn = (lambda en, sl_=sl, i_=i: en.tensor_copy(out=cast2[i_ % 2], in_=sl_))
            self.p.emit(cast_engs[ce], fn,
                wait=[w.last(), gdp.last()], inc=[(cast_ch[ce], 1)], tag=f"mc_c{i}")
            self.p.dma("gpsimd", dst_ap, cast2[i % 2],
                       wait=[cast_ch[ce].last()], inc=[(gdp, 16)], tag=f"mc_s{i}")
        self.g(lambda g_, v=self.c["gd0"].promised: g_.wait_ge(self.c["gd0"].sem, v),
               tag="mcw0")
        self.g(lambda g_, v=self.c["gd1"].promised: g_.wait_ge(self.c["gd1"].sem, v),
               inc=[(gx, 1)], tag="mcw1")
        self.barrier()

    def l0_ffn(self, s):
        S, D, c = self.S, self.D, self.c
        mm, ev, vx = c["mm"], c["ev"], c["vx"]
        cB = c["cB"]

        def src_tile(t):
            return S["r"][:, t * DM:(t + 1) * DM], []
        self.build_xT(src_tile, tag=f"s{s}x1")
        self.barrier()

        pa, fa = self.bank("A"); pb, fb = self.bank("B")
        pc, fc = self.bank("C"); pd, fd = self.bank("D")
        for tg in range(4):
            gu_mm = []
            for hc in range(NHC):
                par = hc % 2
                wB = c["wB0"] if par == 0 else c["wB1"]
                for name, buf in (("wg_c", "wg_hc"), ("wu_c", "wu_hc")):
                    self.p.dma("sync", S[buf][:, par], D[name][hc], inc=[(wB, 16)],
                               wait=[wB.last()] + ([(mm, gu_mm[hc - 2])] if hc >= 2 else []),
                               tag=f"ff{name}{hc}")
                w_rdy = wB.last()
                ph, fh = (pa, fa) if par == 0 else (pb, fb)
                pu, fu = (pc, fc) if par == 0 else (pd, fd)
                for cc in range(NCH):
                    self.t(lambda T, cc_=cc, ps_=ph, par_=par, tg_=tg: T.matmul(
                        ps_[:, :], S["wg_hc"][:, par_, cc_ * 128:(cc_ + 1) * 128],
                        S["x_T"][:, cc_ * SEQ + tg_ * 512: cc_ * SEQ + (tg_ + 1) * 512],
                        start=(cc_ == 0), stop=(cc_ == NCH - 1)),
                        wait=[w_rdy, fh.last()] if cc == 0 else (), tag=f"ffg{hc}")
                for cc in range(NCH):
                    self.t(lambda T, cc_=cc, ps_=pu, par_=par, tg_=tg: T.matmul(
                        ps_[:, :], S["wu_hc"][:, par_, cc_ * 128:(cc_ + 1) * 128],
                        S["x_T"][:, cc_ * SEQ + tg_ * 512: cc_ * SEQ + (tg_ + 1) * 512],
                        start=(cc_ == 0), stop=(cc_ == NCH - 1)),
                        wait=[fu.last()] if cc == 0 else (),
                        inc=[(mm, 1)] if cc == NCH - 1 else (), tag=f"ffu{hc}")
                gu_mm.append(mm.promised)
                self.s(lambda sc, ps_=ph: sc.activation(S["gate_t"], ps_[:, :], AF.Silu,
                                                        bias=0.0, scale=QSC),
                       wait=[mm.last(), vx.last()], inc=[(fh, 1)], tag=f"ffsl{hc}")
                self.s(lambda sc, ps_=pu: sc.activation(S["up_t"], ps_[:, :], AF.Copy,
                                                        bias=0.0, scale=QSC),
                       inc=[(ev, 1), (fu, 1)], tag=f"ffup{hc}")
                self.v(lambda v, hc_=hc: v.tensor_tensor(
                    S["h_grp"][:, hc_ * 512:(hc_ + 1) * 512], S["gate_t"], S["up_t"],
                    ALU.mult), wait=[ev.last()], inc=[(vx, 1)], tag=f"ffh{hc}")
            self.vdrain(inc=[(vx, 1)])
            banks = [self.bank("A"), self.bank("B"), self.bank("C"), self.bank("D")]
            wd_mm = []
            for half in range(2):
                for hc in range(NHC):
                    par = hc % 2
                    wB = c["wB0"] if par == 0 else c["wB1"]
                    k = half * NHC + hc
                    self.p.dma("sync", S["wd_hc"][:, par],
                               D["wd_c"][hc][:, half * 512:(half + 1) * 512],
                               inc=[(wB, 16)],
                               wait=[wB.last()] + ([(mm, wd_mm[k - 2])] if k >= 2 else []),
                               tag=f"ffwd{hc}")
                    w_rdy = wB.last()
                    for tt in range(4):
                        ph, fh = banks[tt]
                        self.t(lambda T, hc_=hc, tt_=tt, ps_=ph, par_=par: T.matmul(
                            ps_[:, :], S["h_grp"][:, hc_ * 512 + tt_ * 128: hc_ * 512 + (tt_ + 1) * 128],
                            S["wd_hc"][:, par_, :],
                            start=(hc_ == 0), stop=(hc_ == NHC - 1)),
                            wait=[w_rdy, vx.last()] + ([fh.last()] if hc == 0 else []),
                            inc=[(mm, 1)] if tt == 3 else (), tag=f"ffd{half}{hc}{tt}")
                    wd_mm.append(mm.promised)
                for tt in range(4):
                    t_ = tg * 4 + tt
                    ph, fh = banks[tt]
                    rt = S["r"][:, t_ * DM:(t_ + 1) * DM]
                    self.v(lambda v, ps_=ph, rt_=rt, h_=half: v.scalar_tensor_tensor(
                        rt_[:, h_ * 512:(h_ + 1) * 512], ps_[:, :], QSC,
                        rt_[:, h_ * 512:(h_ + 1) * 512], ALU.mult, ALU.add),
                        wait=[mm.last()], inc=[(fh, 1)], tag=f"ffde{half}{tt}")
                self.vdrain(inc=[(vx, 1)])
            for tt in range(4):
                t_ = tg * 4 + tt
                rt = S["r"][:, t_ * DM:(t_ + 1) * DM]
                tok = self.ln_tile(rt, wait=[vx.last()])
                self.p.dma("sync", D["r_spill"][t_ * 128:(t_ + 1) * 128, :], rt,
                           wait=[tok], inc=[(c["st"], 16)], tag=f"ffsp{t_}")
        self.barrier()

    def l1_attn(self, s):
        S, D, c = self.S, self.D, self.c
        mm, ev, vx = c["mm"], c["ev"], c["vx"]

        def src_tile(t):
            return S["r"][:, t * DM:(t + 1) * DM], []
        self.build_xT(src_tile, tag=f"s{s}x2")
        self.barrier()
        self.p.emit("sync", lambda e, v=c["st"].promised: e.wait_ge(c["st"].sem, v), tag="spw")
        self.barrier()

        self.stream_w(S["w_str"], [D["W2vT"][i * 128:(i + 1) * 128, :] for i in range(NCH)],
                      tag=f"s{s}w2v")
        self.v(lambda v: v.memset(
            S["V"].rearrange("p (t h u) -> p t h u", t=NT, h=NH)[:, :, :, 64:65], 1.0),
            tag="vone1")
        self.vdrain(inc=[(vx, 1)])
        self.proj_tokmajor(dst_V=True, wq=None, bias_bf=None, tag=f"s{s}v2")
        self.barrier()
        for wname, dst in (("W2qT", "q_fm"), ("W2kT", "k_fm")):
            self.stream_w(S["w_str"], [D[wname][i * 128:(i + 1) * 128, :] for i in range(NCH)],
                          tag=f"s{s}{wname}")
            self.qk_rms_proj(dst, tag=f"s{s}{wname}p")
            self.barrier()
        prev = None
        ix = 0
        for h in range(NH):
            for qg in range(4):
                cur = self.attn_core(h, qg, h % 2, causal=True, tag=f"L1h{h}g{qg}", pv_ix=ix)
                if prev is not None:
                    self.attn_tail(prev)
                prev = cur
                ix += 1
        self.attn_tail(prev)
        self.barrier()
        self.wo_ln(res_dram=D["r_spill"], wT=D["W2oT"], bias_dram=None,
                   scale=1.0, tag=f"s{s}wo1")
        self.barrier()

    def qk_rms_proj(self, dst_name, tag):
        S, c = self.S, self.c
        mm, ev, vx = c["mm"], c["ev"], c["vx"]
        pa, fa = self.bank("A"); pb, fb = self.bank("B")
        psT, fT = self.bank("T")
        dst = S[dst_name]
        for t in range(NT):
            for half, (ph, fh) in enumerate(((pa, fa), (pb, fb))):
                for cc in range(NCH):
                    self.t(lambda T, t_=t, cc_=cc, h_=half, ps_=ph: T.matmul(
                        ps_[:, :],
                        S["x_T"][:, cc_ * SEQ + t_ * 128: cc_ * SEQ + (t_ + 1) * 128],
                        S["w_str"][:, cc_ * DM + h_ * 512: cc_ * DM + h_ * 512 + 512],
                        start=(cc_ == 0), stop=(cc_ == NCH - 1)),
                        wait=[c["cA"].last(), vx.last(), fh.last()] if cc == 0 else (),
                        inc=[(mm, 1)] if (cc == NCH - 1 and half == 1) else (),
                        tag=f"{tag}m{t}")
            self.s(lambda sc: sc.copy(out=S["cast_t"][:, 0:512], in_=pa[:, :]),
                   wait=[mm.last()], inc=[(fa, 1)], tag=f"{tag}e0{t}")
            self.s(lambda sc: sc.copy(out=S["cast_t"][:, 512:1024], in_=pb[:, :]),
                   inc=[(ev, 1), (fb, 1)], tag=f"{tag}e1{t}")
            sq = S["stage"][:, 0]
            self.v(lambda v: v.tensor_tensor(sq, S["cast_t"], S["cast_t"], ALU.mult),
                   wait=[ev.last()], tag=f"{tag}sq{t}")
            self.vdrain()
            self.v(lambda v: v.tensor_reduce(
                S["rms16"], sq.rearrange("p (h u) -> p h u", u=64), AX.X, ALU.add),
                tag=f"{tag}sr{t}")
            self.vdrain(inc=[(vx, 1)])
            self.s(lambda sc: sc.activation(S["rms16b"], S["rms16"], AF.Sqrt,
                                            bias=S["eps_rms"], scale=1.0 / 64.0),
                   wait=[vx.last()], inc=[(ev, 1)], tag=f"{tag}sd{t}")
            self.v(lambda v: v.reciprocal(S["rms16"], S["rms16b"]),
                   wait=[ev.last()], tag=f"{tag}rc{t}")
            self.vdrain()
            self.v(lambda v: v.tensor_tensor(
                S["cast_t"].rearrange("p (h u) -> p h u", u=64),
                S["cast_t"].rearrange("p (h u) -> p h u", u=64),
                S["rms16"].rearrange("p (h o) -> p h o", o=1).broadcast_to([128, 16, 64]),
                ALU.mult), tag=f"{tag}ap{t}")
            self.vdrain(inc=[(vx, 1)])
            for cc in range(NCH):
                self.t(lambda T, cc_=cc: T.transpose(
                    psT[:, cc_ * 128:(cc_ + 1) * 128],
                    S["cast_t"][:, cc_ * 128:(cc_ + 1) * 128], S["ident_bf"]),
                    wait=[vx.last(), fT.last()] if cc == 0 else (),
                    inc=[(mm, 1)] if cc == NCH - 1 else (), tag=f"{tag}tr{t}")
            self.s(lambda sc, t_=t: sc.copy(
                out=dst.rearrange("p (c q) -> p c q", c=NCH)[:, :, t_ * 128:(t_ + 1) * 128],
                in_=psT.rearrange("p (c q) -> p c q", c=NCH)),
                wait=[mm.last()], inc=[(ev, 1), (fT, 1)], tag=f"{tag}te{t}")

    def routing(self, s):
        S, D, c = self.S, self.D, self.c
        mm, ev, vx = c["mm"], c["ev"], c["vx"]
        for t in range(NT):
            self.v(lambda v, t_=t: v.tensor_copy(
                out=S["cast_t"], in_=S["r"][:, t_ * DM:(t_ + 1) * DM]),
                wait=[(c["st"], max(0, c["st"].promised))], inc=[(vx, 1)], tag=f"xm{t}")
            self.p.dma("sync", D["xmoe"][s][t * 128:(t + 1) * 128, :], S["cast_t"],
                       wait=[vx.last()], inc=[(c["st"], 16)], tag=f"xms{t}")
        self.p.emit("sync", lambda e, v=c["st"].promised: e.wait_ge(c["st"].sem, v),
                    tag="xmw")
        self.barrier()

        def src_tile(t):
            return S["r"][:, t * DM:(t + 1) * DM], []
        self.build_xT(src_tile, tag=f"s{s}x3")
        self.barrier()

        self.v(lambda v: v.memset(S["bigc"], OOB_BIG), wait=[c["ev"].last()], tag="bigc")
        self.vdrain(inc=[(c["vx"], 1)])
        self.p.dma("sync", S["iota512"], D["iota512"][:, :], inc=[(c["ld"], 16)], tag="iota")
        self.p.emit("sync", lambda e, v=c["ld"].promised: e.wait_ge(c["ld"].sem, v), tag="iotaw")
        self.barrier()
        src = D["gateW"].rearrange("(cc p) e -> p cc e", p=128)
        self.p.dma("sync", S["stage"][:, 0, 0:NCH * 8].rearrange("p (cc e) -> p cc e", e=NE),
                   src, inc=[(c["ld"], 16)],
                   allow_slow_non_contiguous=True, tag="gw")
        self.v(lambda v: v.tensor_copy(out=S["gw_bf"], in_=S["stage"][:, 0, 0:NCH * 8]),
               wait=[c["ld"].last()], inc=[(vx, 1)], tag="gwc")
        pe, fe = self.bank("E")
        for qg in range(4):
            for cc in range(NCH):
                self.t(lambda T, cc_=cc, qg_=qg: T.matmul(
                    pe[0:8, :], S["gw_bf"][:, cc_ * 8:(cc_ + 1) * 8],
                    S["x_T"][:, cc_ * SEQ + qg_ * 512: cc_ * SEQ + (qg_ + 1) * 512],
                    start=(cc_ == 0), stop=(cc_ == NCH - 1)),
                    wait=[vx.last(), fe.last()] if cc == 0 else (),
                    inc=[(mm, 1)] if cc == NCH - 1 else (), tag=f"gs{qg}")
            self.s(lambda sc, qg_=qg: sc.copy(
                out=S["probs_et"][:, qg_ * 512:(qg_ + 1) * 512], in_=pe[0:8, :]),
                wait=[mm.last()], inc=[(ev, 1), (fe, 1)], tag=f"gse{qg}")
        pf, ff = self.bank("F")
        for t in range(NT):
            self.t(lambda T, t_=t: T.transpose(
                pf[:, 0:8], S["probs_et"][:, t_ * 128:(t_ + 1) * 128],
                S["ident_f32"][0:8, 0:8]),
                wait=[ev.last(), ff.last()], inc=[(mm, 1)], tag=f"smT{t}")
            self.v(lambda v: v.reduce_max(S["rms16"][:, 0:1], pf[:, 0:8], AX.X, negate=True),
                   wait=[mm.last()], tag=f"smm{t}")
            self.vdrain(inc=[(vx, 1)])
            self.s(lambda sc: sc.activation(S["prb8"], pf[:, 0:8], AF.Exp,
                                            bias=S["rms16"][:, 0:1], scale=1.0,
                                            accum_out=S["rms16"][:, 1:2]),
                   wait=[vx.last()], inc=[(ev, 1), (ff, 1)], tag=f"smx{t}")
            self.v(lambda v: v.reciprocal(S["rms16"][:, 2:3], S["rms16"][:, 1:2]),
                   wait=[ev.last()], tag=f"smr{t}")
            self.vdrain()
            self.v(lambda v: v.tensor_scalar(S["prb8"], S["prb8"], S["rms16"][:, 2:3],
                                             None, ALU.mult), tag=f"smn{t}")
            self.vdrain()
            self.v(lambda v, t_=t: v.tensor_copy(
                out=S["probsT"][:, s, t_ * 10 + 2: t_ * 10 + 10], in_=S["prb8"]), tag=f"pT{t}")
            self.v(lambda v, t_=t: v.tensor_copy(
                out=S["probsT"][:, s, t_ * 10: t_ * 10 + 1], in_=S["pcol"]), tag=f"pTc{t}")
            self.v(lambda v, t_=t: v.memset(
                S["probsT"][:, s, t_ * 10 + 1: t_ * 10 + 2], float(t_)), tag=f"pTi{t}")
            self.vdrain(inc=[(vx, 1)])
            self.t(lambda T, t_=t: T.transpose(
                pf[0:8, 128:256], S["prb8"], S["ident_f32"]),
                wait=[vx.last(), ff.last()], inc=[(mm, 1)], tag=f"pb{t}")
            self.s(lambda sc, t_=t: sc.copy(
                out=S["probs_et"][:, t_ * 128:(t_ + 1) * 128], in_=pf[0:8, 128:256]),
                wait=[mm.last()], inc=[(ev, 1), (ff, 1)], tag=f"pbe{t}")
        self.barrier()
        sr = S["srch"]
        self.v(lambda v: v.memset(sr[:, 0:1], 0.0), tag="bl0")
        self.v(lambda v: v.memset(sr[:, 1:2], 1.0), tag="bh0")
        self.vdrain()
        for it in range(N_BISECT):
            self.v(lambda v: v.tensor_add(sr[:, 2:3], sr[:, 0:1], sr[:, 1:2]), tag=f"bm{it}")
            self.vdrain()
            self.v(lambda v: v.tensor_scalar(sr[:, 2:3], sr[:, 2:3], 0.5, None, ALU.mult),
                   tag=f"bm2{it}")
            self.vdrain()
            self.v(lambda v: v.tensor_scalar(S["mask_et"], S["probs_et"], sr[:, 2:3],
                                             None, ALU.is_gt, op1=ALU.add,
                                             accum_out=sr[:, 3:4]), tag=f"bc{it}")
            self.vdrain()
            self.v(lambda v: v.tensor_scalar(S["srch_i"][:, 0:1], sr[:, 3:4], 512.0,
                                             None, ALU.is_ge), tag=f"bg{it}")
            self.v(lambda v: v.tensor_scalar(S["srch_i"][:, 1:2], sr[:, 3:4], 512.0,
                                             None, ALU.is_lt), tag=f"bg2{it}")
            self.vdrain()
            self.v(lambda v: v.select(sr[:, 0:1], S["srch_i"][:, 0:1], sr[:, 2:3],
                                      sr[:, 0:1], add_drain=True), tag=f"bsl{it}")
            self.v(lambda v: v.select(sr[:, 1:2], S["srch_i"][:, 1:2], sr[:, 2:3],
                                      sr[:, 1:2], add_drain=True), tag=f"bsh{it}")
            self.vdrain()
        self.v(lambda v: v.tensor_scalar(S["mask_et"], S["probs_et"], sr[:, 0:1],
                                         None, ALU.is_gt), tag="bmf")
        self.vdrain(inc=[(vx, 1)])
        psT, fT = self.bank("T")
        for t in range(NT):
            self.t(lambda T, t_=t: T.transpose(
                psT[:, t_ * 8:(t_ + 1) * 8], S["mask_et"][:, t_ * 128:(t_ + 1) * 128],
                S["ident_bf"][0:8, 0:8]),
                wait=[vx.last(), fT.last()] if t == 0 else (),
                inc=[(mm, 1)] if t == NT - 1 else (), tag=f"mT{t}")
        self.s(lambda sc: sc.copy(out=S["maskT_sb"], in_=psT[:, 0:128]),
               wait=[mm.last()], inc=[(ev, 1), (fT, 1)], tag="mTe")
        pu, fu = self.bank("U")
        for j in range(NT):
            for i in range(j + 1):
                lhs = S["tri_bf"] if i == j else S["ones_bf"]
                self.t(lambda T, i_=i, j_=j, l_=lhs: T.matmul(
                    pu[:, 0:8], l_, S["maskT_sb"][:, i_ * 8:(i_ + 1) * 8],
                    start=(i_ == 0), stop=(i_ == j_)),
                    wait=[ev.last(), fu.last()] if i == 0 else (),
                    inc=[(mm, 1)] if i == j else (), tag=f"cs{j}")
            self.v(lambda v: v.tensor_scalar(S["idx_t"], pu[:, 0:8], 512.0, None, ALU.is_le),
                   wait=[mm.last()], tag=f"ix1{j}")
            self.vdrain()
            self.v(lambda v, j_=j: v.tensor_tensor(
                S["idx_m"], S["idx_t"], S["maskT_sb"][:, j_ * 8:(j_ + 1) * 8], ALU.mult),
                tag=f"ix2{j}")
            self.v(lambda v: v.tensor_scalar(S["idx_t2"], pu[:, 0:8], 1.0, None, ALU.subtract),
                   inc=[(fu, 1)], tag=f"ix3{j}")
            self.vdrain()
            self.v(lambda v, j_=j: v.select(
                S["scat_all"][:, j_ * 8:(j_ + 1) * 8], S["idx_m"], S["idx_t2"], S["bigc"],
                add_drain=True), tag=f"ix4{j}")
            self.vdrain(inc=[(vx, 1)])
        for e in range(NE):
            for j in range(NT):
                Sb = S["cast_t"][:, (j % 2) * 512:(j % 2) * 512 + 512]
                self.v(lambda v, Sb_=Sb, j_=j, e_=e: v.tensor_scalar(
                    Sb_, S["iota512"], S["scat_all"][:, j_ * 8 + e_: j_ * 8 + e_ + 1],
                    None, ALU.is_equal),
                    wait=[vx.last()] if j == 0 else [mm.last()], tag=f"oh{e}{j}")
                self.vdrain(inc=[(vx, 1)])
                self.t(lambda T, Sb_=Sb, j_=j, e_=e, s_=s: T.matmul(
                    pu[0:10, :], S["probsT"][:, s_, j_ * 10:(j_ + 1) * 10],
                    Sb_, start=(j_ == 0), stop=(j_ == NT - 1)),
                    wait=[vx.last()] + ([fu.last()] if j == 0 else []),
                    inc=[(mm, 1)], tag=f"ohm{e}{j}")
            self.s(lambda sc: sc.copy(out=S["sc_rows"], in_=pu[0:10, :]),
                   wait=[mm.last(), (c["st"], c["st"].promised)],
                   inc=[(ev, 1), (fu, 1)], tag=f"ohe{e}")
            self.p.dma("sync", D["rows"][s, e, 0], S["sc_rows"][0:1, :],
                       wait=[ev.last()], inc=[(c["st"], 16)], tag=f"rw0{e}")
            self.p.dma("sync", D["rows"][s, e, 1], S["sc_rows"][1:2, :],
                       inc=[(c["st"], 16)], tag=f"rw1{e}")
            self.p.dma("sync", D["rows"][s, e, 2], S["sc_rows"][2 + e:3 + e, :],
                       inc=[(c["st"], 16)], tag=f"rwp{e}")
        self.barrier()
        self.p.emit("sync", lambda e, v=c["st"].promised: e.wait_ge(c["st"].sem, v), tag="rw")
        self.barrier()

    def moe(self):
        S, D, c = self.S, self.D, self.c
        mm, ev, vx, gx, gd = c["mm"], c["ev"], c["vx"], c["gx"], c["gdma"]
        self.v(lambda v: v.memset(S["stage"][:, 0], 0.0), inc=[(vx, 1)], tag="yz")
        for s in range(2):
            for t in range(NT):
                self.p.dma("sync", D["ymoe"][s][t * 128:(t + 1) * 128, :], S["stage"][:, 0],
                           wait=[vx.last()], inc=[(c["st"], 16)], tag=f"yz{s}{t}")
        banks4 = [self.bank("A"), self.bank("B"), self.bank("C"), self.bank("D")]
        psT, fT = self.bank("T")
        cA = c["cA"]
        self.p.emit("sync", lambda x_, v=c["st"].promised: x_.wait_ge(c["st"].sem, v), tag="yzw")
        self.barrier()
        w1_mm, w2_mm = [], []
        ks = [(e, s) for e in range(NE) for s in range(2)]
        sc_gd = {}

        def prep(kk):
            e, s = ks[kk]
            par = kk % 2
            self.p.dma("sync", S["gidxf"],
                       D["rows"][s, e, 0].rearrange("(j p) -> p j", p=128),
                       inc=[(c["ld"], 16)], wait=[vx.last()],
                       allow_slow_non_contiguous=True, tag=f"gi{e}{s}")
            self.p.dma("sync", S["gidxg"],
                       D["rows"][s, e, 1].rearrange("(j p) -> p j", p=128),
                       inc=[(c["ld"], 16)],
                       allow_slow_non_contiguous=True, tag=f"gj{e}{s}")
            self.p.dma("sync", S["pcomp2"][:, par],
                       D["rows"][s, e, 2].rearrange("(j p) -> p j", p=128),
                       inc=[(c["ld"], 16)],
                       allow_slow_non_contiguous=True, tag=f"gp{e}{s}")
            self.v(lambda v: v.scalar_tensor_tensor(
                S["gidxf"], S["gidxg"], 128.0, S["gidxf"], ALU.mult, ALU.add),
                wait=[c["ld"].last()], tag=f"giv{e}{s}")
            self.vdrain()
            self.v(lambda v, par_=par: v.tensor_copy(out=S["gidx2"][:, par_],
                                                     in_=S["gidxf"]),
                   wait=[(gd, sc_gd[kk - 2])] if kk >= 2 else (),
                   inc=[(vx, 1)], tag=f"gic{e}{s}")
            self.vdrain()
            for j in range(4):
                self.g(lambda g_, j_=j, s_=s, par_=par: g_.indirect_dma_start(
                    S["xin_tok"][:, j_ * DM:(j_ + 1) * DM], None,
                    D["xmoe"][s_][:, :],
                    IndirectOffsetOnAxis(ap=S["gidx2"][:, par_, j_:j_ + 1], axis=0)),
                    wait=[vx.last(), mm.last()], inc=[(gd, 16)], tag=f"gx{e}{s}{j}",
                    raw_inc=True)

        prep(0)
        for kk, (e, s) in enumerate(ks):
            par = kk % 2
            for j in range(4):
                for cc in range(NCH):
                    self.t(lambda T, j_=j, cc_=cc: T.transpose(
                        psT[:, ((j_ * NCH + cc_) % 8) * 128: ((j_ * NCH + cc_) % 8) * 128 + 128],
                        S["xin_tok"][:, j_ * DM + cc_ * 128: j_ * DM + (cc_ + 1) * 128],
                        S["ident_bf"]),
                        wait=[gd.last(), fT.last()] if (j == 0 and cc == 0) else
                             ([fT.last()] if (j * NCH + cc) % 8 == 0 else ()),
                        inc=[(mm, 1)] if (j * NCH + cc) % 8 == 7 else (), tag=f"xt{e}{s}")
                    if (j * NCH + cc) % 8 == 7:
                        u0 = (j * NCH + cc) - 7
                        def ev_t(sc, u0_=u0):
                            outs = []
                            for u in range(8):
                                jj, ccx = divmod(u0_ + u, NCH)
                                outs.append(sc.copy(
                                    out=S["xin_T"][:, ccx * 512 + jj * 128: ccx * 512 + (jj + 1) * 128],
                                    in_=psT[:, u * 128:(u + 1) * 128]))
                            return outs[-1]
                        self.s(ev_t, wait=[mm.last()], inc=[(ev, 1), (fT, 1)], tag=f"xte{e}{s}", multi=True)
            if kk + 1 < len(ks):
                prep(kk + 1)
            for hs in range(8):
                k1 = len(w1_mm)
                wpar = hs % 2
                wAp = c["wA0"] if wpar == 0 else c["wA1"]
                self.p.dma("sync",
                           S["w1col"][:, wpar].rearrange("p (cc x) -> p cc x", x=512),
                           D["w1_c"][e][:, :, hs * 512:(hs + 1) * 512].rearrange(
                               "cc p x -> p cc x"),
                           inc=[(wAp, 16)],
                           wait=[wAp.last()] + ([(mm, w1_mm[k1 - 2])] if k1 >= 2 else []),
                           tag=f"w1l{e}{hs}")
                w_rdy = wAp.last()
                for st in range(4):
                    ph, fh = banks4[st]
                    for cc in range(NCH):
                        self.t(lambda T, st_=st, cc_=cc, par_=wpar, ps_=ph: T.matmul(
                            ps_[:, :], S["xin_T"][:, cc_ * 512 + st_ * 128: cc_ * 512 + (st_ + 1) * 128],
                            S["w1col"][:, par_, cc_ * 512:(cc_ + 1) * 512],
                            start=(cc_ == 0), stop=(cc_ == NCH - 1)),
                            wait=[w_rdy, ev.last(), fh.last()] if cc == 0 else (),
                            inc=[(mm, 1)] if cc == NCH - 1 else (),
                            tag=f"w1m{e}{hs}{st}")
                    self.s(lambda sc, ps_=ph, st_=st: sc.activation(
                        S["cast_t"][:, (st_ % 2) * 512:(st_ % 2) * 512 + 512],
                        ps_[:, :], AF.Gelu, bias=0.0, scale=1.0),
                        wait=[mm.last()], inc=[(ev, 1), (fh, 1)], tag=f"gl{e}{hs}{st}")
                    for u in range(4):
                        self.t(lambda T, st_=st, u_=u: T.transpose(
                            psT[:, (st_ % 2) * 512 + u_ * 128: (st_ % 2) * 512 + (u_ + 1) * 128],
                            S["cast_t"][:, (st_ % 2) * 512 + u_ * 128:(st_ % 2) * 512 + (u_ + 1) * 128],
                            S["ident_bf"]),
                            wait=[ev.last(), fT.last()] if u == 0 else (),
                            inc=[(mm, 1)] if u == 3 else (), tag=f"ht{e}{hs}{st}")
                    def ev_h(sc, hs_=hs, st_=st):
                        o = None
                        for u in range(4):
                            hc = hs_ * 4 + u
                            o = sc.copy(
                                out=S["h_e"][:, hc * 512 + st_ * 128: hc * 512 + (st_ + 1) * 128],
                                in_=psT[:, (st_ % 2) * 512 + u * 128:(st_ % 2) * 512 + (u + 1) * 128])
                        return o
                    self.s(ev_h, wait=[mm.last()], inc=[(ev, 1), (fT, 1)], tag=f"he{e}{hs}{st}", multi=True)
                w1_mm.append(mm.promised)
            for half in range(2):
                for hc in range(EH // 128):
                    k2 = len(w2_mm)
                    wpar = k2 % 4
                    wAp = [c["wA0"], c["wA1"], c["wB0"], c["wB1"]][wpar]
                    self.p.dma("sync", S["w2_str"][:, wpar],
                               D["w2_c"][e, hc][:, half * 512:(half + 1) * 512],
                               inc=[(wAp, 16)],
                               wait=[wAp.last()] +
                                    ([(mm, w2_mm[k2 - 4])] if k2 >= 4 else []),
                               tag=f"w2l{e}{half}{hc}")
                    w_rdy = wAp.last()
                    for st in range(4):
                        ph, fh = banks4[st]
                        self.t(lambda T, hc_=hc, st_=st, par_=wpar, ps_=ph: T.matmul(
                            ps_[:, :], S["h_e"][:, hc_ * 512 + st_ * 128: hc_ * 512 + (st_ + 1) * 128],
                            S["w2_str"][:, par_, :],
                            start=(hc_ == 0), stop=(hc_ == EH // 128 - 1)),
                            wait=[w_rdy, ev.last()] + ([fh.last()] if hc == 0 else []),
                            inc=[(mm, 1)] if st == 3 else (),
                            tag=f"w2m{e}{half}{st}")
                    w2_mm.append(mm.promised)
                for st in range(4):
                    ph, fh = banks4[st]
                    self.v(lambda v, ps_=ph, st_=st, par_=par: v.tensor_scalar(
                        S["y_ev4"][:, st_, :], ps_[:, :],
                        S["pcomp2"][:, par_, st_:st_ + 1], None, ALU.mult),
                        wait=[mm.last(), gd.last()] if st == 0 else [mm.last()],
                        inc=[(vx, 1), (fh, 1)], tag=f"w2e{e}{half}{st}")
                self.vdrain()
                for st in range(4):
                    self.g(lambda g_, s_=s, st_=st, h_=half, par_=par: g_.indirect_dma_start(
                        D["ymoe"][s_][:, :],
                        IndirectOffsetOnAxis(ap=S["gidx2"][:, par_, st_:st_ + 1], axis=0),
                        S["y_ev4"][:, st_], None, compute_op=ALU.add,
                        element_offset=h_ * 512),
                        wait=[vx.last(), gd.last()], inc=[(gd, 16)], tag=f"sc{e}{s}{st}",
                        raw_inc=True)
            sc_gd[kk] = gd.promised
        self.barrier()
        self.g(lambda g_, v=gd.promised: g_.wait_ge(gd.sem, v), inc=[(gx, 1)], tag="gdw")
        self.barrier()

    def final(self):
        S, D, c = self.S, self.D, self.c
        vx, ev = c["vx"], c["ev"]
        for s in range(2):
            for t in range(NT):
                self.p.dma("sync", S["stage"][:, 0], D["ymoe"][s][t * 128:(t + 1) * 128, :],
                           inc=[(c["ld"], 16)], wait=[ev.last(), c["fin"].last()], tag=f"fy{s}{t}")
                if s == 0:
                    self.p.dma("sync", S["stage"][:, 1], D["r3A"][t * 128:(t + 1) * 128, :],
                               inc=[(c["ld"], 16)], tag=f"fr{s}{t}")
                    res = S["stage"][:, 1]
                else:
                    res = S["r"][:, t * DM:(t + 1) * DM]
                yt = S["stage"][:, 0]
                self.v(lambda v, r_=res: v.tensor_add(yt, yt, r_),
                       wait=[c["ld"].last(), c["fin"].last()], inc=[(vx, 1)], tag=f"fa{s}{t}")
                self.vdrain()
                tok = self.ln_tile(yt, wait=[vx.last()])
                self.p.dma("sync", D["y"][s][t * 128:(t + 1) * 128, :], yt,
                           wait=[tok], inc=[(c["fin"], 16)], tag=f"fo{s}{t}")
        self.p.emit("sync", lambda e, v=c["fin"].promised: e.wait_ge(c["fin"].sem, v), tag="finw")

    def build(self):
        self.load_consts()
        if self.phase_done("consts"):
            return self.dump_r_and_finish()
        self.build_ffn_cache()
        self.build_moe_cache()
        for s in range(2):
            self._seq = s
            self.load_T12()
            self.l0_attn(s)
            if self.phase_done(f"l0a:{s}"):
                return self.dump_r_and_finish()
            self.l0_ffn(s)
            if self.phase_done(f"l0f:{s}"):
                return self.dump_r_and_finish()
            self.l1_attn(s)
            if self.phase_done(f"l1a:{s}"):
                return self.dump_r_and_finish()
            self.routing(s)
            if self.phase_done(f"rout:{s}"):
                return self.dump_r_and_finish()
            if s == 0:
                self.p.dma("sync",
                           self.D["r3A"].rearrange("(t p) d -> p t d", p=128),
                           self.S["r"].rearrange("p (t d) -> p t d", d=DM),
                           wait=[self.c["ev"].last(), self.c["vx"].last()],
                           inc=[(self.c["st"], 16)], tag="r3sp")
                self.p.emit("sync", lambda e, v=self.c["st"].promised: e.wait_ge(
                    self.c["st"].sem, v))
                self.barrier()
        self.moe()
        if self.phase_done("moe"):
            return self.dump_r_and_finish()
        self.final()


def build_program(stop_after=None):
    b = Build(stop_after=stop_after)
    b.build()
    b.p.finalize()
    return b


def kernel(**inputs):
    pre = host_prepare(inputs)
    x = np.asarray(inputs["L0_attn_x"], np.float32)
    b = build_program()
    in_maps = []
    for core in range(NCORE):
        m = dict(pre)
        m["x"] = np.ascontiguousarray(x[2 * core: 2 * core + 2])
        in_maps.append(m)
    res = run_bass_kernel_spmd(b.nc, in_maps, core_ids=list(range(NCORE)))
    out = np.concatenate([r["y"] for r in res.results], axis=0)
    return out.reshape(16, SEQ, DM)


# version 14 (final)
# speedup vs baseline: 1.0770x; 1.0193x over previous
import numpy as np
import ml_dtypes
import concourse.bass as bass
import concourse.mybir as mybir
from concourse.bass import IndirectOffsetOnAxis
from concourse.bass_utils import run_bass_kernel_spmd
from concourse import library_config as _LC

F32 = mybir.dt.float32
BF16 = mybir.dt.bfloat16
I32 = mybir.dt.int32
AF = mybir.ActivationFunctionType
ALU = mybir.AluOpType
AX = mybir.AxisListType
BF = ml_dtypes.bfloat16

ENGINES = ("sync", "scalar", "vector", "tensor", "gpsimd")

NCORE = 8
SEQ = 2048
DM = 1024
NH = 16
DH = 64
HID = 3072
NHC = HID // 128
NE = 8
EH = 4096
CAP = 512
NT = SEQ // 128
NCH = DM // 128
THETA = 10000.0
LN_EPS = 1e-5
RMS_EPS = 1.1920929e-07
QSC = 1.0 / 127.0
N_BISECT = 34
OOB_BIG = 1.0e7


class Chan:
    __slots__ = ("name", "sem", "promised")
    def __init__(self, nc, name):
        self.name = name
        self.sem = nc.alloc_semaphore(name)
        self.promised = 0
    def last(self):
        return (self, self.promised)


class Prog:
    def __init__(self, nc):
        self.nc = nc
        self.streams = {e: [] for e in ENGINES}
        self.n_ops = 0

    def chan(self, name):
        return Chan(self.nc, name)

    def emit(self, eng, fn, wait=(), inc=(), tag="", raw_inc=False, multi=False):
        wait = tuple(wait)
        inc = tuple(inc)
        for ch, v in wait:
            assert v <= ch.promised, (
                f"wait {ch.name}>={v} but promised {ch.promised} ({tag})")
        for ch, k in inc:
            ch.promised += k
        if multi and eng not in ("sync", "tensor"):
            self.streams[eng].append((wait, fn, (), tag))
            self.n_ops += 1
            rest = inc
        else:
            self.streams[eng].append((wait, fn, inc[:1], tag))
            self.n_ops += 1
            rest = inc[1:]
        for ch, k in rest:
            self.streams[eng].append(((), (lambda e: e.drain()), ((ch, k),), tag + "+i"))
            self.n_ops += 1

    def dma(self, eng, out, in_, inc=(), wait=(), tag="", **kw):
        self.emit(eng, (lambda e, o=out, i=in_, k=kw: e.dma_start(out=o, in_=i, **k)),
                  wait=wait, inc=inc, tag=tag, raw_inc=True)

    def finalize(self):
        nc = self.nc
        with nc.Block() as block:
            def runner(stream):
                def run(engine):
                    for wait, fn, inc, tag in stream:
                        for ch, v in wait:
                            engine.wait_ge(ch.sem, v)
                        try:
                            ins = fn(engine)
                        except Exception:
                            import traceback as _tb
                            _tb.print_exc()
                            raise
                        if inc:
                            assert ins is not None, f"inc w/o instruction ({tag})"
                            for ch, k in inc:
                                ins = ins.then_inc(ch.sem, k)
                return run
            block.sync(runner(self.streams["sync"]))
            block.scalar(runner(self.streams["scalar"]))
            block.vector(runner(self.streams["vector"]))
            block.tensor(runner(self.streams["tensor"]))
            block.gpsimd(runner(self.streams["gpsimd"]))



def rope_perm():
    perm = np.zeros(DM, np.int64)
    for h in range(NH):
        for k in range(DH):
            i, s_ = k // 2, k % 2
            perm[h * DH + (i if s_ == 0 else 32 + i)] = h * DH + k
    return perm


def rope_tables():
    t = np.arange(SEQ, dtype=np.float64)
    i = np.arange(32, dtype=np.float64)
    ang = np.outer(i / THETA, t)
    cos = np.cos(ang); sin = np.sin(ang)
    T1 = np.tile(np.concatenate([cos, cos], 0), (1, 2))
    T2 = np.tile(np.concatenate([-sin, sin], 0), (1, 2))
    return T1.astype(BF), T2.astype(BF)


def host_prepare(inp):
    perm = rope_perm()
    g = lambda n: np.asarray(inp[n])
    d = {}
    d["WqT"] = np.ascontiguousarray(g("L0_attn_Wq")[perm, :].T.astype(np.float32))
    d["WkT"] = np.ascontiguousarray(g("L0_attn_Wk")[perm, :].T.astype(np.float32))
    d["WvT"] = np.ascontiguousarray(g("L0_attn_Wv").T.astype(np.float32))
    d["WoT"] = np.ascontiguousarray(g("L0_attn_Wo").T.astype(np.float32))
    d["bq8"] = np.ascontiguousarray(g("L0_attn_bq")[perm].reshape(NCH, 128).T.astype(np.float32))
    d["bk8"] = np.ascontiguousarray(g("L0_attn_bk")[perm].reshape(NCH, 128).T.astype(np.float32))
    d["bv_row"] = g("L0_attn_bv").reshape(1, DM).astype(np.float32)
    d["bo_row"] = g("L0_attn_bo").reshape(1, DM).astype(np.float32)
    d["wgT"] = np.ascontiguousarray(g("L0_ffn_w_gate").T.astype(np.int32))
    d["wuT"] = np.ascontiguousarray(g("L0_ffn_w_up").T.astype(np.int32))
    d["wdT"] = np.ascontiguousarray(g("L0_ffn_w_down").T.astype(np.int32))
    d["W2qT"] = np.ascontiguousarray(g("L1_attn_w_q").T.astype(np.float32))
    d["W2kT"] = np.ascontiguousarray(g("L1_attn_w_k").T.astype(np.float32))
    d["W2vT"] = np.ascontiguousarray(g("L1_attn_w_v").T.astype(np.float32))
    d["W2oT"] = np.ascontiguousarray(g("L1_attn_w_o").T.astype(np.float32))
    d["gateW"] = np.ascontiguousarray(g("L1_ffn_gate_w").astype(np.float32))
    d["w1"] = np.ascontiguousarray(g("L1_ffn_w1").astype(np.float32))
    d["w2"] = np.ascontiguousarray(g("L1_ffn_w2").astype(np.float32))
    T1, T2 = rope_tables()
    d["cT1"] = T1; d["cT2"] = T2
    d["ident_bf"] = np.eye(128).astype(BF)
    d["ident_f32"] = np.eye(128, dtype=np.float32)
    d["tri_bf"] = (np.arange(128)[None, :] >= np.arange(128)[:, None]).astype(BF)
    d["ones_bf"] = np.ones((128, 128), BF)
    d["iota512"] = np.tile(np.arange(512, dtype=np.float32), (128, 1))
    d["pcol"] = np.arange(128, dtype=np.float32).reshape(128, 1)
    return d



class Build:
    def __init__(self, stop_after=None):
        nc = bass.Bass("TRN2", target_bir_lowering=False)
        self.nc = nc
        self.p = Prog(nc)
        self.stop_after = stop_after
        self.stopped = False
        self._mk_dram()
        self._mk_sbuf()
        self._mk_psum()
        self._mk_chans()

    def _mk_dram(self):
        nc = self.nc
        D = {}
        def ein(name, shape, dt):
            D[name] = nc.dram_tensor(name, list(shape), dt, kind="ExternalInput").ap()
        ein("x", (2, SEQ, DM), F32)
        for n in ("WqT", "WkT", "WvT", "WoT", "W2qT", "W2kT", "W2vT", "W2oT"):
            ein(n, (DM, DM), F32)
        ein("bq8", (128, NCH), F32); ein("bk8", (128, NCH), F32)
        ein("bv_row", (1, DM), F32); ein("bo_row", (1, DM), F32)
        ein("wgT", (DM, HID), I32); ein("wuT", (DM, HID), I32); ein("wdT", (HID, DM), I32)
        ein("gateW", (DM, NE), F32)
        ein("w1", (NE, DM, EH), F32); ein("w2", (NE, EH, DM), F32)
        ein("cT1", (64, 2 * SEQ), BF16); ein("cT2", (64, 2 * SEQ), BF16)
        ein("ident_bf", (128, 128), BF16); ein("ident_f32", (128, 128), F32)
        ein("tri_bf", (128, 128), BF16); ein("ones_bf", (128, 128), BF16)
        ein("iota512", (128, 512), F32); ein("pcol", (128, 1), F32)
        D["y"] = nc.dram_tensor("y", [2, SEQ, DM], F32, kind="ExternalOutput").ap()
        D["dbg"] = nc.dram_tensor("dbg", [128, NT * DM], F32, kind="ExternalOutput").ap()
        D["r_spill"] = nc.dram_tensor("r_spill", [SEQ, DM], F32, kind="Internal").ap()
        D["r3A"] = nc.dram_tensor("r3A", [SEQ, DM], F32, kind="Internal").ap()
        D["xmoe"] = [nc.dram_tensor("xmoe0", [SEQ, DM], BF16, kind="Internal").ap(),
                     nc.dram_tensor("xmoe1", [SEQ, DM], BF16, kind="Internal").ap()]
        D["ymoe"] = [nc.dram_tensor("ymoe0", [SEQ, DM], F32, kind="Internal").ap(),
                     nc.dram_tensor("ymoe1", [SEQ, DM], F32, kind="Internal").ap()]
        D["rows"] = nc.dram_tensor("rows", [2, NE, 3, 512], F32, kind="Internal").ap()
        D["wg_c"] = nc.dram_tensor("wg_c", [NHC, 128, NCH * 128], BF16, kind="Internal").ap()
        D["wu_c"] = nc.dram_tensor("wu_c", [NHC, 128, NCH * 128], BF16, kind="Internal").ap()
        D["wd_c"] = nc.dram_tensor("wd_c", [NHC, 128, DM], BF16, kind="Internal").ap()
        D["w1_c"] = nc.dram_tensor("w1_c", [NE, NCH, 128, EH], BF16, kind="Internal").ap()
        D["w2_c"] = nc.dram_tensor("w2_c", [NE, EH // 128, 128, DM], BF16, kind="Internal").ap()
        self.D = D

    def _mk_sbuf(self):
        base = (self.nc.sbuf_base + 127) // 128 * 128
        at = self.nc.alloc_sbuf_tensor_at
        S = {}
        def A(name, shape, dt, offset):
            assert offset % 32 == 0, (name, offset)
            S[name] = at(name, list(shape), dt, offset=base + offset).ap()
        A("r", [128, NT * DM], F32, 0)
        A("q_fm", [128, NCH * SEQ], BF16, 0)
        A("k_fm", [128, NCH * SEQ], BF16, 32768)
        A("T1", [64, 2 * SEQ], BF16, 49152)
        A("T2", [64, 2 * SEQ], BF16, 57344)
        A("x_T", [128, NCH * SEQ], BF16, 65536)
        A("xin_tok", [128, 4 * DM], BF16, 65536)
        A("xin_T", [128, NCH * 512], BF16, 65536 + 8192)
        A("gidxf", [128, 4], F32, 65536 + 16384)
        A("gidxg", [128, 4], F32, 65536 + 16576)
        A("gidx", [128, 4], I32, 65536 + 16448)
        A("pcomp", [128, 4], F32, 65536 + 16512)
        A("gidx2", [128, 2, 4], I32, 65536 + 16640)
        A("pcomp2", [128, 2, 4], F32, 65536 + 16704)
        A("V", [128, NT * NH * 65], BF16, 98304)
        A("h_grp", [128, NHC * 512], BF16, 98304)
        A("attn_fm", [128, NCH * SEQ], BF16, 131584)
        A("h_e", [128, (EH // 128) * 512], BF16, 131584)
        A("P_buf", [128, 8 * 512], BF16, 164352)
        A("mc_st", [128, 2, DM], F32, 164352)
        A("gate_t", [128, 512], BF16, 164352 + 4096)
        A("up_t", [128, 512], BF16, 164352 + 5120)
        A("y_ev4", [128, 4, 512], F32, 164352)
        A("prb8", [128, 8], F32, 164352 + 8192)
        A("ln6", [128, 12], F32, 164352 + 8224)
        A("ln2", [128, 2], F32, 164352 + 8288)
        A("lncol", [128, 3], F32, 164352 + 8320)
        A("rms16", [128, 16], F32, 164352 + 8352)
        A("rms16b", [128, 16], F32, 164352 + 8416)
        QKB = 164352 + 8512
        A("mc_cb", [128, 2, DM], BF16, QKB)
        A("Q_store", [65, 2 * SEQ], BF16, QKB)
        A("K_store", [65, 2 * SEQ], BF16, QKB + 8192)
        A("w_str", [128, NCH * DM], BF16, QKB)
        A("w1col", [128, 2, NCH * 512], BF16, QKB)
        A("probs_et", [NE, SEQ], F32, QKB)
        A("mask_et", [NE, SEQ], BF16, QKB + 8192)
        A("w2_str", [128, 4, 512], BF16, 65536 + 17408)
        A("sc_rows", [10, 512], F32, QKB + 12288)
        A("srch", [NE, 8], F32, QKB + 14336)
        A("gw_bf", [128, NCH * 8], BF16, QKB + 14368)
        A("bigc", [128, 8], F32, QKB + 14496)
        A("srch_i", [NE, 2], I32, QKB + 14528)
        A("wg_hc", [128, 2, NCH * 128], BF16, QKB)
        A("wu_hc", [128, 2, NCH * 128], BF16, QKB + 4096)
        A("wd_hc", [128, 4, 512], BF16, QKB + 8192)
        TMB = 164352 + 8512 + 16384
        A("stage", [128, 2, DM], F32, TMB)
        A("stage_i", [128, 2, DM], I32, TMB)
        A("SWQ", [64, SEQ], BF16, TMB + 8192)
        A("cast_t2", [128, DM], BF16, TMB + 8192)
        A("tanhb", [128, 512], F32, TMB + 8192)
        A("iota512", [128, 512], F32, TMB + 8192)
        A("scat_all", [128, 128], F32, TMB + 8192 + 2048)
        A("maskT_sb", [128, 128], BF16, TMB + 8192 + 2560)
        A("idx_t", [128, 8], F32, TMB + 8192 + 2816)
        A("idx_t2", [128, 8], F32, TMB + 8192 + 2848)
        A("idx_m", [128, 8], I32, TMB + 8192 + 2880)
        A("qk_w", [128, NCH * 128], BF16, TMB + 12288)
        A("cast_t", [128, DM], BF16, TMB + 14336)
        S["att_t"] = S["cast_t"][0:64, 0:512]
        A("inv_t", [1, 512], F32, TMB + 16384)
        A("row_bf", [1, DM], BF16, TMB + 18432)
        off = TMB + 20480
        def C(name, shape, dt, nbytes):
            nonlocal off
            assert off % 32 == 0, (name, off)
            S[name] = at(name, list(shape), dt, offset=base + off).ap()
            off += (nbytes + 31) // 32 * 32
        C("ident_bf", (128, 128), BF16, 256)
        C("tri_bf", (128, 128), BF16, 256)
        C("ones_bf", (128, 128), BF16, 256)
        C("pcol", (128, 1), F32, 4)
        C("eps2", (128, 2), F32, 8)
        S["eps_ln"] = S["eps2"][:, 0:1]
        S["eps_rms"] = S["eps2"][:, 1:2]
        C("ones_col", (128, 1), BF16, 4)
        C("ones1f", (1, 64), F32, 256)
        C("bq8", (128, NCH), F32, 32)
        C("bk8", (128, NCH), F32, 32)
        C("probsT", (128, 2, NT * 10), BF16, 2 * NT * 10 * 2)
        C("ident_f32", (128, 128), F32, 512)
        C("kmax", (1, 2), F32, 8)
        assert base + off <= self.nc.sbuf_top, (base, off, self.nc.sbuf_top)
        self.S = S

    def _mk_psum(self):
        nc = self.nc
        P = {}
        for n in ("psA", "psB", "psC", "psD", "psE", "psF", "psU"):
            P[n] = nc.alloc_psum_tensor(n, [128, 512], F32).ap()
        P["psT"] = nc.alloc_psum_tensor("psT", [128, 1024], BF16).ap()
        self.PS = P

    def _mk_chans(self):
        p = self.p
        self.c = {}
        names = ["ld", "st", "wA", "cA", "wB", "cB", "mm", "ev", "vx", "gx",
                 "gdma", "fin", "bar_s", "bar_c", "bar_v", "bar_t", "bar_g",
                 "wA0", "wA1", "wB0", "wB1", "ld0", "ld1", "gd0", "gd1",
                 "mcA", "mcB", "mcC", "mcD"]
        names += [f"f{b}" for b in ("A", "B", "C", "D", "E", "F", "U", "T")]
        for n in names:
            self.c[n] = p.chan(n)

    def v(self, fn, wait=(), inc=(), tag="", multi=False):
        self.p.emit("vector", fn, wait, inc, tag, multi=multi)
    def s(self, fn, wait=(), inc=(), tag="", multi=False):
        self.p.emit("scalar", fn, wait, inc, tag, multi=multi)
    def t(self, fn, wait=(), inc=(), tag=""): self.p.emit("tensor", fn, wait, inc, tag)
    def g(self, fn, wait=(), inc=(), tag="", raw_inc=False, multi=False):
        self.p.emit("gpsimd", fn, wait, inc, tag, raw_inc=raw_inc, multi=multi)
    def vdrain(self, wait=(), inc=()): self.v(lambda v: v.drain(), wait, inc)
    def sdrain(self, wait=(), inc=()): self.s(lambda s: s.drain(), wait, inc)
    def gdrain(self, wait=(), inc=()): self.g(lambda g: g.drain(), wait, inc)

    def bank(self, n):
        return self.PS["ps" + n], self.c["f" + n]

    def barrier(self):
        c = self.c
        pairs = [("sync", "bar_s"), ("scalar", "bar_c"), ("vector", "bar_v"),
                 ("tensor", "bar_t"), ("gpsimd", "bar_g")]
        for eng, ch in pairs:
            self.p.emit(eng, lambda e: e.nop(), inc=[(c[ch], 1)], tag="bar", multi=True)
        toks = [c[ch].last() for _, ch in pairs]
        for eng, _ in pairs:
            self.p.emit(eng, lambda e: e.nop(), wait=toks, tag="barw")

    def phase_done(self, name):
        if self.stop_after is not None and name == self.stop_after:
            self.stopped = True
            return True
        return False

    def dump_r_and_finish(self):
        self.barrier()
        c = self.c
        self.p.dma("sync", self.D["dbg"][:, :], self.S["r"], inc=[(c["fin"], 16)], tag="dbg")
        self.p.emit("sync", lambda e, v=c["fin"].promised: e.wait_ge(c["fin"].sem, v), tag="finw")

    def load_consts(self):
        p, S, D, c = self.p, self.S, self.D, self.c
        ld = c["ld"]
        for n in ("ident_bf", "ident_f32", "tri_bf", "ones_bf",
                  "pcol", "bq8", "bk8"):
            p.dma("sync", S[n], D[n][:, :], inc=[(ld, 16)], tag=f"cst_{n}")
        self.v(lambda v: v.memset(S["eps_ln"], LN_EPS))
        self.v(lambda v: v.memset(S["eps_rms"], RMS_EPS))
        self.v(lambda v: v.memset(S["ones_col"], 1.0))
        self.v(lambda v: v.memset(S["ones1f"], 1.0))
        self.vdrain(inc=[(c["vx"], 1)])
        self.g(lambda g: g.nop(), inc=[(c["gx"], 1)])
        self.p.emit("sync", lambda e, v=ld.promised: e.wait_ge(ld.sem, v), tag="cstw")
        self.barrier()

    def load_T12(self):
        p, S, D, c = self.p, self.S, self.D, self.c
        ld = c["ld"]
        p.dma("sync", S["T1"], D["cT1"][:, :], inc=[(ld, 16)], tag="T1")
        p.dma("sync", S["T2"], D["cT2"][:, :], inc=[(ld, 16)], tag="T2")
        p.emit("sync", lambda e, v=ld.promised: e.wait_ge(ld.sem, v), tag="T12w")
        self.barrier()

    def build_xT(self, src_tile_fn, tag="xT"):
        S, c = self.S, self.c
        psT, fT = self.bank("T")
        mm, ev, vx = c["mm"], c["ev"], c["vx"]
        cmm = []
        for t in range(NT):
            src, w = src_tile_fn(t)
            ct = S["cast_t"] if t % 2 == 0 else S["cast_t2"]
            self.v(lambda v, s_=src, ct_=ct: v.tensor_copy(out=ct_, in_=s_),
                   wait=list(w) + ([(mm, cmm[t - 2])] if t >= 2 else []),
                   inc=[(vx, 1)], tag=f"{tag}c{t}")
            for cc in range(NCH):
                self.t(lambda T, cc_=cc, ct_=ct: T.transpose(
                    psT[:, cc_ * 128:(cc_ + 1) * 128],
                    ct_[:, cc_ * 128:(cc_ + 1) * 128], S["ident_bf"]),
                    wait=[vx.last(), fT.last()] if cc == 0 else (),
                    inc=[(mm, 1)] if cc == NCH - 1 else (), tag=f"{tag}t{t}")
            cmm.append(mm.promised)
            self.s(lambda s, t_=t: s.copy(
                out=S["x_T"].rearrange("p (c q) -> p c q", c=NCH)[:, :, t_ * 128:(t_ + 1) * 128],
                in_=psT.rearrange("p (c q) -> p c q", c=NCH)),
                wait=[mm.last()], inc=[(ev, 1), (fT, 1)], tag=f"{tag}e{t}")
        return ev.last()

    def hbm_tile(self, dram2d, t, slot):
        S, c = self.S, self.c
        ld = c["ld0"] if slot == 0 else c["ld1"]
        sl = S["stage"][:, slot]
        self.p.dma("sync", sl, dram2d[t * 128:(t + 1) * 128, :], inc=[(ld, 16)],
                   wait=[c["vx"].last(), c["mm"].last(), c["ev"].last(),
                         c["cA"].last(), c["cB"].last()], tag=f"ht{t}")
        return sl, [ld.last()]

    def stream_w(self, dst, srcs, tag="w", eng="vector", chunk_elems=DM, src_int=False,
                 extra_wait=()):
        S, c = self.S, self.c
        ca = c["cA"]
        stage = S["stage_i"] if src_int else S["stage"]
        for i, src in enumerate(srcs):
            w = c["wA0"] if i % 2 == 0 else c["wA1"]
            sl = stage[:, i % 2, :chunk_elems]
            self.p.dma("sync", sl, src, inc=[(w, 16)],
                       wait=[w.last(), (ca, max(0, ca.promised - 1))] +
                            (list(extra_wait) if i == 0 else []),
                       tag=f"{tag}l{i}")
            self.p.emit(eng, lambda e, i_=i, sl_=sl: e.tensor_copy(
                out=dst[:, i_ * chunk_elems:(i_ + 1) * chunk_elems], in_=sl_),
                wait=[w.last()], inc=[(ca, 1)], tag=f"{tag}c{i}")
        return ca.last()

    def ln_tile(self, ytile, wait=()):
        S, c = self.S, self.c
        ev, vx = c["ev"], c["vx"]
        self.v(lambda v: v.bn_stats(S["ln6"][:, 0:6], ytile[:, 0:512]),
               wait=list(wait) + [c["ev"].last()], tag="lns0")
        self.v(lambda v: v.bn_stats(S["ln6"][:, 6:12], ytile[:, 512:1024]), tag="lns1")
        self.vdrain()
        self.v(lambda v: v.bn_aggr(S["ln2"], S["ln6"]), tag="lnagg")
        self.vdrain(inc=[(vx, 1)])
        self.s(lambda s: s.activation(S["lncol"][:, 0:1], S["ln2"][:, 1:2], AF.Sqrt,
                                      bias=S["eps_ln"], scale=1.0),
               wait=[vx.last()], inc=[(ev, 1)], tag="lnsqrt")
        self.v(lambda v: v.reciprocal(S["lncol"][:, 1:2], S["lncol"][:, 0:1]),
               wait=[ev.last()], tag="lnrec")
        self.vdrain()
        self.v(lambda v: v.tensor_scalar(S["lncol"][:, 2:3], S["ln2"][:, 0:1],
                                         S["lncol"][:, 1:2], -1.0, ALU.mult, ALU.mult),
               tag="lnnb")
        self.vdrain(inc=[(vx, 1)])
        self.s(lambda s: s.activation(ytile, ytile, AF.Identity,
                                      bias=S["lncol"][:, 2:3], scale=S["lncol"][:, 1:2]),
               wait=[vx.last()], inc=[(ev, 1)], tag="lnap")
        return ev.last()

    def l0_attn(self, s):
        S, D, c = self.S, self.D, self.c
        mm, ev, vx = c["mm"], c["ev"], c["vx"]

        def src_tile(t):
            return self.hbm_tile(D["x"][s], t, t % 2)
        self.build_xT(src_tile, tag=f"s{s}x0")
        self.barrier()

        self.p.dma("sync", S["stage"][0:1, 0, :], D["bv_row"][:, :],
                   wait=[c["cA"].last(), c["vx"].last()], inc=[(c["ld"], 16)], tag="bv")
        self.v(lambda v: v.tensor_copy(out=S["row_bf"], in_=S["stage"][0:1, 0, :]),
               wait=[c["ld"].last()], tag="bvc")
        self.vdrain(inc=[(c["vx"], 1)])
        self.stream_w(S["w_str"], [D["WvT"][i * 128:(i + 1) * 128, :] for i in range(NCH)],
                      extra_wait=[c["vx"].last()], tag=f"s{s}wv")
        self.v(lambda v: v.memset(
            S["V"].rearrange("p (t h u) -> p t h u", t=NT, h=NH)[:, :, :, 64:65], 1.0),
            tag="vones")
        self.vdrain(inc=[(vx, 1)])
        self.proj_tokmajor(dst_V=True, wq=None, bias_bf=S["row_bf"], tag=f"s{s}V")
        self.barrier()

        self.v(lambda v: v.memset(S["K_store"][64:65, :], 1.0), inc=[(vx, 1)], tag="kone")
        for hp in range(NH // 2):
            self.l0_hp(s, hp)
        self.barrier()

        self.wo_ln(res_dram=D["x"][s], wT=D["WoT"], bias_dram=D["bo_row"],
                   scale=1.0, tag=f"s{s}wo0")
        self.barrier()

    def proj_tokmajor(self, dst_V, wq, bias_bf, tag):
        S, c = self.S, self.c
        mm, ev = c["mm"], c["ev"]
        pa, fa = self.bank("A"); pb, fb = self.bank("B")
        pc, fc = self.bank("C"); pd, fd = self.bank("D")
        for t in range(NT):
            p0, f0, p1, f1 = (pa, fa, pb, fb) if t % 2 == 0 else (pc, fc, pd, fd)
            for half, (ph, fh) in enumerate(((p0, f0), (p1, f1))):
                last_h = (half == 1)
                for cc in range(NCH):
                    self.t(lambda T, t_=t, cc_=cc, h_=half, ps_=ph: T.matmul(
                        ps_[:, :],
                        S["x_T"][:, cc_ * SEQ + t_ * 128: cc_ * SEQ + (t_ + 1) * 128],
                        S["w_str"][:, cc_ * DM + h_ * 512: cc_ * DM + h_ * 512 + 512],
                        start=(cc_ == 0), stop=(bias_bf is None and cc_ == NCH - 1)),
                        wait=[c["cA"].last(), c["vx"].last(), fh.last()] if cc == 0 else (),
                        inc=[(mm, 1)] if (bias_bf is None and cc == NCH - 1 and last_h) else (),
                        tag=f"{tag}m{t}")
                if bias_bf is not None:
                    self.t(lambda T, h_=half, ps_=ph: T.matmul(
                        ps_[:, :], S["ones_bf"][0:1, :],
                        bias_bf[0:1, h_ * 512:(h_ + 1) * 512], start=False, stop=True),
                        inc=[(mm, 1)] if last_h else (), tag=f"{tag}b{t}")
            if dst_V:
                def ev_v(sc, t_=t, p0_=p0, p1_=p1):
                    dst = S["V"].rearrange("p (t h u) -> p t h u", t=NT, h=NH)[:, t_, :, 0:64]
                    sc.copy(out=dst[:, 0:8, :], in_=p0_.rearrange("p (h u) -> p h u", u=64))
                    return sc.copy(out=dst[:, 8:16, :], in_=p1_.rearrange("p (h u) -> p h u", u=64))
                self.s(ev_v, wait=[mm.last()], inc=[(f0, 1), (f1, 1)], tag=f"{tag}e{t}", multi=True)
                self.s(lambda sc: sc.nop(), inc=[(ev, 1)], tag=f"{tag}en{t}")

    def l0_hp(self, s, hp):
        S, D, c = self.S, self.D, self.c
        mm, ev, vx = c["mm"], c["ev"], c["vx"]
        cB = c["cB"]
        pa, fa = self.bank("A"); pb, fb = self.bank("B")
        for which, wname, store, b8 in ((1, "WkT", S["K_store"], S["bk8"]),
                                        (0, "WqT", S["Q_store"], S["bq8"])):
            for cc in range(NCH):
                i = which * NCH + cc
                wB = c["wB0"] if i % 2 == 0 else c["wB1"]
                sl = S["stage"][:, i % 2, :128]
                self.p.dma("sync", sl, D[wname][cc * 128:(cc + 1) * 128, hp * 128:(hp + 1) * 128],
                           inc=[(wB, 16)], wait=[wB.last(), (cB, max(0, cB.promised - 1))] +
                           ([mm.last()] if cc == 0 else []),
                           tag=f"hp{hp}wl")
                self.v(lambda v, sl_=sl, cc_=cc: v.tensor_copy(
                    out=S["qk_w"][:, cc_ * 128:(cc_ + 1) * 128], in_=sl_),
                    wait=[wB.last()], inc=[(cB, 1)], tag=f"hp{hp}wc")
            w_rdy = cB.last()
            for qg in range(4):
                ph, fh = (pa, fa) if qg % 2 == 0 else (pb, fb)
                for cc in range(NCH):
                    self.t(lambda T, cc_=cc, qg_=qg, ps_=ph: T.matmul(
                        ps_[:, :], S["qk_w"][:, cc_ * 128:(cc_ + 1) * 128],
                        S["x_T"][:, cc_ * SEQ + qg_ * 512: cc_ * SEQ + (qg_ + 1) * 512],
                        start=(cc_ == 0), stop=(cc_ == NCH - 1)),
                        wait=[w_rdy, fh.last()] if cc == 0 else (),
                        inc=[(mm, 1)] if cc == NCH - 1 else (), tag=f"hp{hp}p{which}{qg}")
                def ev_qk(sc, ps_=ph, st_=store, qg_=qg, b8_=b8, hp_=hp):
                    sc.activation(st_[0:64, qg_ * 512:(qg_ + 1) * 512], ps_[0:64, :],
                                  AF.Identity, bias=b8_[0:64, hp_:hp_ + 1], scale=1.0)
                    return sc.activation(st_[0:64, SEQ + qg_ * 512: SEQ + (qg_ + 1) * 512],
                                         ps_[64:128, :], AF.Identity,
                                         bias=b8_[64:128, hp_:hp_ + 1], scale=1.0)
                self.s(ev_qk, wait=[mm.last()], inc=[(ev, 1), (fh, 1)], tag=f"hp{hp}e{which}{qg}", multi=True)
            self.rope(store)
        self.l0_m_and_attn(s, hp)

    def rope(self, store):
        S, c = self.S, self.c
        vx, ev = c["vx"], c["ev"]
        SW = S["SWQ"]
        for sd in range(2):
            st = store[0:64, sd * SEQ:(sd + 1) * SEQ]
            self.v(lambda v, st_=st: v.tensor_copy(out=SW[0:32, :], in_=st_[32:64, :]),
                   wait=[ev.last(), vx.last()], tag="sw1")
            self.v(lambda v, st_=st: v.tensor_copy(out=SW[32:64, :], in_=st_[0:32, :]), tag="sw2")
            self.vdrain()
            self.v(lambda v, sd_=sd: v.tensor_tensor(SW, SW, S["T2"][:, sd_ * SEQ:(sd_ + 1) * SEQ],
                                                     ALU.mult), tag="swT2")
            self.v(lambda v, st_=st, sd_=sd: v.tensor_tensor(
                st_, st_, S["T1"][:, sd_ * SEQ:(sd_ + 1) * SEQ], ALU.mult), tag="qT1")
            self.vdrain()
            self.v(lambda v, st_=st: v.tensor_tensor(st_, st_, SW, ALU.add), tag="ropea")
            self.vdrain(inc=[(vx, 1)])

    def l0_m_and_attn(self, s, hp):
        S, c = self.S, self.c
        mm, ev, vx = c["mm"], c["ev"], c["vx"]
        pg, fg = self.PS["psU"], self.c["fU"]
        SW = S["SWQ"]
        self.v(lambda v: v.memset(S["kmax"], 0.0), wait=[vx.last()], tag="km0")
        self.vdrain(inc=[(vx, 1)])
        for side in range(2):
            self.v(lambda v, sd=side: v.tensor_tensor(
                SW, S["K_store"][0:64, sd * SEQ:(sd + 1) * SEQ],
                S["K_store"][0:64, sd * SEQ:(sd + 1) * SEQ], ALU.mult),
                wait=[vx.last()], tag="KK")
            self.vdrain(inc=[(vx, 1)])
            for seg in range(4):
                self.t(lambda T, sg=seg: T.matmul(
                    pg[0:1, :], S["ones_col"][0:64, :],
                    SW[:, sg * 512:(sg + 1) * 512], start=True, stop=True),
                    wait=[vx.last(), fg.last()], inc=[(mm, 1)], tag="kn2")
                self.v(lambda v: v.reduce_max(S["inv_t"][0:1, 0:1], pg[0:1, :], AX.X),
                       wait=[mm.last()], tag="knm")
                self.vdrain()
                self.v(lambda v, sd=side: v.tensor_max(
                    S["kmax"][0:1, sd:sd + 1], S["kmax"][0:1, sd:sd + 1],
                    S["inv_t"][0:1, 0:1]), inc=[(vx, 1), (fg, 1)], tag="knc")
                self.vdrain()
        self.s(lambda sc: sc.activation(S["kmax"], S["kmax"], AF.Sqrt, bias=0.0, scale=1.0),
               wait=[vx.last()], inc=[(ev, 1)], tag="kms")
        for side in range(2):
            self.v(lambda v, sd=side: v.tensor_tensor(
                SW, S["Q_store"][0:64, sd * SEQ:(sd + 1) * SEQ],
                S["Q_store"][0:64, sd * SEQ:(sd + 1) * SEQ], ALU.mult),
                wait=[ev.last(), vx.last()], tag="QQ")
            self.vdrain(inc=[(vx, 1)])
            for seg in range(4):
                self.t(lambda T, sg=seg: T.matmul(
                    pg[0:1, :], S["ones_col"][0:64, :],
                    SW[:, sg * 512:(sg + 1) * 512], start=True, stop=True),
                    wait=[vx.last(), fg.last()], inc=[(mm, 1)], tag="qn2")
                self.s(lambda sc: sc.activation(S["inv_t"], pg[0:1, :], AF.Sqrt,
                                                bias=0.0, scale=1.0),
                       wait=[mm.last()], inc=[(ev, 1)], tag="qns")
                self.v(lambda v, sd=side, sg=seg: v.tensor_scalar(
                    S["Q_store"][64:65, sd * SEQ + sg * 512: sd * SEQ + (sg + 1) * 512],
                    S["inv_t"], S["kmax"][0:1, sd:sd + 1], -1.0, ALU.mult, ALU.mult),
                    wait=[ev.last()], inc=[(vx, 1), (fg, 1)], tag="qrow")
                self.vdrain()
        prev = None
        ix = 0
        for side in range(2):
            h = 2 * hp + side
            for qg in range(4):
                cur = self.attn_core(h, qg, side, causal=False, tag=f"h{h}g{qg}", pv_ix=ix)
                if prev is not None:
                    self.attn_tail(prev)
                prev = cur
                ix += 1
        self.attn_tail(prev)

    def attn_core(self, h, qg, side, causal, tag, pv_ix):
        S, c = self.S, self.c
        mm, ev, vx = c["mm"], c["ev"], c["vx"]
        rot = [self.bank("A"), self.bank("B"), self.bank("E"), self.bank("F")]
        pc, fc = self.bank("C") if pv_ix % 2 == 0 else self.bank("D")
        kt_hi = NT if not causal else min(NT, (qg + 1) * 4)
        ch_ = h // 2
        ev_cnt = {}
        vx_cnt = {}
        for kt0 in range(0, kt_hi, 8):
            kts = range(kt0, min(kt0 + 8, kt_hi))
            for kt in kts:
                ph, fh = rot[kt % 4]
                qlo = 0 if not causal else max(0, kt * 128 - qg * 512)
                if not causal:
                    lhsT = S["K_store"][0:65, side * SEQ + kt * 128: side * SEQ + (kt + 1) * 128]
                    rhs = S["Q_store"][0:65, side * SEQ + qg * 512: side * SEQ + (qg + 1) * 512]
                else:
                    lhsT = S["k_fm"][side * 64:side * 64 + 64,
                                     ch_ * SEQ + kt * 128: ch_ * SEQ + (kt + 1) * 128]
                    rhs = S["q_fm"][side * 64:side * 64 + 64,
                                    ch_ * SEQ + qg * 512 + qlo: ch_ * SEQ + (qg + 1) * 512]
                self.t(lambda T, l_=lhsT, r_=rhs, ps_=ph, q_=qlo: T.matmul(
                    ps_[:, q_:512], l_, r_, start=True, stop=True),
                    wait=[fh.last()], inc=[(mm, 1)], tag=f"{tag}s{kt}")
                sl = (kt % 8) * 512
                if not causal:
                    self.s(lambda sc, ps_=ph, sl_=sl: sc.activation(
                        S["P_buf"][:, sl_:sl_ + 512], ps_[:, :], AF.Exp,
                        bias=0.0, scale=0.125),
                        wait=[mm.last()], inc=[(ev, 1), (fh, 1)], tag=f"{tag}x{kt}")
                else:
                    self.s(lambda sc, ps_=ph, q_=qlo: sc.activation(
                        S["tanhb"][:, q_:512], ps_[:, q_:512], AF.Tanh,
                        bias=0.0, scale=1.0 / (8.0 * 50.0)), wait=[mm.last()], tag=f"{tag}t{kt}")
                    self.sdrain()
                    self.s(lambda sc, sl_=sl, q_=qlo: sc.activation(
                        S["P_buf"][:, sl_ + q_:sl_ + 512], S["tanhb"][:, q_:512],
                        AF.Exp, bias=0.0, scale=50.0),
                        inc=[(ev, 1), (fh, 1)], tag=f"{tag}x{kt}")
                ev_cnt[kt] = ev.promised
                if causal:
                    if qlo > 0:
                        self.v(lambda v, sl_=sl, q_=qlo: v.memset(
                            S["P_buf"][:, sl_:sl_ + q_], 0.0),
                            wait=[(ev, ev_cnt[kt])], tag=f"{tag}z{kt}")
                        self.vdrain(inc=[(vx, 1)])
                    if qg * 512 <= kt * 128 < (qg + 1) * 512:
                        off = sl + (kt * 128 - qg * 512)
                        self.v(lambda v, o_=off: v.tensor_tensor(
                            S["P_buf"][:, o_:o_ + 128], S["P_buf"][:, o_:o_ + 128],
                            S["tri_bf"], ALU.mult), wait=[(ev, ev_cnt[kt])], tag=f"{tag}d{kt}")
                        self.vdrain(inc=[(vx, 1)])
                vx_cnt[kt] = vx.promised
            for kt in kts:
                self.t(lambda T, kt_=kt, pc_=pc: T.matmul(
                    pc_[0:65, :],
                    S["V"][:, kt_ * NH * 65 + h * 65: kt_ * NH * 65 + h * 65 + 65],
                    S["P_buf"][:, (kt_ % 8) * 512:(kt_ % 8) * 512 + 512],
                    start=(kt_ == 0), stop=(kt_ == kt_hi - 1)),
                    wait=[(ev, ev_cnt[kt]), (vx, vx_cnt[kt])] + ([fc.last()] if kt == 0 else []),
                    inc=[(mm, 1)] if kt == kt_hi - 1 else (), tag=f"{tag}pv{kt}")
        return (h, qg, side, pc, fc, tag, mm.promised)

    def attn_tail(self, ctx):
        S, c = self.S, self.c
        mm, ev, vx = c["mm"], c["ev"], c["vx"]
        h, qg, side, pc, fc, tag, mm_pv = ctx
        pu, fu = self.bank("U")
        ch_ = h // 2
        self.v(lambda v: v.reciprocal(S["inv_t"], pc[64:65, :]),
               wait=[(mm, mm_pv)], tag=f"{tag}rc")
        self.vdrain(inc=[(vx, 1)])
        vx_rc = vx.promised
        self.t(lambda T: T.matmul(pu[0:64, :], S["ones1f"], S["inv_t"],
                                  start=True, stop=True),
               wait=[(vx, vx_rc), fu.last()], inc=[(mm, 1)], tag=f"{tag}r1")
        mm_r1 = mm.promised
        self.s(lambda sc: sc.copy(out=S["att_t"], in_=pc[0:64, :]),
               wait=[(mm, mm_pv), (vx, vx_rc)], inc=[(ev, 1), (fc, 1)], tag=f"{tag}ae")
        self.v(lambda v: v.tensor_tensor(
            S["attn_fm"][side * 64:side * 64 + 64,
                         ch_ * SEQ + qg * 512:(ch_) * SEQ + (qg + 1) * 512],
            S["att_t"], pu[0:64, :], ALU.mult),
            wait=[ev.last(), (mm, mm_r1)], inc=[(vx, 1), (fu, 1)], tag=f"{tag}nm")
        self.vdrain()

    def wo_ln(self, res_dram, wT, bias_dram, scale, tag, spill_dram=None):
        S, D, c = self.S, self.D, self.c
        mm, ev, vx = c["mm"], c["ev"], c["vx"]
        if bias_dram is not None:
            self.p.dma("sync", S["stage"][0:1, 0, :], bias_dram[:, :],
                       wait=[c["cA"].last(), c["vx"].last()], inc=[(c["ld"], 16)], tag=f"{tag}b")
            self.v(lambda v: v.tensor_copy(out=S["row_bf"], in_=S["stage"][0:1, 0, :]),
                   wait=[c["ld"].last()], tag=f"{tag}bc")
            self.vdrain(inc=[(vx, 1)])
        self.stream_w(S["w_str"], [wT[i * 128:(i + 1) * 128, :] for i in range(NCH)],
                      extra_wait=[vx.last()], tag=f"{tag}w")
        pa, fa = self.bank("A"); pb, fb = self.bank("B")
        pc, fc = self.bank("C"); pd, fd = self.bank("D")
        for t in range(NT):
            p0, f0, p1, f1 = (pa, fa, pb, fb) if t % 2 == 0 else (pc, fc, pd, fd)
            res_ap, res_w = self.hbm_tile(res_dram, t, t % 2)
            for half, (ph, fh) in enumerate(((p0, f0), (p1, f1))):
                last_h = (half == 1)
                for cc in range(NCH):
                    self.t(lambda T, t_=t, cc_=cc, h_=half, ps_=ph: T.matmul(
                        ps_[:, :],
                        S["attn_fm"][:, cc_ * SEQ + t_ * 128: cc_ * SEQ + (t_ + 1) * 128],
                        S["w_str"][:, cc_ * DM + h_ * 512: cc_ * DM + h_ * 512 + 512],
                        start=(cc_ == 0), stop=(bias_dram is None and cc_ == NCH - 1)),
                        wait=[c["cA"].last(), c["vx"].last(), fh.last()] if cc == 0 else (),
                        inc=[(mm, 1)] if (bias_dram is None and cc == NCH - 1 and last_h) else (),
                        tag=f"{tag}m{t}")
                if bias_dram is not None:
                    self.t(lambda T, h_=half, ps_=ph: T.matmul(
                        ps_[:, :], S["ones_bf"][0:1, :],
                        S["row_bf"][0:1, h_ * 512:(h_ + 1) * 512], start=False, stop=True),
                        inc=[(mm, 1)] if last_h else (), tag=f"{tag}bb{t}")
            rt = S["r"][:, t * DM:(t + 1) * DM]
            for half, (ph, fh) in enumerate(((p0, f0), (p1, f1))):
                self.v(lambda v, ps_=ph, h_=half, rt_=rt, ra=res_ap: v.scalar_tensor_tensor(
                    rt_[:, h_ * 512:(h_ + 1) * 512], ps_[:, :], scale,
                    ra[:, h_ * 512:(h_ + 1) * 512], ALU.mult, ALU.add),
                    wait=[mm.last()] + res_w if half == 0 else (),
                    inc=[(fh, 1)], tag=f"{tag}ev{t}")
            self.vdrain(inc=[(vx, 1)])
            self.ln_tile(rt, wait=[vx.last()])
            if spill_dram is not None:
                self.p.dma("sync", spill_dram[t * 128:(t + 1) * 128, :], rt,
                           wait=[c["ev"].last()], inc=[(c["st"], 16)], tag=f"{tag}sp{t}")

    def build_ffn_cache(self):
        S, D, c = self.S, self.D, self.c
        gx = c["gx"]
        jobs = []
        for hc in range(NHC):
            jobs.append((D["wgT"][:, hc * 128:(hc + 1) * 128].rearrange(
                "(cc p) j -> p cc j", p=128), D["wg_c"][hc], True))
            jobs.append((D["wuT"][:, hc * 128:(hc + 1) * 128].rearrange(
                "(cc p) j -> p cc j", p=128), D["wu_c"][hc], True))
            jobs.append((D["wdT"][hc * 128:(hc + 1) * 128, :], D["wd_c"][hc], False))
        cast2 = [S["cast_t"], S["cast_t2"]]
        for i, (src_ap, dst_ap, slow) in enumerate(jobs):
            w = c["wB0"] if i % 2 == 0 else c["wB1"]
            gdp = c["gd0"] if i % 2 == 0 else c["gd1"]
            sl = S["stage_i"][:, i % 2]
            if len(src_ap.shape) == 3:
                sl = sl.rearrange("p (cc j) -> p cc j", j=128)
            self.p.dma("sync", sl, src_ap, inc=[(w, 16)],
                       wait=[w.last(), (gx, max(0, gx.promised - 1))],
                       allow_slow_non_contiguous=slow, tag=f"fcl{i}")
            self.g(lambda g_, sl_=sl, i_=i: g_.tensor_copy(out=cast2[i_ % 2], in_=sl_),
                   wait=[w.last(), gdp.last()],
                   inc=[(gx, 1)], tag=f"fcc{i}")
            self.p.dma("gpsimd", dst_ap, cast2[i % 2],
                       wait=[gx.last()], inc=[(gdp, 16)], tag=f"fcs{i}")
        self.g(lambda g_, v=self.c["gd0"].promised: g_.wait_ge(self.c["gd0"].sem, v), tag="fcw0")
        self.g(lambda g_, v=self.c["gd1"].promised: g_.wait_ge(self.c["gd1"].sem, v),
               inc=[(gx, 1)], tag="fcw1")
        self.barrier()

    def build_moe_cache(self):
        S, D, c = self.S, self.D, self.c
        gx = c["gx"]
        jobs = []
        for e in range(NE):
            for cc in range(NCH):
                for sub in range(4):
                    jobs.append((D["w1"][e][cc * 128:(cc + 1) * 128, sub * 1024:(sub + 1) * 1024],
                                 D["w1_c"][e, cc][:, sub * 1024:(sub + 1) * 1024]))
            for hc in range(EH // 128):
                jobs.append((D["w2"][e][hc * 128:(hc + 1) * 128, :], D["w2_c"][e, hc]))
        cast2 = [S["cast_t"], S["cast_t2"]]
        cast_engs = ["gpsimd", "vector", "scalar"]
        cast_ch = [c["gx"], c["vx"], c["ev"]]
        base_cnt = [ch.promised for ch in cast_ch]
        for i, (src_ap, dst_ap) in enumerate(jobs):
            w = c["wB0"] if i % 2 == 0 else c["wB1"]
            gdp = c["gd0"] if i % 2 == 0 else c["gd1"]
            sl = S["stage"][:, i % 2]
            ce = i % 3
            self.p.dma("sync", sl, src_ap, inc=[(w, 16)],
                       wait=[w.last(), cast_ch[(i + 1) % 3].last()], tag=f"mc_l{i}")
            if cast_engs[ce] == "scalar":
                fn = (lambda en, sl_=sl, i_=i: en.copy(out=cast2[i_ % 2], in_=sl_))
            else:
                fn = (lambda en, sl_=sl, i_=i: en.tensor_copy(out=cast2[i_ % 2], in_=sl_))
            self.p.emit(cast_engs[ce], fn,
                wait=[w.last(), gdp.last()], inc=[(cast_ch[ce], 1)], tag=f"mc_c{i}")
            self.p.dma("gpsimd", dst_ap, cast2[i % 2],
                       wait=[cast_ch[ce].last()], inc=[(gdp, 16)], tag=f"mc_s{i}")
        self.g(lambda g_, v=self.c["gd0"].promised: g_.wait_ge(self.c["gd0"].sem, v),
               tag="mcw0")
        self.g(lambda g_, v=self.c["gd1"].promised: g_.wait_ge(self.c["gd1"].sem, v),
               inc=[(gx, 1)], tag="mcw1")
        self.barrier()

    def l0_ffn(self, s):
        S, D, c = self.S, self.D, self.c
        mm, ev, vx = c["mm"], c["ev"], c["vx"]
        cB = c["cB"]

        def src_tile(t):
            return S["r"][:, t * DM:(t + 1) * DM], []
        self.build_xT(src_tile, tag=f"s{s}x1")
        self.barrier()

        pa, fa = self.bank("A"); pb, fb = self.bank("B")
        pc, fc = self.bank("C"); pd, fd = self.bank("D")
        for tg in range(4):
            gu_mm = []
            for hc in range(NHC):
                par = hc % 2
                wB = c["wB0"] if par == 0 else c["wB1"]
                for name, buf in (("wg_c", "wg_hc"), ("wu_c", "wu_hc")):
                    self.p.dma("sync", S[buf][:, par], D[name][hc], inc=[(wB, 16)],
                               wait=[wB.last()] + ([(mm, gu_mm[hc - 2])] if hc >= 2 else []),
                               tag=f"ff{name}{hc}")
                w_rdy = wB.last()
                ph, fh = (pa, fa) if par == 0 else (pb, fb)
                pu, fu = (pc, fc) if par == 0 else (pd, fd)
                for cc in range(NCH):
                    self.t(lambda T, cc_=cc, ps_=ph, par_=par, tg_=tg: T.matmul(
                        ps_[:, :], S["wg_hc"][:, par_, cc_ * 128:(cc_ + 1) * 128],
                        S["x_T"][:, cc_ * SEQ + tg_ * 512: cc_ * SEQ + (tg_ + 1) * 512],
                        start=(cc_ == 0), stop=(cc_ == NCH - 1)),
                        wait=[w_rdy, fh.last()] if cc == 0 else (), tag=f"ffg{hc}")
                for cc in range(NCH):
                    self.t(lambda T, cc_=cc, ps_=pu, par_=par, tg_=tg: T.matmul(
                        ps_[:, :], S["wu_hc"][:, par_, cc_ * 128:(cc_ + 1) * 128],
                        S["x_T"][:, cc_ * SEQ + tg_ * 512: cc_ * SEQ + (tg_ + 1) * 512],
                        start=(cc_ == 0), stop=(cc_ == NCH - 1)),
                        wait=[fu.last()] if cc == 0 else (),
                        inc=[(mm, 1)] if cc == NCH - 1 else (), tag=f"ffu{hc}")
                gu_mm.append(mm.promised)
                self.s(lambda sc, ps_=ph: sc.activation(S["gate_t"], ps_[:, :], AF.Silu,
                                                        bias=0.0, scale=QSC),
                       wait=[mm.last(), vx.last()], inc=[(fh, 1)], tag=f"ffsl{hc}")
                self.s(lambda sc, ps_=pu: sc.activation(S["up_t"], ps_[:, :], AF.Copy,
                                                        bias=0.0, scale=QSC),
                       inc=[(ev, 1), (fu, 1)], tag=f"ffup{hc}")
                self.v(lambda v, hc_=hc: v.tensor_tensor(
                    S["h_grp"][:, hc_ * 512:(hc_ + 1) * 512], S["gate_t"], S["up_t"],
                    ALU.mult), wait=[ev.last()], inc=[(vx, 1)], tag=f"ffh{hc}")
            self.vdrain(inc=[(vx, 1)])
            banks = [self.bank("A"), self.bank("B"), self.bank("C"), self.bank("D")]
            wd_mm = self.__dict__.setdefault("_wd_mm_g", [])
            for half in range(2):
                for hc in range(NHC):
                    k = len(wd_mm)
                    par = k % 4
                    wB = [c["wB0"], c["wB1"], c["wA0"], c["wA1"]][par]
                    self.p.dma("sync", S["wd_hc"][:, par],
                               D["wd_c"][hc][:, half * 512:(half + 1) * 512],
                               inc=[(wB, 16)],
                               wait=[wB.last()] + ([(mm, wd_mm[k - 4])] if k >= 4 else []),
                               tag=f"ffwd{hc}")
                    w_rdy = wB.last()
                    for tt in range(4):
                        ph, fh = banks[tt]
                        self.t(lambda T, hc_=hc, tt_=tt, ps_=ph, par_=par: T.matmul(
                            ps_[:, :], S["h_grp"][:, hc_ * 512 + tt_ * 128: hc_ * 512 + (tt_ + 1) * 128],
                            S["wd_hc"][:, par_, :],
                            start=(hc_ == 0), stop=(hc_ == NHC - 1)),
                            wait=[w_rdy, vx.last()] + ([fh.last()] if hc == 0 else []),
                            inc=[(mm, 1)] if tt == 3 else (), tag=f"ffd{half}{hc}{tt}")
                    wd_mm.append(mm.promised)
                for tt in range(4):
                    t_ = tg * 4 + tt
                    ph, fh = banks[tt]
                    rt = S["r"][:, t_ * DM:(t_ + 1) * DM]
                    self.v(lambda v, ps_=ph, rt_=rt, h_=half: v.scalar_tensor_tensor(
                        rt_[:, h_ * 512:(h_ + 1) * 512], ps_[:, :], QSC,
                        rt_[:, h_ * 512:(h_ + 1) * 512], ALU.mult, ALU.add),
                        wait=[mm.last()], inc=[(fh, 1)], tag=f"ffde{half}{tt}")
                self.vdrain(inc=[(vx, 1)])
            for tt in range(4):
                t_ = tg * 4 + tt
                rt = S["r"][:, t_ * DM:(t_ + 1) * DM]
                tok = self.ln_tile(rt, wait=[vx.last()])
                self.p.dma("sync", D["r_spill"][t_ * 128:(t_ + 1) * 128, :], rt,
                           wait=[tok], inc=[(c["st"], 16)], tag=f"ffsp{t_}")
        self.barrier()

    def l1_attn(self, s):
        S, D, c = self.S, self.D, self.c
        mm, ev, vx = c["mm"], c["ev"], c["vx"]

        def src_tile(t):
            return S["r"][:, t * DM:(t + 1) * DM], []
        self.build_xT(src_tile, tag=f"s{s}x2")
        self.barrier()
        self.p.emit("sync", lambda e, v=c["st"].promised: e.wait_ge(c["st"].sem, v), tag="spw")
        self.barrier()

        self.stream_w(S["w_str"], [D["W2vT"][i * 128:(i + 1) * 128, :] for i in range(NCH)],
                      tag=f"s{s}w2v")
        self.v(lambda v: v.memset(
            S["V"].rearrange("p (t h u) -> p t h u", t=NT, h=NH)[:, :, :, 64:65], 1.0),
            tag="vone1")
        self.vdrain(inc=[(vx, 1)])
        self.proj_tokmajor(dst_V=True, wq=None, bias_bf=None, tag=f"s{s}v2")
        self.barrier()
        for wname, dst in (("W2qT", "q_fm"), ("W2kT", "k_fm")):
            self.stream_w(S["w_str"], [D[wname][i * 128:(i + 1) * 128, :] for i in range(NCH)],
                          tag=f"s{s}{wname}")
            self.qk_rms_proj(dst, tag=f"s{s}{wname}p")
            self.barrier()
        prev = None
        ix = 0
        for h in range(NH):
            for qg in range(4):
                cur = self.attn_core(h, qg, h % 2, causal=True, tag=f"L1h{h}g{qg}", pv_ix=ix)
                if prev is not None:
                    self.attn_tail(prev)
                prev = cur
                ix += 1
        self.attn_tail(prev)
        self.barrier()
        self.wo_ln(res_dram=D["r_spill"], wT=D["W2oT"], bias_dram=None,
                   scale=1.0, tag=f"s{s}wo1")
        self.barrier()

    def qk_rms_proj(self, dst_name, tag):
        S, c = self.S, self.c
        mm, ev, vx = c["mm"], c["ev"], c["vx"]
        pa, fa = self.bank("A"); pb, fb = self.bank("B")
        psT, fT = self.bank("T")
        dst = S[dst_name]
        for t in range(NT):
            for half, (ph, fh) in enumerate(((pa, fa), (pb, fb))):
                for cc in range(NCH):
                    self.t(lambda T, t_=t, cc_=cc, h_=half, ps_=ph: T.matmul(
                        ps_[:, :],
                        S["x_T"][:, cc_ * SEQ + t_ * 128: cc_ * SEQ + (t_ + 1) * 128],
                        S["w_str"][:, cc_ * DM + h_ * 512: cc_ * DM + h_ * 512 + 512],
                        start=(cc_ == 0), stop=(cc_ == NCH - 1)),
                        wait=[c["cA"].last(), vx.last(), fh.last()] if cc == 0 else (),
                        inc=[(mm, 1)] if (cc == NCH - 1 and half == 1) else (),
                        tag=f"{tag}m{t}")
            self.s(lambda sc: sc.copy(out=S["cast_t"][:, 0:512], in_=pa[:, :]),
                   wait=[mm.last()], inc=[(fa, 1)], tag=f"{tag}e0{t}")
            self.s(lambda sc: sc.copy(out=S["cast_t"][:, 512:1024], in_=pb[:, :]),
                   inc=[(ev, 1), (fb, 1)], tag=f"{tag}e1{t}")
            sq = S["stage"][:, 0]
            self.v(lambda v: v.tensor_tensor(sq, S["cast_t"], S["cast_t"], ALU.mult),
                   wait=[ev.last()], tag=f"{tag}sq{t}")
            self.vdrain()
            self.v(lambda v: v.tensor_reduce(
                S["rms16"], sq.rearrange("p (h u) -> p h u", u=64), AX.X, ALU.add),
                tag=f"{tag}sr{t}")
            self.vdrain(inc=[(vx, 1)])
            self.s(lambda sc: sc.activation(S["rms16b"], S["rms16"], AF.Sqrt,
                                            bias=S["eps_rms"], scale=1.0 / 64.0),
                   wait=[vx.last()], inc=[(ev, 1)], tag=f"{tag}sd{t}")
            self.v(lambda v: v.reciprocal(S["rms16"], S["rms16b"]),
                   wait=[ev.last()], tag=f"{tag}rc{t}")
            self.vdrain()
            self.v(lambda v: v.tensor_tensor(
                S["cast_t"].rearrange("p (h u) -> p h u", u=64),
                S["cast_t"].rearrange("p (h u) -> p h u", u=64),
                S["rms16"].rearrange("p (h o) -> p h o", o=1).broadcast_to([128, 16, 64]),
                ALU.mult), tag=f"{tag}ap{t}")
            self.vdrain(inc=[(vx, 1)])
            for cc in range(NCH):
                self.t(lambda T, cc_=cc: T.transpose(
                    psT[:, cc_ * 128:(cc_ + 1) * 128],
                    S["cast_t"][:, cc_ * 128:(cc_ + 1) * 128], S["ident_bf"]),
                    wait=[vx.last(), fT.last()] if cc == 0 else (),
                    inc=[(mm, 1)] if cc == NCH - 1 else (), tag=f"{tag}tr{t}")
            self.s(lambda sc, t_=t: sc.copy(
                out=dst.rearrange("p (c q) -> p c q", c=NCH)[:, :, t_ * 128:(t_ + 1) * 128],
                in_=psT.rearrange("p (c q) -> p c q", c=NCH)),
                wait=[mm.last()], inc=[(ev, 1), (fT, 1)], tag=f"{tag}te{t}")

    def routing(self, s):
        S, D, c = self.S, self.D, self.c
        mm, ev, vx = c["mm"], c["ev"], c["vx"]
        for t in range(NT):
            self.v(lambda v, t_=t: v.tensor_copy(
                out=S["cast_t"], in_=S["r"][:, t_ * DM:(t_ + 1) * DM]),
                wait=[(c["st"], max(0, c["st"].promised))], inc=[(vx, 1)], tag=f"xm{t}")
            self.p.dma("sync", D["xmoe"][s][t * 128:(t + 1) * 128, :], S["cast_t"],
                       wait=[vx.last()], inc=[(c["st"], 16)], tag=f"xms{t}")
        self.p.emit("sync", lambda e, v=c["st"].promised: e.wait_ge(c["st"].sem, v),
                    tag="xmw")
        self.barrier()

        def src_tile(t):
            return S["r"][:, t * DM:(t + 1) * DM], []
        self.build_xT(src_tile, tag=f"s{s}x3")
        self.barrier()

        self.v(lambda v: v.memset(S["bigc"], OOB_BIG), wait=[c["ev"].last()], tag="bigc")
        self.vdrain(inc=[(c["vx"], 1)])
        self.p.dma("sync", S["iota512"], D["iota512"][:, :], inc=[(c["ld"], 16)], tag="iota")
        self.p.emit("sync", lambda e, v=c["ld"].promised: e.wait_ge(c["ld"].sem, v), tag="iotaw")
        self.barrier()
        src = D["gateW"].rearrange("(cc p) e -> p cc e", p=128)
        self.p.dma("sync", S["stage"][:, 0, 0:NCH * 8].rearrange("p (cc e) -> p cc e", e=NE),
                   src, inc=[(c["ld"], 16)],
                   allow_slow_non_contiguous=True, tag="gw")
        self.v(lambda v: v.tensor_copy(out=S["gw_bf"], in_=S["stage"][:, 0, 0:NCH * 8]),
               wait=[c["ld"].last()], inc=[(vx, 1)], tag="gwc")
        pe, fe = self.bank("E")
        for qg in range(4):
            for cc in range(NCH):
                self.t(lambda T, cc_=cc, qg_=qg: T.matmul(
                    pe[0:8, :], S["gw_bf"][:, cc_ * 8:(cc_ + 1) * 8],
                    S["x_T"][:, cc_ * SEQ + qg_ * 512: cc_ * SEQ + (qg_ + 1) * 512],
                    start=(cc_ == 0), stop=(cc_ == NCH - 1)),
                    wait=[vx.last(), fe.last()] if cc == 0 else (),
                    inc=[(mm, 1)] if cc == NCH - 1 else (), tag=f"gs{qg}")
            self.s(lambda sc, qg_=qg: sc.copy(
                out=S["probs_et"][:, qg_ * 512:(qg_ + 1) * 512], in_=pe[0:8, :]),
                wait=[mm.last()], inc=[(ev, 1), (fe, 1)], tag=f"gse{qg}")
        pf, ff = self.bank("F")
        for t in range(NT):
            self.t(lambda T, t_=t: T.transpose(
                pf[:, 0:8], S["probs_et"][:, t_ * 128:(t_ + 1) * 128],
                S["ident_f32"][0:8, 0:8]),
                wait=[ev.last(), ff.last()], inc=[(mm, 1)], tag=f"smT{t}")
            self.v(lambda v: v.reduce_max(S["rms16"][:, 0:1], pf[:, 0:8], AX.X, negate=True),
                   wait=[mm.last()], tag=f"smm{t}")
            self.vdrain(inc=[(vx, 1)])
            self.s(lambda sc: sc.activation(S["prb8"], pf[:, 0:8], AF.Exp,
                                            bias=S["rms16"][:, 0:1], scale=1.0,
                                            accum_out=S["rms16"][:, 1:2]),
                   wait=[vx.last()], inc=[(ev, 1), (ff, 1)], tag=f"smx{t}")
            self.v(lambda v: v.reciprocal(S["rms16"][:, 2:3], S["rms16"][:, 1:2]),
                   wait=[ev.last()], tag=f"smr{t}")
            self.vdrain()
            self.v(lambda v: v.tensor_scalar(S["prb8"], S["prb8"], S["rms16"][:, 2:3],
                                             None, ALU.mult), tag=f"smn{t}")
            self.vdrain()
            self.v(lambda v, t_=t: v.tensor_copy(
                out=S["probsT"][:, s, t_ * 10 + 2: t_ * 10 + 10], in_=S["prb8"]), tag=f"pT{t}")
            self.v(lambda v, t_=t: v.tensor_copy(
                out=S["probsT"][:, s, t_ * 10: t_ * 10 + 1], in_=S["pcol"]), tag=f"pTc{t}")
            self.v(lambda v, t_=t: v.memset(
                S["probsT"][:, s, t_ * 10 + 1: t_ * 10 + 2], float(t_)), tag=f"pTi{t}")
            self.vdrain(inc=[(vx, 1)])
            self.t(lambda T, t_=t: T.transpose(
                pf[0:8, 128:256], S["prb8"], S["ident_f32"]),
                wait=[vx.last(), ff.last()], inc=[(mm, 1)], tag=f"pb{t}")
            self.s(lambda sc, t_=t: sc.copy(
                out=S["probs_et"][:, t_ * 128:(t_ + 1) * 128], in_=pf[0:8, 128:256]),
                wait=[mm.last()], inc=[(ev, 1), (ff, 1)], tag=f"pbe{t}")
        self.barrier()
        sr = S["srch"]
        self.v(lambda v: v.memset(sr[:, 0:1], 0.0), tag="bl0")
        self.v(lambda v: v.memset(sr[:, 1:2], 1.0), tag="bh0")
        self.vdrain()
        for it in range(N_BISECT):
            self.v(lambda v: v.tensor_add(sr[:, 2:3], sr[:, 0:1], sr[:, 1:2]), tag=f"bm{it}")
            self.vdrain()
            self.v(lambda v: v.tensor_scalar(sr[:, 2:3], sr[:, 2:3], 0.5, None, ALU.mult),
                   tag=f"bm2{it}")
            self.vdrain()
            self.v(lambda v: v.tensor_scalar(S["mask_et"], S["probs_et"], sr[:, 2:3],
                                             None, ALU.is_gt, op1=ALU.add,
                                             accum_out=sr[:, 3:4]), tag=f"bc{it}")
            self.vdrain()
            self.v(lambda v: v.tensor_scalar(S["srch_i"][:, 0:1], sr[:, 3:4], 512.0,
                                             None, ALU.is_ge), tag=f"bg{it}")
            self.v(lambda v: v.tensor_scalar(S["srch_i"][:, 1:2], sr[:, 3:4], 512.0,
                                             None, ALU.is_lt), tag=f"bg2{it}")
            self.vdrain()
            self.v(lambda v: v.select(sr[:, 0:1], S["srch_i"][:, 0:1], sr[:, 2:3],
                                      sr[:, 0:1], add_drain=True), tag=f"bsl{it}")
            self.v(lambda v: v.select(sr[:, 1:2], S["srch_i"][:, 1:2], sr[:, 2:3],
                                      sr[:, 1:2], add_drain=True), tag=f"bsh{it}")
            self.vdrain()
        self.v(lambda v: v.tensor_scalar(S["mask_et"], S["probs_et"], sr[:, 0:1],
                                         None, ALU.is_gt), tag="bmf")
        self.vdrain(inc=[(vx, 1)])
        psT, fT = self.bank("T")
        for t in range(NT):
            self.t(lambda T, t_=t: T.transpose(
                psT[:, t_ * 8:(t_ + 1) * 8], S["mask_et"][:, t_ * 128:(t_ + 1) * 128],
                S["ident_bf"][0:8, 0:8]),
                wait=[vx.last(), fT.last()] if t == 0 else (),
                inc=[(mm, 1)] if t == NT - 1 else (), tag=f"mT{t}")
        self.s(lambda sc: sc.copy(out=S["maskT_sb"], in_=psT[:, 0:128]),
               wait=[mm.last()], inc=[(ev, 1), (fT, 1)], tag="mTe")
        pu, fu = self.bank("U")
        for j in range(NT):
            for i in range(j + 1):
                lhs = S["tri_bf"] if i == j else S["ones_bf"]
                self.t(lambda T, i_=i, j_=j, l_=lhs: T.matmul(
                    pu[:, 0:8], l_, S["maskT_sb"][:, i_ * 8:(i_ + 1) * 8],
                    start=(i_ == 0), stop=(i_ == j_)),
                    wait=[ev.last(), fu.last()] if i == 0 else (),
                    inc=[(mm, 1)] if i == j else (), tag=f"cs{j}")
            self.v(lambda v: v.tensor_scalar(S["idx_t"], pu[:, 0:8], 512.0, None, ALU.is_le),
                   wait=[mm.last()], tag=f"ix1{j}")
            self.vdrain()
            self.v(lambda v, j_=j: v.tensor_tensor(
                S["idx_m"], S["idx_t"], S["maskT_sb"][:, j_ * 8:(j_ + 1) * 8], ALU.mult),
                tag=f"ix2{j}")
            self.v(lambda v: v.tensor_scalar(S["idx_t2"], pu[:, 0:8], 1.0, None, ALU.subtract),
                   inc=[(fu, 1)], tag=f"ix3{j}")
            self.vdrain()
            self.v(lambda v, j_=j: v.select(
                S["scat_all"][:, j_ * 8:(j_ + 1) * 8], S["idx_m"], S["idx_t2"], S["bigc"],
                add_drain=True), tag=f"ix4{j}")
            self.vdrain(inc=[(vx, 1)])
        for e in range(NE):
            for j in range(NT):
                Sb = S["cast_t"][:, (j % 2) * 512:(j % 2) * 512 + 512]
                self.v(lambda v, Sb_=Sb, j_=j, e_=e: v.tensor_scalar(
                    Sb_, S["iota512"], S["scat_all"][:, j_ * 8 + e_: j_ * 8 + e_ + 1],
                    None, ALU.is_equal),
                    wait=[vx.last()] if j == 0 else [mm.last()], tag=f"oh{e}{j}")
                self.vdrain(inc=[(vx, 1)])
                self.t(lambda T, Sb_=Sb, j_=j, e_=e, s_=s: T.matmul(
                    pu[0:10, :], S["probsT"][:, s_, j_ * 10:(j_ + 1) * 10],
                    Sb_, start=(j_ == 0), stop=(j_ == NT - 1)),
                    wait=[vx.last()] + ([fu.last()] if j == 0 else []),
                    inc=[(mm, 1)], tag=f"ohm{e}{j}")
            self.s(lambda sc: sc.copy(out=S["sc_rows"], in_=pu[0:10, :]),
                   wait=[mm.last(), (c["st"], c["st"].promised)],
                   inc=[(ev, 1), (fu, 1)], tag=f"ohe{e}")
            self.p.dma("sync", D["rows"][s, e, 0], S["sc_rows"][0:1, :],
                       wait=[ev.last()], inc=[(c["st"], 16)], tag=f"rw0{e}")
            self.p.dma("sync", D["rows"][s, e, 1], S["sc_rows"][1:2, :],
                       inc=[(c["st"], 16)], tag=f"rw1{e}")
            self.p.dma("sync", D["rows"][s, e, 2], S["sc_rows"][2 + e:3 + e, :],
                       inc=[(c["st"], 16)], tag=f"rwp{e}")
        self.barrier()
        self.p.emit("sync", lambda e, v=c["st"].promised: e.wait_ge(c["st"].sem, v), tag="rw")
        self.barrier()

    def moe(self):
        S, D, c = self.S, self.D, self.c
        mm, ev, vx, gx, gd = c["mm"], c["ev"], c["vx"], c["gx"], c["gdma"]
        self.v(lambda v: v.memset(S["stage"][:, 0], 0.0), inc=[(vx, 1)], tag="yz")
        for s in range(2):
            for t in range(NT):
                self.p.dma("sync", D["ymoe"][s][t * 128:(t + 1) * 128, :], S["stage"][:, 0],
                           wait=[vx.last()], inc=[(c["st"], 16)], tag=f"yz{s}{t}")
        banks4 = [self.bank("A"), self.bank("B"), self.bank("C"), self.bank("D")]
        psT, fT = self.bank("T")
        cA = c["cA"]
        self.p.emit("sync", lambda x_, v=c["st"].promised: x_.wait_ge(c["st"].sem, v), tag="yzw")
        self.barrier()
        w1_mm, w2_mm = [], []
        ks = [(e, s) for e in range(NE) for s in range(2)]
        sc_gd = {}

        def prep(kk):
            e, s = ks[kk]
            par = kk % 2
            self.p.dma("sync", S["gidxf"],
                       D["rows"][s, e, 0].rearrange("(j p) -> p j", p=128),
                       inc=[(c["ld"], 16)], wait=[vx.last()],
                       allow_slow_non_contiguous=True, tag=f"gi{e}{s}")
            self.p.dma("sync", S["gidxg"],
                       D["rows"][s, e, 1].rearrange("(j p) -> p j", p=128),
                       inc=[(c["ld"], 16)],
                       allow_slow_non_contiguous=True, tag=f"gj{e}{s}")
            self.p.dma("sync", S["pcomp2"][:, par],
                       D["rows"][s, e, 2].rearrange("(j p) -> p j", p=128),
                       inc=[(c["ld"], 16)],
                       allow_slow_non_contiguous=True, tag=f"gp{e}{s}")
            self.v(lambda v: v.scalar_tensor_tensor(
                S["gidxf"], S["gidxg"], 128.0, S["gidxf"], ALU.mult, ALU.add),
                wait=[c["ld"].last()], tag=f"giv{e}{s}")
            self.vdrain()
            self.v(lambda v, par_=par: v.tensor_copy(out=S["gidx2"][:, par_],
                                                     in_=S["gidxf"]),
                   wait=[(gd, sc_gd[kk - 2])] if kk >= 2 else (),
                   inc=[(vx, 1)], tag=f"gic{e}{s}")
            self.vdrain()
            for j in range(4):
                self.g(lambda g_, j_=j, s_=s, par_=par: g_.indirect_dma_start(
                    S["xin_tok"][:, j_ * DM:(j_ + 1) * DM], None,
                    D["xmoe"][s_][:, :],
                    IndirectOffsetOnAxis(ap=S["gidx2"][:, par_, j_:j_ + 1], axis=0)),
                    wait=[vx.last(), mm.last()], inc=[(gd, 16)], tag=f"gx{e}{s}{j}",
                    raw_inc=True)

        prep(0)
        for kk, (e, s) in enumerate(ks):
            par = kk % 2
            for j in range(4):
                for cc in range(NCH):
                    self.t(lambda T, j_=j, cc_=cc: T.transpose(
                        psT[:, ((j_ * NCH + cc_) % 8) * 128: ((j_ * NCH + cc_) % 8) * 128 + 128],
                        S["xin_tok"][:, j_ * DM + cc_ * 128: j_ * DM + (cc_ + 1) * 128],
                        S["ident_bf"]),
                        wait=[gd.last(), fT.last()] if (j == 0 and cc == 0) else
                             ([fT.last()] if (j * NCH + cc) % 8 == 0 else ()),
                        inc=[(mm, 1)] if (j * NCH + cc) % 8 == 7 else (), tag=f"xt{e}{s}")
                    if (j * NCH + cc) % 8 == 7:
                        u0 = (j * NCH + cc) - 7
                        def ev_t(sc, u0_=u0):
                            outs = []
                            for u in range(8):
                                jj, ccx = divmod(u0_ + u, NCH)
                                outs.append(sc.copy(
                                    out=S["xin_T"][:, ccx * 512 + jj * 128: ccx * 512 + (jj + 1) * 128],
                                    in_=psT[:, u * 128:(u + 1) * 128]))
                            return outs[-1]
                        self.s(ev_t, wait=[mm.last()], inc=[(ev, 1), (fT, 1)], tag=f"xte{e}{s}", multi=True)
            if kk + 1 < len(ks):
                prep(kk + 1)
            for hs in range(8):
                k1 = len(w1_mm)
                wpar = hs % 2
                wAp = c["wA0"] if wpar == 0 else c["wA1"]
                self.p.dma("sync",
                           S["w1col"][:, wpar].rearrange("p (cc x) -> p cc x", x=512),
                           D["w1_c"][e][:, :, hs * 512:(hs + 1) * 512].rearrange(
                               "cc p x -> p cc x"),
                           inc=[(wAp, 16)],
                           wait=[wAp.last()] + ([(mm, w1_mm[k1 - 2])] if k1 >= 2 else []),
                           tag=f"w1l{e}{hs}")
                w_rdy = wAp.last()
                for st in range(4):
                    ph, fh = banks4[st]
                    for cc in range(NCH):
                        self.t(lambda T, st_=st, cc_=cc, par_=wpar, ps_=ph: T.matmul(
                            ps_[:, :], S["xin_T"][:, cc_ * 512 + st_ * 128: cc_ * 512 + (st_ + 1) * 128],
                            S["w1col"][:, par_, cc_ * 512:(cc_ + 1) * 512],
                            start=(cc_ == 0), stop=(cc_ == NCH - 1)),
                            wait=[w_rdy, ev.last(), fh.last()] if cc == 0 else (),
                            inc=[(mm, 1)] if cc == NCH - 1 else (),
                            tag=f"w1m{e}{hs}{st}")
                    self.s(lambda sc, ps_=ph, st_=st: sc.activation(
                        S["cast_t"][:, (st_ % 2) * 512:(st_ % 2) * 512 + 512],
                        ps_[:, :], AF.Gelu, bias=0.0, scale=1.0),
                        wait=[mm.last()], inc=[(ev, 1), (fh, 1)], tag=f"gl{e}{hs}{st}")
                    for u in range(4):
                        self.t(lambda T, st_=st, u_=u: T.transpose(
                            psT[:, (st_ % 2) * 512 + u_ * 128: (st_ % 2) * 512 + (u_ + 1) * 128],
                            S["cast_t"][:, (st_ % 2) * 512 + u_ * 128:(st_ % 2) * 512 + (u_ + 1) * 128],
                            S["ident_bf"]),
                            wait=[ev.last(), fT.last()] if u == 0 else (),
                            inc=[(mm, 1)] if u == 3 else (), tag=f"ht{e}{hs}{st}")
                    def ev_h(sc, hs_=hs, st_=st):
                        o = None
                        for u in range(4):
                            hc = hs_ * 4 + u
                            o = sc.copy(
                                out=S["h_e"][:, hc * 512 + st_ * 128: hc * 512 + (st_ + 1) * 128],
                                in_=psT[:, (st_ % 2) * 512 + u * 128:(st_ % 2) * 512 + (u + 1) * 128])
                        return o
                    self.s(ev_h, wait=[mm.last()], inc=[(ev, 1), (fT, 1)], tag=f"he{e}{hs}{st}", multi=True)
                w1_mm.append(mm.promised)
            for half in range(2):
                for hc in range(EH // 128):
                    k2 = len(w2_mm)
                    wpar = k2 % 4
                    wAp = [c["wA0"], c["wA1"], c["wB0"], c["wB1"]][wpar]
                    self.p.dma("sync", S["w2_str"][:, wpar],
                               D["w2_c"][e, hc][:, half * 512:(half + 1) * 512],
                               inc=[(wAp, 16)],
                               wait=[wAp.last()] +
                                    ([(mm, w2_mm[k2 - 4])] if k2 >= 4 else []),
                               tag=f"w2l{e}{half}{hc}")
                    w_rdy = wAp.last()
                    for st in range(4):
                        ph, fh = banks4[st]
                        self.t(lambda T, hc_=hc, st_=st, par_=wpar, ps_=ph: T.matmul(
                            ps_[:, :], S["h_e"][:, hc_ * 512 + st_ * 128: hc_ * 512 + (st_ + 1) * 128],
                            S["w2_str"][:, par_, :],
                            start=(hc_ == 0), stop=(hc_ == EH // 128 - 1)),
                            wait=[w_rdy, ev.last()] + ([fh.last()] if hc == 0 else []),
                            inc=[(mm, 1)] if st == 3 else (),
                            tag=f"w2m{e}{half}{st}")
                    w2_mm.append(mm.promised)
                for st in range(4):
                    ph, fh = banks4[st]
                    self.v(lambda v, ps_=ph, st_=st, par_=par: v.tensor_scalar(
                        S["y_ev4"][:, st_, :], ps_[:, :],
                        S["pcomp2"][:, par_, st_:st_ + 1], None, ALU.mult),
                        wait=[mm.last(), gd.last()] if st == 0 else [mm.last()],
                        inc=[(vx, 1), (fh, 1)], tag=f"w2e{e}{half}{st}")
                self.vdrain()
                for st in range(4):
                    self.g(lambda g_, s_=s, st_=st, h_=half, par_=par: g_.indirect_dma_start(
                        D["ymoe"][s_][:, :],
                        IndirectOffsetOnAxis(ap=S["gidx2"][:, par_, st_:st_ + 1], axis=0),
                        S["y_ev4"][:, st_], None, compute_op=ALU.add,
                        element_offset=h_ * 512),
                        wait=[vx.last(), gd.last()], inc=[(gd, 16)], tag=f"sc{e}{s}{st}",
                        raw_inc=True)
            sc_gd[kk] = gd.promised
        self.barrier()
        self.g(lambda g_, v=gd.promised: g_.wait_ge(gd.sem, v), inc=[(gx, 1)], tag="gdw")
        self.barrier()

    def final(self):
        S, D, c = self.S, self.D, self.c
        vx, ev = c["vx"], c["ev"]
        for s in range(2):
            for t in range(NT):
                self.p.dma("sync", S["stage"][:, 0], D["ymoe"][s][t * 128:(t + 1) * 128, :],
                           inc=[(c["ld"], 16)], wait=[ev.last(), c["fin"].last()], tag=f"fy{s}{t}")
                if s == 0:
                    self.p.dma("sync", S["stage"][:, 1], D["r3A"][t * 128:(t + 1) * 128, :],
                               inc=[(c["ld"], 16)], tag=f"fr{s}{t}")
                    res = S["stage"][:, 1]
                else:
                    res = S["r"][:, t * DM:(t + 1) * DM]
                yt = S["stage"][:, 0]
                self.v(lambda v, r_=res: v.tensor_add(yt, yt, r_),
                       wait=[c["ld"].last(), c["fin"].last()], inc=[(vx, 1)], tag=f"fa{s}{t}")
                self.vdrain()
                tok = self.ln_tile(yt, wait=[vx.last()])
                self.p.dma("sync", D["y"][s][t * 128:(t + 1) * 128, :], yt,
                           wait=[tok], inc=[(c["fin"], 16)], tag=f"fo{s}{t}")
        self.p.emit("sync", lambda e, v=c["fin"].promised: e.wait_ge(c["fin"].sem, v), tag="finw")

    def build(self):
        self.load_consts()
        if self.phase_done("consts"):
            return self.dump_r_and_finish()
        self.build_ffn_cache()
        self.build_moe_cache()
        for s in range(2):
            self._seq = s
            self.load_T12()
            self.l0_attn(s)
            if self.phase_done(f"l0a:{s}"):
                return self.dump_r_and_finish()
            self.l0_ffn(s)
            if self.phase_done(f"l0f:{s}"):
                return self.dump_r_and_finish()
            self.l1_attn(s)
            if self.phase_done(f"l1a:{s}"):
                return self.dump_r_and_finish()
            self.routing(s)
            if self.phase_done(f"rout:{s}"):
                return self.dump_r_and_finish()
            if s == 0:
                self.p.dma("sync",
                           self.D["r3A"].rearrange("(t p) d -> p t d", p=128),
                           self.S["r"].rearrange("p (t d) -> p t d", d=DM),
                           wait=[self.c["ev"].last(), self.c["vx"].last()],
                           inc=[(self.c["st"], 16)], tag="r3sp")
                self.p.emit("sync", lambda e, v=self.c["st"].promised: e.wait_ge(
                    self.c["st"].sem, v))
                self.barrier()
        self.moe()
        if self.phase_done("moe"):
            return self.dump_r_and_finish()
        self.final()


def build_program(stop_after=None):
    b = Build(stop_after=stop_after)
    b.build()
    b.p.finalize()
    return b


def kernel(**inputs):
    pre = host_prepare(inputs)
    x = np.asarray(inputs["L0_attn_x"], np.float32)
    b = build_program()
    in_maps = []
    for core in range(NCORE):
        m = dict(pre)
        m["x"] = np.ascontiguousarray(x[2 * core: 2 * core + 2])
        in_maps.append(m)
    res = run_bass_kernel_spmd(b.nc, in_maps, core_ids=list(range(NCORE)))
    out = np.concatenate([r["y"] for r in res.results], axis=0)
    return out.reshape(16, SEQ, DM)


# version 15
# speedup vs baseline: 1.0805x; 1.0032x over previous
import numpy as np
import ml_dtypes
import concourse.bass as bass
import concourse.mybir as mybir
from concourse.bass import IndirectOffsetOnAxis
from concourse.bass_utils import run_bass_kernel_spmd
from concourse import library_config as _LC

F32 = mybir.dt.float32
BF16 = mybir.dt.bfloat16
I32 = mybir.dt.int32
AF = mybir.ActivationFunctionType
ALU = mybir.AluOpType
AX = mybir.AxisListType
BF = ml_dtypes.bfloat16

ENGINES = ("sync", "scalar", "vector", "tensor", "gpsimd")

NCORE = 8
SEQ = 2048
DM = 1024
NH = 16
DH = 64
HID = 3072
NHC = HID // 128
NE = 8
EH = 4096
CAP = 512
NT = SEQ // 128
NCH = DM // 128
THETA = 10000.0
LN_EPS = 1e-5
RMS_EPS = 1.1920929e-07
QSC = 1.0 / 127.0
N_BISECT = 34
OOB_BIG = 1.0e7


class Chan:
    __slots__ = ("name", "sem", "promised")
    def __init__(self, nc, name):
        self.name = name
        self.sem = nc.alloc_semaphore(name)
        self.promised = 0
    def last(self):
        return (self, self.promised)


class Prog:
    def __init__(self, nc):
        self.nc = nc
        self.streams = {e: [] for e in ENGINES}
        self.n_ops = 0

    def chan(self, name):
        return Chan(self.nc, name)

    def emit(self, eng, fn, wait=(), inc=(), tag="", raw_inc=False, multi=False):
        wait = tuple(wait)
        inc = tuple(inc)
        for ch, v in wait:
            assert v <= ch.promised, (
                f"wait {ch.name}>={v} but promised {ch.promised} ({tag})")
        for ch, k in inc:
            ch.promised += k
        if multi and eng not in ("sync", "tensor"):
            self.streams[eng].append((wait, fn, (), tag))
            self.n_ops += 1
            rest = inc
        else:
            self.streams[eng].append((wait, fn, inc[:1], tag))
            self.n_ops += 1
            rest = inc[1:]
        for ch, k in rest:
            self.streams[eng].append(((), (lambda e: e.drain()), ((ch, k),), tag + "+i"))
            self.n_ops += 1

    def dma(self, eng, out, in_, inc=(), wait=(), tag="", **kw):
        self.emit(eng, (lambda e, o=out, i=in_, k=kw: e.dma_start(out=o, in_=i, **k)),
                  wait=wait, inc=inc, tag=tag, raw_inc=True)

    def finalize(self):
        nc = self.nc
        with nc.Block() as block:
            def runner(stream):
                def run(engine):
                    for wait, fn, inc, tag in stream:
                        for ch, v in wait:
                            engine.wait_ge(ch.sem, v)
                        try:
                            ins = fn(engine)
                        except Exception:
                            import traceback as _tb
                            _tb.print_exc()
                            raise
                        if inc:
                            assert ins is not None, f"inc w/o instruction ({tag})"
                            for ch, k in inc:
                                ins = ins.then_inc(ch.sem, k)
                return run
            block.sync(runner(self.streams["sync"]))
            block.scalar(runner(self.streams["scalar"]))
            block.vector(runner(self.streams["vector"]))
            block.tensor(runner(self.streams["tensor"]))
            block.gpsimd(runner(self.streams["gpsimd"]))



def rope_perm():
    perm = np.zeros(DM, np.int64)
    for h in range(NH):
        for k in range(DH):
            i, s_ = k // 2, k % 2
            perm[h * DH + (i if s_ == 0 else 32 + i)] = h * DH + k
    return perm


def rope_tables():
    t = np.arange(SEQ, dtype=np.float64)
    i = np.arange(32, dtype=np.float64)
    ang = np.outer(i / THETA, t)
    cos = np.cos(ang); sin = np.sin(ang)
    T1 = np.tile(np.concatenate([cos, cos], 0), (1, 2))
    T2 = np.tile(np.concatenate([-sin, sin], 0), (1, 2))
    return T1.astype(BF), T2.astype(BF)


def host_prepare(inp):
    perm = rope_perm()
    g = lambda n: np.asarray(inp[n])
    d = {}
    d["WqT"] = np.ascontiguousarray(g("L0_attn_Wq")[perm, :].T.astype(np.float32))
    d["WkT"] = np.ascontiguousarray(g("L0_attn_Wk")[perm, :].T.astype(np.float32))
    d["WvT"] = np.ascontiguousarray(g("L0_attn_Wv").T.astype(np.float32))
    d["WoT"] = np.ascontiguousarray(g("L0_attn_Wo").T.astype(np.float32))
    d["bq8"] = np.ascontiguousarray(g("L0_attn_bq")[perm].reshape(NCH, 128).T.astype(np.float32))
    d["bk8"] = np.ascontiguousarray(g("L0_attn_bk")[perm].reshape(NCH, 128).T.astype(np.float32))
    d["bv_row"] = g("L0_attn_bv").reshape(1, DM).astype(np.float32)
    d["bo_row"] = g("L0_attn_bo").reshape(1, DM).astype(np.float32)
    d["wgT"] = np.ascontiguousarray(g("L0_ffn_w_gate").T.astype(np.int32))
    d["wuT"] = np.ascontiguousarray(g("L0_ffn_w_up").T.astype(np.int32))
    d["wdT"] = np.ascontiguousarray(g("L0_ffn_w_down").T.astype(np.int32))
    d["W2qT"] = np.ascontiguousarray(g("L1_attn_w_q").T.astype(np.float32))
    d["W2kT"] = np.ascontiguousarray(g("L1_attn_w_k").T.astype(np.float32))
    d["W2vT"] = np.ascontiguousarray(g("L1_attn_w_v").T.astype(np.float32))
    d["W2oT"] = np.ascontiguousarray(g("L1_attn_w_o").T.astype(np.float32))
    d["gateW"] = np.ascontiguousarray(g("L1_ffn_gate_w").astype(np.float32))
    d["w1"] = np.ascontiguousarray(g("L1_ffn_w1").astype(np.float32))
    d["w2"] = np.ascontiguousarray(g("L1_ffn_w2").astype(np.float32))
    T1, T2 = rope_tables()
    d["cT1"] = T1; d["cT2"] = T2
    d["ident_bf"] = np.eye(128).astype(BF)
    d["ident_f32"] = np.eye(128, dtype=np.float32)
    d["tri_bf"] = (np.arange(128)[None, :] >= np.arange(128)[:, None]).astype(BF)
    d["ones_bf"] = np.ones((128, 128), BF)
    d["iota512"] = np.tile(np.arange(512, dtype=np.float32), (128, 1))
    d["pcol"] = np.arange(128, dtype=np.float32).reshape(128, 1)
    return d



class Build:
    def __init__(self, stop_after=None):
        nc = bass.Bass("TRN2", target_bir_lowering=False)
        self.nc = nc
        self.p = Prog(nc)
        self.stop_after = stop_after
        self.stopped = False
        self._mk_dram()
        self._mk_sbuf()
        self._mk_psum()
        self._mk_chans()

    def _mk_dram(self):
        nc = self.nc
        D = {}
        def ein(name, shape, dt):
            D[name] = nc.dram_tensor(name, list(shape), dt, kind="ExternalInput").ap()
        ein("x", (2, SEQ, DM), F32)
        for n in ("WqT", "WkT", "WvT", "WoT", "W2qT", "W2kT", "W2vT", "W2oT"):
            ein(n, (DM, DM), F32)
        ein("bq8", (128, NCH), F32); ein("bk8", (128, NCH), F32)
        ein("bv_row", (1, DM), F32); ein("bo_row", (1, DM), F32)
        ein("wgT", (DM, HID), I32); ein("wuT", (DM, HID), I32); ein("wdT", (HID, DM), I32)
        ein("gateW", (DM, NE), F32)
        ein("w1", (NE, DM, EH), F32); ein("w2", (NE, EH, DM), F32)
        ein("cT1", (64, 2 * SEQ), BF16); ein("cT2", (64, 2 * SEQ), BF16)
        ein("ident_bf", (128, 128), BF16); ein("ident_f32", (128, 128), F32)
        ein("tri_bf", (128, 128), BF16); ein("ones_bf", (128, 128), BF16)
        ein("iota512", (128, 512), F32); ein("pcol", (128, 1), F32)
        D["y"] = nc.dram_tensor("y", [2, SEQ, DM], F32, kind="ExternalOutput").ap()
        D["dbg"] = nc.dram_tensor("dbg", [128, NT * DM], F32, kind="ExternalOutput").ap()
        D["r_spill"] = nc.dram_tensor("r_spill", [SEQ, DM], F32, kind="Internal").ap()
        D["r3A"] = nc.dram_tensor("r3A", [SEQ, DM], F32, kind="Internal").ap()
        D["xmoe"] = [nc.dram_tensor("xmoe0", [SEQ, DM], BF16, kind="Internal").ap(),
                     nc.dram_tensor("xmoe1", [SEQ, DM], BF16, kind="Internal").ap()]
        D["ymoe"] = [nc.dram_tensor("ymoe0", [SEQ, DM], F32, kind="Internal").ap(),
                     nc.dram_tensor("ymoe1", [SEQ, DM], F32, kind="Internal").ap()]
        D["rows"] = nc.dram_tensor("rows", [2, NE, 3, 512], F32, kind="Internal").ap()
        D["wg_c"] = nc.dram_tensor("wg_c", [NHC, 128, NCH * 128], BF16, kind="Internal").ap()
        D["wu_c"] = nc.dram_tensor("wu_c", [NHC, 128, NCH * 128], BF16, kind="Internal").ap()
        D["wd_c"] = nc.dram_tensor("wd_c", [NHC, 128, DM], BF16, kind="Internal").ap()
        D["w1_c"] = nc.dram_tensor("w1_c", [NE, NCH, 128, EH], BF16, kind="Internal").ap()
        D["w2_c"] = nc.dram_tensor("w2_c", [NE, EH // 128, 128, DM], BF16, kind="Internal").ap()
        self.D = D

    def _mk_sbuf(self):
        base = (self.nc.sbuf_base + 127) // 128 * 128
        at = self.nc.alloc_sbuf_tensor_at
        S = {}
        def A(name, shape, dt, offset):
            assert offset % 32 == 0, (name, offset)
            S[name] = at(name, list(shape), dt, offset=base + offset).ap()
        A("r", [128, NT * DM], F32, 0)
        A("q_fm", [128, NCH * SEQ], BF16, 0)
        A("k_fm", [128, NCH * SEQ], BF16, 32768)
        A("T1", [64, 2 * SEQ], BF16, 49152)
        A("T2", [64, 2 * SEQ], BF16, 57344)
        A("x_T", [128, NCH * SEQ], BF16, 65536)
        A("xin_tok", [128, 4 * DM], BF16, 65536)
        A("xin_T", [128, NCH * 512], BF16, 65536 + 8192)
        A("gidxf", [128, 4], F32, 65536 + 16384)
        A("gidxg", [128, 4], F32, 65536 + 16576)
        A("gidx", [128, 4], I32, 65536 + 16448)
        A("pcomp", [128, 4], F32, 65536 + 16512)
        A("gidx2", [128, 2, 4], I32, 65536 + 16640)
        A("pcomp2", [128, 2, 4], F32, 65536 + 16704)
        A("V", [128, NT * NH * 65], BF16, 98304)
        A("h_grp", [128, NHC * 512], BF16, 98304)
        A("attn_fm", [128, NCH * SEQ], BF16, 131584)
        A("h_e", [128, (EH // 128) * 512], BF16, 131584)
        A("P_buf", [128, 8 * 512], BF16, 164352)
        A("mc_st", [128, 2, DM], F32, 164352)
        A("gate_t", [128, 512], BF16, 164352 + 4096)
        A("up_t", [128, 512], BF16, 164352 + 5120)
        A("y_ev4", [128, 4, 512], F32, 164352)
        A("prb8", [128, 8], F32, 164352 + 8192)
        A("ln6", [128, 12], F32, 164352 + 8224)
        A("ln2", [128, 2], F32, 164352 + 8288)
        A("lncol", [128, 3], F32, 164352 + 8320)
        A("rms16", [128, 16], F32, 164352 + 8352)
        A("rms16b", [128, 16], F32, 164352 + 8416)
        QKB = 164352 + 8512
        A("mc_cb", [128, 2, DM], BF16, QKB)
        A("Q_store", [65, 2 * SEQ], BF16, QKB)
        A("K_store", [65, 2 * SEQ], BF16, QKB + 8192)
        A("w_str", [128, NCH * DM], BF16, QKB)
        A("w1col", [128, 2, NCH * 512], BF16, QKB)
        A("probs_et", [NE, SEQ], F32, QKB)
        A("mask_et", [NE, SEQ], BF16, QKB + 8192)
        A("w2_str", [128, 4, 512], BF16, 65536 + 17408)
        A("sc_rows", [10, 512], F32, QKB + 12288)
        A("srch", [NE, 8], F32, QKB + 14336)
        A("gw_bf", [128, NCH * 8], BF16, QKB + 14368)
        A("bigc", [128, 8], F32, QKB + 14496)
        A("srch_i", [NE, 2], I32, QKB + 14528)
        A("wg_hc", [128, 2, NCH * 128], BF16, QKB)
        A("wu_hc", [128, 2, NCH * 128], BF16, QKB + 4096)
        A("wd_hc", [128, 4, 512], BF16, QKB + 8192)
        TMB = 164352 + 8512 + 16384
        A("stage", [128, 2, DM], F32, TMB)
        A("stage_i", [128, 2, DM], I32, TMB)
        A("SWQ", [64, SEQ], BF16, TMB + 8192)
        A("cast_t2", [128, DM], BF16, TMB + 8192)
        A("tanhb", [128, 512], F32, TMB + 8192)
        A("iota512", [128, 512], F32, TMB + 8192)
        A("scat_all", [128, 128], F32, TMB + 8192 + 2048)
        A("maskT_sb", [128, 128], BF16, TMB + 8192 + 2560)
        A("idx_t", [128, 8], F32, TMB + 8192 + 2816)
        A("idx_t2", [128, 8], F32, TMB + 8192 + 2848)
        A("idx_m", [128, 8], I32, TMB + 8192 + 2880)
        A("qk_w", [128, NCH * 128], BF16, TMB + 12288)
        A("cast_t", [128, DM], BF16, TMB + 14336)
        S["att_t"] = S["cast_t"][0:64, 0:512]
        A("inv_t", [1, 512], F32, TMB + 16384)
        A("row_bf", [1, DM], BF16, TMB + 18432)
        off = TMB + 20480
        def C(name, shape, dt, nbytes):
            nonlocal off
            assert off % 32 == 0, (name, off)
            S[name] = at(name, list(shape), dt, offset=base + off).ap()
            off += (nbytes + 31) // 32 * 32
        C("ident_bf", (128, 128), BF16, 256)
        C("tri_bf", (128, 128), BF16, 256)
        C("ones_bf", (128, 128), BF16, 256)
        C("pcol", (128, 1), F32, 4)
        C("eps2", (128, 2), F32, 8)
        S["eps_ln"] = S["eps2"][:, 0:1]
        S["eps_rms"] = S["eps2"][:, 1:2]
        C("ones_col", (128, 1), BF16, 4)
        C("ones1f", (1, 64), F32, 256)
        C("bq8", (128, NCH), F32, 32)
        C("bk8", (128, NCH), F32, 32)
        C("probsT", (128, 2, NT * 10), BF16, 2 * NT * 10 * 2)
        C("ident_f32", (128, 128), F32, 512)
        C("kmax", (1, 2), F32, 8)
        assert base + off <= self.nc.sbuf_top, (base, off, self.nc.sbuf_top)
        self.S = S

    def _mk_psum(self):
        nc = self.nc
        P = {}
        for n in ("psA", "psB", "psC", "psD", "psE", "psF", "psU"):
            P[n] = nc.alloc_psum_tensor(n, [128, 512], F32).ap()
        P["psT"] = nc.alloc_psum_tensor("psT", [128, 1024], BF16).ap()
        self.PS = P

    def _mk_chans(self):
        p = self.p
        self.c = {}
        names = ["ld", "st", "wA", "cA", "wB", "cB", "mm", "ev", "vx", "gx",
                 "gdma", "fin", "bar_s", "bar_c", "bar_v", "bar_t", "bar_g",
                 "wA0", "wA1", "wB0", "wB1", "ld0", "ld1", "gd0", "gd1",
                 "mcA", "mcB", "mcC", "mcD"]
        names += [f"f{b}" for b in ("A", "B", "C", "D", "E", "F", "U", "T")]
        for n in names:
            self.c[n] = p.chan(n)

    def v(self, fn, wait=(), inc=(), tag="", multi=False):
        self.p.emit("vector", fn, wait, inc, tag, multi=multi)
    def s(self, fn, wait=(), inc=(), tag="", multi=False):
        self.p.emit("scalar", fn, wait, inc, tag, multi=multi)
    def t(self, fn, wait=(), inc=(), tag=""): self.p.emit("tensor", fn, wait, inc, tag)
    def g(self, fn, wait=(), inc=(), tag="", raw_inc=False, multi=False):
        self.p.emit("gpsimd", fn, wait, inc, tag, raw_inc=raw_inc, multi=multi)
    def vdrain(self, wait=(), inc=()): self.v(lambda v: v.drain(), wait, inc)
    def sdrain(self, wait=(), inc=()): self.s(lambda s: s.drain(), wait, inc)
    def gdrain(self, wait=(), inc=()): self.g(lambda g: g.drain(), wait, inc)

    def bank(self, n):
        return self.PS["ps" + n], self.c["f" + n]

    def barrier(self):
        c = self.c
        pairs = [("sync", "bar_s"), ("scalar", "bar_c"), ("vector", "bar_v"),
                 ("tensor", "bar_t"), ("gpsimd", "bar_g")]
        for eng, ch in pairs:
            self.p.emit(eng, lambda e: e.nop(), inc=[(c[ch], 1)], tag="bar", multi=True)
        toks = [c[ch].last() for _, ch in pairs]
        for eng, _ in pairs:
            self.p.emit(eng, lambda e: e.nop(), wait=toks, tag="barw")

    def phase_done(self, name):
        if self.stop_after is not None and name == self.stop_after:
            self.stopped = True
            return True
        return False

    def dump_r_and_finish(self):
        self.barrier()
        c = self.c
        self.p.dma("sync", self.D["dbg"][:, :], self.S["r"], inc=[(c["fin"], 16)], tag="dbg")
        self.p.emit("sync", lambda e, v=c["fin"].promised: e.wait_ge(c["fin"].sem, v), tag="finw")

    def load_consts(self):
        p, S, D, c = self.p, self.S, self.D, self.c
        ld = c["ld"]
        for n in ("ident_bf", "ident_f32", "tri_bf", "ones_bf",
                  "pcol", "bq8", "bk8"):
            p.dma("sync", S[n], D[n][:, :], inc=[(ld, 16)], tag=f"cst_{n}")
        self.v(lambda v: v.memset(S["eps_ln"], LN_EPS))
        self.v(lambda v: v.memset(S["eps_rms"], RMS_EPS))
        self.v(lambda v: v.memset(S["ones_col"], 1.0))
        self.v(lambda v: v.memset(S["ones1f"], 1.0))
        self.vdrain(inc=[(c["vx"], 1)])
        self.g(lambda g: g.nop(), inc=[(c["gx"], 1)])
        self.p.emit("sync", lambda e, v=ld.promised: e.wait_ge(ld.sem, v), tag="cstw")
        self.barrier()

    def load_T12(self):
        p, S, D, c = self.p, self.S, self.D, self.c
        ld = c["ld"]
        p.dma("sync", S["T1"], D["cT1"][:, :], inc=[(ld, 16)], tag="T1")
        p.dma("sync", S["T2"], D["cT2"][:, :], inc=[(ld, 16)], tag="T2")
        p.emit("sync", lambda e, v=ld.promised: e.wait_ge(ld.sem, v), tag="T12w")
        self.barrier()

    def build_xT(self, src_tile_fn, tag="xT"):
        S, c = self.S, self.c
        psT, fT = self.bank("T")
        mm, ev, vx = c["mm"], c["ev"], c["vx"]
        cmm = []
        for t in range(NT):
            src, w = src_tile_fn(t)
            ct = S["cast_t"] if t % 2 == 0 else S["cast_t2"]
            self.v(lambda v, s_=src, ct_=ct: v.tensor_copy(out=ct_, in_=s_),
                   wait=list(w) + ([(mm, cmm[t - 2])] if t >= 2 else []),
                   inc=[(vx, 1)], tag=f"{tag}c{t}")
            for cc in range(NCH):
                self.t(lambda T, cc_=cc, ct_=ct: T.transpose(
                    psT[:, cc_ * 128:(cc_ + 1) * 128],
                    ct_[:, cc_ * 128:(cc_ + 1) * 128], S["ident_bf"]),
                    wait=[vx.last(), fT.last()] if cc == 0 else (),
                    inc=[(mm, 1)] if cc == NCH - 1 else (), tag=f"{tag}t{t}")
            cmm.append(mm.promised)
            self.s(lambda s, t_=t: s.copy(
                out=S["x_T"].rearrange("p (c q) -> p c q", c=NCH)[:, :, t_ * 128:(t_ + 1) * 128],
                in_=psT.rearrange("p (c q) -> p c q", c=NCH)),
                wait=[mm.last()], inc=[(ev, 1), (fT, 1)], tag=f"{tag}e{t}")
        return ev.last()

    def hbm_tile(self, dram2d, t, slot):
        S, c = self.S, self.c
        ld = c["ld0"] if slot == 0 else c["ld1"]
        sl = S["stage"][:, slot]
        self.p.dma("sync", sl, dram2d[t * 128:(t + 1) * 128, :], inc=[(ld, 16)],
                   wait=[c["vx"].last(), c["mm"].last(), c["ev"].last(),
                         c["cA"].last(), c["cB"].last()], tag=f"ht{t}")
        return sl, [ld.last()]

    def stream_w(self, dst, srcs, tag="w", eng="vector", chunk_elems=DM, src_int=False,
                 extra_wait=()):
        S, c = self.S, self.c
        ca = c["cA"]
        stage = S["stage_i"] if src_int else S["stage"]
        for i, src in enumerate(srcs):
            w = c["wA0"] if i % 2 == 0 else c["wA1"]
            sl = stage[:, i % 2, :chunk_elems]
            self.p.dma("sync", sl, src, inc=[(w, 16)],
                       wait=[w.last(), (ca, max(0, ca.promised - 1))] +
                            (list(extra_wait) if i == 0 else []),
                       tag=f"{tag}l{i}")
            self.p.emit(eng, lambda e, i_=i, sl_=sl: e.tensor_copy(
                out=dst[:, i_ * chunk_elems:(i_ + 1) * chunk_elems], in_=sl_),
                wait=[w.last()], inc=[(ca, 1)], tag=f"{tag}c{i}")
        return ca.last()

    def ln_tile(self, ytile, wait=()):
        S, c = self.S, self.c
        ev, vx = c["ev"], c["vx"]
        self.v(lambda v: v.bn_stats(S["ln6"][:, 0:6], ytile[:, 0:512]),
               wait=list(wait) + [c["ev"].last()], tag="lns0")
        self.v(lambda v: v.bn_stats(S["ln6"][:, 6:12], ytile[:, 512:1024]), tag="lns1")
        self.vdrain()
        self.v(lambda v: v.bn_aggr(S["ln2"], S["ln6"]), tag="lnagg")
        self.vdrain(inc=[(vx, 1)])
        self.s(lambda s: s.activation(S["lncol"][:, 0:1], S["ln2"][:, 1:2], AF.Sqrt,
                                      bias=S["eps_ln"], scale=1.0),
               wait=[vx.last()], inc=[(ev, 1)], tag="lnsqrt")
        self.v(lambda v: v.reciprocal(S["lncol"][:, 1:2], S["lncol"][:, 0:1]),
               wait=[ev.last()], tag="lnrec")
        self.vdrain()
        self.v(lambda v: v.tensor_scalar(S["lncol"][:, 2:3], S["ln2"][:, 0:1],
                                         S["lncol"][:, 1:2], -1.0, ALU.mult, ALU.mult),
               tag="lnnb")
        self.vdrain(inc=[(vx, 1)])
        self.s(lambda s: s.activation(ytile, ytile, AF.Identity,
                                      bias=S["lncol"][:, 2:3], scale=S["lncol"][:, 1:2]),
               wait=[vx.last()], inc=[(ev, 1)], tag="lnap")
        return ev.last()

    def l0_attn(self, s):
        S, D, c = self.S, self.D, self.c
        mm, ev, vx = c["mm"], c["ev"], c["vx"]

        def src_tile(t):
            return self.hbm_tile(D["x"][s], t, t % 2)
        self.build_xT(src_tile, tag=f"s{s}x0")
        self.barrier()

        self.p.dma("sync", S["stage"][0:1, 0, :], D["bv_row"][:, :],
                   wait=[c["cA"].last(), c["vx"].last()], inc=[(c["ld"], 16)], tag="bv")
        self.v(lambda v: v.tensor_copy(out=S["row_bf"], in_=S["stage"][0:1, 0, :]),
               wait=[c["ld"].last()], tag="bvc")
        self.vdrain(inc=[(c["vx"], 1)])
        self.stream_w(S["w_str"], [D["WvT"][i * 128:(i + 1) * 128, :] for i in range(NCH)],
                      extra_wait=[c["vx"].last()], tag=f"s{s}wv")
        self.v(lambda v: v.memset(
            S["V"].rearrange("p (t h u) -> p t h u", t=NT, h=NH)[:, :, :, 64:65], 1.0),
            tag="vones")
        self.vdrain(inc=[(vx, 1)])
        self.proj_tokmajor(dst_V=True, wq=None, bias_bf=S["row_bf"], tag=f"s{s}V")
        self.barrier()

        self.v(lambda v: v.memset(S["K_store"][64:65, :], 1.0), inc=[(vx, 1)], tag="kone")
        for hp in range(NH // 2):
            self.l0_hp(s, hp)
        self.barrier()

        self.wo_ln(res_dram=D["x"][s], wT=D["WoT"], bias_dram=D["bo_row"],
                   scale=1.0, tag=f"s{s}wo0")
        self.barrier()

    def proj_tokmajor(self, dst_V, wq, bias_bf, tag):
        S, c = self.S, self.c
        mm, ev = c["mm"], c["ev"]
        pa, fa = self.bank("A"); pb, fb = self.bank("B")
        pc, fc = self.bank("C"); pd, fd = self.bank("D")
        for t in range(NT):
            p0, f0, p1, f1 = (pa, fa, pb, fb) if t % 2 == 0 else (pc, fc, pd, fd)
            for half, (ph, fh) in enumerate(((p0, f0), (p1, f1))):
                last_h = (half == 1)
                for cc in range(NCH):
                    self.t(lambda T, t_=t, cc_=cc, h_=half, ps_=ph: T.matmul(
                        ps_[:, :],
                        S["x_T"][:, cc_ * SEQ + t_ * 128: cc_ * SEQ + (t_ + 1) * 128],
                        S["w_str"][:, cc_ * DM + h_ * 512: cc_ * DM + h_ * 512 + 512],
                        start=(cc_ == 0), stop=(bias_bf is None and cc_ == NCH - 1)),
                        wait=[c["cA"].last(), c["vx"].last(), fh.last()] if cc == 0 else (),
                        inc=[(mm, 1)] if (bias_bf is None and cc == NCH - 1 and last_h) else (),
                        tag=f"{tag}m{t}")
                if bias_bf is not None:
                    self.t(lambda T, h_=half, ps_=ph: T.matmul(
                        ps_[:, :], S["ones_bf"][0:1, :],
                        bias_bf[0:1, h_ * 512:(h_ + 1) * 512], start=False, stop=True),
                        inc=[(mm, 1)] if last_h else (), tag=f"{tag}b{t}")
            if dst_V:
                def ev_v(sc, t_=t, p0_=p0, p1_=p1):
                    dst = S["V"].rearrange("p (t h u) -> p t h u", t=NT, h=NH)[:, t_, :, 0:64]
                    sc.copy(out=dst[:, 0:8, :], in_=p0_.rearrange("p (h u) -> p h u", u=64))
                    return sc.copy(out=dst[:, 8:16, :], in_=p1_.rearrange("p (h u) -> p h u", u=64))
                self.s(ev_v, wait=[mm.last()], inc=[(f0, 1), (f1, 1)], tag=f"{tag}e{t}", multi=True)
                self.s(lambda sc: sc.nop(), inc=[(ev, 1)], tag=f"{tag}en{t}")

    def l0_hp(self, s, hp):
        S, D, c = self.S, self.D, self.c
        mm, ev, vx = c["mm"], c["ev"], c["vx"]
        cB = c["cB"]
        pa, fa = self.bank("A"); pb, fb = self.bank("B")
        for which, wname, store, b8 in ((1, "WkT", S["K_store"], S["bk8"]),
                                        (0, "WqT", S["Q_store"], S["bq8"])):
            for cc in range(NCH):
                i = which * NCH + cc
                wB = c["wB0"] if i % 2 == 0 else c["wB1"]
                sl = S["stage"][:, i % 2, :128]
                self.p.dma("sync", sl, D[wname][cc * 128:(cc + 1) * 128, hp * 128:(hp + 1) * 128],
                           inc=[(wB, 16)], wait=[wB.last(), (cB, max(0, cB.promised - 1))] +
                           ([mm.last()] if cc == 0 else []),
                           tag=f"hp{hp}wl")
                self.v(lambda v, sl_=sl, cc_=cc: v.tensor_copy(
                    out=S["qk_w"][:, cc_ * 128:(cc_ + 1) * 128], in_=sl_),
                    wait=[wB.last()], inc=[(cB, 1)], tag=f"hp{hp}wc")
            w_rdy = cB.last()
            for qg in range(4):
                ph, fh = (pa, fa) if qg % 2 == 0 else (pb, fb)
                for cc in range(NCH):
                    self.t(lambda T, cc_=cc, qg_=qg, ps_=ph: T.matmul(
                        ps_[:, :], S["qk_w"][:, cc_ * 128:(cc_ + 1) * 128],
                        S["x_T"][:, cc_ * SEQ + qg_ * 512: cc_ * SEQ + (qg_ + 1) * 512],
                        start=(cc_ == 0), stop=(cc_ == NCH - 1)),
                        wait=[w_rdy, fh.last()] if cc == 0 else (),
                        inc=[(mm, 1)] if cc == NCH - 1 else (), tag=f"hp{hp}p{which}{qg}")
                def ev_qk(sc, ps_=ph, st_=store, qg_=qg, b8_=b8, hp_=hp):
                    sc.activation(st_[0:64, qg_ * 512:(qg_ + 1) * 512], ps_[0:64, :],
                                  AF.Identity, bias=b8_[0:64, hp_:hp_ + 1], scale=1.0)
                    return sc.activation(st_[0:64, SEQ + qg_ * 512: SEQ + (qg_ + 1) * 512],
                                         ps_[64:128, :], AF.Identity,
                                         bias=b8_[64:128, hp_:hp_ + 1], scale=1.0)
                self.s(ev_qk, wait=[mm.last()], inc=[(ev, 1), (fh, 1)], tag=f"hp{hp}e{which}{qg}", multi=True)
            self.rope(store)
        self.l0_m_and_attn(s, hp)

    def rope(self, store):
        S, c = self.S, self.c
        vx, ev = c["vx"], c["ev"]
        SW = S["SWQ"]
        for sd in range(2):
            st = store[0:64, sd * SEQ:(sd + 1) * SEQ]
            self.v(lambda v, st_=st: v.tensor_copy(out=SW[0:32, :], in_=st_[32:64, :]),
                   wait=[ev.last(), vx.last()], tag="sw1")
            self.v(lambda v, st_=st: v.tensor_copy(out=SW[32:64, :], in_=st_[0:32, :]), tag="sw2")
            self.vdrain()
            self.v(lambda v, sd_=sd: v.tensor_tensor(SW, SW, S["T2"][:, sd_ * SEQ:(sd_ + 1) * SEQ],
                                                     ALU.mult), tag="swT2")
            self.v(lambda v, st_=st, sd_=sd: v.tensor_tensor(
                st_, st_, S["T1"][:, sd_ * SEQ:(sd_ + 1) * SEQ], ALU.mult), tag="qT1")
            self.vdrain()
            self.v(lambda v, st_=st: v.tensor_tensor(st_, st_, SW, ALU.add), tag="ropea")
            self.vdrain(inc=[(vx, 1)])

    def l0_m_and_attn(self, s, hp):
        S, c = self.S, self.c
        mm, ev, vx = c["mm"], c["ev"], c["vx"]
        pg, fg = self.PS["psU"], self.c["fU"]
        SW = S["SWQ"]
        self.v(lambda v: v.memset(S["kmax"], 0.0), wait=[vx.last()], tag="km0")
        self.vdrain(inc=[(vx, 1)])
        for side in range(2):
            self.v(lambda v, sd=side: v.tensor_tensor(
                SW, S["K_store"][0:64, sd * SEQ:(sd + 1) * SEQ],
                S["K_store"][0:64, sd * SEQ:(sd + 1) * SEQ], ALU.mult),
                wait=[vx.last()], tag="KK")
            self.vdrain(inc=[(vx, 1)])
            for seg in range(4):
                self.t(lambda T, sg=seg: T.matmul(
                    pg[0:1, :], S["ones_col"][0:64, :],
                    SW[:, sg * 512:(sg + 1) * 512], start=True, stop=True),
                    wait=[vx.last(), fg.last()], inc=[(mm, 1)], tag="kn2")
                self.v(lambda v: v.reduce_max(S["inv_t"][0:1, 0:1], pg[0:1, :], AX.X),
                       wait=[mm.last()], tag="knm")
                self.vdrain()
                self.v(lambda v, sd=side: v.tensor_max(
                    S["kmax"][0:1, sd:sd + 1], S["kmax"][0:1, sd:sd + 1],
                    S["inv_t"][0:1, 0:1]), inc=[(vx, 1), (fg, 1)], tag="knc")
                self.vdrain()
        self.s(lambda sc: sc.activation(S["kmax"], S["kmax"], AF.Sqrt, bias=0.0, scale=1.0),
               wait=[vx.last()], inc=[(ev, 1)], tag="kms")
        for side in range(2):
            self.v(lambda v, sd=side: v.tensor_tensor(
                SW, S["Q_store"][0:64, sd * SEQ:(sd + 1) * SEQ],
                S["Q_store"][0:64, sd * SEQ:(sd + 1) * SEQ], ALU.mult),
                wait=[ev.last(), vx.last()], tag="QQ")
            self.vdrain(inc=[(vx, 1)])
            for seg in range(4):
                self.t(lambda T, sg=seg: T.matmul(
                    pg[0:1, :], S["ones_col"][0:64, :],
                    SW[:, sg * 512:(sg + 1) * 512], start=True, stop=True),
                    wait=[vx.last(), fg.last()], inc=[(mm, 1)], tag="qn2")
                self.s(lambda sc: sc.activation(S["inv_t"], pg[0:1, :], AF.Sqrt,
                                                bias=0.0, scale=1.0),
                       wait=[mm.last()], inc=[(ev, 1)], tag="qns")
                self.v(lambda v, sd=side, sg=seg: v.tensor_scalar(
                    S["Q_store"][64:65, sd * SEQ + sg * 512: sd * SEQ + (sg + 1) * 512],
                    S["inv_t"], S["kmax"][0:1, sd:sd + 1], -1.0, ALU.mult, ALU.mult),
                    wait=[ev.last()], inc=[(vx, 1), (fg, 1)], tag="qrow")
                self.vdrain()
        prev = None
        ix = 0
        for side in range(2):
            h = 2 * hp + side
            for qg in range(4):
                cur = self.attn_core(h, qg, side, causal=False, tag=f"h{h}g{qg}", pv_ix=ix)
                if prev is not None:
                    self.attn_tail(prev)
                prev = cur
                ix += 1
        self.attn_tail(prev)

    def attn_core(self, h, qg, side, causal, tag, pv_ix):
        S, c = self.S, self.c
        mm, ev, vx = c["mm"], c["ev"], c["vx"]
        rot = [self.bank("A"), self.bank("B"), self.bank("E"), self.bank("F")]
        pc, fc = self.bank("C") if pv_ix % 2 == 0 else self.bank("D")
        kt_hi = NT if not causal else min(NT, (qg + 1) * 4)
        ch_ = h // 2
        ev_cnt = {}
        vx_cnt = {}
        for kt0 in range(0, kt_hi, 8):
            kts = range(kt0, min(kt0 + 8, kt_hi))
            for kt in kts:
                ph, fh = rot[kt % 4]
                qlo = 0 if not causal else max(0, kt * 128 - qg * 512)
                if not causal:
                    lhsT = S["K_store"][0:65, side * SEQ + kt * 128: side * SEQ + (kt + 1) * 128]
                    rhs = S["Q_store"][0:65, side * SEQ + qg * 512: side * SEQ + (qg + 1) * 512]
                else:
                    lhsT = S["k_fm"][side * 64:side * 64 + 64,
                                     ch_ * SEQ + kt * 128: ch_ * SEQ + (kt + 1) * 128]
                    rhs = S["q_fm"][side * 64:side * 64 + 64,
                                    ch_ * SEQ + qg * 512 + qlo: ch_ * SEQ + (qg + 1) * 512]
                self.t(lambda T, l_=lhsT, r_=rhs, ps_=ph, q_=qlo: T.matmul(
                    ps_[:, q_:512], l_, r_, start=True, stop=True),
                    wait=[fh.last()], inc=[(mm, 1)], tag=f"{tag}s{kt}")
                sl = (kt % 8) * 512
                if not causal:
                    self.s(lambda sc, ps_=ph, sl_=sl: sc.activation(
                        S["P_buf"][:, sl_:sl_ + 512], ps_[:, :], AF.Exp,
                        bias=0.0, scale=0.125),
                        wait=[mm.last()], inc=[(ev, 1), (fh, 1)], tag=f"{tag}x{kt}")
                else:
                    self.s(lambda sc, ps_=ph, q_=qlo: sc.activation(
                        S["tanhb"][:, q_:512], ps_[:, q_:512], AF.Tanh,
                        bias=0.0, scale=1.0 / (8.0 * 50.0)), wait=[mm.last()], tag=f"{tag}t{kt}")
                    self.sdrain()
                    self.s(lambda sc, sl_=sl, q_=qlo: sc.activation(
                        S["P_buf"][:, sl_ + q_:sl_ + 512], S["tanhb"][:, q_:512],
                        AF.Exp, bias=0.0, scale=50.0),
                        inc=[(ev, 1), (fh, 1)], tag=f"{tag}x{kt}")
                ev_cnt[kt] = ev.promised
                if causal:
                    if qlo > 0:
                        self.v(lambda v, sl_=sl, q_=qlo: v.memset(
                            S["P_buf"][:, sl_:sl_ + q_], 0.0),
                            wait=[(ev, ev_cnt[kt])], tag=f"{tag}z{kt}")
                        self.vdrain(inc=[(vx, 1)])
                    if qg * 512 <= kt * 128 < (qg + 1) * 512:
                        off = sl + (kt * 128 - qg * 512)
                        self.v(lambda v, o_=off: v.tensor_tensor(
                            S["P_buf"][:, o_:o_ + 128], S["P_buf"][:, o_:o_ + 128],
                            S["tri_bf"], ALU.mult), wait=[(ev, ev_cnt[kt])], tag=f"{tag}d{kt}")
                        self.vdrain(inc=[(vx, 1)])
                vx_cnt[kt] = vx.promised
            for kt in kts:
                self.t(lambda T, kt_=kt, pc_=pc: T.matmul(
                    pc_[0:65, :],
                    S["V"][:, kt_ * NH * 65 + h * 65: kt_ * NH * 65 + h * 65 + 65],
                    S["P_buf"][:, (kt_ % 8) * 512:(kt_ % 8) * 512 + 512],
                    start=(kt_ == 0), stop=(kt_ == kt_hi - 1)),
                    wait=[(ev, ev_cnt[kt]), (vx, vx_cnt[kt])] + ([fc.last()] if kt == 0 else []),
                    inc=[(mm, 1)] if kt == kt_hi - 1 else (), tag=f"{tag}pv{kt}")
        return (h, qg, side, pc, fc, tag, mm.promised)

    def attn_tail(self, ctx):
        S, c = self.S, self.c
        mm, ev, vx = c["mm"], c["ev"], c["vx"]
        h, qg, side, pc, fc, tag, mm_pv = ctx
        pu, fu = self.bank("U")
        ch_ = h // 2
        self.v(lambda v: v.reciprocal(S["inv_t"], pc[64:65, :]),
               wait=[(mm, mm_pv)], tag=f"{tag}rc")
        self.vdrain(inc=[(vx, 1)])
        vx_rc = vx.promised
        self.t(lambda T: T.matmul(pu[0:64, :], S["ones1f"], S["inv_t"],
                                  start=True, stop=True),
               wait=[(vx, vx_rc), fu.last()], inc=[(mm, 1)], tag=f"{tag}r1")
        mm_r1 = mm.promised
        self.s(lambda sc: sc.copy(out=S["att_t"], in_=pc[0:64, :]),
               wait=[(mm, mm_pv), (vx, vx_rc)], inc=[(ev, 1), (fc, 1)], tag=f"{tag}ae")
        self.v(lambda v: v.tensor_tensor(
            S["attn_fm"][side * 64:side * 64 + 64,
                         ch_ * SEQ + qg * 512:(ch_) * SEQ + (qg + 1) * 512],
            S["att_t"], pu[0:64, :], ALU.mult),
            wait=[ev.last(), (mm, mm_r1)], inc=[(vx, 1), (fu, 1)], tag=f"{tag}nm")
        self.vdrain()

    def wo_ln(self, res_dram, wT, bias_dram, scale, tag, spill_dram=None):
        S, D, c = self.S, self.D, self.c
        mm, ev, vx = c["mm"], c["ev"], c["vx"]
        if bias_dram is not None:
            self.p.dma("sync", S["stage"][0:1, 0, :], bias_dram[:, :],
                       wait=[c["cA"].last(), c["vx"].last()], inc=[(c["ld"], 16)], tag=f"{tag}b")
            self.v(lambda v: v.tensor_copy(out=S["row_bf"], in_=S["stage"][0:1, 0, :]),
                   wait=[c["ld"].last()], tag=f"{tag}bc")
            self.vdrain(inc=[(vx, 1)])
        self.stream_w(S["w_str"], [wT[i * 128:(i + 1) * 128, :] for i in range(NCH)],
                      extra_wait=[vx.last()], tag=f"{tag}w")
        pa, fa = self.bank("A"); pb, fb = self.bank("B")
        pc, fc = self.bank("C"); pd, fd = self.bank("D")
        for t in range(NT):
            p0, f0, p1, f1 = (pa, fa, pb, fb) if t % 2 == 0 else (pc, fc, pd, fd)
            res_ap, res_w = self.hbm_tile(res_dram, t, t % 2)
            for half, (ph, fh) in enumerate(((p0, f0), (p1, f1))):
                last_h = (half == 1)
                for cc in range(NCH):
                    self.t(lambda T, t_=t, cc_=cc, h_=half, ps_=ph: T.matmul(
                        ps_[:, :],
                        S["attn_fm"][:, cc_ * SEQ + t_ * 128: cc_ * SEQ + (t_ + 1) * 128],
                        S["w_str"][:, cc_ * DM + h_ * 512: cc_ * DM + h_ * 512 + 512],
                        start=(cc_ == 0), stop=(bias_dram is None and cc_ == NCH - 1)),
                        wait=[c["cA"].last(), c["vx"].last(), fh.last()] if cc == 0 else (),
                        inc=[(mm, 1)] if (bias_dram is None and cc == NCH - 1 and last_h) else (),
                        tag=f"{tag}m{t}")
                if bias_dram is not None:
                    self.t(lambda T, h_=half, ps_=ph: T.matmul(
                        ps_[:, :], S["ones_bf"][0:1, :],
                        S["row_bf"][0:1, h_ * 512:(h_ + 1) * 512], start=False, stop=True),
                        inc=[(mm, 1)] if last_h else (), tag=f"{tag}bb{t}")
            rt = S["r"][:, t * DM:(t + 1) * DM]
            for half, (ph, fh) in enumerate(((p0, f0), (p1, f1))):
                self.v(lambda v, ps_=ph, h_=half, rt_=rt, ra=res_ap: v.scalar_tensor_tensor(
                    rt_[:, h_ * 512:(h_ + 1) * 512], ps_[:, :], scale,
                    ra[:, h_ * 512:(h_ + 1) * 512], ALU.mult, ALU.add),
                    wait=[mm.last()] + res_w if half == 0 else (),
                    inc=[(fh, 1)], tag=f"{tag}ev{t}")
            self.vdrain(inc=[(vx, 1)])
            self.ln_tile(rt, wait=[vx.last()])
            if spill_dram is not None:
                self.p.dma("sync", spill_dram[t * 128:(t + 1) * 128, :], rt,
                           wait=[c["ev"].last()], inc=[(c["st"], 16)], tag=f"{tag}sp{t}")

    def build_ffn_cache(self):
        S, D, c = self.S, self.D, self.c
        gx = c["gx"]
        jobs = []
        for hc in range(NHC):
            jobs.append((D["wgT"][:, hc * 128:(hc + 1) * 128].rearrange(
                "(cc p) j -> p cc j", p=128), D["wg_c"][hc], True))
            jobs.append((D["wuT"][:, hc * 128:(hc + 1) * 128].rearrange(
                "(cc p) j -> p cc j", p=128), D["wu_c"][hc], True))
            jobs.append((D["wdT"][hc * 128:(hc + 1) * 128, :], D["wd_c"][hc], False))
        cast2 = [S["cast_t"], S["cast_t2"]]
        for i, (src_ap, dst_ap, slow) in enumerate(jobs):
            w = c["wB0"] if i % 2 == 0 else c["wB1"]
            gdp = c["gd0"] if i % 2 == 0 else c["gd1"]
            sl = S["stage_i"][:, i % 2]
            if len(src_ap.shape) == 3:
                sl = sl.rearrange("p (cc j) -> p cc j", j=128)
            self.p.dma("sync", sl, src_ap, inc=[(w, 16)],
                       wait=[w.last(), (gx, max(0, gx.promised - 1))],
                       allow_slow_non_contiguous=slow, tag=f"fcl{i}")
            self.g(lambda g_, sl_=sl, i_=i: g_.tensor_copy(out=cast2[i_ % 2], in_=sl_),
                   wait=[w.last(), gdp.last()],
                   inc=[(gx, 1)], tag=f"fcc{i}")
            self.p.dma("gpsimd", dst_ap, cast2[i % 2],
                       wait=[gx.last()], inc=[(gdp, 16)], tag=f"fcs{i}")
        self.g(lambda g_, v=self.c["gd0"].promised: g_.wait_ge(self.c["gd0"].sem, v), tag="fcw0")
        self.g(lambda g_, v=self.c["gd1"].promised: g_.wait_ge(self.c["gd1"].sem, v),
               inc=[(gx, 1)], tag="fcw1")
        self.barrier()

    def build_moe_cache(self):
        S, D, c = self.S, self.D, self.c
        gx = c["gx"]
        jobs = []
        for e in range(NE):
            for cc in range(NCH):
                for sub in range(4):
                    jobs.append((D["w1"][e][cc * 128:(cc + 1) * 128, sub * 1024:(sub + 1) * 1024],
                                 D["w1_c"][e, cc][:, sub * 1024:(sub + 1) * 1024]))
            for hc in range(EH // 128):
                jobs.append((D["w2"][e][hc * 128:(hc + 1) * 128, :], D["w2_c"][e, hc]))
        cast2 = [S["cast_t"], S["cast_t2"]]
        cast_engs = ["gpsimd", "vector", "scalar"]
        cast_ch = [c["gx"], c["vx"], c["ev"]]
        base_cnt = [ch.promised for ch in cast_ch]
        for i, (src_ap, dst_ap) in enumerate(jobs):
            w = c["wB0"] if i % 2 == 0 else c["wB1"]
            gdp = c["gd0"] if i % 2 == 0 else c["gd1"]
            sl = S["stage"][:, i % 2]
            ce = i % 3
            self.p.dma("sync", sl, src_ap, inc=[(w, 16)],
                       wait=[w.last(), cast_ch[(i + 1) % 3].last()], tag=f"mc_l{i}")
            if cast_engs[ce] == "scalar":
                fn = (lambda en, sl_=sl, i_=i: en.copy(out=cast2[i_ % 2], in_=sl_))
            else:
                fn = (lambda en, sl_=sl, i_=i: en.tensor_copy(out=cast2[i_ % 2], in_=sl_))
            self.p.emit(cast_engs[ce], fn,
                wait=[w.last(), gdp.last()], inc=[(cast_ch[ce], 1)], tag=f"mc_c{i}")
            self.p.dma("gpsimd", dst_ap, cast2[i % 2],
                       wait=[cast_ch[ce].last()], inc=[(gdp, 16)], tag=f"mc_s{i}")
        self.g(lambda g_, v=self.c["gd0"].promised: g_.wait_ge(self.c["gd0"].sem, v),
               tag="mcw0")
        self.g(lambda g_, v=self.c["gd1"].promised: g_.wait_ge(self.c["gd1"].sem, v),
               inc=[(gx, 1)], tag="mcw1")
        self.barrier()

    def l0_ffn(self, s):
        S, D, c = self.S, self.D, self.c
        mm, ev, vx = c["mm"], c["ev"], c["vx"]
        cB = c["cB"]

        def src_tile(t):
            return S["r"][:, t * DM:(t + 1) * DM], []
        self.build_xT(src_tile, tag=f"s{s}x1")
        self.barrier()

        pa, fa = self.bank("A"); pb, fb = self.bank("B")
        pc, fc = self.bank("C"); pd, fd = self.bank("D")
        for tg in range(4):
            gu_mm = []
            for hc in range(NHC):
                par = hc % 2
                wB = c["wB0"] if par == 0 else c["wB1"]
                for name, buf in (("wg_c", "wg_hc"), ("wu_c", "wu_hc")):
                    self.p.dma("sync", S[buf][:, par], D[name][hc], inc=[(wB, 16)],
                               wait=[wB.last()] + ([(mm, gu_mm[hc - 2])] if hc >= 2 else []),
                               tag=f"ff{name}{hc}")
                w_rdy = wB.last()
                ph, fh = (pa, fa) if par == 0 else (pb, fb)
                pu, fu = (pc, fc) if par == 0 else (pd, fd)
                for cc in range(NCH):
                    self.t(lambda T, cc_=cc, ps_=ph, par_=par, tg_=tg: T.matmul(
                        ps_[:, :], S["wg_hc"][:, par_, cc_ * 128:(cc_ + 1) * 128],
                        S["x_T"][:, cc_ * SEQ + tg_ * 512: cc_ * SEQ + (tg_ + 1) * 512],
                        start=(cc_ == 0), stop=(cc_ == NCH - 1)),
                        wait=[w_rdy, fh.last()] if cc == 0 else (), tag=f"ffg{hc}")
                for cc in range(NCH):
                    self.t(lambda T, cc_=cc, ps_=pu, par_=par, tg_=tg: T.matmul(
                        ps_[:, :], S["wu_hc"][:, par_, cc_ * 128:(cc_ + 1) * 128],
                        S["x_T"][:, cc_ * SEQ + tg_ * 512: cc_ * SEQ + (tg_ + 1) * 512],
                        start=(cc_ == 0), stop=(cc_ == NCH - 1)),
                        wait=[fu.last()] if cc == 0 else (),
                        inc=[(mm, 1)] if cc == NCH - 1 else (), tag=f"ffu{hc}")
                gu_mm.append(mm.promised)
                self.s(lambda sc, ps_=ph: sc.activation(S["gate_t"], ps_[:, :], AF.Silu,
                                                        bias=0.0, scale=QSC),
                       wait=[mm.last(), vx.last()], inc=[(fh, 1)], tag=f"ffsl{hc}")
                self.s(lambda sc, ps_=pu: sc.activation(S["up_t"], ps_[:, :], AF.Copy,
                                                        bias=0.0, scale=QSC),
                       inc=[(ev, 1), (fu, 1)], tag=f"ffup{hc}")
                self.v(lambda v, hc_=hc: v.tensor_tensor(
                    S["h_grp"][:, hc_ * 512:(hc_ + 1) * 512], S["gate_t"], S["up_t"],
                    ALU.mult), wait=[ev.last()], inc=[(vx, 1)], tag=f"ffh{hc}")
            self.vdrain(inc=[(vx, 1)])
            banks = [self.bank("A"), self.bank("B"), self.bank("C"), self.bank("D")]
            wd_mm = self.__dict__.setdefault("_wd_mm_g", [])
            for half in range(2):
                for hc in range(NHC):
                    k = len(wd_mm)
                    par = k % 4
                    wB = [c["wB0"], c["wB1"], c["wA0"], c["wA1"]][par]
                    self.p.dma("sync", S["wd_hc"][:, par],
                               D["wd_c"][hc][:, half * 512:(half + 1) * 512],
                               inc=[(wB, 16)],
                               wait=[wB.last()] + ([(mm, wd_mm[k - 4])] if k >= 4 else []),
                               tag=f"ffwd{hc}")
                    w_rdy = wB.last()
                    for tt in range(4):
                        ph, fh = banks[tt]
                        self.t(lambda T, hc_=hc, tt_=tt, ps_=ph, par_=par: T.matmul(
                            ps_[:, :], S["h_grp"][:, hc_ * 512 + tt_ * 128: hc_ * 512 + (tt_ + 1) * 128],
                            S["wd_hc"][:, par_, :],
                            start=(hc_ == 0), stop=(hc_ == NHC - 1)),
                            wait=[w_rdy, vx.last()] + ([fh.last()] if hc == 0 else []),
                            inc=[(mm, 1)] if tt == 3 else (), tag=f"ffd{half}{hc}{tt}")
                    wd_mm.append(mm.promised)
                for tt in range(4):
                    t_ = tg * 4 + tt
                    ph, fh = banks[tt]
                    rt = S["r"][:, t_ * DM:(t_ + 1) * DM]
                    self.v(lambda v, ps_=ph, rt_=rt, h_=half: v.scalar_tensor_tensor(
                        rt_[:, h_ * 512:(h_ + 1) * 512], ps_[:, :], QSC,
                        rt_[:, h_ * 512:(h_ + 1) * 512], ALU.mult, ALU.add),
                        wait=[mm.last()], inc=[(fh, 1)], tag=f"ffde{half}{tt}")
                self.vdrain(inc=[(vx, 1)])
            for tt in range(4):
                t_ = tg * 4 + tt
                rt = S["r"][:, t_ * DM:(t_ + 1) * DM]
                tok = self.ln_tile(rt, wait=[vx.last()])
                self.p.dma("sync", D["r_spill"][t_ * 128:(t_ + 1) * 128, :], rt,
                           wait=[tok], inc=[(c["st"], 16)], tag=f"ffsp{t_}")
        self.barrier()

    def l1_attn(self, s):
        S, D, c = self.S, self.D, self.c
        mm, ev, vx = c["mm"], c["ev"], c["vx"]

        def src_tile(t):
            return S["r"][:, t * DM:(t + 1) * DM], []
        self.build_xT(src_tile, tag=f"s{s}x2")
        self.barrier()
        self.p.emit("sync", lambda e, v=c["st"].promised: e.wait_ge(c["st"].sem, v), tag="spw")
        self.barrier()

        self.stream_w(S["w_str"], [D["W2vT"][i * 128:(i + 1) * 128, :] for i in range(NCH)],
                      tag=f"s{s}w2v")
        self.v(lambda v: v.memset(
            S["V"].rearrange("p (t h u) -> p t h u", t=NT, h=NH)[:, :, :, 64:65], 1.0),
            tag="vone1")
        self.vdrain(inc=[(vx, 1)])
        self.proj_tokmajor(dst_V=True, wq=None, bias_bf=None, tag=f"s{s}v2")
        self.barrier()
        for wname, dst in (("W2qT", "q_fm"), ("W2kT", "k_fm")):
            self.stream_w(S["w_str"], [D[wname][i * 128:(i + 1) * 128, :] for i in range(NCH)],
                          tag=f"s{s}{wname}")
            self.qk_rms_proj(dst, tag=f"s{s}{wname}p")
            self.barrier()
        prev = None
        ix = 0
        for h in range(NH):
            for qg in range(4):
                cur = self.attn_core(h, qg, h % 2, causal=True, tag=f"L1h{h}g{qg}", pv_ix=ix)
                if prev is not None:
                    self.attn_tail(prev)
                prev = cur
                ix += 1
        self.attn_tail(prev)
        self.barrier()
        self.wo_ln(res_dram=D["r_spill"], wT=D["W2oT"], bias_dram=None,
                   scale=1.0, tag=f"s{s}wo1")
        self.barrier()

    def qk_rms_proj(self, dst_name, tag):
        S, c = self.S, self.c
        mm, ev, vx = c["mm"], c["ev"], c["vx"]
        pa, fa = self.bank("A"); pb, fb = self.bank("B")
        psT, fT = self.bank("T")
        dst = S[dst_name]
        for t in range(NT):
            for half, (ph, fh) in enumerate(((pa, fa), (pb, fb))):
                for cc in range(NCH):
                    self.t(lambda T, t_=t, cc_=cc, h_=half, ps_=ph: T.matmul(
                        ps_[:, :],
                        S["x_T"][:, cc_ * SEQ + t_ * 128: cc_ * SEQ + (t_ + 1) * 128],
                        S["w_str"][:, cc_ * DM + h_ * 512: cc_ * DM + h_ * 512 + 512],
                        start=(cc_ == 0), stop=(cc_ == NCH - 1)),
                        wait=[c["cA"].last(), vx.last(), fh.last()] if cc == 0 else (),
                        inc=[(mm, 1)] if (cc == NCH - 1 and half == 1) else (),
                        tag=f"{tag}m{t}")
            self.s(lambda sc: sc.copy(out=S["cast_t"][:, 0:512], in_=pa[:, :]),
                   wait=[mm.last()], inc=[(fa, 1)], tag=f"{tag}e0{t}")
            self.s(lambda sc: sc.copy(out=S["cast_t"][:, 512:1024], in_=pb[:, :]),
                   inc=[(ev, 1), (fb, 1)], tag=f"{tag}e1{t}")
            sq = S["stage"][:, 0]
            self.v(lambda v: v.tensor_tensor(sq, S["cast_t"], S["cast_t"], ALU.mult),
                   wait=[ev.last()], tag=f"{tag}sq{t}")
            self.vdrain()
            self.v(lambda v: v.tensor_reduce(
                S["rms16"], sq.rearrange("p (h u) -> p h u", u=64), AX.X, ALU.add),
                tag=f"{tag}sr{t}")
            self.vdrain(inc=[(vx, 1)])
            self.s(lambda sc: sc.activation(S["rms16b"], S["rms16"], AF.Sqrt,
                                            bias=S["eps_rms"], scale=1.0 / 64.0),
                   wait=[vx.last()], inc=[(ev, 1)], tag=f"{tag}sd{t}")
            self.v(lambda v: v.reciprocal(S["rms16"], S["rms16b"]),
                   wait=[ev.last()], tag=f"{tag}rc{t}")
            self.vdrain()
            self.v(lambda v: v.tensor_tensor(
                S["cast_t"].rearrange("p (h u) -> p h u", u=64),
                S["cast_t"].rearrange("p (h u) -> p h u", u=64),
                S["rms16"].rearrange("p (h o) -> p h o", o=1).broadcast_to([128, 16, 64]),
                ALU.mult), tag=f"{tag}ap{t}")
            self.vdrain(inc=[(vx, 1)])
            for cc in range(NCH):
                self.t(lambda T, cc_=cc: T.transpose(
                    psT[:, cc_ * 128:(cc_ + 1) * 128],
                    S["cast_t"][:, cc_ * 128:(cc_ + 1) * 128], S["ident_bf"]),
                    wait=[vx.last(), fT.last()] if cc == 0 else (),
                    inc=[(mm, 1)] if cc == NCH - 1 else (), tag=f"{tag}tr{t}")
            self.s(lambda sc, t_=t: sc.copy(
                out=dst.rearrange("p (c q) -> p c q", c=NCH)[:, :, t_ * 128:(t_ + 1) * 128],
                in_=psT.rearrange("p (c q) -> p c q", c=NCH)),
                wait=[mm.last()], inc=[(ev, 1), (fT, 1)], tag=f"{tag}te{t}")

    def routing(self, s):
        S, D, c = self.S, self.D, self.c
        mm, ev, vx = c["mm"], c["ev"], c["vx"]
        for t in range(NT):
            self.v(lambda v, t_=t: v.tensor_copy(
                out=S["cast_t"], in_=S["r"][:, t_ * DM:(t_ + 1) * DM]),
                wait=[(c["st"], max(0, c["st"].promised))], inc=[(vx, 1)], tag=f"xm{t}")
            self.p.dma("sync", D["xmoe"][s][t * 128:(t + 1) * 128, :], S["cast_t"],
                       wait=[vx.last()], inc=[(c["st"], 16)], tag=f"xms{t}")
        self.p.emit("sync", lambda e, v=c["st"].promised: e.wait_ge(c["st"].sem, v),
                    tag="xmw")
        self.barrier()

        def src_tile(t):
            return S["r"][:, t * DM:(t + 1) * DM], []
        self.build_xT(src_tile, tag=f"s{s}x3")
        self.barrier()

        self.v(lambda v: v.memset(S["bigc"], OOB_BIG), wait=[c["ev"].last()], tag="bigc")
        self.vdrain(inc=[(c["vx"], 1)])
        self.p.dma("sync", S["iota512"], D["iota512"][:, :], inc=[(c["ld"], 16)], tag="iota")
        self.p.emit("sync", lambda e, v=c["ld"].promised: e.wait_ge(c["ld"].sem, v), tag="iotaw")
        self.barrier()
        src = D["gateW"].rearrange("(cc p) e -> p cc e", p=128)
        self.p.dma("sync", S["stage"][:, 0, 0:NCH * 8].rearrange("p (cc e) -> p cc e", e=NE),
                   src, inc=[(c["ld"], 16)],
                   allow_slow_non_contiguous=True, tag="gw")
        self.v(lambda v: v.tensor_copy(out=S["gw_bf"], in_=S["stage"][:, 0, 0:NCH * 8]),
               wait=[c["ld"].last()], inc=[(vx, 1)], tag="gwc")
        pe, fe = self.bank("E")
        for qg in range(4):
            for cc in range(NCH):
                self.t(lambda T, cc_=cc, qg_=qg: T.matmul(
                    pe[0:8, :], S["gw_bf"][:, cc_ * 8:(cc_ + 1) * 8],
                    S["x_T"][:, cc_ * SEQ + qg_ * 512: cc_ * SEQ + (qg_ + 1) * 512],
                    start=(cc_ == 0), stop=(cc_ == NCH - 1)),
                    wait=[vx.last(), fe.last()] if cc == 0 else (),
                    inc=[(mm, 1)] if cc == NCH - 1 else (), tag=f"gs{qg}")
            self.s(lambda sc, qg_=qg: sc.copy(
                out=S["probs_et"][:, qg_ * 512:(qg_ + 1) * 512], in_=pe[0:8, :]),
                wait=[mm.last()], inc=[(ev, 1), (fe, 1)], tag=f"gse{qg}")
        pf, ff = self.bank("F")
        for t in range(NT):
            self.t(lambda T, t_=t: T.transpose(
                pf[:, 0:8], S["probs_et"][:, t_ * 128:(t_ + 1) * 128],
                S["ident_f32"][0:8, 0:8]),
                wait=[ev.last(), ff.last()], inc=[(mm, 1)], tag=f"smT{t}")
            self.v(lambda v: v.reduce_max(S["rms16"][:, 0:1], pf[:, 0:8], AX.X, negate=True),
                   wait=[mm.last()], tag=f"smm{t}")
            self.vdrain(inc=[(vx, 1)])
            self.s(lambda sc: sc.activation(S["prb8"], pf[:, 0:8], AF.Exp,
                                            bias=S["rms16"][:, 0:1], scale=1.0,
                                            accum_out=S["rms16"][:, 1:2]),
                   wait=[vx.last()], inc=[(ev, 1), (ff, 1)], tag=f"smx{t}")
            self.v(lambda v: v.reciprocal(S["rms16"][:, 2:3], S["rms16"][:, 1:2]),
                   wait=[ev.last()], tag=f"smr{t}")
            self.vdrain()
            self.v(lambda v: v.tensor_scalar(S["prb8"], S["prb8"], S["rms16"][:, 2:3],
                                             None, ALU.mult), tag=f"smn{t}")
            self.vdrain()
            self.v(lambda v, t_=t: v.tensor_copy(
                out=S["probsT"][:, s, t_ * 10 + 2: t_ * 10 + 10], in_=S["prb8"]), tag=f"pT{t}")
            self.v(lambda v, t_=t: v.tensor_copy(
                out=S["probsT"][:, s, t_ * 10: t_ * 10 + 1], in_=S["pcol"]), tag=f"pTc{t}")
            self.v(lambda v, t_=t: v.memset(
                S["probsT"][:, s, t_ * 10 + 1: t_ * 10 + 2], float(t_)), tag=f"pTi{t}")
            self.vdrain(inc=[(vx, 1)])
            self.t(lambda T, t_=t: T.transpose(
                pf[0:8, 128:256], S["prb8"], S["ident_f32"]),
                wait=[vx.last(), ff.last()], inc=[(mm, 1)], tag=f"pb{t}")
            self.s(lambda sc, t_=t: sc.copy(
                out=S["probs_et"][:, t_ * 128:(t_ + 1) * 128], in_=pf[0:8, 128:256]),
                wait=[mm.last()], inc=[(ev, 1), (ff, 1)], tag=f"pbe{t}")
        self.barrier()
        sr = S["srch"]
        self.v(lambda v: v.memset(sr[:, 0:1], 0.0), tag="bl0")
        self.v(lambda v: v.memset(sr[:, 1:2], 1.0), tag="bh0")
        self.vdrain()
        for it in range(N_BISECT):
            self.v(lambda v: v.tensor_add(sr[:, 2:3], sr[:, 0:1], sr[:, 1:2]), tag=f"bm{it}")
            self.vdrain()
            self.v(lambda v: v.tensor_scalar(sr[:, 2:3], sr[:, 2:3], 0.5, None, ALU.mult),
                   tag=f"bm2{it}")
            self.vdrain()
            self.v(lambda v: v.tensor_scalar(S["mask_et"], S["probs_et"], sr[:, 2:3],
                                             None, ALU.is_gt, op1=ALU.add,
                                             accum_out=sr[:, 3:4]), tag=f"bc{it}")
            self.vdrain()
            self.v(lambda v: v.tensor_scalar(S["srch_i"][:, 0:1], sr[:, 3:4], 512.0,
                                             None, ALU.is_ge), tag=f"bg{it}")
            self.v(lambda v: v.tensor_scalar(S["srch_i"][:, 1:2], sr[:, 3:4], 512.0,
                                             None, ALU.is_lt), tag=f"bg2{it}")
            self.vdrain()
            self.v(lambda v: v.select(sr[:, 0:1], S["srch_i"][:, 0:1], sr[:, 2:3],
                                      sr[:, 0:1], add_drain=True), tag=f"bsl{it}")
            self.v(lambda v: v.select(sr[:, 1:2], S["srch_i"][:, 1:2], sr[:, 2:3],
                                      sr[:, 1:2], add_drain=True), tag=f"bsh{it}")
            self.vdrain()
        self.v(lambda v: v.tensor_scalar(S["mask_et"], S["probs_et"], sr[:, 0:1],
                                         None, ALU.is_gt), tag="bmf")
        self.vdrain(inc=[(vx, 1)])
        psT, fT = self.bank("T")
        for t in range(NT):
            self.t(lambda T, t_=t: T.transpose(
                psT[:, t_ * 8:(t_ + 1) * 8], S["mask_et"][:, t_ * 128:(t_ + 1) * 128],
                S["ident_bf"][0:8, 0:8]),
                wait=[vx.last(), fT.last()] if t == 0 else (),
                inc=[(mm, 1)] if t == NT - 1 else (), tag=f"mT{t}")
        self.s(lambda sc: sc.copy(out=S["maskT_sb"], in_=psT[:, 0:128]),
               wait=[mm.last()], inc=[(ev, 1), (fT, 1)], tag="mTe")
        pu, fu = self.bank("U")
        for j in range(NT):
            for i in range(j + 1):
                lhs = S["tri_bf"] if i == j else S["ones_bf"]
                self.t(lambda T, i_=i, j_=j, l_=lhs: T.matmul(
                    pu[:, 0:8], l_, S["maskT_sb"][:, i_ * 8:(i_ + 1) * 8],
                    start=(i_ == 0), stop=(i_ == j_)),
                    wait=[ev.last(), fu.last()] if i == 0 else (),
                    inc=[(mm, 1)] if i == j else (), tag=f"cs{j}")
            self.v(lambda v: v.tensor_scalar(S["idx_t"], pu[:, 0:8], 512.0, None, ALU.is_le),
                   wait=[mm.last()], tag=f"ix1{j}")
            self.vdrain()
            self.v(lambda v, j_=j: v.tensor_tensor(
                S["idx_m"], S["idx_t"], S["maskT_sb"][:, j_ * 8:(j_ + 1) * 8], ALU.mult),
                tag=f"ix2{j}")
            self.v(lambda v: v.tensor_scalar(S["idx_t2"], pu[:, 0:8], 1.0, None, ALU.subtract),
                   inc=[(fu, 1)], tag=f"ix3{j}")
            self.vdrain()
            self.v(lambda v, j_=j: v.select(
                S["scat_all"][:, j_ * 8:(j_ + 1) * 8], S["idx_m"], S["idx_t2"], S["bigc"],
                add_drain=True), tag=f"ix4{j}")
            self.vdrain(inc=[(vx, 1)])
        for e in range(NE):
            for j in range(NT):
                Sb = S["cast_t"][:, (j % 2) * 512:(j % 2) * 512 + 512]
                self.v(lambda v, Sb_=Sb, j_=j, e_=e: v.tensor_scalar(
                    Sb_, S["iota512"], S["scat_all"][:, j_ * 8 + e_: j_ * 8 + e_ + 1],
                    None, ALU.is_equal),
                    wait=[vx.last()] if j == 0 else [mm.last()], tag=f"oh{e}{j}")
                self.vdrain(inc=[(vx, 1)])
                self.t(lambda T, Sb_=Sb, j_=j, e_=e, s_=s: T.matmul(
                    pu[0:10, :], S["probsT"][:, s_, j_ * 10:(j_ + 1) * 10],
                    Sb_, start=(j_ == 0), stop=(j_ == NT - 1)),
                    wait=[vx.last()] + ([fu.last()] if j == 0 else []),
                    inc=[(mm, 1)], tag=f"ohm{e}{j}")
            self.s(lambda sc: sc.copy(out=S["sc_rows"], in_=pu[0:10, :]),
                   wait=[mm.last(), (c["st"], c["st"].promised)],
                   inc=[(ev, 1), (fu, 1)], tag=f"ohe{e}")
            self.p.dma("sync", D["rows"][s, e, 0], S["sc_rows"][0:1, :],
                       wait=[ev.last()], inc=[(c["st"], 16)], tag=f"rw0{e}")
            self.p.dma("sync", D["rows"][s, e, 1], S["sc_rows"][1:2, :],
                       inc=[(c["st"], 16)], tag=f"rw1{e}")
            self.p.dma("sync", D["rows"][s, e, 2], S["sc_rows"][2 + e:3 + e, :],
                       inc=[(c["st"], 16)], tag=f"rwp{e}")
        self.barrier()
        self.p.emit("sync", lambda e, v=c["st"].promised: e.wait_ge(c["st"].sem, v), tag="rw")
        self.barrier()

    def moe(self):
        S, D, c = self.S, self.D, self.c
        mm, ev, vx, gx, gd = c["mm"], c["ev"], c["vx"], c["gx"], c["gdma"]
        self.v(lambda v: v.memset(S["stage"][:, 0], 0.0), inc=[(vx, 1)], tag="yz")
        for s in range(2):
            for t in range(NT):
                self.p.dma("sync", D["ymoe"][s][t * 128:(t + 1) * 128, :], S["stage"][:, 0],
                           wait=[vx.last()], inc=[(c["st"], 16)], tag=f"yz{s}{t}")
        banks4 = [self.bank("A"), self.bank("B"), self.bank("C"), self.bank("D")]
        psT, fT = self.bank("T")
        cA = c["cA"]
        self.p.emit("sync", lambda x_, v=c["st"].promised: x_.wait_ge(c["st"].sem, v), tag="yzw")
        self.barrier()
        w1_mm, w2_mm = [], []
        ks = [(e, s) for e in range(NE) for s in range(2)]
        sc_gd = {}

        def prep(kk):
            e, s = ks[kk]
            par = kk % 2
            self.p.dma("sync", S["gidxf"],
                       D["rows"][s, e, 0].rearrange("(j p) -> p j", p=128),
                       inc=[(c["ld"], 16)], wait=[vx.last()],
                       allow_slow_non_contiguous=True, tag=f"gi{e}{s}")
            self.p.dma("sync", S["gidxg"],
                       D["rows"][s, e, 1].rearrange("(j p) -> p j", p=128),
                       inc=[(c["ld"], 16)],
                       allow_slow_non_contiguous=True, tag=f"gj{e}{s}")
            self.p.dma("sync", S["pcomp2"][:, par],
                       D["rows"][s, e, 2].rearrange("(j p) -> p j", p=128),
                       inc=[(c["ld"], 16)],
                       allow_slow_non_contiguous=True, tag=f"gp{e}{s}")
            self.v(lambda v: v.scalar_tensor_tensor(
                S["gidxf"], S["gidxg"], 128.0, S["gidxf"], ALU.mult, ALU.add),
                wait=[c["ld"].last()], tag=f"giv{e}{s}")
            self.vdrain()
            self.v(lambda v, par_=par: v.tensor_copy(out=S["gidx2"][:, par_],
                                                     in_=S["gidxf"]),
                   wait=[(gd, sc_gd[kk - 2])] if kk >= 2 else (),
                   inc=[(vx, 1)], tag=f"gic{e}{s}")
            self.vdrain()
            for j in range(4):
                self.g(lambda g_, j_=j, s_=s, par_=par: g_.indirect_dma_start(
                    S["xin_tok"][:, j_ * DM:(j_ + 1) * DM], None,
                    D["xmoe"][s_][:, :],
                    IndirectOffsetOnAxis(ap=S["gidx2"][:, par_, j_:j_ + 1], axis=0)),
                    wait=[vx.last(), mm.last()], inc=[(gd, 16)], tag=f"gx{e}{s}{j}",
                    raw_inc=True)

        prep(0)
        for kk, (e, s) in enumerate(ks):
            par = kk % 2
            for j in range(4):
                for cc in range(NCH):
                    self.t(lambda T, j_=j, cc_=cc: T.transpose(
                        psT[:, ((j_ * NCH + cc_) % 8) * 128: ((j_ * NCH + cc_) % 8) * 128 + 128],
                        S["xin_tok"][:, j_ * DM + cc_ * 128: j_ * DM + (cc_ + 1) * 128],
                        S["ident_bf"]),
                        wait=[gd.last(), fT.last()] if (j == 0 and cc == 0) else
                             ([fT.last()] if (j * NCH + cc) % 8 == 0 else ()),
                        inc=[(mm, 1)] if (j * NCH + cc) % 8 == 7 else (), tag=f"xt{e}{s}")
                    if (j * NCH + cc) % 8 == 7:
                        u0 = (j * NCH + cc) - 7
                        def ev_t(sc, u0_=u0):
                            outs = []
                            for u in range(8):
                                jj, ccx = divmod(u0_ + u, NCH)
                                outs.append(sc.copy(
                                    out=S["xin_T"][:, ccx * 512 + jj * 128: ccx * 512 + (jj + 1) * 128],
                                    in_=psT[:, u * 128:(u + 1) * 128]))
                            return outs[-1]
                        self.s(ev_t, wait=[mm.last()], inc=[(ev, 1), (fT, 1)], tag=f"xte{e}{s}", multi=True)
            if kk + 1 < len(ks):
                prep(kk + 1)
            for hs in range(8):
                k1 = len(w1_mm)
                wpar = hs % 2
                wAp = c["wA0"] if wpar == 0 else c["wA1"]
                self.p.dma("sync",
                           S["w1col"][:, wpar].rearrange("p (cc x) -> p cc x", x=512),
                           D["w1_c"][e][:, :, hs * 512:(hs + 1) * 512].rearrange(
                               "cc p x -> p cc x"),
                           inc=[(wAp, 16)],
                           wait=[wAp.last()] + ([(mm, w1_mm[k1 - 2])] if k1 >= 2 else []),
                           tag=f"w1l{e}{hs}")
                w_rdy = wAp.last()
                for st in range(4):
                    ph, fh = banks4[st]
                    for cc in range(NCH):
                        self.t(lambda T, st_=st, cc_=cc, par_=wpar, ps_=ph: T.matmul(
                            ps_[:, :], S["xin_T"][:, cc_ * 512 + st_ * 128: cc_ * 512 + (st_ + 1) * 128],
                            S["w1col"][:, par_, cc_ * 512:(cc_ + 1) * 512],
                            start=(cc_ == 0), stop=(cc_ == NCH - 1)),
                            wait=[w_rdy, ev.last(), fh.last()] if cc == 0 else (),
                            inc=[(mm, 1)] if cc == NCH - 1 else (),
                            tag=f"w1m{e}{hs}{st}")
                    self.s(lambda sc, ps_=ph, st_=st: sc.activation(
                        S["cast_t"][:, (st_ % 2) * 512:(st_ % 2) * 512 + 512],
                        ps_[:, :], AF.Gelu, bias=0.0, scale=1.0),
                        wait=[mm.last()], inc=[(ev, 1), (fh, 1)], tag=f"gl{e}{hs}{st}")
                    for u in range(4):
                        self.t(lambda T, st_=st, u_=u: T.transpose(
                            psT[:, (st_ % 2) * 512 + u_ * 128: (st_ % 2) * 512 + (u_ + 1) * 128],
                            S["cast_t"][:, (st_ % 2) * 512 + u_ * 128:(st_ % 2) * 512 + (u_ + 1) * 128],
                            S["ident_bf"]),
                            wait=[ev.last(), fT.last()] if u == 0 else (),
                            inc=[(mm, 1)] if u == 3 else (), tag=f"ht{e}{hs}{st}")
                    def ev_h(sc, hs_=hs, st_=st):
                        o = None
                        for u in range(4):
                            hc = hs_ * 4 + u
                            o = sc.copy(
                                out=S["h_e"][:, hc * 512 + st_ * 128: hc * 512 + (st_ + 1) * 128],
                                in_=psT[:, (st_ % 2) * 512 + u * 128:(st_ % 2) * 512 + (u + 1) * 128])
                        return o
                    self.s(ev_h, wait=[mm.last()], inc=[(ev, 1), (fT, 1)], tag=f"he{e}{hs}{st}", multi=True)
                w1_mm.append(mm.promised)
            for half in range(2):
                for hc in range(EH // 128):
                    k2 = len(w2_mm)
                    wpar = k2 % 4
                    wAp = [c["wA0"], c["wA1"], c["wB0"], c["wB1"]][wpar]
                    self.p.dma("sync", S["w2_str"][:, wpar],
                               D["w2_c"][e, hc][:, half * 512:(half + 1) * 512],
                               inc=[(wAp, 16)],
                               wait=[wAp.last()] +
                                    ([(mm, w2_mm[k2 - 4])] if k2 >= 4 else []),
                               tag=f"w2l{e}{half}{hc}")
                    w_rdy = wAp.last()
                    for st in range(4):
                        ph, fh = banks4[st]
                        self.t(lambda T, hc_=hc, st_=st, par_=wpar, ps_=ph: T.matmul(
                            ps_[:, :], S["h_e"][:, hc_ * 512 + st_ * 128: hc_ * 512 + (st_ + 1) * 128],
                            S["w2_str"][:, par_, :],
                            start=(hc_ == 0), stop=(hc_ == EH // 128 - 1)),
                            wait=[w_rdy, ev.last()] + ([fh.last()] if hc == 0 else []),
                            inc=[(mm, 1)] if st == 3 else (),
                            tag=f"w2m{e}{half}{st}")
                    w2_mm.append(mm.promised)
                for st in range(4):
                    ph, fh = banks4[st]
                    self.v(lambda v, ps_=ph, st_=st, par_=par: v.tensor_scalar(
                        S["y_ev4"][:, st_, :], ps_[:, :],
                        S["pcomp2"][:, par_, st_:st_ + 1], None, ALU.mult),
                        wait=[mm.last(), gd.last()] if st == 0 else [mm.last()],
                        inc=[(vx, 1), (fh, 1)], tag=f"w2e{e}{half}{st}")
                self.vdrain()
                for st in range(4):
                    self.g(lambda g_, s_=s, st_=st, h_=half, par_=par: g_.indirect_dma_start(
                        D["ymoe"][s_][:, :],
                        IndirectOffsetOnAxis(ap=S["gidx2"][:, par_, st_:st_ + 1], axis=0),
                        S["y_ev4"][:, st_], None, compute_op=ALU.add,
                        element_offset=h_ * 512),
                        wait=[vx.last(), gd.last()], inc=[(gd, 16)], tag=f"sc{e}{s}{st}",
                        raw_inc=True)
            sc_gd[kk] = gd.promised
        self.barrier()
        self.g(lambda g_, v=gd.promised: g_.wait_ge(gd.sem, v), inc=[(gx, 1)], tag="gdw")
        self.barrier()

    def final(self):
        S, D, c = self.S, self.D, self.c
        vx, ev = c["vx"], c["ev"]
        ycnt = {}
        idx = 0
        for s in range(2):
            for t in range(NT):
                sl = idx % 2
                buf = S["stage"] if sl == 0 else S["mc_st"]
                ld = c["ld0"] if sl == 0 else c["ld1"]
                yt = buf[:, 0]
                self.p.dma("sync", yt, D["ymoe"][s][t * 128:(t + 1) * 128, :],
                           inc=[(ld, 16)],
                           wait=[ld.last()] +
                                ([(c["fin"], ycnt[idx - 2])] if idx >= 2 else
                                 [ev.last(), c["fin"].last()]),
                           tag=f"fy{s}{t}")
                if s == 0:
                    self.p.dma("sync", buf[:, 1], D["r3A"][t * 128:(t + 1) * 128, :],
                               inc=[(ld, 16)], tag=f"fr{s}{t}")
                    res = buf[:, 1]
                else:
                    res = S["r"][:, t * DM:(t + 1) * DM]
                self.v(lambda v, r_=res, y_=yt: v.tensor_add(y_, y_, r_),
                       wait=[ld.last()], inc=[(vx, 1)], tag=f"fa{s}{t}")
                self.vdrain()
                tok = self.ln_tile(yt, wait=[vx.last()])
                self.p.dma("sync", D["y"][s][t * 128:(t + 1) * 128, :], yt,
                           wait=[tok], inc=[(c["fin"], 16)], tag=f"fo{s}{t}")
                ycnt[idx] = c["fin"].promised
                idx += 1
        self.p.emit("sync", lambda e, v=c["fin"].promised: e.wait_ge(c["fin"].sem, v), tag="finw")

    def build(self):
        self.load_consts()
        if self.phase_done("consts"):
            return self.dump_r_and_finish()
        self.build_ffn_cache()
        self.build_moe_cache()
        for s in range(2):
            self._seq = s
            self.load_T12()
            self.l0_attn(s)
            if self.phase_done(f"l0a:{s}"):
                return self.dump_r_and_finish()
            self.l0_ffn(s)
            if self.phase_done(f"l0f:{s}"):
                return self.dump_r_and_finish()
            self.l1_attn(s)
            if self.phase_done(f"l1a:{s}"):
                return self.dump_r_and_finish()
            self.routing(s)
            if self.phase_done(f"rout:{s}"):
                return self.dump_r_and_finish()
            if s == 0:
                self.p.dma("sync",
                           self.D["r3A"].rearrange("(t p) d -> p t d", p=128),
                           self.S["r"].rearrange("p (t d) -> p t d", d=DM),
                           wait=[self.c["ev"].last(), self.c["vx"].last()],
                           inc=[(self.c["st"], 16)], tag="r3sp")
                self.p.emit("sync", lambda e, v=self.c["st"].promised: e.wait_ge(
                    self.c["st"].sem, v))
                self.barrier()
        self.moe()
        if self.phase_done("moe"):
            return self.dump_r_and_finish()
        self.final()


def build_program(stop_after=None):
    b = Build(stop_after=stop_after)
    b.build()
    b.p.finalize()
    return b


def kernel(**inputs):
    pre = host_prepare(inputs)
    x = np.asarray(inputs["L0_attn_x"], np.float32)
    b = build_program()
    in_maps = []
    for core in range(NCORE):
        m = dict(pre)
        m["x"] = np.ascontiguousarray(x[2 * core: 2 * core + 2])
        in_maps.append(m)
    res = run_bass_kernel_spmd(b.nc, in_maps, core_ids=list(range(NCORE)))
    out = np.concatenate([r["y"] for r in res.results], axis=0)
    return out.reshape(16, SEQ, DM)
